# version 01
# baseline (speedup 1.0000x reference)
import contextlib
import numpy as np
import ml_dtypes
import concourse.bass as bass
import concourse.mybir as mybir
from concourse.ap import AP

BF16 = mybir.dt.bfloat16
F32 = mybir.dt.float32
AF = mybir.ActivationFunctionType
OP = mybir.AluOpType
AX = mybir.AxisListType

D = 1024
S = 4096
SQ = 2048
NP = 8
NCH = 8
MR = 1024
PAD = 254
CP = 2049 + 2 * PAD
GQB = 4
NG = SQ // (GQB * 128)
BAND = MR + 127
NKB = S // 128
FSL = 128
NFS = SQ // FSL

ENGINES = ("tensor", "scalar", "vector", "sync", "gpsimd")


def band_range(qb):
    q0 = qb * 128
    kb_lo = max(0, -(-(q0 - BAND) // 128))
    kb_hi = min(NKB - 1, (q0 + BAND) // 128)
    return kb_lo, kb_hi


class Prog:
    def __init__(self, nc, es):
        self.nc = nc
        self.ops = {e: [] for e in ENGINES}
        self.last_has_inc = {}
        self.cnt = {}
        self.sems = {}
        for s in ("dg", "ds", "pe", "ac", "dv", "dsg", "dsc", "dss0", "dss1",
                  "dso", "dgm", "dgz", "dgw", "dgo", "dgx0", "dgx1", "dgy0", "dgy1", "dgf", "dgd",
                  "hv", "ha", "ht", "hg"):
            self.sems[s] = es.enter_context(nc.semaphore(s))
            self.cnt[s] = 0

    def wait(self, eng, sem, val):
        if val <= 0:
            return
        h = self.sems[sem]
        self.ops[eng].append(lambda e, h=h, v=val: e.wait_ge(h, v))
        self.last_has_inc[eng] = False

    def inc_last(self, eng, sem, n=1):
        assert not self.last_has_inc.get(eng), f"double inc on {eng}"
        h = self.sems[sem]
        self.cnt[sem] += n
        f = self.ops[eng].pop()
        self.ops[eng].append(lambda e, f=f, h=h, n=n: f(e).then_inc(h, n))
        self.last_has_inc[eng] = True
        return self.cnt[sem]

    def dbg_barrier(self):
        for eng in ("tensor", "scalar", "vector", "gpsimd", "sync"):
            self.wait(eng, "dgd", self.cnt["dgd"])

    def hop(self, eng):
        sem = {"vector": "hv", "scalar": "ha", "tensor": "ht", "gpsimd": "hg"}[eng]
        h = self.sems[sem]
        self.cnt[sem] += 1
        v = self.cnt[sem]
        assert not self.last_has_inc.get(eng), f"double inc (hop) on {eng}"
        f = self.ops[eng].pop()
        self.ops[eng].append(lambda e, f=f, h=h: f(e).then_inc(h, 1))
        self.ops[eng].append(lambda e, h=h, v=v: e.wait_ge(h, v))
        self.last_has_inc[eng] = False

    def do(self, eng, fn, inc=None, inc_n=1):
        if inc is None:
            self.ops[eng].append(fn)
            self.last_has_inc[eng] = False
            return None
        h = self.sems[inc]
        self.cnt[inc] += inc_n
        self.ops[eng].append(lambda e, fn=fn, h=h, n=inc_n: fn(e).then_inc(h, n))
        self.last_has_inc[eng] = True
        return self.cnt[inc]


def _view(tensor_ap, off_bytes, shape, dt):
    es_ = mybir.dt.size(dt)
    nel = int(np.prod(shape[1:]))
    assert off_bytes % 2 == 0
    v = tensor_ap[0:shape[0], off_bytes // 2: off_bytes // 2 + (nel * es_) // 2]
    if dt != BF16:
        v = v.bitcast(dt)
    if len(shape) == 2:
        return v
    if len(shape) == 3:
        return v.rearrange("p (a b) -> p a b", a=shape[1])
    if len(shape) == 4:
        return v.rearrange("p (a b c) -> p a b c", a=shape[1], b=shape[2])
    raise ValueError(shape)


def build_program(debug=0):
    nc = bass.Bass("TRN2", target_bir_lowering=False)
    es = contextlib.ExitStack()

    dd = {}
    def din(name, shape, dt=F32):
        dd[name] = nc.dram_tensor(name, shape, dt, kind="ExternalInput").ap()
    din("xT", [D, S])
    din("c_col", [128, 8])
    din("wmod", [D, 6 * D])
    din("wq_p", [NP, D, 128]); din("wk_p", [NP, D, 128]); din("wv_p", [NP, D, 128])
    din("woT", [D, D])
    din("w1T", [D, 4 * D])
    din("w2T", [4 * D, D])
    din("relg", [128, 2049])
    din("mzp", [NP, 128, 65])
    din("bq_c", [128, 8]); din("bk_c", [128, 8]); din("bv_c", [128, 8]); din("bo_c", [128, 8])
    din("b1_c", [128, 32]); din("b2_c", [128, 8])
    din("galpha", [1, 1])
    din("ident", [128, 128], BF16)
    din("ones_r", [1, 128], BF16)
    din("ones_rf", [1, 128], F32)
    din("onesc", [128, 1], BF16)
    din("blkd", [128, 2], BF16)
    din("o642", [64, 2], BF16)
    din("i2", [2, 2], BF16)
    din("sel_r", [4, 512], BF16)
    din("sel_c", [4, 4], BF16)
    out_cm = nc.dram_tensor("out_cm", [D, SQ], F32, kind="ExternalOutput").ap()
    attn_sp = nc.dram_tensor("attn_sp", [NCH, 128, SQ], BF16, kind="Internal").ap()

    dbg = {}
    if debug:
        def dout(name, shape, dt):
            dbg[name] = nc.dram_tensor(name, shape, dt, kind="ExternalOutput").ap()
        dout("d_mod", [1, 6 * D], BF16)
        dout("d_h1", [128, NCH, S], BF16)
        dout("d_qh", [128, SQ], BF16)
        dout("d_k", [128, S], BF16)
        dout("d_vp", [128, NKB, 2, 65], BF16)
        dout("d_pg", [128, CP], BF16)
        dout("d_rg", [128, 2304], BF16)
        dout("d_negb", [128, 1], F32)
        dout("d_edge", [4, SQ], BF16)
        dout("d_o65", [65, 2, 512], F32)
        dout("d_comb", [128, NP, SQ], BF16)
        dout("d_attn", [NCH, 128, SQ], BF16)
        dout("d_s1", [128, NCH, 4 * D], BF16)
        dout("d_h3", [128, NCH, FSL], BF16)

    sb = {}
    def sbuf(name, shape, dt):
        sb[name] = es.enter_context(nc.sbuf_tensor("s_" + name, shape, dt))
    sbuf("ident", [128, 128], BF16)
    sbuf("ones_r", [1, 128], BF16)
    sbuf("onesc", [128, 1], BF16)
    sbuf("blkd", [128, 2], BF16)
    sbuf("o642", [64, 2], BF16)
    sbuf("i2", [2, 2], BF16)
    sbuf("sel_r", [4, 512], BF16)
    sbuf("sel_c", [4, 4], BF16)
    sbuf("c_col", [128, 8], F32)
    sbuf("silu_c", [128, 8], F32)
    sbuf("ones_rf", [1, 128], F32)
    for n in ("sa", "ca1", "ga", "sbv", "cb1", "gb", "bq_h", "bk_c", "bv_c", "bo_c", "b2_c"):
        sbuf(n, [128, 8], F32)
    sbuf("b1_c", [128, 32], F32)
    sbuf("a1c", [128, 32], F32)
    sbuf("a2c", [128, 8], F32)
    sbuf("gate", [128, 1], F32); sbuf("gate1m", [128, 1], F32)
    sbuf("relg", [128, 2049], BF16)
    sbuf("lnrel", [2, 1], F32)
    sbuf("tiny", [2, 10], F32)
    sbuf("tinyb", [2, 2], BF16)
    sbuf("rowb", [1, 512], BF16)
    sbuf("rowm", [1, 512], F32); sbuf("rowr", [1, 512], F32); sbuf("rowmr", [1, 512], F32)
    sbuf("tmpa", [128, 512], F32)
    sbuf("sqs", [128, 512], BF16)
    sbuf("xslab", [128, 2, NCH, 512], F32)
    sbuf("o65", [65, 2, 512], F32)
    sbuf("ct", [64, 2, 512], BF16)
    sbuf("negb", [128, 1], F32)
    sbuf("edges4", [4, SQ], BF16)
    sbuf("eW", [128, 4], BF16)
    sbuf("mz", [128, 65], BF16)

    ARENA_B = 147584
    sbuf("arena", [128, ARENA_B // 2], BF16)
    ar = sb["arena"][:]

    V = {}
    V["h1"] = _view(ar, 0, [128, NCH, S], BF16)
    V["s1"] = _view(ar, 0, [128, NCH, 4 * D], BF16)
    B0 = 65536
    V["mrow"] = _view(ar, B0, [1, 6 * D], F32)
    V["wmod_ch"] = _view(ar, B0 + 24576, [128, 2, 3 * D], F32)
    V["xbf1"] = _view(ar, B0 + 24576, [128, NCH, 512], BF16)
    V["xsq1"] = _view(ar, B0 + 32768, [128, NCH, 512], BF16)
    V["kk"] = _view(ar, B0, [128, S], BF16)
    V["qh"] = _view(ar, B0 + 8192, [128, SQ], BF16)
    V["vp"] = _view(ar, B0 + 12288, [128, NKB, 2, 65], BF16)
    V["vf"] = _view(ar, B0 + 20608, [128, 1024], BF16)
    V["pg"] = _view(ar, B0 + 22656, [128, 2, CP], BF16)
    V["ee"] = _view(ar, B0 + 32896, [128, 2, 1024], BF16)
    V["sig"] = _view(ar, B0 + 36992, [128, 512], BF16)
    V["sigm"] = _view(ar, B0 + 38016, [128, 512], BF16)
    V["wqkv_wq"] = _view(ar, B0 + 39040, [128, NCH, 128], BF16)
    V["wqkv_wk"] = _view(ar, B0 + 41088, [128, NCH, 128], BF16)
    V["wqkv_wv"] = _view(ar, B0 + 43136, [128, NCH, 128], BF16)
    V["rg"] = _view(ar, B0 + 45184, [128, 18432], BF16)
    V["wo"] = _view(ar, B0, [128, NCH, D], BF16)
    V["stag"] = _view(ar, B0 + 16384, [128, 2, SQ], BF16)
    V["s2"] = _view(ar, B0, [128, 32, D], BF16)
    xs_ap = sb["xslab"][:].rearrange("p a b c -> p (a b c)").bitcast(BF16)
    V["comb"] = _view(xs_ap, 0, [128, NP, SQ], BF16)
    V["wsl"] = _view(xs_ap, 0, [128, 2, 4 * D], BF16)
    V["fx"] = _view(xs_ap, 0, [128, NCH, FSL], F32)
    V["x2sl"] = _view(xs_ap, 4096, [128, NCH, FSL], F32)
    V["h3sl"] = _view(xs_ap, 8192, [128, NCH, FSL], BF16)
    V["h2sl"] = _view(xs_ap, 10240, [128, 32, FSL], BF16)
    V["outsl"] = _view(xs_ap, 18432, [128, NCH, FSL], F32)
    V["ffnc"] = _view(xs_ap, 22528, [128, FSL], BF16)
    V["fat"] = _view(xs_ap, 22784, [128, NCH, FSL], BF16)
    V["xbf2"] = _view(xs_ap, 24832, [128, NCH, FSL], BF16)
    V["xsq2"] = _view(xs_ap, 26880, [128, NCH, FSL], BF16)
    assert 26880 + 2048 <= 32768

    pA = es.enter_context(nc.psum_tensor("pA", [128, 2048], F32))
    pB = es.enter_context(nc.psum_tensor("pB", [128, 2048], F32))

    p = Prog(nc, es)
    gen(p, dd, sb, V, pA, pB, dbg, out_cm, attn_sp, debug)

    with nc.Block() as block:
        @block.tensor
        def _(e):
            for f in p.ops["tensor"]:
                f(e)
        @block.scalar
        def _(e):
            for f in p.ops["scalar"]:
                f(e)
        @block.vector
        def _(e):
            for f in p.ops["vector"]:
                f(e)
        @block.sync
        def _(e):
            for f in p.ops["sync"]:
                f(e)
        @block.gpsimd
        def _(e):
            for f in p.ops["gpsimd"]:
                f(e)
    es.close()
    return nc


def gen(p, dd, sb, V, pA, pB, dbg, out_cm, attn_sp, debug):
    for name, dst in (("ident", "ident"), ("ones_r", "ones_r"), ("ones_rf", "ones_rf"),
                      ("onesc", "onesc"),
                      ("blkd", "blkd"), ("o642", "o642"), ("i2", "i2"),
                      ("sel_r", "sel_r"), ("sel_c", "sel_c"),
                      ("c_col", "c_col"), ("bq_c", "bq_h"), ("bk_c", "bk_c"),
                      ("bv_c", "bv_c"), ("bo_c", "bo_c"), ("b1_c", "b1_c"),
                      ("b2_c", "b2_c")):
        p.do("sync", lambda e, d=dd[name], s=sb[dst]: e.dma_start(out=s[:], in_=d),
             inc="ds", inc_n=16)
    p.do("sync", lambda e: e.dma_start(out=sb["gate"][0:1, :], in_=dd["galpha"]),
         inc="ds", inc_n=16)
    ds_consts = p.cnt["ds"]

    dg_relg = p.do("gpsimd", lambda e: e.dma_start(out=sb["relg"][:], in_=dd["relg"]),
                   inc="dgm", inc_n=16)
    wmod_idx = [(h, c) for h in range(2) for c in range(NCH)]
    wmod_loads = {}
    def emit_wmod_load(i, wait_pe=None):
        half, c = wmod_idx[i]
        if wait_pe is not None:
            p.wait("gpsimd", "pe", wait_pe)
        wmod_loads[(half, c)] = ("dgx0" if c % 2 == 0 else "dgx1",
            p.do("gpsimd", lambda e, half=half, c=c: e.dma_start(
                out=V["wmod_ch"][:, c % 2, :],
                in_=dd["wmod"][c * 128:(c + 1) * 128, half * 3 * D:(half + 1) * 3 * D]),
                inc="dgx0" if c % 2 == 0 else "dgx1", inc_n=16))
    emit_wmod_load(0)
    emit_wmod_load(1)
    x_loads = {}
    def emit_x_load(sl, wait_dv=None):
        ap = dd["xT"][:, sl * 512:(sl + 1) * 512].rearrange("(c p) t -> p c t", p=128)
        if wait_dv is not None:
            p.wait("gpsimd", "dv", wait_dv)
        x_loads[sl] = ("dgy0" if sl % 2 == 0 else "dgy1",
            p.do("gpsimd", lambda e, ap=ap, sl=sl: e.dma_start(
                out=sb["xslab"][:, sl % 2, :, :], in_=ap),
                inc="dgy0" if sl % 2 == 0 else "dgy1", inc_n=16))
    emit_x_load(0)
    emit_x_load(1)

    p.wait("scalar", "ds", ds_consts)
    p.do("scalar", lambda e: e.activation(out=sb["tmpa"][:, 0:8], in_=sb["c_col"][:],
                                          func=AF.Sigmoid))
    p.do("scalar", lambda e: e.activation(out=sb["tmpa"][0:1, 8:9], in_=sb["gate"][0:1, :],
                                          func=AF.Sigmoid), inc="ac")
    p.wait("vector", "ac", p.cnt["ac"])
    dv_silu = p.do("vector", lambda e: e.tensor_tensor(
        out=sb["silu_c"][:], in0=sb["c_col"][:], in1=sb["tmpa"][:, 0:8], op=OP.mult), inc="dv")
    p.wait("tensor", "dv", dv_silu)
    for half in range(2):
        if half == 1:
            p.wait("tensor", "dv", p.cnt["dv"])
        for c in range(NCH):
            wsem, wval = wmod_loads[(half, c)]
            p.wait("tensor", wsem, wval)
            for npos in range(6):
                ps = pA[0:1, npos * 512:(npos + 1) * 512] if npos < 4 else \
                     pB[0:1, (npos - 4) * 512:(npos - 3) * 512]
                p.do("tensor", lambda e, ps=ps, c=c, npos=npos: e.matmul(
                    ps, sb["silu_c"][:, c:c + 1],
                    V["wmod_ch"][:, c % 2, npos * 512:(npos + 1) * 512],
                    start=(c == 0), stop=(c == NCH - 1)),
                    inc="pe" if npos == 5 else None)
            gi = half * NCH + c
            if gi + 2 < 16:
                emit_wmod_load(gi + 2, wait_pe=p.cnt["pe"])
        p.wait("vector", "pe", p.cnt["pe"])
        for npos in range(6):
            ps = pA[0:1, npos * 512:(npos + 1) * 512] if npos < 4 else \
                 pB[0:1, (npos - 4) * 512:(npos - 3) * 512]
            p.do("vector", lambda e, ps=ps, half=half, npos=npos: e.tensor_copy(
                out=V["mrow"][0:1, half * 3 * D + npos * 512: half * 3 * D + (npos + 1) * 512],
                in_=ps), inc="dv")
    dv_mod = p.cnt["dv"]

    p.wait("scalar", "dv", dv_mod)
    for idx in (2, 5):
        sl = slice(idx * D, (idx + 1) * D)
        for ph in range(2):
            p.wait("scalar", "dv", p.cnt["dv"])
            p.do("scalar", lambda e, sl=sl, ph=ph: e.activation(
                out=sb["rowr"][0:1, :], in_=V["mrow"][0:1, sl][:, ph * 512:(ph + 1) * 512],
                func=AF.Abs), inc="ac")
            p.wait("vector", "ac", p.cnt["ac"])
            p.do("vector", lambda e: e.tensor_scalar(
                out=sb["rowr"][0:1, :], in0=sb["rowr"][0:1, :],
                scalar1=0.25, scalar2=None, op0=OP.max))
            p.do("vector", lambda e, sl=sl, ph=ph: e.tensor_scalar(
                out=sb["rowm"][0:1, :], in0=V["mrow"][0:1, sl][:, ph * 512:(ph + 1) * 512],
                scalar1=0.0, scalar2=2.0, op0=OP.is_ge, op1=OP.mult))
            p.hop("vector")
            p.do("vector", lambda e, sl=sl, ph=ph: e.scalar_tensor_tensor(
                out=V["mrow"][0:1, sl][:, ph * 512:(ph + 1) * 512],
                in0=sb["rowm"][0:1, :], scalar=-1.0,
                in1=sb["rowr"][0:1, :], op0=OP.add, op1=OP.mult), inc="dv")
    dv_clamp = p.cnt["dv"]

    p.wait("tensor", "dv", dv_clamp)
    names = ("sa", "ca1", "ga", "sbv", "cb1", "gb")
    for vi, vn in enumerate(names):
        if vi > 0:
            p.wait("tensor", "dv", p.cnt["dv"])
        for c in range(NCH):
            p.do("tensor", lambda e, vi=vi, c=c: e.matmul(
                pB[:, 1024 + c:1025 + c],
                V["mrow"][0:1, vi * D + c * 128: vi * D + (c + 1) * 128],
                sb["ones_rf"][0:1, 0:1], start=True, stop=True),
                inc="pe" if c == NCH - 1 else None)
        p.wait("vector", "pe", p.cnt["pe"])
        if vn in ("ca1", "cb1"):
            p.do("vector", lambda e, vn=vn: e.tensor_scalar(
                out=sb[vn][:], in0=pB[:, 1024:1032], scalar1=1.0, scalar2=None,
                op0=OP.add), inc="dv")
        else:
            p.do("vector", lambda e, vn=vn: e.tensor_copy(
                out=sb[vn][:], in_=pB[:, 1024:1032]), inc="dv")
    p.do("vector", lambda e: e.tensor_copy(out=sb["tinyb"][0:1, 0:1], in_=sb["tmpa"][0:1, 8:9]),
         inc="dv")
    p.wait("tensor", "dv", p.cnt["dv"])
    p.do("tensor", lambda e: e.matmul(pB[:, 1024:1025], sb["ones_r"][:],
                                      sb["tinyb"][0:1, 0:1], start=True, stop=True), inc="pe")
    p.wait("vector", "pe", p.cnt["pe"])
    p.do("vector", lambda e: e.tensor_copy(out=sb["gate"][:], in_=pB[:, 1024:1025]))
    p.do("vector", lambda e: e.tensor_scalar(
        out=sb["gate1m"][:], in0=pB[:, 1024:1025], scalar1=-1.0, scalar2=1.0,
        op0=OP.mult, op1=OP.add))
    p.do("vector", lambda e: e.tensor_scalar(
        out=sb["bq_h"][:], in0=sb["bq_h"][:], scalar1=0.125, scalar2=None, op0=OP.mult),
        inc="dv")
    dv_modcols = p.cnt["dv"]
    if debug == 1:
        p.wait("gpsimd", "dv", dv_clamp)
        p.do("gpsimd", lambda e: e.dma_start(out=dbg["d_mod"], in_=V["mrow"]),
             inc="dgd", inc_n=16)
        p.dbg_barrier()

    for sl in range(8):
        xs = sb["xslab"][:, sl % 2, :, :]
        if sl % 2 == 0:
            rsum, rsq = pB[0:1, 1024:1536], pB[0:1, 1536:2048]
            rrep, rmrep = pB[:, 0:512], pB[:, 512:1024]
        else:
            rsum, rsq = pA[0:1, 0:512], pA[0:1, 512:1024]
            rrep, rmrep = pA[:, 1024:1536], pA[:, 1536:2048]
        xsem, xval = x_loads[sl]
        p.wait("vector", xsem, xval)
        if sl == 0:
            p.wait("vector", "dv", dv_modcols)
        p.do("vector", lambda e, xs=xs: e.tensor_copy(out=V["xbf1"], in_=xs))
        p.hop("vector")
        p.do("vector", lambda e: e.tensor_tensor(
            out=V["xsq1"], in0=V["xbf1"], in1=V["xbf1"], op=OP.mult), inc="dv")
        p.wait("tensor", "dv", p.cnt["dv"])
        for c in range(NCH):
            p.do("tensor", lambda e, c=c, rsum=rsum: e.matmul(
                rsum, sb["onesc"][:], V["xbf1"][:, c, :],
                start=(c == 0), stop=(c == NCH - 1)))
        for c in range(NCH):
            p.do("tensor", lambda e, c=c, rsq=rsq: e.matmul(
                rsq, sb["onesc"][:], V["xsq1"][:, c, :],
                start=(c == 0), stop=(c == NCH - 1)),
                inc="pe" if c == NCH - 1 else None)
        p.wait("vector", "pe", p.cnt["pe"])
        emit_ln_rows(p, sb, rsum, rsq, 512)
        p.wait("tensor", "dv", p.cnt["dv"])
        p.do("tensor", lambda e, rrep=rrep: e.matmul(
            rrep, sb["ones_r"][:], sb["rowb"][:], start=True, stop=True), inc="pe")
        p.wait("vector", "pe", p.cnt["pe"])
        p.do("vector", lambda e: e.tensor_copy(out=sb["rowb"][:], in_=sb["rowmr"][:]), inc="dv")
        p.wait("tensor", "dv", p.cnt["dv"])
        p.do("tensor", lambda e, rmrep=rmrep: e.matmul(
            rmrep, sb["ones_r"][:], sb["rowb"][:], start=True, stop=True), inc="pe")
        p.wait("vector", "pe", p.cnt["pe"])
        for c in range(NCH):
            p.do("vector", lambda e, xs=xs, c=c, rrep=rrep: e.tensor_tensor(
                out=sb["tmpa"][:], in0=xs[:, c, :], in1=rrep, op=OP.mult))
            p.hop("vector")
            p.do("vector", lambda e, c=c, rmrep=rmrep: e.tensor_tensor(
                out=sb["tmpa"][:], in0=sb["tmpa"][:], in1=rmrep, op=OP.subtract))
            p.hop("vector")
            p.do("vector", lambda e, sl=sl, c=c: e.tensor_scalar(
                out=V["h1"][:, c, sl * 512:(sl + 1) * 512], in0=sb["tmpa"][:],
                scalar1=sb["ca1"][:, c:c + 1], scalar2=sb["sa"][:, c:c + 1],
                op0=OP.mult, op1=OP.add), inc="dv" if c == NCH - 1 else None)
            if c < NCH - 1:
                p.hop("vector")
        if sl + 2 < 8:
            emit_x_load(sl + 2, wait_dv=p.cnt["dv"])
    dv_h1 = p.cnt["dv"]
    if debug == 1:
        p.wait("gpsimd", "dv", dv_h1)
        p.do("gpsimd", lambda e: e.dma_start(out=dbg["d_h1"], in_=V["h1"]),
             inc="dgd", inc_n=16)
        p.dbg_barrier()

    p.wait("scalar", "dgm", dg_relg)
    p.wait("tensor", "dv", dv_h1)
    for ch in range(5):
        cs = slice(ch * 512, (ch + 1) * 512) if ch < 4 else slice(2048, 2049)
        n = 512 if ch < 4 else 1
        p.wait("scalar", "pe", p.cnt["pe"])
        p.do("scalar", lambda e, cs=cs, n=n: e.activation(
            out=sb["sqs"][0:64, 0:n], in_=sb["relg"][0:64, cs], func=AF.Square), inc="ac")
        p.wait("tensor", "ac", p.cnt["ac"])
        if ch > 0:
            p.wait("tensor", "dv", p.cnt["dv"])
        p.do("tensor", lambda e, n=n: e.matmul(
            pB[0:2, 1024:1024 + n], sb["o642"][:], sb["sqs"][0:64, 0:n],
            start=True, stop=True), inc="pe")
        p.wait("vector", "pe", p.cnt["pe"])
        if ch == 0:
            p.do("vector", lambda e: e.reduce_max(
                out=sb["lnrel"][:], in_=pB[0:2, 1024:1536], axis=AX.X), inc="dv")
        else:
            p.do("vector", lambda e, n=n: e.reduce_max(
                out=sb["tiny"][:, 9:10], in_=pB[0:2, 1024:1024 + n], axis=AX.X))
            p.hop("vector")
            p.do("vector", lambda e: e.tensor_tensor(
                out=sb["lnrel"][:], in0=sb["lnrel"][:], in1=sb["tiny"][:, 9:10],
                op=OP.max), inc="dv")
    p.wait("scalar", "dv", p.cnt["dv"])
    p.do("scalar", lambda e: e.activation(out=sb["lnrel"][:], in_=sb["lnrel"][:],
                                          func=AF.Ln), inc="ac")
    ac_lnrel = p.cnt["ac"]
    p.do("vector", lambda e: e.memset(sb["eW"][:], 0.0))
    p.hop("vector")
    p.do("vector", lambda e: e.tensor_copy(out=sb["eW"][0:64, 0:1], in_=sb["relg"][0:64, 0:1]))
    p.do("vector", lambda e: e.tensor_copy(out=sb["eW"][0:64, 1:2],
                                           in_=sb["relg"][0:64, 2048:2049]))
    p.do("vector", lambda e: e.tensor_copy(out=sb["eW"][64:128, 2:3],
                                           in_=sb["relg"][64:128, 0:1]))
    p.do("vector", lambda e: e.tensor_copy(out=sb["eW"][64:128, 3:4],
                                           in_=sb["relg"][64:128, 2048:2049]), inc="dv")
    dv_eW = p.cnt["dv"]

    kb_ranges = [band_range(qb) for qb in range(16)]
    for hp in range(NP):
        gen_pair(p, dd, sb, V, pA, pB, hp, kb_ranges, dbg, debug, ac_lnrel, dv_eW)

    p.wait("gpsimd", "pe", p.cnt["pe"])
    p.wait("gpsimd", "dv", p.cnt["dv"])
    p.wait("gpsimd", "ac", p.cnt["ac"])
    p.wait("gpsimd", "dsc", p.cnt["dsc"])
    dg_wo = p.do("gpsimd", lambda e: e.dma_start(
        out=V["wo"], in_=dd["woT"].rearrange("(c p) m -> p c m", p=128)), inc="dgo", inc_n=16)
    p.wait("tensor", "dgo", dg_wo)
    p.wait("tensor", "dv", p.cnt["dv"])
    p.wait("tensor", "ac", p.cnt["ac"])
    p.wait("tensor", "dsc", p.cnt["dsc"])
    stag_dma = {}
    for blk in range(NCH):
        if blk > 0:
            p.wait("tensor", "dv", p.cnt["dv"])
        if blk >= 2:
            ssem, sval = stag_dma[blk - 2]
            p.wait("vector", ssem, sval)
        for c in range(NCH):
            for nch in range(4):
                ps = (pA if nch < 2 else pB)[:, (nch % 2) * 1024:(nch % 2) * 1024 + 512]
                p.do("tensor", lambda e, blk=blk, c=c, nch=nch, ps=ps: e.matmul(
                    ps, V["wo"][:, c, blk * 128:(blk + 1) * 128],
                    V["comb"][:, c, nch * 512:(nch + 1) * 512],
                    start=(c == 0), stop=(c == NCH - 1)),
                    inc="pe" if (c == NCH - 1 and nch == 3) else None)
        p.wait("vector", "pe", p.cnt["pe"])
        for nch in range(4):
            ps = (pA if nch < 2 else pB)[:, (nch % 2) * 1024:(nch % 2) * 1024 + 512]
            p.do("vector", lambda e, blk=blk, nch=nch, ps=ps: e.tensor_scalar(
                out=V["stag"][:, blk % 2, nch * 512:(nch + 1) * 512], in0=ps, scalar1=1.0,
                scalar2=sb["bo_c"][:, blk:blk + 1], op0=OP.mult, op1=OP.add),
                inc="dv" if nch == 3 else None)
        p.wait("sync", "dv", p.cnt["dv"])
        stag_dma[blk] = ("dss0" if blk % 2 == 0 else "dss1",
            p.do("sync", lambda e, blk=blk: e.dma_start(
                out=attn_sp[blk], in_=V["stag"][:, blk % 2, :]),
                inc="dss0" if blk % 2 == 0 else "dss1", inc_n=16))
    ds_attn = (p.cnt["dss0"], p.cnt["dss1"])
    if debug:
        p.wait("gpsimd", "dss0", ds_attn[0])
        p.wait("gpsimd", "dss1", ds_attn[1])
        p.do("gpsimd", lambda e: e.dma_start(out=dbg["d_attn"], in_=attn_sp), inc="dgd", inc_n=16)
        p.do("gpsimd", lambda e: e.dma_start(out=dbg["d_comb"], in_=V["comb"]), inc="dgd", inc_n=16)
        p.dbg_barrier()

    gen_ffn(p, dd, sb, V, pA, pB, out_cm, attn_sp, ds_attn, dbg, debug)

    p.wait("sync", "dso", p.cnt["dso"])
    p.wait("sync", "dsc", p.cnt["dsc"])
    p.wait("sync", "dsg", p.cnt["dsg"])
    p.wait("sync", "dss0", p.cnt["dss0"])
    p.wait("sync", "dss1", p.cnt["dss1"])
    p.wait("sync", "ds", p.cnt["ds"])
    for s in ("dgm", "dgz", "dgw", "dgo", "dgx0", "dgx1", "dgy0", "dgy1", "dgf", "dgd",
                  "hv", "ha", "ht", "hg"):
        p.wait("gpsimd", s, p.cnt[s])


def emit_ln_rows(p, sb, rsum, rsq, n):
    p.do("vector", lambda e, rsum=rsum: e.tensor_scalar(
        out=sb["rowm"][0:1, 0:n], in0=rsum, scalar1=1.0 / D, scalar2=None, op0=OP.mult))
    p.hop("vector")
    p.do("vector", lambda e: e.tensor_tensor(
        out=sb["rowr"][0:1, 0:n], in0=sb["rowm"][0:1, 0:n], in1=sb["rowm"][0:1, 0:n],
        op=OP.mult))
    p.hop("vector")
    p.do("vector", lambda e: e.tensor_scalar(
        out=sb["rowr"][0:1, 0:n], in0=sb["rowr"][0:1, 0:n], scalar1=-1e-5, scalar2=None,
        op0=OP.add))
    p.hop("vector")
    p.do("vector", lambda e, rsq=rsq: e.scalar_tensor_tensor(
        out=sb["rowr"][0:1, 0:n], in0=rsq, scalar=1.0 / D, in1=sb["rowr"][0:1, 0:n],
        op0=OP.mult, op1=OP.subtract), inc="dv")
    p.wait("scalar", "dv", p.cnt["dv"])
    p.do("scalar", lambda e: e.activation(out=sb["rowr"][0:1, 0:n],
                                          in_=sb["rowr"][0:1, 0:n], func=AF.Sqrt), inc="ac")
    p.wait("vector", "ac", p.cnt["ac"])
    p.do("vector", lambda e: e.reciprocal(out=sb["rowr"][0:1, 0:n], in_=sb["rowr"][0:1, 0:n]))
    p.hop("vector")
    p.do("vector", lambda e: e.tensor_tensor(
        out=sb["rowmr"][0:1, 0:n], in0=sb["rowm"][0:1, 0:n], in1=sb["rowr"][0:1, 0:n],
        op=OP.mult))
    p.do("vector", lambda e: e.tensor_copy(out=sb["rowb"][0:1, 0:n],
                                           in_=sb["rowr"][0:1, 0:n]), inc="dv")


def gen_pair(p, dd, sb, V, pA, pB, hp, kb_ranges, dbg, debug, ac_lnrel, dv_eW):
    p.wait("gpsimd", "pe", p.cnt["pe"])
    lw = []
    for wn, dn in (("wq", "wq_p"), ("wk", "wk_p"), ("wv", "wv_p")):
        lw.append(p.do("gpsimd", lambda e, wn=wn, dn=dn: e.dma_start(
            out=V["wqkv_" + wn],
            in_=dd[dn][hp].rearrange("(c p) m -> p c m", p=128)), inc="dgw", inc_n=16))
    dg_mz = p.do("gpsimd", lambda e: e.dma_start(out=sb["mz"][:], in_=dd["mzp"][hp]),
                 inc="dgz", inc_n=16)

    p.wait("tensor", "dgw", lw[2])
    p.wait("tensor", "ac", p.cnt["ac"])
    p.wait("tensor", "dv", p.cnt["dv"])
    for c in range(NCH):
        for nch in range(4):
            p.do("tensor", lambda e, c=c, nch=nch: e.matmul(
                pA[:, nch * 512:(nch + 1) * 512], V["wqkv_wq"][:, c, :],
                V["h1"][:, c, nch * 512:(nch + 1) * 512],
                start=(c == 0), stop=(c == NCH - 1)),
                inc="pe" if (c == NCH - 1 and nch == 3) else None)
    p.wait("vector", "pe", p.cnt["pe"])
    p.do("vector", lambda e: e.tensor_scalar(
        out=V["qh"][:, :], in0=pA[:, 0:2048], scalar1=0.125,
        scalar2=sb["bq_h"][:, hp:hp + 1], op0=OP.mult, op1=OP.add), inc="dv")
    dv_q = p.cnt["dv"]
    for half in range(2):
        p.wait("tensor", "dv", p.cnt["dv"])
        for c in range(NCH):
            for nch in range(4):
                t0 = half * 2048 + nch * 512
                p.do("tensor", lambda e, c=c, nch=nch, t0=t0: e.matmul(
                    pA[:, nch * 512:(nch + 1) * 512], V["wqkv_wk"][:, c, :],
                    V["h1"][:, c, t0:t0 + 512],
                    start=(c == 0), stop=(c == NCH - 1)),
                    inc="pe" if (c == NCH - 1 and nch == 3) else None)
        p.wait("vector", "pe", p.cnt["pe"])
        p.do("vector", lambda e, half=half: e.tensor_scalar(
            out=V["kk"][:, half * 2048:(half + 1) * 2048], in0=pA[:, 0:2048], scalar1=1.0,
            scalar2=sb["bk_c"][:, hp:hp + 1], op0=OP.mult, op1=OP.add), inc="dv")
    for vch in range(4):
        p.wait("tensor", "dv", p.cnt["dv"])
        for c in range(NCH):
            for nn in range(2):
                t0 = vch * 1024 + nn * 512
                p.do("tensor", lambda e, c=c, nn=nn, t0=t0: e.matmul(
                    pB[:, nn * 512:(nn + 1) * 512], V["wqkv_wv"][:, c, :],
                    V["h1"][:, c, t0:t0 + 512],
                    start=(c == 0), stop=(c == NCH - 1)),
                    inc="pe" if (c == NCH - 1 and nn == 1) else None)
        p.wait("vector", "pe", p.cnt["pe"])
        p.do("vector", lambda e: e.tensor_scalar(
            out=V["vf"][:], in0=pB[:, 0:1024], scalar1=1.0,
            scalar2=sb["bv_c"][:, hp:hp + 1], op0=OP.mult, op1=OP.add), inc="dv")
        p.wait("tensor", "dv", p.cnt["dv"])
        pBb = pB[:, 1024:1536].bitcast(BF16)
        for t in range(8):
            p.do("tensor", lambda e, t=t, pBb=pBb: e.transpose(
                pBb[:, t * 128:(t + 1) * 128], V["vf"][:, t * 128:(t + 1) * 128],
                sb["ident"][:]), inc="pe" if t == 7 else None)
        p.wait("vector", "pe", p.cnt["pe"])
        for t in range(8):
            kb = vch * 8 + t
            p.do("vector", lambda e, t=t, kb=kb, pBb=pBb: e.tensor_copy(
                out=V["vp"][:, kb, 0, 0:64],
                in_=pBb[:, t * 128:t * 128 + 64]))
            p.do("vector", lambda e, t=t, kb=kb, pBb=pBb: e.tensor_copy(
                out=V["vp"][:, kb, 1, 0:64],
                in_=pBb[:, t * 128 + 64:t * 128 + 128]),
                inc="dv" if t == 7 else None)
        p.wait("tensor", "dv", p.cnt["dv"])
    p.do("vector", lambda e: e.memset(V["vp"][:, :, :, 64:65], 1.0), inc="dv")


    for si, (srcv, n) in enumerate(((V["qh"], SQ), (V["kk"], S))):
        for ch in range(n // 512):
            p.do("vector", lambda e, srcv=srcv, ch=ch: e.tensor_tensor(
                out=sb["sqs"][:], in0=srcv[:, ch * 512:(ch + 1) * 512],
                in1=srcv[:, ch * 512:(ch + 1) * 512], op=OP.mult), inc="dv")
            p.wait("tensor", "dv", p.cnt["dv"])
            p.do("tensor", lambda e: e.matmul(
                pB[0:2, 1024:1536], sb["blkd"][:], sb["sqs"][:],
                start=True, stop=True), inc="pe")
            p.wait("vector", "pe", p.cnt["pe"])
            dst = sb["tiny"][:, si:si + 1]
            if ch == 0:
                p.do("vector", lambda e, dst=dst: e.reduce_max(
                    out=dst, in_=pB[0:2, 1024:1536], axis=AX.X), inc="dv")
            else:
                p.do("vector", lambda e: e.reduce_max(
                    out=sb["tiny"][:, 9:10], in_=pB[0:2, 1024:1536], axis=AX.X))
                p.hop("vector")
                p.do("vector", lambda e, dst=dst: e.tensor_tensor(
                    out=dst, in0=dst, in1=sb["tiny"][:, 9:10], op=OP.max), inc="dv")
    p.wait("scalar", "dv", p.cnt["dv"])
    p.wait("scalar", "ac", ac_lnrel)
    p.do("scalar", lambda e: e.activation(out=sb["tiny"][:, 2:3], in_=sb["tiny"][:, 0:1],
                                          func=AF.Ln))
    p.do("scalar", lambda e: e.activation(out=sb["tiny"][:, 3:4], in_=sb["tiny"][:, 1:2],
                                          func=AF.Ln), inc="ac")
    p.wait("vector", "ac", p.cnt["ac"])
    p.do("vector", lambda e: e.tensor_tensor(out=sb["tiny"][:, 4:5], in0=sb["tiny"][:, 2:3],
                                             in1=sb["tiny"][:, 3:4], op=OP.add))
    p.hop("vector")
    p.do("vector", lambda e: e.tensor_scalar(out=sb["tiny"][:, 4:5], in0=sb["tiny"][:, 4:5],
                                             scalar1=0.5, scalar2=None, op0=OP.mult))
    p.do("vector", lambda e: e.tensor_tensor(out=sb["tiny"][:, 5:6], in0=sb["tiny"][:, 2:3],
                                             in1=sb["lnrel"][:], op=OP.add))
    p.hop("vector")
    p.do("vector", lambda e: e.tensor_scalar(out=sb["tiny"][:, 5:6], in0=sb["tiny"][:, 5:6],
                                             scalar1=0.5, scalar2=None, op0=OP.mult), inc="dv")
    p.wait("scalar", "dv", p.cnt["dv"])
    p.do("scalar", lambda e: e.activation(out=sb["tiny"][:, 6:7], in_=sb["tiny"][:, 4:5],
                                          func=AF.Exp))
    p.do("scalar", lambda e: e.activation(out=sb["tiny"][:, 7:8], in_=sb["tiny"][:, 5:6],
                                          func=AF.Exp), inc="ac")
    p.wait("vector", "ac", p.cnt["ac"])
    p.do("vector", lambda e: e.tensor_tensor(out=sb["tiny"][:, 8:9], in0=sb["tiny"][:, 6:7],
                                             in1=sb["tiny"][:, 7:8], op=OP.add))
    p.hop("vector")
    p.do("vector", lambda e: e.tensor_copy(out=sb["tinyb"][:, 0:1], in_=sb["tiny"][:, 8:9]),
         inc="dv")
    p.wait("tensor", "dv", p.cnt["dv"])
    p.do("tensor", lambda e: e.matmul(pB[0:1, 1024:1026], sb["tinyb"][:, 0:1], sb["i2"][:],
                                      start=True, stop=True), inc="pe")
    p.wait("vector", "pe", p.cnt["pe"])
    p.do("vector", lambda e: e.reduce_max(out=sb["tiny"][0:1, 9:10], in_=pB[0:1, 1024:1026],
                                          axis=AX.X))
    p.hop("vector")
    p.do("vector", lambda e: e.tensor_copy(out=sb["tinyb"][0:1, 1:2],
                                           in_=sb["tiny"][0:1, 9:10]), inc="dv")
    p.wait("tensor", "dv", p.cnt["dv"])
    p.do("tensor", lambda e: e.matmul(pB[:, 1026:1027], sb["ones_r"][:], sb["tinyb"][0:1, 1:2],
                                      start=True, stop=True), inc="pe")
    p.wait("scalar", "pe", p.cnt["pe"])
    p.do("scalar", lambda e: e.activation(out=sb["negb"][:], in_=pB[:, 1026:1027],
                                          func=AF.Copy, scale=-1.0), inc="ac")
    ac_negb = p.cnt["ac"]

    p.wait("tensor", "dv", dv_eW)
    p.wait("tensor", "ac", ac_negb)
    for nch in range(4):
        reg = pB[0:4, 1024 + (nch % 2) * 512:1536 + (nch % 2) * 512]
        if nch >= 2:
            p.wait("tensor", "dv", p.cnt["dv"])
        p.do("tensor", lambda e, nch=nch, reg=reg: e.matmul(
            reg, sb["eW"][:], V["qh"][:, nch * 512:(nch + 1) * 512],
            start=True, stop=True), inc="pe")
        p.wait("vector", "pe", p.cnt["pe"])
        p.do("vector", lambda e, nch=nch, reg=reg: e.tensor_copy(
            out=sb["edges4"][:, nch * 512:(nch + 1) * 512],
            in_=reg), inc="dv")
    dv_edges = p.cnt["dv"]
    if debug == 1 and hp == 0:
        p.wait("gpsimd", "ac", ac_negb)
        p.wait("gpsimd", "dv", dv_edges)
        for nm, src in (("d_negb", sb["negb"][:]), ("d_qh", V["qh"]), ("d_k", V["kk"]),
                        ("d_vp", V["vp"]), ("d_edge", sb["edges4"][:])):
            p.do("gpsimd", lambda e, nm=nm, src=src: e.dma_start(out=dbg[nm], in_=src),
                 inc="dgd", inc_n=16)
        p.dbg_barrier()

    for qg in range(NG):
        gen_group(p, sb, V, pA, pB, hp, qg, kb_ranges, dbg, debug)


def rg_layout(qg):
    offs = {}
    pos = 0
    for qi in range(GQB):
        qb = qg * GQB + qi
        kb_lo, kb_hi = band_range(qb)
        w = (kb_hi - kb_lo + 1) * 128
        for h in range(2):
            offs[(qi, h)] = (pos, w)
            pos += w
    assert pos <= 18432, pos
    return offs


def gen_group(p, sb, V, pA, pB, hp, qg, kb_ranges, dbg, debug):
    qbs = [qg * GQB + i for i in range(GQB)]
    q0g = qg * GQB * 128
    offs = rg_layout(qg)

    p.wait("tensor", "ac", p.cnt["ac"])
    p.wait("tensor", "dv", p.cnt["dv"])
    pe_prev_group = p.cnt["pe"]
    gath = []
    for qi, qb in enumerate(qbs):
        kb_lo, kb_hi = kb_ranges[qb]
        wb = (kb_hi - kb_lo + 1) * 128
        p.wait("tensor", "dv", p.cnt["dv"])
        p.wait("vector", "dsg", p.cnt["dsg"])
        for j in range(4):
            p.do("tensor", lambda e, j=j, qb=qb: e.matmul(
                pA[:, j:j + 1], sb["edges4"][:, qb * 128:(qb + 1) * 128],
                sb["sel_c"][:, j:j + 1], start=True, stop=True),
                inc="pe" if j == 3 else None)
        p.wait("vector", "pe", p.cnt["pe"])
        for h in range(2):
            p.do("vector", lambda e, h=h: e.tensor_copy(
                out=V["pg"][:, h, 0:PAD],
                in_=pA[:, 2 * h:2 * h + 1].broadcast_to([128, PAD])))
            p.do("vector", lambda e, h=h: e.tensor_copy(
                out=V["pg"][:, h, PAD + 2048:CP],
                in_=pA[:, 2 * h + 1:2 * h + 2].broadcast_to([128, CP - PAD - 2048])),
                inc="dv" if h == 1 else None)
        dv_pads = p.cnt["dv"]
        p.wait("tensor", "dv", dv_pads)
        for nn in range(4):
            p.do("tensor", lambda e, qb=qb, nn=nn: e.matmul(
                pA[:, nn * 512:(nn + 1) * 512], V["qh"][0:64, qb * 128:(qb + 1) * 128],
                sb["relg"][0:64, nn * 512:(nn + 1) * 512], start=True, stop=True,
                tile_position=(0, 0)))
            p.do("tensor", lambda e, qb=qb, nn=nn: e.matmul(
                pB[:, nn * 512:(nn + 1) * 512], V["qh"][64:128, qb * 128:(qb + 1) * 128],
                sb["relg"][64:128, nn * 512:(nn + 1) * 512], start=True, stop=True,
                tile_position=(64, 0)), inc="pe" if nn == 3 else None)
        p.wait("vector", "pe", p.cnt["pe"])
        p.do("vector", lambda e: e.tensor_copy(
            out=V["pg"][:, 0, PAD:PAD + 2048], in_=pA[:, 0:2048]))
        p.do("vector", lambda e: e.tensor_copy(
            out=V["pg"][:, 1, PAD:PAD + 2048], in_=pB[:, 0:2048]), inc="dv")
        dv_pg = p.cnt["dv"]
        p.wait("sync", "pe", pe_prev_group)
        p.wait("sync", "dv", dv_pg)
        pg_t = V["pg"].tensor
        row_ext = sb["arena"][:].ap[0][0]
        pg_off = V["pg"].offset
        for h in range(2):
            colstart = kb_lo * 128 - qb * 128 + 1024 + PAD
            assert colstart - 127 >= 0 and colstart + wb - 1 <= CP - 1
            off0, w0 = offs[(qi, h)]
            assert w0 == wb
            src = AP(pg_t, pg_off + h * CP + colstart,
                     [[row_ext - 1, 128], [1, wb]])
            g = p.do("sync", lambda e, src=src, off0=off0, wb=wb: e.dma_start(
                out=V["rg"][:, off0:off0 + wb], in_=src), inc="dsg", inc_n=16)
            gath.append(g)
        if debug == 1 and hp == 0 and qg == 0 and qb == 0:
            p.wait("gpsimd", "dv", dv_pg)
            p.do("gpsimd", lambda e: e.dma_start(
                out=dbg["d_pg"], in_=V["pg"][:, 0, :]), inc="dgd", inc_n=16)
            p.wait("gpsimd", "dsg", p.cnt["dsg"])
            p.do("gpsimd", lambda e, wb=wb: e.dma_start(
                out=dbg["d_rg"][:, 0:wb], in_=V["rg"][:, 0:wb]), inc="dgd", inc_n=16)
            p.dbg_barrier()

    p.wait("tensor", "dsg", gath[-1])
    p.wait("tensor", "dv", p.cnt["dv"])
    exp_base = p.cnt["ac"]
    sdone = []
    avdone = []

    def emit_S(kb):
        pp = kb % 2
        reg = pA[:, pp * 1024:(pp + 1) * 1024]
        if kb >= 2:
            p.wait("tensor", "ac", exp_base + (kb - 1))
        for h in range(2):
            hreg = reg[:, h * 512:(h + 1) * 512]
            tp = (0, 0) if h == 0 else (64, 0)
            band = [(qi, qb, kb - kb_ranges[qb][0]) for qi, qb in enumerate(qbs)
                    if kb_ranges[qb][0] <= kb <= kb_ranges[qb][1]]
            farA = [qi for qi, qb in enumerate(qbs) if kb < kb_ranges[qb][0]]
            farB = [qi for qi, qb in enumerate(qbs) if kb > kb_ranges[qb][1]]
            nacc = len(band) + (1 if farA else 0) + (1 if farB else 0)
            p.do("tensor", lambda e, hreg=hreg, h=h, kb=kb, tp=tp, nacc=nacc: e.matmul(
                hreg, V["kk"][h * 64:(h + 1) * 64, kb * 128:(kb + 1) * 128],
                V["qh"][h * 64:(h + 1) * 64, q0g:q0g + 512],
                start=True, stop=(nacc == 0), tile_position=tp))
            i = 0
            for qi, qb, off in band:
                i += 1
                off0, w0 = offs[(qi, h)]
                last = (i == nacc)
                p.do("tensor", lambda e, hreg=hreg, qi=qi, off0=off0, off=off, last=last: e.matmul(
                    hreg[:, qi * 128:(qi + 1) * 128],
                    V["rg"][:, off0 + off * 128:off0 + (off + 1) * 128],
                    sb["ident"][:], start=False, stop=last))
            for falist, eidx in ((farA, 0), (farB, 1)):
                if not falist:
                    continue
                i += 1
                lo, hi = falist[0], falist[-1]
                assert falist == list(range(lo, hi + 1))
                last = (i == nacc)
                j = 2 * h + eidx
                p.do("tensor", lambda e, hreg=hreg, j=j, lo=lo, hi=hi, last=last: e.matmul(
                    hreg[:, lo * 128:(hi + 1) * 128], sb["sel_r"][:, j * 128:(j + 1) * 128],
                    sb["edges4"][:, q0g + lo * 128:q0g + (hi + 1) * 128],
                    start=False, stop=last))
        v = p.inc_last("tensor", "pe")
        sdone.append(v)

    def emit_attnv(kb):
        p.wait("tensor", "ac", exp_base + kb + 1)
        for h in range(2):
            p.do("tensor", lambda e, kb=kb, h=h: e.matmul(
                pB[0:65, h * 512:(h + 1) * 512], V["vp"][:, kb, h, :],
                V["ee"][:, kb % 2, h * 512:(h + 1) * 512],
                start=(kb == 0), stop=(kb == NKB - 1)))
        v = p.inc_last("tensor", "pe")
        avdone.append(v)

    emit_S(0)
    emit_S(1)
    for kb in range(NKB):
        if kb + 2 < NKB:
            emit_S(kb + 2)
        emit_attnv(kb)
    for kb in range(NKB):
        p.wait("scalar", "pe", sdone[kb])
        if kb >= 2:
            p.wait("scalar", "pe", avdone[kb - 2])
        p.do("scalar", lambda e, kb=kb: e.activation(
            out=V["ee"][:, kb % 2, :], in_=pA[:, (kb % 2) * 1024:(kb % 2 + 1) * 1024],
            func=AF.Exp, bias=sb["negb"][:], scale=1.0), inc="ac")

    p.wait("scalar", "pe", avdone[-1])
    p.do("scalar", lambda e: e.activation(out=sb["o65"][:, 0, :], in_=pB[0:65, 0:512],
                                          func=AF.Copy))
    p.do("scalar", lambda e: e.activation(out=sb["o65"][:, 1, :], in_=pB[0:65, 512:1024],
                                          func=AF.Copy), inc="ac")
    ac_o65 = p.cnt["ac"]
    if debug == 1 and hp == 0 and qg == 0:
        p.wait("gpsimd", "ac", ac_o65)
        p.do("gpsimd", lambda e: e.dma_start(out=dbg["d_o65"], in_=sb["o65"][:]),
             inc="dgd", inc_n=16)
        p.dbg_barrier()
    p.do("vector", lambda e: e.tensor_scalar(
        out=V["sigm"][:, :], in0=V["qh"][:, q0g:q0g + 512], scalar1=0.0, scalar2=None,
        op0=OP.min), inc="dv")
    p.wait("scalar", "dv", p.cnt["dv"])
    p.do("scalar", lambda e: e.activation(out=V["sig"][:, :], in_=V["sigm"][:, :],
                                          func=AF.Exp, scale=8.0), inc="ac")
    p.wait("vector", "ac", p.cnt["ac"])
    p.do("vector", lambda e: e.tensor_scalar(
        out=V["sigm"][:, :], in0=V["qh"][:, q0g:q0g + 512], scalar1=0.0, scalar2=None,
        op0=OP.max))
    p.hop("vector")
    p.do("vector", lambda e: e.scalar_tensor_tensor(
        out=V["sig"][:, :], in0=V["sigm"][:, :], scalar=8.0, in1=V["sig"][:, :],
        op0=OP.mult, op1=OP.add), inc="dv")
    p.wait("tensor", "ac", ac_o65)
    p.wait("tensor", "dv", p.cnt["dv"])
    p.wait("tensor", "dgz", p.cnt["dgz"])
    p.do("tensor", lambda e: e.matmul(pB[0:65, 0:512], sb["mz"][0:64, :],
                                      V["sig"][0:64, :], start=True, stop=True,
                                      tile_position=(0, 0)))
    p.do("tensor", lambda e: e.matmul(pB[0:65, 512:1024], sb["mz"][64:128, :],
                                      V["sig"][64:128, :], start=True, stop=True,
                                      tile_position=(64, 0)), inc="pe")
    p.wait("vector", "pe", p.cnt["pe"])
    for h in range(2):
        p.do("vector", lambda e, h=h: e.reciprocal(out=sb["o65"][64:65, h, :],
                                                   in_=sb["o65"][64:65, h, :]))
        p.do("vector", lambda e, h=h: e.tensor_scalar(
            out=sb["tmpa"][0:1, 0:512], in0=pB[64:65, h * 512:(h + 1) * 512],
            scalar1=1e-6, scalar2=None, op0=OP.add))
        p.hop("vector")
        p.do("vector", lambda e: e.reciprocal(out=sb["tmpa"][0:1, 0:512],
                                              in_=sb["tmpa"][0:1, 0:512]))
        p.hop("vector")
        p.do("vector", lambda e, h=h: e.tensor_copy(out=sb["rowb"][:],
                                                    in_=sb["o65"][64:65, h, :]), inc="dv")
        p.wait("tensor", "dv", p.cnt["dv"])
        p.do("tensor", lambda e: e.matmul(
            pA[0:64, 0:512], sb["ones_r"][0:1, 0:64], sb["rowb"][:],
            start=True, stop=True), inc="pe")
        p.wait("vector", "pe", p.cnt["pe"])
        p.do("vector", lambda e: e.tensor_copy(out=sb["rowb"][:], in_=sb["tmpa"][0:1, 0:512]),
             inc="dv")
        p.wait("tensor", "dv", p.cnt["dv"])
        p.do("tensor", lambda e: e.matmul(
            pA[0:64, 512:1024], sb["ones_r"][0:1, 0:64], sb["rowb"][:],
            start=True, stop=True), inc="pe")
        p.wait("vector", "pe", p.cnt["pe"])
        p.do("vector", lambda e, h=h: e.tensor_copy(
            out=sb["ct"][:, 1, :], in_=pB[0:64, h * 512:(h + 1) * 512]), inc="dv")
        p.do("vector", lambda e, h=h: e.scalar_tensor_tensor(
            out=sb["ct"][:, 0, :], in0=sb["o65"][0:64, h, :], scalar=sb["gate"][0:64, :],
            in1=pA[0:64, 0:512], op0=OP.mult, op1=OP.mult))
        p.hop("vector")
        p.do("vector", lambda e, h=h: e.scalar_tensor_tensor(
            out=sb["ct"][:, 1, :], in0=sb["ct"][:, 1, :], scalar=sb["gate1m"][0:64, :],
            in1=pA[0:64, 512:1024], op0=OP.mult, op1=OP.mult))
        p.hop("vector")
        if h == 0:
            p.do("vector", lambda e: e.tensor_tensor(
                out=V["comb"][0:64, hp, q0g:q0g + 512], in0=sb["ct"][:, 0, :],
                in1=sb["ct"][:, 1, :], op=OP.add), inc="dv")
        else:
            p.do("vector", lambda e: e.tensor_tensor(
                out=sb["ct"][:, 0, :], in0=sb["ct"][:, 0, :],
                in1=sb["ct"][:, 1, :], op=OP.add), inc="dv")
            p.wait("sync", "dv", p.cnt["dv"])
            p.do("sync", lambda e: e.dma_start(
                out=V["comb"][64:128, hp, q0g:q0g + 512], in_=sb["ct"][:, 0, :]),
                inc="dsc", inc_n=16)
    p.wait("vector", "dsc", p.cnt["dsc"])


def gen_ffn(p, dd, sb, V, pA, pB, out_cm, attn_sp, ds_attn, dbg, debug):
    def wload(dname, c, width, wait_ac=None):
        if wait_ac is not None:
            p.wait("gpsimd", "ac", wait_ac)
        s = "dgx0" if c % 2 == 0 else "dgx1"
        return (s, p.do("gpsimd", lambda e, c=c, dname=dname, width=width: e.dma_start(
            out=V["wsl"][:, c % 2, 0:width], in_=dd[dname][c * 128:(c + 1) * 128, :]),
            inc=s, inc_n=16))

    p.wait("tensor", "ac", p.cnt["ac"])
    p.wait("tensor", "dv", p.cnt["dv"])
    p.wait("gpsimd", "dss0", ds_attn[0])
    p.wait("gpsimd", "dss1", ds_attn[1])
    loads = {0: wload("w1T", 0, 4 * D), 1: wload("w1T", 1, 4 * D)}
    done = {}
    p.wait("scalar", "pe", p.cnt["pe"])
    for c in range(NCH):
        lsem, lval = loads[c]
        p.wait("scalar", lsem, lval)
        p.do("scalar", lambda e, c=c: e.activation(
            out=V["s1"][:, c, :], in_=V["wsl"][:, c % 2, :], func=AF.Sign), inc="ac")
        for ch in range(8):
            p.wait("scalar", "pe", p.cnt["pe"])
            p.do("scalar", lambda e, c=c, ch=ch: e.activation(
                out=sb["sqs"][:], in_=V["wsl"][:, c % 2, ch * 512:(ch + 1) * 512],
                func=AF.Abs), inc="ac")
            p.wait("tensor", "ac", p.cnt["ac"])
            ps = (pA if ch < 4 else pB)[0:1, (ch % 4) * 512:(ch % 4 + 1) * 512]
            p.do("tensor", lambda e, ps=ps, c=c: e.matmul(
                ps, sb["onesc"][:], sb["sqs"][:], start=(c == 0), stop=(c == NCH - 1)),
                inc="pe")
        done[c] = p.cnt["ac"]
        if c + 2 < NCH:
            loads[c + 2] = wload("w1T", c + 2, 4 * D, wait_ac=done[c])
    p.wait("vector", "pe", p.cnt["pe"])
    for ch in range(8):
        ps = (pA if ch < 4 else pB)[0:1, (ch % 4) * 512:(ch % 4 + 1) * 512]
        p.do("vector", lambda e, ps=ps: e.tensor_scalar(
            out=sb["rowb"][:], in0=ps, scalar1=1.0 / D, scalar2=None, op0=OP.mult),
            inc="dv")
        p.wait("tensor", "dv", p.cnt["dv"])
        for q in range(4):
            p.do("tensor", lambda e, ch=ch, q=q: e.matmul(
                pB[:, 1536 + 4 * (ch % 2) + q:1537 + 4 * (ch % 2) + q],
                sb["rowb"][0:1, q * 128:(q + 1) * 128],
                sb["ones_r"][0:1, 0:1], start=True, stop=True),
                inc="pe" if q == 3 else None)
        p.wait("vector", "pe", p.cnt["pe"])
        p.do("vector", lambda e, ch=ch: e.tensor_copy(
            out=sb["a1c"][:, ch * 4:(ch + 1) * 4],
            in_=pB[:, 1536 + 4 * (ch % 2):1540 + 4 * (ch % 2)]), inc="dv")
    loads = {0: wload("w2T", 0, D, wait_ac=done[NCH - 2]),
             1: wload("w2T", 1, D, wait_ac=done[NCH - 1])}
    done2 = {}
    p.wait("scalar", "pe", p.cnt["pe"])
    p.wait("scalar", "dss0", p.cnt["dss0"])
    p.wait("scalar", "dss1", p.cnt["dss1"])
    for c in range(32):
        lsem, lval = loads[c]
        p.wait("scalar", lsem, lval)
        p.do("scalar", lambda e, c=c: e.activation(
            out=V["s2"][:, c, :], in_=V["wsl"][:, c % 2, 0:1024], func=AF.Sign), inc="ac")
        for ch in range(2):
            p.wait("scalar", "pe", p.cnt["pe"])
            p.do("scalar", lambda e, c=c, ch=ch: e.activation(
                out=sb["sqs"][:], in_=V["wsl"][:, c % 2, ch * 512:(ch + 1) * 512],
                func=AF.Abs), inc="ac")
            p.wait("tensor", "ac", p.cnt["ac"])
            ps = pA[0:1, ch * 512:(ch + 1) * 512]
            p.do("tensor", lambda e, ps=ps, c=c: e.matmul(
                ps, sb["onesc"][:], sb["sqs"][:], start=(c == 0), stop=(c == 31)),
                inc="pe")
        done2[c] = p.cnt["ac"]
        if c + 2 < 32:
            loads[c + 2] = wload("w2T", c + 2, D, wait_ac=done2[c])
    p.wait("vector", "pe", p.cnt["pe"])
    for ch in range(2):
        ps = pA[0:1, ch * 512:(ch + 1) * 512]
        p.do("vector", lambda e, ps=ps: e.tensor_scalar(
            out=sb["rowb"][:], in0=ps, scalar1=1.0 / (4 * D), scalar2=None, op0=OP.mult),
            inc="dv")
        p.wait("tensor", "dv", p.cnt["dv"])
        for q in range(4):
            p.do("tensor", lambda e, ch=ch, q=q: e.matmul(
                pB[:, 1536 + 4 * ch + q:1537 + 4 * ch + q],
                sb["rowb"][0:1, q * 128:(q + 1) * 128],
                sb["ones_r"][0:1, 0:1], start=True, stop=True),
                inc="pe" if q == 3 else None)
        p.wait("vector", "pe", p.cnt["pe"])
        p.do("vector", lambda e, ch=ch: e.tensor_copy(
            out=sb["a2c"][:, ch * 4:(ch + 1) * 4],
            in_=pB[:, 1536 + 4 * ch:1540 + 4 * ch]), inc="dv")
    dv_bw = p.cnt["dv"]
    if debug == 1:
        p.wait("gpsimd", "dv", dv_bw)
        p.do("gpsimd", lambda e: e.dma_start(out=dbg["d_s1"], in_=V["s1"]),
             inc="dgd", inc_n=16)
        p.dbg_barrier()

    xloads = {}
    atloads = {}
    ac_signs_done = p.cnt["ac"]
    def emit_fx(sl, wait_dv=None):
        ap = dd["xT"][:, sl * FSL:(sl + 1) * FSL].rearrange("(c p) t -> p c t", p=128)
        if wait_dv is not None:
            p.wait("gpsimd", "dv", wait_dv)
        if sl == 0:
            p.wait("gpsimd", "ac", ac_signs_done)
        xloads[sl] = p.do("gpsimd", lambda e, ap=ap: e.dma_start(
            out=V["fx"], in_=ap), inc="dgf", inc_n=16)
        ap2 = attn_sp[:, :, sl * FSL:(sl + 1) * FSL].rearrange("c p t -> p c t")
        atloads[sl] = p.do("gpsimd", lambda e, ap2=ap2: e.dma_start(
            out=V["fat"], in_=ap2), inc="dgf", inc_n=16)
    emit_fx(0)

    R8 = [pA[:, i * 512:(i + 1) * 512] for i in range(4)] + \
         [pB[:, i * 512:(i + 1) * 512] for i in range(4)]
    out_dma = {}
    import os
    SLAB_BARRIER = os.environ.get("DK_SLAB_BARRIER", "0") == "1"
    for sl in range(NFS):
        if SLAB_BARRIER and sl > 0:
            for eng in ("tensor", "scalar", "vector", "gpsimd", "sync"):
                p.wait(eng, "pe", p.cnt["pe"])
                p.wait(eng, "ac", p.cnt["ac"])
                p.wait(eng, "dv", p.cnt["dv"])
                p.wait(eng, "dso", p.cnt["dso"])
        t0 = sl * FSL
        p.wait("vector", "dgf", atloads[sl])
        if sl == 0:
            p.wait("vector", "dv", dv_bw)
            p.wait("vector", "ac", ac_signs_done)
        xs = V["fx"]
        at = V["fat"]
        p.wait("vector", "dv", p.cnt["dv"])
        for c in range(NCH):
            p.do("vector", lambda e, xs=xs, at=at, c=c: e.scalar_tensor_tensor(
                out=V["x2sl"][:, c, :], in0=at[:, c, :],
                scalar=sb["ga"][:, c:c + 1], in1=xs[:, c, :], op0=OP.mult, op1=OP.add),
                inc="dv" if c == NCH - 1 else None)
        if sl + 1 < NFS:
            emit_fx(sl + 1, wait_dv=p.cnt["dv"])
        p.wait("vector", "dv", p.cnt["dv"])
        p.do("vector", lambda e: e.tensor_copy(out=V["xbf2"], in_=V["x2sl"]))
        p.hop("vector")
        p.do("vector", lambda e: e.tensor_tensor(
            out=V["xsq2"], in0=V["xbf2"], in1=V["xbf2"], op=OP.mult), inc="dv")
        p.wait("tensor", "dv", p.cnt["dv"])
        p.wait("tensor", "ac", p.cnt["ac"])
        rsum, rsq = pB[0:1, 1024:1024 + FSL], pB[0:1, 1536:1536 + FSL]
        rrep, rmrep = pA[:, 1024:1024 + FSL], pA[:, 1536:1536 + FSL]
        for c in range(NCH):
            p.do("tensor", lambda e, c=c, rsum=rsum: e.matmul(
                rsum, sb["onesc"][:], V["xbf2"][:, c, :],
                start=(c == 0), stop=(c == NCH - 1)))
        for c in range(NCH):
            p.do("tensor", lambda e, c=c, rsq=rsq: e.matmul(
                rsq, sb["onesc"][:], V["xsq2"][:, c, :],
                start=(c == 0), stop=(c == NCH - 1)),
                inc="pe" if c == NCH - 1 else None)
        p.wait("vector", "pe", p.cnt["pe"])
        emit_ln_rows(p, sb, rsum, rsq, FSL)
        p.wait("tensor", "dv", p.cnt["dv"])
        p.do("tensor", lambda e, rrep=rrep: e.matmul(
            rrep, sb["ones_r"][:], sb["rowb"][0:1, 0:FSL], start=True, stop=True), inc="pe")
        p.wait("vector", "pe", p.cnt["pe"])
        p.do("vector", lambda e: e.tensor_copy(out=sb["rowb"][0:1, 0:FSL],
                                               in_=sb["rowmr"][0:1, 0:FSL]), inc="dv")
        p.wait("tensor", "dv", p.cnt["dv"])
        p.do("tensor", lambda e, rmrep=rmrep: e.matmul(
            rmrep, sb["ones_r"][:], sb["rowb"][0:1, 0:FSL], start=True, stop=True), inc="pe")
        p.wait("vector", "pe", p.cnt["pe"])
        for c in range(NCH):
            p.do("vector", lambda e, c=c, rrep=rrep: e.tensor_tensor(
                out=sb["tmpa"][:, 0:FSL], in0=V["x2sl"][:, c, :], in1=rrep, op=OP.mult))
            p.hop("vector")
            p.do("vector", lambda e, c=c, rmrep=rmrep: e.tensor_tensor(
                out=sb["tmpa"][:, 0:FSL], in0=sb["tmpa"][:, 0:FSL], in1=rmrep,
                op=OP.subtract))
            p.hop("vector")
            p.do("vector", lambda e, c=c: e.tensor_scalar(
                out=V["h3sl"][:, c, :], in0=sb["tmpa"][:, 0:FSL],
                scalar1=sb["cb1"][:, c:c + 1], scalar2=sb["sbv"][:, c:c + 1],
                op0=OP.mult, op1=OP.add), inc="dv" if c == NCH - 1 else None)
            if c < NCH - 1:
                p.hop("vector")
        dv_h3 = p.cnt["dv"]
        if debug == 1 and sl == 0:
            p.wait("gpsimd", "dv", dv_h3)
            p.do("gpsimd", lambda e: e.dma_start(out=dbg["d_h3"], in_=V["h3sl"]),
                 inc="dgd", inc_n=16)
            p.dbg_barrier()
        p.wait("tensor", "dv", dv_h3)
        evac = {}
        for ho in range(32):
            reg = R8[ho % 8]
            if ho >= 8:
                p.wait("tensor", "ac", evac[ho - 8])
            for c in range(NCH):
                p.do("tensor", lambda e, ho=ho, c=c, reg=reg: e.matmul(
                    reg[:, 0:FSL], V["s1"][:, c, ho * 128:(ho + 1) * 128],
                    V["h3sl"][:, c, :], start=(c == 0), stop=(c == NCH - 1)),
                    inc="pe" if c == NCH - 1 else None)
            p.wait("scalar", "pe", p.cnt["pe"])
            evac[ho] = p.do("scalar", lambda e, ho=ho, reg=reg: e.activation(
                out=V["h2sl"][:, ho, :], in_=reg[:, 0:FSL], func=AF.Gelu,
                bias=sb["b1_c"][:, ho:ho + 1], scale=sb["a1c"][:, ho:ho + 1]), inc="ac")
        if sl >= 1:
            p.wait("vector", "dso", out_dma[sl - 1])
        p.wait("tensor", "ac", evac[31])
        for blk in range(NCH):
            reg = R8[blk]
            for hc in range(32):
                p.do("tensor", lambda e, blk=blk, hc=hc, reg=reg: e.matmul(
                    reg[:, 0:FSL], V["s2"][:, hc, blk * 128:(blk + 1) * 128],
                    V["h2sl"][:, hc, :], start=(hc == 0), stop=(hc == 31)),
                    inc="pe" if hc == 31 else None)
            p.wait("vector", "pe", p.cnt["pe"])
            p.wait("vector", "dv", p.cnt["dv"])
            p.do("vector", lambda e, blk=blk, reg=reg: e.tensor_scalar(
                out=V["ffnc"][:, :], in0=reg[:, 0:FSL], scalar1=sb["a2c"][:, blk:blk + 1],
                scalar2=sb["b2_c"][:, blk:blk + 1], op0=OP.mult, op1=OP.add))
            p.hop("vector")
            p.do("vector", lambda e, sl=sl, blk=blk: e.scalar_tensor_tensor(
                out=V["outsl"][:, blk, :], in0=V["ffnc"][:, :],
                scalar=sb["gb"][:, blk:blk + 1], in1=V["x2sl"][:, blk, :],
                op0=OP.mult, op1=OP.add), inc="dv")
        dv_out = p.cnt["dv"]
        p.wait("sync", "dv", dv_out)
        dst = out_cm[:, t0:t0 + FSL].rearrange("(c p) t -> p c t", p=128)
        out_dma[sl] = p.do("sync", lambda e, sl=sl, dst=dst: e.dma_start(
            out=dst, in_=V["outsl"][:, :, :]), inc="dso", inc_n=16)



def prep_inputs(inputs):
    f32 = np.float32
    bf = ml_dtypes.bfloat16
    x = np.asarray(inputs["L0_attn_hidden_states"], f32)
    relT = np.ascontiguousarray(np.asarray(inputs["L0_attn_rel_emb"], f32).T)
    rel_rev = np.ascontiguousarray(relT[:, ::-1])
    Mz = np.concatenate([np.asarray(inputs["L0_attn_M"], f32),
                         np.asarray(inputs["L0_attn_z"], f32)[:, :, None]], axis=2)
    mzp = np.ascontiguousarray(Mz.reshape(8, 128, 65))
    WqT = np.ascontiguousarray(np.asarray(inputs["L0_attn_Wq"], f32).T)
    WkT = np.ascontiguousarray(np.asarray(inputs["L0_attn_Wk"], f32).T)
    WvT = np.ascontiguousarray(np.asarray(inputs["L0_attn_Wv"], f32).T)
    WoT = np.ascontiguousarray(np.asarray(inputs["L0_attn_Wo"], f32).T)
    w1T = np.ascontiguousarray(np.asarray(inputs["L0_ffn_w1"], f32).T)
    w2T = np.ascontiguousarray(np.asarray(inputs["L0_ffn_w2"], f32).T)
    wq_p = np.ascontiguousarray(WqT.reshape(D, NP, 128).transpose(1, 0, 2))
    wk_p = np.ascontiguousarray(WkT.reshape(D, NP, 128).transpose(1, 0, 2))
    wv_p = np.ascontiguousarray(WvT.reshape(D, NP, 128).transpose(1, 0, 2))

    def col8(v):
        return np.ascontiguousarray(np.asarray(v, f32).reshape(8, 128).T)
    def col32(v):
        return np.ascontiguousarray(np.asarray(v, f32).reshape(32, 128).T)

    sel_r = np.zeros((4, 512), bf)
    sel_c = np.zeros((4, 4), bf)
    for j in range(4):
        sel_r[j, j * 128:(j + 1) * 128] = 1
        sel_c[j, j] = 1
    blkd = np.zeros((128, 2), bf)
    blkd[0:64, 0] = 1
    blkd[64:128, 1] = 1

    common = dict(
        wmod=np.asarray(inputs["L0_wmod"], f32),
        wq_p=wq_p, wk_p=wk_p, wv_p=wv_p, woT=WoT, w1T=w1T, w2T=w2T,
        mzp=mzp,
        bq_c=col8(inputs["L0_attn_bq"]), bk_c=col8(inputs["L0_attn_bk"]),
        bv_c=col8(inputs["L0_attn_bv"]), bo_c=col8(inputs["L0_attn_bo"]),
        b1_c=col32(inputs["L0_ffn_b1"]), b2_c=col8(inputs["L0_ffn_b2"]),
        galpha=np.asarray(inputs["L0_attn_gate_alpha"], f32).reshape(1, 1),
        ident=np.eye(128, dtype=bf),
        ones_r=np.ones((1, 128), bf),
        ones_rf=np.ones((1, 128), np.float32),
        onesc=np.ones((128, 1), bf),
        blkd=blkd,
        o642=np.ones((64, 2), bf),
        i2=np.eye(2, dtype=bf),
        sel_r=sel_r, sel_c=sel_c,
    )
    relg0 = np.ascontiguousarray(np.concatenate([rel_rev, rel_rev], 0))
    relg1 = np.ascontiguousarray(np.concatenate([relT, relT], 0))
    in_maps = []
    for core in range(8):
        b, half = core // 2, core % 2
        xT = np.ascontiguousarray(x[b].T)
        if half == 1:
            xT = np.ascontiguousarray(xT[:, ::-1])
        m = dict(common)
        m["xT"] = xT
        m["relg"] = relg0 if half == 0 else relg1
        m["c_col"] = np.ascontiguousarray(
            np.asarray(inputs["L0_c"], f32)[b].reshape(8, 128).T)
        in_maps.append(m)
    return in_maps


def assemble(results):
    out = np.zeros((4, S, D), np.float32)
    for core in range(8):
        b, half = core // 2, core % 2
        oc = results[core]["out_cm"]
        if half == 0:
            out[b, 0:SQ, :] = oc.T
        else:
            out[b, SQ:S, :] = oc[:, ::-1].T
    return out


def kernel(**inputs):
    from concourse.bass_utils import run_bass_kernel_spmd
    in_maps = prep_inputs(inputs)
    nc = build_program(debug=0)
    res = run_bass_kernel_spmd(nc, in_maps, core_ids=list(range(8)))
    return assemble(res.results)


# version 02
# speedup vs baseline: 1.0150x; 1.0150x over previous
import contextlib
import numpy as np
import ml_dtypes
import concourse.bass as bass
import concourse.mybir as mybir
from concourse.ap import AP

BF16 = mybir.dt.bfloat16
F32 = mybir.dt.float32
AF = mybir.ActivationFunctionType
OP = mybir.AluOpType
AX = mybir.AxisListType

D = 1024
S = 4096
SQ = 2048
NP = 8
NCH = 8
MR = 1024
PAD = 254
CP = 2049 + 2 * PAD
GQB = 4
NG = SQ // (GQB * 128)
BAND = MR + 127
NKB = S // 128
FSL = 256
NFS = SQ // FSL

ENGINES = ("tensor", "scalar", "vector", "sync", "gpsimd")


def band_range(qb):
    q0 = qb * 128
    kb_lo = max(0, -(-(q0 - BAND) // 128))
    kb_hi = min(NKB - 1, (q0 + BAND) // 128)
    return kb_lo, kb_hi


class Prog:
    def __init__(self, nc, es):
        self.nc = nc
        self.ops = {e: [] for e in ENGINES}
        self.last_has_inc = {}
        self.cnt = {}
        self.sems = {}
        for s in ("dg", "ds", "pe", "ac", "dv", "dsg", "dsc", "dss0", "dss1",
                  "dso", "dgm", "dgz", "dgw", "dgo", "dgx0", "dgx1", "dgy0", "dgy1", "dgf", "dgd",
                  "hv", "ha", "ht", "hg"):
            self.sems[s] = es.enter_context(nc.semaphore(s))
            self.cnt[s] = 0

    def wait(self, eng, sem, val):
        if val <= 0:
            return
        h = self.sems[sem]
        self.ops[eng].append(lambda e, h=h, v=val: e.wait_ge(h, v))
        self.last_has_inc[eng] = False

    def inc_last(self, eng, sem, n=1):
        assert not self.last_has_inc.get(eng), f"double inc on {eng}"
        h = self.sems[sem]
        self.cnt[sem] += n
        f = self.ops[eng].pop()
        self.ops[eng].append(lambda e, f=f, h=h, n=n: f(e).then_inc(h, n))
        self.last_has_inc[eng] = True
        return self.cnt[sem]

    def dbg_barrier(self):
        for eng in ("tensor", "scalar", "vector", "gpsimd", "sync"):
            self.wait(eng, "dgd", self.cnt["dgd"])

    def hop(self, eng):
        sem = {"vector": "hv", "scalar": "ha", "tensor": "ht", "gpsimd": "hg"}[eng]
        h = self.sems[sem]
        self.cnt[sem] += 1
        v = self.cnt[sem]
        assert not self.last_has_inc.get(eng), f"double inc (hop) on {eng}"
        f = self.ops[eng].pop()
        self.ops[eng].append(lambda e, f=f, h=h: f(e).then_inc(h, 1))
        self.ops[eng].append(lambda e, h=h, v=v: e.wait_ge(h, v))
        self.last_has_inc[eng] = False

    def do(self, eng, fn, inc=None, inc_n=1):
        if inc is None:
            self.ops[eng].append(fn)
            self.last_has_inc[eng] = False
            return None
        h = self.sems[inc]
        self.cnt[inc] += inc_n
        self.ops[eng].append(lambda e, fn=fn, h=h, n=inc_n: fn(e).then_inc(h, n))
        self.last_has_inc[eng] = True
        return self.cnt[inc]


def _view(tensor_ap, off_bytes, shape, dt):
    es_ = mybir.dt.size(dt)
    nel = int(np.prod(shape[1:]))
    assert off_bytes % 2 == 0
    v = tensor_ap[0:shape[0], off_bytes // 2: off_bytes // 2 + (nel * es_) // 2]
    if dt != BF16:
        v = v.bitcast(dt)
    if len(shape) == 2:
        return v
    if len(shape) == 3:
        return v.rearrange("p (a b) -> p a b", a=shape[1])
    if len(shape) == 4:
        return v.rearrange("p (a b c) -> p a b c", a=shape[1], b=shape[2])
    raise ValueError(shape)


def build_program(debug=0):
    nc = bass.Bass("TRN2", target_bir_lowering=False)
    es = contextlib.ExitStack()

    dd = {}
    def din(name, shape, dt=F32):
        dd[name] = nc.dram_tensor(name, shape, dt, kind="ExternalInput").ap()
    din("xT", [D, S])
    din("c_col", [128, 8])
    din("wmod", [D, 6 * D])
    din("wq_p", [NP, D, 128]); din("wk_p", [NP, D, 128]); din("wv_p", [NP, D, 128])
    din("woT", [D, D])
    din("w1T", [D, 4 * D])
    din("w2T", [4 * D, D])
    din("relg", [128, 2049])
    din("mzp", [NP, 128, 65])
    din("bq_c", [128, 8]); din("bk_c", [128, 8]); din("bv_c", [128, 8]); din("bo_c", [128, 8])
    din("b1_c", [128, 32]); din("b2_c", [128, 8])
    din("galpha", [1, 1])
    din("ident", [128, 128], BF16)
    din("ones_r", [1, 128], BF16)
    din("ones_rf", [1, 128], F32)
    din("onesc", [128, 1], BF16)
    din("onescf", [128, 1], F32)
    din("blkd", [128, 2], BF16)
    din("o642", [64, 2], BF16)
    din("i2", [2, 2], BF16)
    din("sel_r", [4, 512], BF16)
    din("sel_c", [4, 4], BF16)
    out_cm = nc.dram_tensor("out_cm", [D, SQ], F32, kind="ExternalOutput").ap()
    attn_sp = nc.dram_tensor("attn_sp", [NCH, 128, SQ], BF16, kind="Internal").ap()

    dbg = {}
    if debug:
        def dout(name, shape, dt):
            dbg[name] = nc.dram_tensor(name, shape, dt, kind="ExternalOutput").ap()
        dout("d_mod", [1, 6 * D], BF16)
        dout("d_h1", [128, NCH, S], BF16)
        dout("d_qh", [128, SQ], BF16)
        dout("d_k", [128, S], BF16)
        dout("d_vp", [128, NKB, 2, 65], BF16)
        dout("d_pg", [128, CP], BF16)
        dout("d_rg", [128, 2304], BF16)
        dout("d_negb", [128, 1], F32)
        dout("d_edge", [4, SQ], BF16)
        dout("d_o65", [65, 2, 512], F32)
        dout("d_comb", [128, NP, SQ], BF16)
        dout("d_attn", [NCH, 128, SQ], BF16)
        dout("d_s1", [128, NCH, 4 * D], BF16)
        dout("d_h3", [128, NCH, FSL], BF16)

    sb = {}
    def sbuf(name, shape, dt):
        sb[name] = es.enter_context(nc.sbuf_tensor("s_" + name, shape, dt))
    sbuf("ident", [128, 128], BF16)
    sbuf("ones_r", [1, 128], BF16)
    sbuf("onesc", [128, 1], BF16)
    sbuf("onescf", [128, 1], F32)
    sbuf("blkd", [128, 2], BF16)
    sbuf("o642", [64, 2], BF16)
    sbuf("i2", [2, 2], BF16)
    sbuf("sel_r", [4, 512], BF16)
    sbuf("sel_c", [4, 4], BF16)
    sbuf("c_col", [128, 8], F32)
    sbuf("silu_c", [128, 8], F32)
    sbuf("ones_rf", [1, 128], F32)
    for n in ("sa", "ca1", "ga", "sbv", "cb1", "gb", "bq_h", "bk_c", "bv_c", "bo_c", "b2_c"):
        sbuf(n, [128, 8], F32)
    sbuf("b1_c", [128, 32], F32)
    sbuf("a1c", [128, 32], F32)
    sbuf("a2c", [128, 8], F32)
    sbuf("gate", [128, 1], F32); sbuf("gate1m", [128, 1], F32)
    sbuf("relg", [128, 2049], BF16)
    sbuf("lnrel", [2, 1], F32)
    sbuf("tiny", [2, 10], F32)
    sbuf("tinyb", [2, 2], BF16)
    sbuf("rowb", [1, 512], BF16)
    sbuf("rowm", [1, 512], F32); sbuf("rowr", [1, 512], F32); sbuf("rowmr", [1, 512], F32)
    sbuf("tmpa", [128, 512], F32)
    sbuf("sqs", [128, 512], BF16)
    sbuf("xslab", [128, 2, NCH, 512], F32)
    sbuf("o65", [65, 2, 512], F32)
    sbuf("ct", [64, 2, 512], BF16)
    sbuf("negb", [128, 1], F32)
    sbuf("edges4", [4, SQ], BF16)
    sbuf("eW", [128, 4], BF16)
    sbuf("mz", [128, 65], BF16)

    ARENA_B = 147584
    sbuf("arena", [128, ARENA_B // 2], BF16)
    ar = sb["arena"][:]

    V = {}
    V["h1"] = _view(ar, 0, [128, NCH, S], BF16)
    V["s1"] = _view(ar, 0, [128, NCH, 4 * D], BF16)
    B0 = 65536
    V["mrow"] = _view(ar, B0, [1, 6 * D], F32)
    V["wmod_ch"] = _view(ar, B0 + 24576, [128, 2, 3 * D], F32)
    V["xbf1"] = _view(ar, B0 + 24576, [128, NCH, 512], BF16)
    V["xsq1"] = _view(ar, B0 + 32768, [128, NCH, 512], BF16)
    V["kk"] = _view(ar, B0, [128, S], BF16)
    V["qh"] = _view(ar, B0 + 8192, [128, SQ], BF16)
    V["vp"] = _view(ar, B0 + 12288, [128, NKB, 2, 65], BF16)
    V["vf"] = _view(ar, B0 + 20608, [128, 1024], BF16)
    V["pg"] = _view(ar, B0 + 22656, [128, 2, CP], BF16)
    V["ee"] = _view(ar, B0 + 32896, [128, 2, 1024], BF16)
    V["sig"] = _view(ar, B0 + 36992, [128, 512], BF16)
    V["sigm"] = _view(ar, B0 + 38016, [128, 512], BF16)
    V["wqkv_wq"] = _view(ar, B0 + 39040, [128, NCH, 128], BF16)
    V["wqkv_wk"] = _view(ar, B0 + 41088, [128, NCH, 128], BF16)
    V["wqkv_wv"] = _view(ar, B0 + 43136, [128, NCH, 128], BF16)
    V["rg"] = _view(ar, B0 + 45184, [128, 18432], BF16)
    V["wo"] = _view(ar, B0, [128, NCH, D], BF16)
    V["stag"] = _view(ar, B0 + 16384, [128, 2, SQ], BF16)
    V["s2"] = _view(ar, B0, [128, 32, D], BF16)
    xs_ap = sb["xslab"][:].rearrange("p a b c -> p (a b c)").bitcast(BF16)
    V["comb"] = _view(xs_ap, 0, [128, NP, SQ], BF16)
    V["fx"] = _view(xs_ap, 0, [128, NCH, FSL], F32)
    V["x2sl"] = _view(xs_ap, 8192, [128, NCH, FSL], F32)
    V["outsl"] = _view(xs_ap, 16384, [128, 4, FSL], F32)
    V["fat"] = _view(xs_ap, 20480, [128, NCH, FSL], BF16)
    V["xsq2"] = _view(xs_ap, 24576, [128, NCH, FSL], BF16)
    V["h3sl"] = _view(xs_ap, 28672, [128, NCH, FSL], BF16)
    assert 28672 + 4096 <= 32768
    V["wsl"] = _view(ar, 131072, [128, 2, 4 * D], BF16)
    V["h2sl"] = _view(ar, 131072, [128, 32, FSL], BF16)
    assert 131072 + 16384 <= ARENA_B

    pA = es.enter_context(nc.psum_tensor("pA", [128, 2048], F32))
    pB = es.enter_context(nc.psum_tensor("pB", [128, 2048], F32))

    p = Prog(nc, es)
    gen(p, dd, sb, V, pA, pB, dbg, out_cm, attn_sp, debug)

    with nc.Block() as block:
        @block.tensor
        def _(e):
            for f in p.ops["tensor"]:
                f(e)
        @block.scalar
        def _(e):
            for f in p.ops["scalar"]:
                f(e)
        @block.vector
        def _(e):
            for f in p.ops["vector"]:
                f(e)
        @block.sync
        def _(e):
            for f in p.ops["sync"]:
                f(e)
        @block.gpsimd
        def _(e):
            for f in p.ops["gpsimd"]:
                f(e)
    es.close()
    return nc


def gen(p, dd, sb, V, pA, pB, dbg, out_cm, attn_sp, debug):
    for name, dst in (("ident", "ident"), ("ones_r", "ones_r"), ("ones_rf", "ones_rf"),
                      ("onesc", "onesc"), ("onescf", "onescf"),
                      ("blkd", "blkd"), ("o642", "o642"), ("i2", "i2"),
                      ("sel_r", "sel_r"), ("sel_c", "sel_c"),
                      ("c_col", "c_col"), ("bq_c", "bq_h"), ("bk_c", "bk_c"),
                      ("bv_c", "bv_c"), ("bo_c", "bo_c"), ("b1_c", "b1_c"),
                      ("b2_c", "b2_c")):
        p.do("sync", lambda e, d=dd[name], s=sb[dst]: e.dma_start(out=s[:], in_=d),
             inc="ds", inc_n=16)
    p.do("sync", lambda e: e.dma_start(out=sb["gate"][0:1, :], in_=dd["galpha"]),
         inc="ds", inc_n=16)
    ds_consts = p.cnt["ds"]

    dg_relg = p.do("gpsimd", lambda e: e.dma_start(out=sb["relg"][:], in_=dd["relg"]),
                   inc="dgm", inc_n=16)
    wmod_idx = [(h, c) for h in range(2) for c in range(NCH)]
    wmod_loads = {}
    def emit_wmod_load(i, wait_pe=None):
        half, c = wmod_idx[i]
        if wait_pe is not None:
            p.wait("gpsimd", "pe", wait_pe)
        wmod_loads[(half, c)] = ("dgx0" if c % 2 == 0 else "dgx1",
            p.do("gpsimd", lambda e, half=half, c=c: e.dma_start(
                out=V["wmod_ch"][:, c % 2, :],
                in_=dd["wmod"][c * 128:(c + 1) * 128, half * 3 * D:(half + 1) * 3 * D]),
                inc="dgx0" if c % 2 == 0 else "dgx1", inc_n=16))
    emit_wmod_load(0)
    emit_wmod_load(1)
    x_loads = {}
    def emit_x_load(sl, wait_dv=None):
        ap = dd["xT"][:, sl * 512:(sl + 1) * 512].rearrange("(c p) t -> p c t", p=128)
        if wait_dv is not None:
            p.wait("gpsimd", "dv", wait_dv)
        x_loads[sl] = ("dgy0" if sl % 2 == 0 else "dgy1",
            p.do("gpsimd", lambda e, ap=ap, sl=sl: e.dma_start(
                out=sb["xslab"][:, sl % 2, :, :], in_=ap),
                inc="dgy0" if sl % 2 == 0 else "dgy1", inc_n=16))
    emit_x_load(0)
    emit_x_load(1)

    p.wait("scalar", "ds", ds_consts)
    p.do("scalar", lambda e: e.activation(out=sb["tmpa"][:, 0:8], in_=sb["c_col"][:],
                                          func=AF.Sigmoid))
    p.do("scalar", lambda e: e.activation(out=sb["tmpa"][0:1, 8:9], in_=sb["gate"][0:1, :],
                                          func=AF.Sigmoid), inc="ac")
    p.wait("vector", "ac", p.cnt["ac"])
    dv_silu = p.do("vector", lambda e: e.tensor_tensor(
        out=sb["silu_c"][:], in0=sb["c_col"][:], in1=sb["tmpa"][:, 0:8], op=OP.mult), inc="dv")
    p.wait("tensor", "dv", dv_silu)
    for half in range(2):
        if half == 1:
            p.wait("tensor", "dv", p.cnt["dv"])
        for c in range(NCH):
            wsem, wval = wmod_loads[(half, c)]
            p.wait("tensor", wsem, wval)
            for npos in range(6):
                ps = pA[0:1, npos * 512:(npos + 1) * 512] if npos < 4 else \
                     pB[0:1, (npos - 4) * 512:(npos - 3) * 512]
                p.do("tensor", lambda e, ps=ps, c=c, npos=npos: e.matmul(
                    ps, sb["silu_c"][:, c:c + 1],
                    V["wmod_ch"][:, c % 2, npos * 512:(npos + 1) * 512],
                    start=(c == 0), stop=(c == NCH - 1)),
                    inc="pe" if npos == 5 else None)
            gi = half * NCH + c
            if gi + 2 < 16:
                emit_wmod_load(gi + 2, wait_pe=p.cnt["pe"])
        p.wait("vector", "pe", p.cnt["pe"])
        for npos in range(6):
            ps = pA[0:1, npos * 512:(npos + 1) * 512] if npos < 4 else \
                 pB[0:1, (npos - 4) * 512:(npos - 3) * 512]
            p.do("vector", lambda e, ps=ps, half=half, npos=npos: e.tensor_copy(
                out=V["mrow"][0:1, half * 3 * D + npos * 512: half * 3 * D + (npos + 1) * 512],
                in_=ps), inc="dv")
    dv_mod = p.cnt["dv"]

    p.wait("scalar", "dv", dv_mod)
    for idx in (2, 5):
        sl = slice(idx * D, (idx + 1) * D)
        for ph in range(2):
            p.wait("scalar", "dv", p.cnt["dv"])
            p.do("scalar", lambda e, sl=sl, ph=ph: e.activation(
                out=sb["rowr"][0:1, :], in_=V["mrow"][0:1, sl][:, ph * 512:(ph + 1) * 512],
                func=AF.Abs), inc="ac")
            p.wait("vector", "ac", p.cnt["ac"])
            p.do("vector", lambda e: e.tensor_scalar(
                out=sb["rowr"][0:1, :], in0=sb["rowr"][0:1, :],
                scalar1=0.25, scalar2=None, op0=OP.max))
            p.do("vector", lambda e, sl=sl, ph=ph: e.tensor_scalar(
                out=sb["rowm"][0:1, :], in0=V["mrow"][0:1, sl][:, ph * 512:(ph + 1) * 512],
                scalar1=0.0, scalar2=2.0, op0=OP.is_ge, op1=OP.mult))
            p.hop("vector")
            p.do("vector", lambda e, sl=sl, ph=ph: e.scalar_tensor_tensor(
                out=V["mrow"][0:1, sl][:, ph * 512:(ph + 1) * 512],
                in0=sb["rowm"][0:1, :], scalar=-1.0,
                in1=sb["rowr"][0:1, :], op0=OP.add, op1=OP.mult), inc="dv")
    dv_clamp = p.cnt["dv"]

    p.wait("tensor", "dv", dv_clamp)
    names = ("sa", "ca1", "ga", "sbv", "cb1", "gb")
    for vi, vn in enumerate(names):
        if vi > 0:
            p.wait("tensor", "dv", p.cnt["dv"])
        for c in range(NCH):
            p.do("tensor", lambda e, vi=vi, c=c: e.matmul(
                pB[:, 1024 + c:1025 + c],
                V["mrow"][0:1, vi * D + c * 128: vi * D + (c + 1) * 128],
                sb["ones_rf"][0:1, 0:1], start=True, stop=True),
                inc="pe" if c == NCH - 1 else None)
        p.wait("vector", "pe", p.cnt["pe"])
        if vn in ("ca1", "cb1"):
            p.do("vector", lambda e, vn=vn: e.tensor_scalar(
                out=sb[vn][:], in0=pB[:, 1024:1032], scalar1=1.0, scalar2=None,
                op0=OP.add), inc="dv")
        else:
            p.do("vector", lambda e, vn=vn: e.tensor_copy(
                out=sb[vn][:], in_=pB[:, 1024:1032]), inc="dv")
    p.do("vector", lambda e: e.tensor_copy(out=sb["tinyb"][0:1, 0:1], in_=sb["tmpa"][0:1, 8:9]),
         inc="dv")
    p.wait("tensor", "dv", p.cnt["dv"])
    p.do("tensor", lambda e: e.matmul(pB[:, 1024:1025], sb["ones_r"][:],
                                      sb["tinyb"][0:1, 0:1], start=True, stop=True), inc="pe")
    p.wait("vector", "pe", p.cnt["pe"])
    p.do("vector", lambda e: e.tensor_copy(out=sb["gate"][:], in_=pB[:, 1024:1025]))
    p.do("vector", lambda e: e.tensor_scalar(
        out=sb["gate1m"][:], in0=pB[:, 1024:1025], scalar1=-1.0, scalar2=1.0,
        op0=OP.mult, op1=OP.add))
    p.do("vector", lambda e: e.tensor_scalar(
        out=sb["bq_h"][:], in0=sb["bq_h"][:], scalar1=0.125, scalar2=None, op0=OP.mult),
        inc="dv")
    dv_modcols = p.cnt["dv"]
    if debug == 1:
        p.wait("gpsimd", "dv", dv_clamp)
        p.do("gpsimd", lambda e: e.dma_start(out=dbg["d_mod"], in_=V["mrow"]),
             inc="dgd", inc_n=16)
        p.dbg_barrier()

    for sl in range(8):
        xs = sb["xslab"][:, sl % 2, :, :]
        if sl % 2 == 0:
            rsum, rsq = pB[0:1, 1024:1536], pB[0:1, 1536:2048]
            rrep, rmrep = pB[:, 0:512], pB[:, 512:1024]
        else:
            rsum, rsq = pA[0:1, 0:512], pA[0:1, 512:1024]
            rrep, rmrep = pA[:, 1024:1536], pA[:, 1536:2048]
        xsem, xval = x_loads[sl]
        p.wait("vector", xsem, xval)
        if sl == 0:
            p.wait("vector", "dv", dv_modcols)
        p.do("vector", lambda e, xs=xs: e.tensor_copy(out=V["xbf1"], in_=xs))
        p.hop("vector")
        p.do("vector", lambda e: e.tensor_tensor(
            out=V["xsq1"], in0=V["xbf1"], in1=V["xbf1"], op=OP.mult), inc="dv")
        p.wait("tensor", "dv", p.cnt["dv"])
        for c in range(NCH):
            p.do("tensor", lambda e, c=c, rsum=rsum: e.matmul(
                rsum, sb["onesc"][:], V["xbf1"][:, c, :],
                start=(c == 0), stop=(c == NCH - 1)))
        for c in range(NCH):
            p.do("tensor", lambda e, c=c, rsq=rsq: e.matmul(
                rsq, sb["onesc"][:], V["xsq1"][:, c, :],
                start=(c == 0), stop=(c == NCH - 1)),
                inc="pe" if c == NCH - 1 else None)
        p.wait("vector", "pe", p.cnt["pe"])
        emit_ln_rows(p, sb, rsum, rsq, 512)
        p.wait("tensor", "dv", p.cnt["dv"])
        p.do("tensor", lambda e, rrep=rrep: e.matmul(
            rrep, sb["ones_r"][:], sb["rowb"][:], start=True, stop=True), inc="pe")
        p.wait("vector", "pe", p.cnt["pe"])
        p.do("vector", lambda e: e.tensor_copy(out=sb["rowb"][:], in_=sb["rowmr"][:]), inc="dv")
        p.wait("tensor", "dv", p.cnt["dv"])
        p.do("tensor", lambda e, rmrep=rmrep: e.matmul(
            rmrep, sb["ones_r"][:], sb["rowb"][:], start=True, stop=True), inc="pe")
        p.wait("vector", "pe", p.cnt["pe"])
        for c in range(NCH):
            p.do("vector", lambda e, xs=xs, c=c, rrep=rrep: e.tensor_tensor(
                out=sb["tmpa"][:], in0=xs[:, c, :], in1=rrep, op=OP.mult))
            p.hop("vector")
            p.do("vector", lambda e, c=c, rmrep=rmrep: e.tensor_tensor(
                out=sb["tmpa"][:], in0=sb["tmpa"][:], in1=rmrep, op=OP.subtract))
            p.hop("vector")
            p.do("vector", lambda e, sl=sl, c=c: e.tensor_scalar(
                out=V["h1"][:, c, sl * 512:(sl + 1) * 512], in0=sb["tmpa"][:],
                scalar1=sb["ca1"][:, c:c + 1], scalar2=sb["sa"][:, c:c + 1],
                op0=OP.mult, op1=OP.add), inc="dv" if c == NCH - 1 else None)
            if c < NCH - 1:
                p.hop("vector")
        if sl + 2 < 8:
            emit_x_load(sl + 2, wait_dv=p.cnt["dv"])
    dv_h1 = p.cnt["dv"]
    if debug == 1:
        p.wait("gpsimd", "dv", dv_h1)
        p.do("gpsimd", lambda e: e.dma_start(out=dbg["d_h1"], in_=V["h1"]),
             inc="dgd", inc_n=16)
        p.dbg_barrier()

    p.wait("scalar", "dgm", dg_relg)
    p.wait("tensor", "dv", dv_h1)
    for ch in range(5):
        cs = slice(ch * 512, (ch + 1) * 512) if ch < 4 else slice(2048, 2049)
        n = 512 if ch < 4 else 1
        p.wait("scalar", "pe", p.cnt["pe"])
        p.do("scalar", lambda e, cs=cs, n=n: e.activation(
            out=sb["sqs"][0:64, 0:n], in_=sb["relg"][0:64, cs], func=AF.Square), inc="ac")
        p.wait("tensor", "ac", p.cnt["ac"])
        if ch > 0:
            p.wait("tensor", "dv", p.cnt["dv"])
        p.do("tensor", lambda e, n=n: e.matmul(
            pB[0:2, 1024:1024 + n], sb["o642"][:], sb["sqs"][0:64, 0:n],
            start=True, stop=True), inc="pe")
        p.wait("vector", "pe", p.cnt["pe"])
        if ch == 0:
            p.do("vector", lambda e: e.reduce_max(
                out=sb["lnrel"][:], in_=pB[0:2, 1024:1536], axis=AX.X), inc="dv")
        else:
            p.do("vector", lambda e, n=n: e.reduce_max(
                out=sb["tiny"][:, 9:10], in_=pB[0:2, 1024:1024 + n], axis=AX.X))
            p.hop("vector")
            p.do("vector", lambda e: e.tensor_tensor(
                out=sb["lnrel"][:], in0=sb["lnrel"][:], in1=sb["tiny"][:, 9:10],
                op=OP.max), inc="dv")
    p.wait("scalar", "dv", p.cnt["dv"])
    p.do("scalar", lambda e: e.activation(out=sb["lnrel"][:], in_=sb["lnrel"][:],
                                          func=AF.Ln), inc="ac")
    ac_lnrel = p.cnt["ac"]
    p.do("vector", lambda e: e.memset(sb["eW"][:], 0.0))
    p.hop("vector")
    p.do("vector", lambda e: e.tensor_copy(out=sb["eW"][0:64, 0:1], in_=sb["relg"][0:64, 0:1]))
    p.do("vector", lambda e: e.tensor_copy(out=sb["eW"][0:64, 1:2],
                                           in_=sb["relg"][0:64, 2048:2049]))
    p.do("vector", lambda e: e.tensor_copy(out=sb["eW"][64:128, 2:3],
                                           in_=sb["relg"][64:128, 0:1]))
    p.do("vector", lambda e: e.tensor_copy(out=sb["eW"][64:128, 3:4],
                                           in_=sb["relg"][64:128, 2048:2049]), inc="dv")
    dv_eW = p.cnt["dv"]

    kb_ranges = [band_range(qb) for qb in range(16)]
    for hp in range(NP):
        gen_pair(p, dd, sb, V, pA, pB, hp, kb_ranges, dbg, debug, ac_lnrel, dv_eW)

    p.wait("gpsimd", "pe", p.cnt["pe"])
    p.wait("gpsimd", "dv", p.cnt["dv"])
    p.wait("gpsimd", "ac", p.cnt["ac"])
    p.wait("gpsimd", "dsc", p.cnt["dsc"])
    dg_wo = p.do("gpsimd", lambda e: e.dma_start(
        out=V["wo"], in_=dd["woT"].rearrange("(c p) m -> p c m", p=128)), inc="dgo", inc_n=16)
    p.wait("tensor", "dgo", dg_wo)
    p.wait("tensor", "dv", p.cnt["dv"])
    p.wait("tensor", "ac", p.cnt["ac"])
    p.wait("tensor", "dsc", p.cnt["dsc"])
    stag_dma = {}
    for blk in range(NCH):
        if blk > 0:
            p.wait("tensor", "dv", p.cnt["dv"])
        if blk >= 2:
            ssem, sval = stag_dma[blk - 2]
            p.wait("vector", ssem, sval)
        for c in range(NCH):
            for nch in range(4):
                ps = (pA if nch < 2 else pB)[:, (nch % 2) * 1024:(nch % 2) * 1024 + 512]
                p.do("tensor", lambda e, blk=blk, c=c, nch=nch, ps=ps: e.matmul(
                    ps, V["wo"][:, c, blk * 128:(blk + 1) * 128],
                    V["comb"][:, c, nch * 512:(nch + 1) * 512],
                    start=(c == 0), stop=(c == NCH - 1)),
                    inc="pe" if (c == NCH - 1 and nch == 3) else None)
        p.wait("vector", "pe", p.cnt["pe"])
        for nch in range(4):
            ps = (pA if nch < 2 else pB)[:, (nch % 2) * 1024:(nch % 2) * 1024 + 512]
            p.do("vector", lambda e, blk=blk, nch=nch, ps=ps: e.tensor_scalar(
                out=V["stag"][:, blk % 2, nch * 512:(nch + 1) * 512], in0=ps, scalar1=1.0,
                scalar2=sb["bo_c"][:, blk:blk + 1], op0=OP.mult, op1=OP.add),
                inc="dv" if nch == 3 else None)
        p.wait("sync", "dv", p.cnt["dv"])
        stag_dma[blk] = ("dss0" if blk % 2 == 0 else "dss1",
            p.do("sync", lambda e, blk=blk: e.dma_start(
                out=attn_sp[blk], in_=V["stag"][:, blk % 2, :]),
                inc="dss0" if blk % 2 == 0 else "dss1", inc_n=16))
    ds_attn = (p.cnt["dss0"], p.cnt["dss1"])
    if debug:
        p.wait("gpsimd", "dss0", ds_attn[0])
        p.wait("gpsimd", "dss1", ds_attn[1])
        p.do("gpsimd", lambda e: e.dma_start(out=dbg["d_attn"], in_=attn_sp), inc="dgd", inc_n=16)
        p.do("gpsimd", lambda e: e.dma_start(out=dbg["d_comb"], in_=V["comb"]), inc="dgd", inc_n=16)
        p.dbg_barrier()

    gen_ffn(p, dd, sb, V, pA, pB, out_cm, attn_sp, ds_attn, dbg, debug)

    p.wait("sync", "dso", p.cnt["dso"])
    p.wait("sync", "dsc", p.cnt["dsc"])
    p.wait("sync", "dsg", p.cnt["dsg"])
    p.wait("sync", "dss0", p.cnt["dss0"])
    p.wait("sync", "dss1", p.cnt["dss1"])
    p.wait("sync", "ds", p.cnt["ds"])
    for s in ("dgm", "dgz", "dgw", "dgo", "dgx0", "dgx1", "dgy0", "dgy1", "dgf", "dgd",
                  "hv", "ha", "ht", "hg"):
        p.wait("gpsimd", s, p.cnt[s])


def emit_ln_rows(p, sb, rsum, rsq, n):
    p.do("vector", lambda e, rsum=rsum: e.tensor_scalar(
        out=sb["rowm"][0:1, 0:n], in0=rsum, scalar1=1.0 / D, scalar2=None, op0=OP.mult))
    p.hop("vector")
    p.do("vector", lambda e: e.tensor_tensor(
        out=sb["rowr"][0:1, 0:n], in0=sb["rowm"][0:1, 0:n], in1=sb["rowm"][0:1, 0:n],
        op=OP.mult))
    p.hop("vector")
    p.do("vector", lambda e: e.tensor_scalar(
        out=sb["rowr"][0:1, 0:n], in0=sb["rowr"][0:1, 0:n], scalar1=-1e-5, scalar2=None,
        op0=OP.add))
    p.hop("vector")
    p.do("vector", lambda e, rsq=rsq: e.scalar_tensor_tensor(
        out=sb["rowr"][0:1, 0:n], in0=rsq, scalar=1.0 / D, in1=sb["rowr"][0:1, 0:n],
        op0=OP.mult, op1=OP.subtract), inc="dv")
    p.wait("scalar", "dv", p.cnt["dv"])
    p.do("scalar", lambda e: e.activation(out=sb["rowr"][0:1, 0:n],
                                          in_=sb["rowr"][0:1, 0:n], func=AF.Sqrt), inc="ac")
    p.wait("vector", "ac", p.cnt["ac"])
    p.do("vector", lambda e: e.reciprocal(out=sb["rowr"][0:1, 0:n], in_=sb["rowr"][0:1, 0:n]))
    p.hop("vector")
    p.do("vector", lambda e: e.tensor_tensor(
        out=sb["rowmr"][0:1, 0:n], in0=sb["rowm"][0:1, 0:n], in1=sb["rowr"][0:1, 0:n],
        op=OP.mult))
    p.do("vector", lambda e: e.tensor_copy(out=sb["rowb"][0:1, 0:n],
                                           in_=sb["rowr"][0:1, 0:n]), inc="dv")


def gen_pair(p, dd, sb, V, pA, pB, hp, kb_ranges, dbg, debug, ac_lnrel, dv_eW):
    p.wait("gpsimd", "pe", p.cnt["pe"])
    lw = []
    for wn, dn in (("wq", "wq_p"), ("wk", "wk_p"), ("wv", "wv_p")):
        lw.append(p.do("gpsimd", lambda e, wn=wn, dn=dn: e.dma_start(
            out=V["wqkv_" + wn],
            in_=dd[dn][hp].rearrange("(c p) m -> p c m", p=128)), inc="dgw", inc_n=16))
    dg_mz = p.do("gpsimd", lambda e: e.dma_start(out=sb["mz"][:], in_=dd["mzp"][hp]),
                 inc="dgz", inc_n=16)

    p.wait("tensor", "dgw", lw[2])
    p.wait("tensor", "ac", p.cnt["ac"])
    p.wait("tensor", "dv", p.cnt["dv"])
    for c in range(NCH):
        for nch in range(4):
            p.do("tensor", lambda e, c=c, nch=nch: e.matmul(
                pA[:, nch * 512:(nch + 1) * 512], V["wqkv_wq"][:, c, :],
                V["h1"][:, c, nch * 512:(nch + 1) * 512],
                start=(c == 0), stop=(c == NCH - 1)),
                inc="pe" if (c == NCH - 1 and nch == 3) else None)
    p.wait("vector", "pe", p.cnt["pe"])
    p.do("vector", lambda e: e.tensor_scalar(
        out=V["qh"][:, :], in0=pA[:, 0:2048], scalar1=0.125,
        scalar2=sb["bq_h"][:, hp:hp + 1], op0=OP.mult, op1=OP.add), inc="dv")
    dv_q = p.cnt["dv"]
    for half in range(2):
        p.wait("tensor", "dv", p.cnt["dv"])
        for c in range(NCH):
            for nch in range(4):
                t0 = half * 2048 + nch * 512
                p.do("tensor", lambda e, c=c, nch=nch, t0=t0: e.matmul(
                    pA[:, nch * 512:(nch + 1) * 512], V["wqkv_wk"][:, c, :],
                    V["h1"][:, c, t0:t0 + 512],
                    start=(c == 0), stop=(c == NCH - 1)),
                    inc="pe" if (c == NCH - 1 and nch == 3) else None)
        p.wait("vector", "pe", p.cnt["pe"])
        p.do("vector", lambda e, half=half: e.tensor_scalar(
            out=V["kk"][:, half * 2048:(half + 1) * 2048], in0=pA[:, 0:2048], scalar1=1.0,
            scalar2=sb["bk_c"][:, hp:hp + 1], op0=OP.mult, op1=OP.add), inc="dv")
    for vch in range(4):
        p.wait("tensor", "dv", p.cnt["dv"])
        for c in range(NCH):
            for nn in range(2):
                t0 = vch * 1024 + nn * 512
                p.do("tensor", lambda e, c=c, nn=nn, t0=t0: e.matmul(
                    pB[:, nn * 512:(nn + 1) * 512], V["wqkv_wv"][:, c, :],
                    V["h1"][:, c, t0:t0 + 512],
                    start=(c == 0), stop=(c == NCH - 1)),
                    inc="pe" if (c == NCH - 1 and nn == 1) else None)
        p.wait("vector", "pe", p.cnt["pe"])
        p.do("vector", lambda e: e.tensor_scalar(
            out=V["vf"][:], in0=pB[:, 0:1024], scalar1=1.0,
            scalar2=sb["bv_c"][:, hp:hp + 1], op0=OP.mult, op1=OP.add), inc="dv")
        p.wait("tensor", "dv", p.cnt["dv"])
        pBb = pB[:, 1024:1536].bitcast(BF16)
        for t in range(8):
            p.do("tensor", lambda e, t=t, pBb=pBb: e.transpose(
                pBb[:, t * 128:(t + 1) * 128], V["vf"][:, t * 128:(t + 1) * 128],
                sb["ident"][:]), inc="pe" if t == 7 else None)
        p.wait("vector", "pe", p.cnt["pe"])
        for t in range(8):
            kb = vch * 8 + t
            p.do("vector", lambda e, t=t, kb=kb, pBb=pBb: e.tensor_copy(
                out=V["vp"][:, kb, 0, 0:64],
                in_=pBb[:, t * 128:t * 128 + 64]))
            p.do("vector", lambda e, t=t, kb=kb, pBb=pBb: e.tensor_copy(
                out=V["vp"][:, kb, 1, 0:64],
                in_=pBb[:, t * 128 + 64:t * 128 + 128]),
                inc="dv" if t == 7 else None)
        p.wait("tensor", "dv", p.cnt["dv"])
    p.do("vector", lambda e: e.memset(V["vp"][:, :, :, 64:65], 1.0), inc="dv")


    for si, (srcv, n) in enumerate(((V["qh"], SQ), (V["kk"], S))):
        for ch in range(n // 512):
            p.do("vector", lambda e, srcv=srcv, ch=ch: e.tensor_tensor(
                out=sb["sqs"][:], in0=srcv[:, ch * 512:(ch + 1) * 512],
                in1=srcv[:, ch * 512:(ch + 1) * 512], op=OP.mult), inc="dv")
            p.wait("tensor", "dv", p.cnt["dv"])
            p.do("tensor", lambda e: e.matmul(
                pB[0:2, 1024:1536], sb["blkd"][:], sb["sqs"][:],
                start=True, stop=True), inc="pe")
            p.wait("vector", "pe", p.cnt["pe"])
            dst = sb["tiny"][:, si:si + 1]
            if ch == 0:
                p.do("vector", lambda e, dst=dst: e.reduce_max(
                    out=dst, in_=pB[0:2, 1024:1536], axis=AX.X), inc="dv")
            else:
                p.do("vector", lambda e: e.reduce_max(
                    out=sb["tiny"][:, 9:10], in_=pB[0:2, 1024:1536], axis=AX.X))
                p.hop("vector")
                p.do("vector", lambda e, dst=dst: e.tensor_tensor(
                    out=dst, in0=dst, in1=sb["tiny"][:, 9:10], op=OP.max), inc="dv")
    p.wait("scalar", "dv", p.cnt["dv"])
    p.wait("scalar", "ac", ac_lnrel)
    p.do("scalar", lambda e: e.activation(out=sb["tiny"][:, 2:3], in_=sb["tiny"][:, 0:1],
                                          func=AF.Ln))
    p.do("scalar", lambda e: e.activation(out=sb["tiny"][:, 3:4], in_=sb["tiny"][:, 1:2],
                                          func=AF.Ln), inc="ac")
    p.wait("vector", "ac", p.cnt["ac"])
    p.do("vector", lambda e: e.tensor_tensor(out=sb["tiny"][:, 4:5], in0=sb["tiny"][:, 2:3],
                                             in1=sb["tiny"][:, 3:4], op=OP.add))
    p.hop("vector")
    p.do("vector", lambda e: e.tensor_scalar(out=sb["tiny"][:, 4:5], in0=sb["tiny"][:, 4:5],
                                             scalar1=0.5, scalar2=None, op0=OP.mult))
    p.do("vector", lambda e: e.tensor_tensor(out=sb["tiny"][:, 5:6], in0=sb["tiny"][:, 2:3],
                                             in1=sb["lnrel"][:], op=OP.add))
    p.hop("vector")
    p.do("vector", lambda e: e.tensor_scalar(out=sb["tiny"][:, 5:6], in0=sb["tiny"][:, 5:6],
                                             scalar1=0.5, scalar2=None, op0=OP.mult), inc="dv")
    p.wait("scalar", "dv", p.cnt["dv"])
    p.do("scalar", lambda e: e.activation(out=sb["tiny"][:, 6:7], in_=sb["tiny"][:, 4:5],
                                          func=AF.Exp))
    p.do("scalar", lambda e: e.activation(out=sb["tiny"][:, 7:8], in_=sb["tiny"][:, 5:6],
                                          func=AF.Exp), inc="ac")
    p.wait("vector", "ac", p.cnt["ac"])
    p.do("vector", lambda e: e.tensor_tensor(out=sb["tiny"][:, 8:9], in0=sb["tiny"][:, 6:7],
                                             in1=sb["tiny"][:, 7:8], op=OP.add))
    p.hop("vector")
    p.do("vector", lambda e: e.tensor_copy(out=sb["tinyb"][:, 0:1], in_=sb["tiny"][:, 8:9]),
         inc="dv")
    p.wait("tensor", "dv", p.cnt["dv"])
    p.do("tensor", lambda e: e.matmul(pB[0:1, 1024:1026], sb["tinyb"][:, 0:1], sb["i2"][:],
                                      start=True, stop=True), inc="pe")
    p.wait("vector", "pe", p.cnt["pe"])
    p.do("vector", lambda e: e.reduce_max(out=sb["tiny"][0:1, 9:10], in_=pB[0:1, 1024:1026],
                                          axis=AX.X))
    p.hop("vector")
    p.do("vector", lambda e: e.tensor_copy(out=sb["tinyb"][0:1, 1:2],
                                           in_=sb["tiny"][0:1, 9:10]), inc="dv")
    p.wait("tensor", "dv", p.cnt["dv"])
    p.do("tensor", lambda e: e.matmul(pB[:, 1026:1027], sb["ones_r"][:], sb["tinyb"][0:1, 1:2],
                                      start=True, stop=True), inc="pe")
    p.wait("scalar", "pe", p.cnt["pe"])
    p.do("scalar", lambda e: e.activation(out=sb["negb"][:], in_=pB[:, 1026:1027],
                                          func=AF.Copy, scale=-1.0), inc="ac")
    ac_negb = p.cnt["ac"]

    p.wait("tensor", "dv", dv_eW)
    p.wait("tensor", "ac", ac_negb)
    for nch in range(4):
        reg = pB[0:4, 1024 + (nch % 2) * 512:1536 + (nch % 2) * 512]
        if nch >= 2:
            p.wait("tensor", "dv", p.cnt["dv"])
        p.do("tensor", lambda e, nch=nch, reg=reg: e.matmul(
            reg, sb["eW"][:], V["qh"][:, nch * 512:(nch + 1) * 512],
            start=True, stop=True), inc="pe")
        p.wait("vector", "pe", p.cnt["pe"])
        p.do("vector", lambda e, nch=nch, reg=reg: e.tensor_copy(
            out=sb["edges4"][:, nch * 512:(nch + 1) * 512],
            in_=reg), inc="dv")
    dv_edges = p.cnt["dv"]
    if debug == 1 and hp == 0:
        p.wait("gpsimd", "ac", ac_negb)
        p.wait("gpsimd", "dv", dv_edges)
        for nm, src in (("d_negb", sb["negb"][:]), ("d_qh", V["qh"]), ("d_k", V["kk"]),
                        ("d_vp", V["vp"]), ("d_edge", sb["edges4"][:])):
            p.do("gpsimd", lambda e, nm=nm, src=src: e.dma_start(out=dbg[nm], in_=src),
                 inc="dgd", inc_n=16)
        p.dbg_barrier()

    for qg in range(NG):
        gen_group(p, sb, V, pA, pB, hp, qg, kb_ranges, dbg, debug)


def rg_layout(qg):
    offs = {}
    pos = 0
    for qi in range(GQB):
        qb = qg * GQB + qi
        kb_lo, kb_hi = band_range(qb)
        w = (kb_hi - kb_lo + 1) * 128
        for h in range(2):
            offs[(qi, h)] = (pos, w)
            pos += w
    assert pos <= 18432, pos
    return offs


def gen_group(p, sb, V, pA, pB, hp, qg, kb_ranges, dbg, debug):
    qbs = [qg * GQB + i for i in range(GQB)]
    q0g = qg * GQB * 128
    offs = rg_layout(qg)

    p.wait("tensor", "ac", p.cnt["ac"])
    p.wait("tensor", "dv", p.cnt["dv"])
    pe_prev_group = p.cnt["pe"]
    gath = []
    for qi, qb in enumerate(qbs):
        kb_lo, kb_hi = kb_ranges[qb]
        wb = (kb_hi - kb_lo + 1) * 128
        p.wait("tensor", "dv", p.cnt["dv"])
        p.wait("vector", "dsg", p.cnt["dsg"])
        for j in range(4):
            p.do("tensor", lambda e, j=j, qb=qb: e.matmul(
                pA[:, j:j + 1], sb["edges4"][:, qb * 128:(qb + 1) * 128],
                sb["sel_c"][:, j:j + 1], start=True, stop=True),
                inc="pe" if j == 3 else None)
        p.wait("vector", "pe", p.cnt["pe"])
        for h in range(2):
            p.do("vector", lambda e, h=h: e.tensor_copy(
                out=V["pg"][:, h, 0:PAD],
                in_=pA[:, 2 * h:2 * h + 1].broadcast_to([128, PAD])))
            p.do("vector", lambda e, h=h: e.tensor_copy(
                out=V["pg"][:, h, PAD + 2048:CP],
                in_=pA[:, 2 * h + 1:2 * h + 2].broadcast_to([128, CP - PAD - 2048])),
                inc="dv" if h == 1 else None)
        dv_pads = p.cnt["dv"]
        p.wait("tensor", "dv", dv_pads)
        for nn in range(4):
            p.do("tensor", lambda e, qb=qb, nn=nn: e.matmul(
                pA[:, nn * 512:(nn + 1) * 512], V["qh"][0:64, qb * 128:(qb + 1) * 128],
                sb["relg"][0:64, nn * 512:(nn + 1) * 512], start=True, stop=True,
                tile_position=(0, 0)))
            p.do("tensor", lambda e, qb=qb, nn=nn: e.matmul(
                pB[:, nn * 512:(nn + 1) * 512], V["qh"][64:128, qb * 128:(qb + 1) * 128],
                sb["relg"][64:128, nn * 512:(nn + 1) * 512], start=True, stop=True,
                tile_position=(64, 0)), inc="pe" if nn == 3 else None)
        p.wait("vector", "pe", p.cnt["pe"])
        p.do("vector", lambda e: e.tensor_copy(
            out=V["pg"][:, 0, PAD:PAD + 2048], in_=pA[:, 0:2048]))
        p.do("vector", lambda e: e.tensor_copy(
            out=V["pg"][:, 1, PAD:PAD + 2048], in_=pB[:, 0:2048]), inc="dv")
        dv_pg = p.cnt["dv"]
        p.wait("sync", "pe", pe_prev_group)
        p.wait("sync", "dv", dv_pg)
        pg_t = V["pg"].tensor
        row_ext = sb["arena"][:].ap[0][0]
        pg_off = V["pg"].offset
        for h in range(2):
            colstart = kb_lo * 128 - qb * 128 + 1024 + PAD
            assert colstart - 127 >= 0 and colstart + wb - 1 <= CP - 1
            off0, w0 = offs[(qi, h)]
            assert w0 == wb
            src = AP(pg_t, pg_off + h * CP + colstart,
                     [[row_ext - 1, 128], [1, wb]])
            g = p.do("sync", lambda e, src=src, off0=off0, wb=wb: e.dma_start(
                out=V["rg"][:, off0:off0 + wb], in_=src), inc="dsg", inc_n=16)
            gath.append(g)
        if debug == 1 and hp == 0 and qg == 0 and qb == 0:
            p.wait("gpsimd", "dv", dv_pg)
            p.do("gpsimd", lambda e: e.dma_start(
                out=dbg["d_pg"], in_=V["pg"][:, 0, :]), inc="dgd", inc_n=16)
            p.wait("gpsimd", "dsg", p.cnt["dsg"])
            p.do("gpsimd", lambda e, wb=wb: e.dma_start(
                out=dbg["d_rg"][:, 0:wb], in_=V["rg"][:, 0:wb]), inc="dgd", inc_n=16)
            p.dbg_barrier()

    p.wait("tensor", "dsg", gath[-1])
    p.wait("tensor", "dv", p.cnt["dv"])
    exp_base = p.cnt["ac"]
    sdone = []
    avdone = []

    def emit_S(kb):
        pp = kb % 2
        reg = pA[:, pp * 1024:(pp + 1) * 1024]
        if kb >= 2:
            p.wait("tensor", "ac", exp_base + (kb - 1))
        for h in range(2):
            hreg = reg[:, h * 512:(h + 1) * 512]
            tp = (0, 0) if h == 0 else (64, 0)
            band = [(qi, qb, kb - kb_ranges[qb][0]) for qi, qb in enumerate(qbs)
                    if kb_ranges[qb][0] <= kb <= kb_ranges[qb][1]]
            farA = [qi for qi, qb in enumerate(qbs) if kb < kb_ranges[qb][0]]
            farB = [qi for qi, qb in enumerate(qbs) if kb > kb_ranges[qb][1]]
            nacc = len(band) + (1 if farA else 0) + (1 if farB else 0)
            p.do("tensor", lambda e, hreg=hreg, h=h, kb=kb, tp=tp, nacc=nacc: e.matmul(
                hreg, V["kk"][h * 64:(h + 1) * 64, kb * 128:(kb + 1) * 128],
                V["qh"][h * 64:(h + 1) * 64, q0g:q0g + 512],
                start=True, stop=(nacc == 0), tile_position=tp))
            i = 0
            for qi, qb, off in band:
                i += 1
                off0, w0 = offs[(qi, h)]
                last = (i == nacc)
                p.do("tensor", lambda e, hreg=hreg, qi=qi, off0=off0, off=off, last=last: e.matmul(
                    hreg[:, qi * 128:(qi + 1) * 128],
                    V["rg"][:, off0 + off * 128:off0 + (off + 1) * 128],
                    sb["ident"][:], start=False, stop=last))
            for falist, eidx in ((farA, 0), (farB, 1)):
                if not falist:
                    continue
                i += 1
                lo, hi = falist[0], falist[-1]
                assert falist == list(range(lo, hi + 1))
                last = (i == nacc)
                j = 2 * h + eidx
                p.do("tensor", lambda e, hreg=hreg, j=j, lo=lo, hi=hi, last=last: e.matmul(
                    hreg[:, lo * 128:(hi + 1) * 128], sb["sel_r"][:, j * 128:(j + 1) * 128],
                    sb["edges4"][:, q0g + lo * 128:q0g + (hi + 1) * 128],
                    start=False, stop=last))
        v = p.inc_last("tensor", "pe")
        sdone.append(v)

    def emit_attnv(kb):
        p.wait("tensor", "ac", exp_base + kb + 1)
        for h in range(2):
            p.do("tensor", lambda e, kb=kb, h=h: e.matmul(
                pB[0:65, h * 512:(h + 1) * 512], V["vp"][:, kb, h, :],
                V["ee"][:, kb % 2, h * 512:(h + 1) * 512],
                start=(kb == 0), stop=(kb == NKB - 1)))
        v = p.inc_last("tensor", "pe")
        avdone.append(v)

    emit_S(0)
    emit_S(1)
    for kb in range(NKB):
        if kb + 2 < NKB:
            emit_S(kb + 2)
        emit_attnv(kb)
    for kb in range(NKB):
        p.wait("scalar", "pe", sdone[kb])
        if kb >= 2:
            p.wait("scalar", "pe", avdone[kb - 2])
        p.do("scalar", lambda e, kb=kb: e.activation(
            out=V["ee"][:, kb % 2, :], in_=pA[:, (kb % 2) * 1024:(kb % 2 + 1) * 1024],
            func=AF.Exp, bias=sb["negb"][:], scale=1.0), inc="ac")

    p.wait("scalar", "pe", avdone[-1])
    p.do("scalar", lambda e: e.activation(out=sb["o65"][:, 0, :], in_=pB[0:65, 0:512],
                                          func=AF.Copy))
    p.do("scalar", lambda e: e.activation(out=sb["o65"][:, 1, :], in_=pB[0:65, 512:1024],
                                          func=AF.Copy), inc="ac")
    ac_o65 = p.cnt["ac"]
    if debug == 1 and hp == 0 and qg == 0:
        p.wait("gpsimd", "ac", ac_o65)
        p.do("gpsimd", lambda e: e.dma_start(out=dbg["d_o65"], in_=sb["o65"][:]),
             inc="dgd", inc_n=16)
        p.dbg_barrier()
    p.do("vector", lambda e: e.tensor_scalar(
        out=V["sigm"][:, :], in0=V["qh"][:, q0g:q0g + 512], scalar1=0.0, scalar2=None,
        op0=OP.min), inc="dv")
    p.wait("scalar", "dv", p.cnt["dv"])
    p.do("scalar", lambda e: e.activation(out=V["sig"][:, :], in_=V["sigm"][:, :],
                                          func=AF.Exp, scale=8.0), inc="ac")
    p.wait("vector", "ac", p.cnt["ac"])
    p.do("vector", lambda e: e.tensor_scalar(
        out=V["sigm"][:, :], in0=V["qh"][:, q0g:q0g + 512], scalar1=0.0, scalar2=None,
        op0=OP.max))
    p.hop("vector")
    p.do("vector", lambda e: e.scalar_tensor_tensor(
        out=V["sig"][:, :], in0=V["sigm"][:, :], scalar=8.0, in1=V["sig"][:, :],
        op0=OP.mult, op1=OP.add), inc="dv")
    p.wait("tensor", "ac", ac_o65)
    p.wait("tensor", "dv", p.cnt["dv"])
    p.wait("tensor", "dgz", p.cnt["dgz"])
    p.do("tensor", lambda e: e.matmul(pB[0:65, 0:512], sb["mz"][0:64, :],
                                      V["sig"][0:64, :], start=True, stop=True,
                                      tile_position=(0, 0)))
    p.do("tensor", lambda e: e.matmul(pB[0:65, 512:1024], sb["mz"][64:128, :],
                                      V["sig"][64:128, :], start=True, stop=True,
                                      tile_position=(64, 0)), inc="pe")
    p.wait("vector", "pe", p.cnt["pe"])
    for h in range(2):
        p.do("vector", lambda e, h=h: e.reciprocal(out=sb["o65"][64:65, h, :],
                                                   in_=sb["o65"][64:65, h, :]))
        p.do("vector", lambda e, h=h: e.tensor_scalar(
            out=sb["tmpa"][0:1, 0:512], in0=pB[64:65, h * 512:(h + 1) * 512],
            scalar1=1e-6, scalar2=None, op0=OP.add))
        p.hop("vector")
        p.do("vector", lambda e: e.reciprocal(out=sb["tmpa"][0:1, 0:512],
                                              in_=sb["tmpa"][0:1, 0:512]))
        p.hop("vector")
        p.do("vector", lambda e, h=h: e.tensor_copy(out=sb["rowb"][:],
                                                    in_=sb["o65"][64:65, h, :]), inc="dv")
        p.wait("tensor", "dv", p.cnt["dv"])
        p.do("tensor", lambda e: e.matmul(
            pA[0:64, 0:512], sb["ones_r"][0:1, 0:64], sb["rowb"][:],
            start=True, stop=True), inc="pe")
        p.wait("vector", "pe", p.cnt["pe"])
        p.do("vector", lambda e: e.tensor_copy(out=sb["rowb"][:], in_=sb["tmpa"][0:1, 0:512]),
             inc="dv")
        p.wait("tensor", "dv", p.cnt["dv"])
        p.do("tensor", lambda e: e.matmul(
            pA[0:64, 512:1024], sb["ones_r"][0:1, 0:64], sb["rowb"][:],
            start=True, stop=True), inc="pe")
        p.wait("vector", "pe", p.cnt["pe"])
        p.do("vector", lambda e, h=h: e.tensor_copy(
            out=sb["ct"][:, 1, :], in_=pB[0:64, h * 512:(h + 1) * 512]), inc="dv")
        p.do("vector", lambda e, h=h: e.scalar_tensor_tensor(
            out=sb["ct"][:, 0, :], in0=sb["o65"][0:64, h, :], scalar=sb["gate"][0:64, :],
            in1=pA[0:64, 0:512], op0=OP.mult, op1=OP.mult))
        p.hop("vector")
        p.do("vector", lambda e, h=h: e.scalar_tensor_tensor(
            out=sb["ct"][:, 1, :], in0=sb["ct"][:, 1, :], scalar=sb["gate1m"][0:64, :],
            in1=pA[0:64, 512:1024], op0=OP.mult, op1=OP.mult))
        p.hop("vector")
        if h == 0:
            p.do("vector", lambda e: e.tensor_tensor(
                out=V["comb"][0:64, hp, q0g:q0g + 512], in0=sb["ct"][:, 0, :],
                in1=sb["ct"][:, 1, :], op=OP.add), inc="dv")
        else:
            p.do("vector", lambda e: e.tensor_tensor(
                out=sb["ct"][:, 0, :], in0=sb["ct"][:, 0, :],
                in1=sb["ct"][:, 1, :], op=OP.add), inc="dv")
            p.wait("sync", "dv", p.cnt["dv"])
            p.do("sync", lambda e: e.dma_start(
                out=V["comb"][64:128, hp, q0g:q0g + 512], in_=sb["ct"][:, 0, :]),
                inc="dsc", inc_n=16)
    p.wait("vector", "dsc", p.cnt["dsc"])


def gen_ffn(p, dd, sb, V, pA, pB, out_cm, attn_sp, ds_attn, dbg, debug):
    def wload(dname, c, width, wait_ac=None):
        if wait_ac is not None:
            p.wait("gpsimd", "ac", wait_ac)
        s = "dgx0" if c % 2 == 0 else "dgx1"
        return (s, p.do("gpsimd", lambda e, c=c, dname=dname, width=width: e.dma_start(
            out=V["wsl"][:, c % 2, 0:width], in_=dd[dname][c * 128:(c + 1) * 128, :]),
            inc=s, inc_n=16))

    p.wait("tensor", "ac", p.cnt["ac"])
    p.wait("tensor", "dv", p.cnt["dv"])
    p.wait("gpsimd", "dss0", ds_attn[0])
    p.wait("gpsimd", "dss1", ds_attn[1])
    loads = {0: wload("w1T", 0, 4 * D), 1: wload("w1T", 1, 4 * D)}
    done = {}
    p.wait("scalar", "pe", p.cnt["pe"])
    for c in range(NCH):
        lsem, lval = loads[c]
        p.wait("scalar", lsem, lval)
        p.do("scalar", lambda e, c=c: e.activation(
            out=V["s1"][:, c, :], in_=V["wsl"][:, c % 2, :], func=AF.Sign), inc="ac")
        for ch in range(8):
            p.wait("scalar", "pe", p.cnt["pe"])
            p.do("scalar", lambda e, c=c, ch=ch: e.activation(
                out=sb["sqs"][:], in_=V["wsl"][:, c % 2, ch * 512:(ch + 1) * 512],
                func=AF.Abs), inc="ac")
            p.wait("tensor", "ac", p.cnt["ac"])
            ps = (pA if ch < 4 else pB)[0:1, (ch % 4) * 512:(ch % 4 + 1) * 512]
            p.do("tensor", lambda e, ps=ps, c=c: e.matmul(
                ps, sb["onesc"][:], sb["sqs"][:], start=(c == 0), stop=(c == NCH - 1)),
                inc="pe")
        done[c] = p.cnt["ac"]
        if c + 2 < NCH:
            loads[c + 2] = wload("w1T", c + 2, 4 * D, wait_ac=done[c])
    p.wait("vector", "pe", p.cnt["pe"])
    for ch in range(8):
        ps = (pA if ch < 4 else pB)[0:1, (ch % 4) * 512:(ch % 4 + 1) * 512]
        p.do("vector", lambda e, ps=ps: e.tensor_scalar(
            out=sb["rowb"][:], in0=ps, scalar1=1.0 / D, scalar2=None, op0=OP.mult),
            inc="dv")
        p.wait("tensor", "dv", p.cnt["dv"])
        for q in range(4):
            p.do("tensor", lambda e, ch=ch, q=q: e.matmul(
                pB[:, 1536 + 4 * (ch % 2) + q:1537 + 4 * (ch % 2) + q],
                sb["rowb"][0:1, q * 128:(q + 1) * 128],
                sb["ones_r"][0:1, 0:1], start=True, stop=True),
                inc="pe" if q == 3 else None)
        p.wait("vector", "pe", p.cnt["pe"])
        p.do("vector", lambda e, ch=ch: e.tensor_copy(
            out=sb["a1c"][:, ch * 4:(ch + 1) * 4],
            in_=pB[:, 1536 + 4 * (ch % 2):1540 + 4 * (ch % 2)]), inc="dv")
    loads = {0: wload("w2T", 0, D, wait_ac=done[NCH - 2]),
             1: wload("w2T", 1, D, wait_ac=done[NCH - 1])}
    done2 = {}
    p.wait("scalar", "pe", p.cnt["pe"])
    p.wait("scalar", "dss0", p.cnt["dss0"])
    p.wait("scalar", "dss1", p.cnt["dss1"])
    for c in range(32):
        lsem, lval = loads[c]
        p.wait("scalar", lsem, lval)
        p.do("scalar", lambda e, c=c: e.activation(
            out=V["s2"][:, c, :], in_=V["wsl"][:, c % 2, 0:1024], func=AF.Sign), inc="ac")
        for ch in range(2):
            p.wait("scalar", "pe", p.cnt["pe"])
            p.do("scalar", lambda e, c=c, ch=ch: e.activation(
                out=sb["sqs"][:], in_=V["wsl"][:, c % 2, ch * 512:(ch + 1) * 512],
                func=AF.Abs), inc="ac")
            p.wait("tensor", "ac", p.cnt["ac"])
            ps = pA[0:1, ch * 512:(ch + 1) * 512]
            p.do("tensor", lambda e, ps=ps, c=c: e.matmul(
                ps, sb["onesc"][:], sb["sqs"][:], start=(c == 0), stop=(c == 31)),
                inc="pe")
        done2[c] = p.cnt["ac"]
        if c + 2 < 32:
            loads[c + 2] = wload("w2T", c + 2, D, wait_ac=done2[c])
    p.wait("vector", "pe", p.cnt["pe"])
    for ch in range(2):
        ps = pA[0:1, ch * 512:(ch + 1) * 512]
        p.do("vector", lambda e, ps=ps: e.tensor_scalar(
            out=sb["rowb"][:], in0=ps, scalar1=1.0 / (4 * D), scalar2=None, op0=OP.mult),
            inc="dv")
        p.wait("tensor", "dv", p.cnt["dv"])
        for q in range(4):
            p.do("tensor", lambda e, ch=ch, q=q: e.matmul(
                pB[:, 1536 + 4 * ch + q:1537 + 4 * ch + q],
                sb["rowb"][0:1, q * 128:(q + 1) * 128],
                sb["ones_r"][0:1, 0:1], start=True, stop=True),
                inc="pe" if q == 3 else None)
        p.wait("vector", "pe", p.cnt["pe"])
        p.do("vector", lambda e, ch=ch: e.tensor_copy(
            out=sb["a2c"][:, ch * 4:(ch + 1) * 4],
            in_=pB[:, 1536 + 4 * ch:1540 + 4 * ch]), inc="dv")
    dv_bw = p.cnt["dv"]
    if debug == 1:
        p.wait("gpsimd", "dv", dv_bw)
        p.do("gpsimd", lambda e: e.dma_start(out=dbg["d_s1"], in_=V["s1"]),
             inc="dgd", inc_n=16)
        p.dbg_barrier()

    xloads = {}
    atloads = {}
    ac_signs_done = p.cnt["ac"]
    def emit_fx(sl, wait_dv=None):
        ap = dd["xT"][:, sl * FSL:(sl + 1) * FSL].rearrange("(c p) t -> p c t", p=128)
        if wait_dv is not None:
            p.wait("gpsimd", "dv", wait_dv)
        if sl == 0:
            p.wait("gpsimd", "ac", ac_signs_done)
        xloads[sl] = p.do("gpsimd", lambda e, ap=ap: e.dma_start(
            out=V["fx"], in_=ap), inc="dgf", inc_n=16)
        ap2 = attn_sp[:, :, sl * FSL:(sl + 1) * FSL].rearrange("c p t -> p c t")
        atloads[sl] = p.do("gpsimd", lambda e, ap2=ap2: e.dma_start(
            out=V["fat"], in_=ap2), inc="dgf", inc_n=16)
    emit_fx(0)

    R8 = [pA[:, i * 512:(i + 1) * 512] for i in range(4)] + \
         [pB[:, i * 512:(i + 1) * 512] for i in range(4)]
    out_dma = {}
    import os
    SLAB_BARRIER = os.environ.get("DK_SLAB_BARRIER", "0") == "1"
    for sl in range(NFS):
        if SLAB_BARRIER and sl > 0:
            for eng in ("tensor", "scalar", "vector", "gpsimd", "sync"):
                p.wait(eng, "pe", p.cnt["pe"])
                p.wait(eng, "ac", p.cnt["ac"])
                p.wait(eng, "dv", p.cnt["dv"])
                p.wait(eng, "dso", p.cnt["dso"])
        t0 = sl * FSL
        p.wait("vector", "dgf", atloads[sl])
        if sl == 0:
            p.wait("vector", "dv", dv_bw)
            p.wait("vector", "ac", ac_signs_done)
        xs = V["fx"]
        at = V["fat"]
        p.wait("vector", "dv", p.cnt["dv"])
        for c in range(NCH):
            p.do("vector", lambda e, xs=xs, at=at, c=c: e.scalar_tensor_tensor(
                out=V["x2sl"][:, c, :], in0=at[:, c, :],
                scalar=sb["ga"][:, c:c + 1], in1=xs[:, c, :], op0=OP.mult, op1=OP.add),
                inc="dv" if c == NCH - 1 else None)
        if sl + 1 < NFS:
            emit_fx(sl + 1, wait_dv=p.cnt["dv"])
        p.do("vector", lambda e: e.tensor_tensor(
            out=V["xsq2"], in0=V["x2sl"], in1=V["x2sl"], op=OP.mult), inc="dv")
        p.wait("tensor", "dv", p.cnt["dv"])
        p.wait("tensor", "ac", p.cnt["ac"])
        rsum, rsq = pB[0:1, 1024:1024 + FSL], pB[0:1, 1536:1536 + FSL]
        rrep, rmrep = pA[:, 1024:1024 + FSL], pA[:, 1536:1536 + FSL]
        for c in range(NCH):
            p.do("tensor", lambda e, c=c, rsum=rsum: e.matmul(
                rsum, sb["onescf"][:], V["x2sl"][:, c, :],
                start=(c == 0), stop=(c == NCH - 1)))
        for c in range(NCH):
            p.do("tensor", lambda e, c=c, rsq=rsq: e.matmul(
                rsq, sb["onesc"][:], V["xsq2"][:, c, :],
                start=(c == 0), stop=(c == NCH - 1)),
                inc="pe" if c == NCH - 1 else None)
        p.wait("vector", "pe", p.cnt["pe"])
        emit_ln_rows(p, sb, rsum, rsq, FSL)
        p.wait("tensor", "dv", p.cnt["dv"])
        p.do("tensor", lambda e, rrep=rrep: e.matmul(
            rrep, sb["ones_r"][:], sb["rowb"][0:1, 0:FSL], start=True, stop=True), inc="pe")
        p.wait("vector", "pe", p.cnt["pe"])
        p.do("vector", lambda e: e.tensor_copy(out=sb["rowb"][0:1, 0:FSL],
                                               in_=sb["rowmr"][0:1, 0:FSL]), inc="dv")
        p.wait("tensor", "dv", p.cnt["dv"])
        p.do("tensor", lambda e, rmrep=rmrep: e.matmul(
            rmrep, sb["ones_r"][:], sb["rowb"][0:1, 0:FSL], start=True, stop=True), inc="pe")
        p.wait("vector", "pe", p.cnt["pe"])
        for c in range(NCH):
            p.do("vector", lambda e, c=c, rrep=rrep: e.tensor_tensor(
                out=sb["tmpa"][:, 0:FSL], in0=V["x2sl"][:, c, :], in1=rrep, op=OP.mult))
            p.hop("vector")
            p.do("vector", lambda e, c=c, rmrep=rmrep: e.tensor_tensor(
                out=sb["tmpa"][:, 0:FSL], in0=sb["tmpa"][:, 0:FSL], in1=rmrep,
                op=OP.subtract))
            p.hop("vector")
            p.do("vector", lambda e, c=c: e.tensor_scalar(
                out=V["h3sl"][:, c, :], in0=sb["tmpa"][:, 0:FSL],
                scalar1=sb["cb1"][:, c:c + 1], scalar2=sb["sbv"][:, c:c + 1],
                op0=OP.mult, op1=OP.add), inc="dv" if c == NCH - 1 else None)
            if c < NCH - 1:
                p.hop("vector")
        dv_h3 = p.cnt["dv"]
        if debug == 1 and sl == 0:
            p.wait("gpsimd", "dv", dv_h3)
            p.do("gpsimd", lambda e: e.dma_start(out=dbg["d_h3"], in_=V["h3sl"]),
                 inc="dgd", inc_n=16)
            p.dbg_barrier()
        p.wait("tensor", "dv", dv_h3)
        evac = {}
        for ho in range(32):
            reg = R8[ho % 8]
            if ho >= 8:
                p.wait("tensor", "ac", evac[ho - 8])
            for c in range(NCH):
                p.do("tensor", lambda e, ho=ho, c=c, reg=reg: e.matmul(
                    reg[:, 0:FSL], V["s1"][:, c, ho * 128:(ho + 1) * 128],
                    V["h3sl"][:, c, :], start=(c == 0), stop=(c == NCH - 1)),
                    inc="pe" if c == NCH - 1 else None)
            p.wait("scalar", "pe", p.cnt["pe"])
            evac[ho] = p.do("scalar", lambda e, ho=ho, reg=reg: e.activation(
                out=V["h2sl"][:, ho, :], in_=reg[:, 0:FSL], func=AF.Gelu,
                bias=sb["b1_c"][:, ho:ho + 1], scale=sb["a1c"][:, ho:ho + 1]), inc="ac")
        if sl >= 1:
            p.wait("vector", "dso", out_dma[(sl - 1, 1)])
        p.wait("tensor", "ac", evac[31])
        for blk in range(NCH):
            reg = R8[blk]
            for hc in range(32):
                p.do("tensor", lambda e, blk=blk, hc=hc, reg=reg: e.matmul(
                    reg[:, 0:FSL], V["s2"][:, hc, blk * 128:(blk + 1) * 128],
                    V["h2sl"][:, hc, :], start=(hc == 0), stop=(hc == 31)),
                    inc="pe" if hc == 31 else None)
            p.wait("vector", "pe", p.cnt["pe"])
            p.wait("vector", "dv", p.cnt["dv"])
            if blk == 4:
                p.wait("vector", "dso", out_dma[(sl, 0)])
            p.do("vector", lambda e, blk=blk, reg=reg: e.tensor_scalar(
                out=sb["sqs"][:, 0:FSL], in0=reg[:, 0:FSL], scalar1=sb["a2c"][:, blk:blk + 1],
                scalar2=sb["b2_c"][:, blk:blk + 1], op0=OP.mult, op1=OP.add))
            p.hop("vector")
            p.do("vector", lambda e, sl=sl, blk=blk: e.scalar_tensor_tensor(
                out=V["outsl"][:, blk % 4, :], in0=sb["sqs"][:, 0:FSL],
                scalar=sb["gb"][:, blk:blk + 1], in1=V["x2sl"][:, blk, :],
                op0=OP.mult, op1=OP.add), inc="dv")
            if blk % 4 == 3:
                half4 = blk // 4
                p.wait("sync", "dv", p.cnt["dv"])
                dst = out_cm[half4 * 4 * 128:(half4 + 1) * 4 * 128,
                             t0:t0 + FSL].rearrange("(c p) t -> p c t", p=128)
                out_dma[(sl, half4)] = p.do("sync", lambda e, dst=dst, sl=sl: e.dma_start(
                    out=dst, in_=V["outsl"][:, :, :]), inc="dso", inc_n=16)
        dv_out = p.cnt["dv"]



def prep_inputs(inputs):
    f32 = np.float32
    bf = ml_dtypes.bfloat16
    x = np.asarray(inputs["L0_attn_hidden_states"], f32)
    relT = np.ascontiguousarray(np.asarray(inputs["L0_attn_rel_emb"], f32).T)
    rel_rev = np.ascontiguousarray(relT[:, ::-1])
    Mz = np.concatenate([np.asarray(inputs["L0_attn_M"], f32),
                         np.asarray(inputs["L0_attn_z"], f32)[:, :, None]], axis=2)
    mzp = np.ascontiguousarray(Mz.reshape(8, 128, 65))
    WqT = np.ascontiguousarray(np.asarray(inputs["L0_attn_Wq"], f32).T)
    WkT = np.ascontiguousarray(np.asarray(inputs["L0_attn_Wk"], f32).T)
    WvT = np.ascontiguousarray(np.asarray(inputs["L0_attn_Wv"], f32).T)
    WoT = np.ascontiguousarray(np.asarray(inputs["L0_attn_Wo"], f32).T)
    w1T = np.ascontiguousarray(np.asarray(inputs["L0_ffn_w1"], f32).T)
    w2T = np.ascontiguousarray(np.asarray(inputs["L0_ffn_w2"], f32).T)
    wq_p = np.ascontiguousarray(WqT.reshape(D, NP, 128).transpose(1, 0, 2))
    wk_p = np.ascontiguousarray(WkT.reshape(D, NP, 128).transpose(1, 0, 2))
    wv_p = np.ascontiguousarray(WvT.reshape(D, NP, 128).transpose(1, 0, 2))

    def col8(v):
        return np.ascontiguousarray(np.asarray(v, f32).reshape(8, 128).T)
    def col32(v):
        return np.ascontiguousarray(np.asarray(v, f32).reshape(32, 128).T)

    sel_r = np.zeros((4, 512), bf)
    sel_c = np.zeros((4, 4), bf)
    for j in range(4):
        sel_r[j, j * 128:(j + 1) * 128] = 1
        sel_c[j, j] = 1
    blkd = np.zeros((128, 2), bf)
    blkd[0:64, 0] = 1
    blkd[64:128, 1] = 1

    common = dict(
        wmod=np.asarray(inputs["L0_wmod"], f32),
        wq_p=wq_p, wk_p=wk_p, wv_p=wv_p, woT=WoT, w1T=w1T, w2T=w2T,
        mzp=mzp,
        bq_c=col8(inputs["L0_attn_bq"]), bk_c=col8(inputs["L0_attn_bk"]),
        bv_c=col8(inputs["L0_attn_bv"]), bo_c=col8(inputs["L0_attn_bo"]),
        b1_c=col32(inputs["L0_ffn_b1"]), b2_c=col8(inputs["L0_ffn_b2"]),
        galpha=np.asarray(inputs["L0_attn_gate_alpha"], f32).reshape(1, 1),
        ident=np.eye(128, dtype=bf),
        ones_r=np.ones((1, 128), bf),
        ones_rf=np.ones((1, 128), np.float32),
        onesc=np.ones((128, 1), bf),
        onescf=np.ones((128, 1), np.float32),
        blkd=blkd,
        o642=np.ones((64, 2), bf),
        i2=np.eye(2, dtype=bf),
        sel_r=sel_r, sel_c=sel_c,
    )
    relg0 = np.ascontiguousarray(np.concatenate([rel_rev, rel_rev], 0))
    relg1 = np.ascontiguousarray(np.concatenate([relT, relT], 0))
    in_maps = []
    for core in range(8):
        b, half = core // 2, core % 2
        xT = np.ascontiguousarray(x[b].T)
        if half == 1:
            xT = np.ascontiguousarray(xT[:, ::-1])
        m = dict(common)
        m["xT"] = xT
        m["relg"] = relg0 if half == 0 else relg1
        m["c_col"] = np.ascontiguousarray(
            np.asarray(inputs["L0_c"], f32)[b].reshape(8, 128).T)
        in_maps.append(m)
    return in_maps


def assemble(results):
    out = np.zeros((4, S, D), np.float32)
    for core in range(8):
        b, half = core // 2, core % 2
        oc = results[core]["out_cm"]
        if half == 0:
            out[b, 0:SQ, :] = oc.T
        else:
            out[b, SQ:S, :] = oc[:, ::-1].T
    return out


def kernel(**inputs):
    from concourse.bass_utils import run_bass_kernel_spmd
    in_maps = prep_inputs(inputs)
    nc = build_program(debug=0)
    res = run_bass_kernel_spmd(nc, in_maps, core_ids=list(range(8)))
    return assemble(res.results)


# version 03
# speedup vs baseline: 1.2114x; 1.1935x over previous
import contextlib
import numpy as np
import ml_dtypes
import concourse.bass as bass
import concourse.mybir as mybir
from concourse.ap import AP

BF16 = mybir.dt.bfloat16
F32 = mybir.dt.float32
AF = mybir.ActivationFunctionType
OP = mybir.AluOpType
AX = mybir.AxisListType

D = 1024
S = 4096
SQ = 2048
NP = 8
NCH = 8
MR = 1024
PAD = 254
CP = 2049 + 2 * PAD
GQB = 4
NG = SQ // (GQB * 128)
BAND = MR + 127
NKB = S // 128
FSL = 256
NFS = SQ // FSL

ENGINES = ("tensor", "scalar", "vector", "sync", "gpsimd")


def band_range(qb):
    q0 = qb * 128
    kb_lo = max(0, -(-(q0 - BAND) // 128))
    kb_hi = min(NKB - 1, (q0 + BAND) // 128)
    return kb_lo, kb_hi


class Prog:
    def __init__(self, nc, es):
        self.nc = nc
        self.ops = {e: [] for e in ENGINES}
        self.last_has_inc = {}
        self.cnt = {}
        self.sems = {}
        for s in ("dg", "ds", "pe", "ac", "dv", "dsg", "dsh", "dsc", "dss0", "dss1",
                  "dso", "dgm", "dgz", "dgw", "dgo", "dgx0", "dgx1", "dgy0", "dgy1", "dgf", "dgd",
                  "hv", "ha", "ht", "hg", "dp"):
            self.sems[s] = es.enter_context(nc.semaphore(s))
            self.cnt[s] = 0

    def wait(self, eng, sem, val):
        if val <= 0:
            return
        h = self.sems[sem]
        self.ops[eng].append(lambda e, h=h, v=val: e.wait_ge(h, v))
        self.last_has_inc[eng] = False

    def inc_last(self, eng, sem, n=1):
        assert not self.last_has_inc.get(eng), f"double inc on {eng}"
        h = self.sems[sem]
        self.cnt[sem] += n
        f = self.ops[eng].pop()
        self.ops[eng].append(lambda e, f=f, h=h, n=n: f(e).then_inc(h, n))
        self.last_has_inc[eng] = True
        return self.cnt[sem]

    def flush_pending(self):
        fn = getattr(self, "pending", None)
        self.pending = None
        if fn is not None:
            fn()

    def dbg_barrier(self):
        for eng in ("tensor", "scalar", "vector", "gpsimd", "sync"):
            self.wait(eng, "dgd", self.cnt["dgd"])

    def hop(self, eng):
        sem = {"vector": "hv", "scalar": "ha", "tensor": "ht", "gpsimd": "hg"}[eng]
        h = self.sems[sem]
        self.cnt[sem] += 1
        v = self.cnt[sem]
        assert not self.last_has_inc.get(eng), f"double inc (hop) on {eng}"
        f = self.ops[eng].pop()
        self.ops[eng].append(lambda e, f=f, h=h: f(e).then_inc(h, 1))
        self.ops[eng].append(lambda e, h=h, v=v: e.wait_ge(h, v))
        self.last_has_inc[eng] = False

    def do(self, eng, fn, inc=None, inc_n=1):
        if inc is None:
            self.ops[eng].append(fn)
            self.last_has_inc[eng] = False
            return None
        h = self.sems[inc]
        self.cnt[inc] += inc_n
        self.ops[eng].append(lambda e, fn=fn, h=h, n=inc_n: fn(e).then_inc(h, n))
        self.last_has_inc[eng] = True
        return self.cnt[inc]


def _view(tensor_ap, off_bytes, shape, dt):
    es_ = mybir.dt.size(dt)
    nel = int(np.prod(shape[1:]))
    assert off_bytes % 2 == 0
    v = tensor_ap[0:shape[0], off_bytes // 2: off_bytes // 2 + (nel * es_) // 2]
    if dt != BF16:
        v = v.bitcast(dt)
    if len(shape) == 2:
        return v
    if len(shape) == 3:
        return v.rearrange("p (a b) -> p a b", a=shape[1])
    if len(shape) == 4:
        return v.rearrange("p (a b c) -> p a b c", a=shape[1], b=shape[2])
    raise ValueError(shape)


def build_program(debug=0):
    nc = bass.Bass("TRN2", target_bir_lowering=False)
    es = contextlib.ExitStack()

    dd = {}
    def din(name, shape, dt=F32):
        dd[name] = nc.dram_tensor(name, shape, dt, kind="ExternalInput").ap()
    din("xT", [D, S])
    din("c_col", [128, 8])
    din("wmod", [D, 6 * D])
    din("wq_p", [NP, D, 128]); din("wk_p", [NP, D, 128]); din("wv_p", [NP, D, 128])
    din("woT", [D, D])
    din("w1T", [D, 4 * D])
    din("w2T", [4 * D, D])
    din("relg", [128, 2049])
    din("mzp", [NP, 128, 65])
    din("bq_c", [128, 8]); din("bk_c", [128, 8]); din("bv_c", [128, 8]); din("bo_c", [128, 8])
    din("b1_c", [128, 32]); din("b2_c", [128, 8])
    din("galpha", [1, 1])
    din("ident", [128, 128], BF16)
    din("ones_r", [1, 128], BF16)
    din("ones_rf", [1, 128], F32)
    din("onesc", [128, 1], BF16)
    din("onescf", [128, 1], F32)
    din("blkd", [128, 2], BF16)
    din("o642", [64, 2], BF16)
    din("i2", [2, 2], BF16)
    din("sel_r", [4, 512], BF16)
    din("sel_c", [4, 4], BF16)
    out_cm = nc.dram_tensor("out_cm", [D, SQ], F32, kind="ExternalOutput").ap()
    attn_sp = nc.dram_tensor("attn_sp", [NCH, 128, SQ], BF16, kind="Internal").ap()

    dbg = {}
    if debug:
        def dout(name, shape, dt):
            dbg[name] = nc.dram_tensor(name, shape, dt, kind="ExternalOutput").ap()
        dout("d_mod", [1, 6 * D], BF16)
        dout("d_h1", [128, NCH, S], BF16)
        dout("d_qh", [128, SQ], BF16)
        dout("d_k", [128, S], BF16)
        dout("d_vp", [128, NKB, 2, 65], BF16)
        dout("d_pg", [128, CP], BF16)
        dout("d_rg", [128, 2304], BF16)
        dout("d_negb", [128, 1], F32)
        dout("d_edge", [4, SQ], BF16)
        dout("d_o65", [65, 2, 512], F32)
        dout("d_comb", [128, NP, SQ], BF16)
        dout("d_attn", [NCH, 128, SQ], BF16)
        dout("d_s1", [128, NCH, 4 * D], BF16)
        dout("d_h3", [128, NCH, FSL], BF16)

    sb = {}
    def sbuf(name, shape, dt):
        sb[name] = es.enter_context(nc.sbuf_tensor("s_" + name, shape, dt))
    sbuf("ident", [128, 128], BF16)
    sbuf("ones_r", [1, 128], BF16)
    sbuf("onesc", [128, 1], BF16)
    sbuf("onescf", [128, 1], F32)
    sbuf("blkd", [128, 2], BF16)
    sbuf("o642", [64, 2], BF16)
    sbuf("i2", [2, 2], BF16)
    sbuf("sel_r", [4, 512], BF16)
    sbuf("sel_c", [4, 4], BF16)
    sbuf("c_col", [128, 8], F32)
    sbuf("silu_c", [128, 8], F32)
    sbuf("ones_rf", [1, 128], F32)
    for n in ("sa", "ca1", "ga", "sbv", "cb1", "gb", "bq_h", "bk_c", "bv_c", "bo_c", "b2_c"):
        sbuf(n, [128, 8], F32)
    sbuf("b1_c", [128, 32], F32)
    sbuf("a1c", [128, 32], F32)
    sbuf("a2c", [128, 8], F32)
    sbuf("gate", [128, 1], F32); sbuf("gate1m", [128, 1], F32)
    sbuf("relg", [128, 2049], BF16)
    sbuf("lnrel", [2, 1], F32)
    sbuf("tiny", [2, 10], F32)
    sbuf("tinyb", [2, 2], BF16)
    sbuf("rowb", [1, 512], BF16)
    sbuf("rowm", [1, 512], F32); sbuf("rowr", [1, 512], F32); sbuf("rowmr", [1, 512], F32)
    sbuf("tmpa", [128, 512], F32)
    sbuf("sqs", [128, 512], BF16)
    sbuf("xslab", [128, 2, NCH, 512], F32)
    sbuf("o65", [65, 2, 512], F32)
    sbuf("ct", [64, 2, 512], BF16)
    sbuf("negb", [128, 1], F32)
    sbuf("edges4", [4, SQ], BF16)
    sbuf("ecols", [128, 64], BF16)
    sbuf("g65s", [65, 2, 512], BF16)
    sbuf("rcol", [128, 16], BF16)
    sbuf("rcolf", [128, 16], F32)
    sbuf("eW", [128, 4], BF16)
    sbuf("mz", [128, 65], BF16)

    ARENA_B = 147584
    sbuf("arena", [128, ARENA_B // 2], BF16)
    ar = sb["arena"][:]

    V = {}
    V["h1"] = _view(ar, 0, [128, NCH, S], BF16)
    V["s1"] = _view(ar, 0, [128, NCH, 4 * D], BF16)
    B0 = 65536
    V["mrow"] = _view(ar, B0, [1, 6 * D], F32)
    V["wmod_ch"] = _view(ar, B0 + 24576, [128, 2, 3 * D], F32)
    V["xbf1"] = _view(ar, B0 + 24576, [128, NCH, 512], BF16)
    V["xsq1"] = _view(ar, B0 + 32768, [128, NCH, 512], BF16)
    V["kk"] = _view(ar, B0, [128, S], BF16)
    V["qh"] = _view(ar, B0 + 8192, [128, SQ], BF16)
    V["vp"] = _view(ar, B0 + 12288, [128, NKB, 2, 65], BF16)
    V["vf"] = _view(ar, B0 + 20608, [128, 1024], BF16)
    V["pgs"] = [_view(ar, B0 + 22656, [128, CP], BF16),
                _view(ar, B0 + 22656 + 2 * CP, [128, CP], BF16),
                _view(ar, B0 + 39040, [128, CP], BF16)]
    V["ee"] = _view(ar, B0 + 32896, [128, 2, 1024], BF16)
    V["sig"] = _view(ar, B0 + 36992, [128, 512], BF16)
    V["sigm"] = _view(ar, B0 + 38016, [128, 512], BF16)
    V["wqkv_wq"] = _view(ar, B0 + 39040, [128, NCH, 128], BF16)
    V["wqkv_wk"] = _view(ar, B0 + 41088, [128, NCH, 128], BF16)
    V["wqkv_wv"] = _view(ar, B0 + 43136, [128, NCH, 128], BF16)
    V["rg"] = _view(ar, B0 + 45184, [128, 18432], BF16)
    V["wo"] = _view(ar, B0, [128, NCH, D], BF16)
    V["stag"] = _view(ar, B0 + 16384, [128, 2, SQ], BF16)
    V["s2"] = _view(ar, B0, [128, 32, D], BF16)
    xs_ap = sb["xslab"][:].rearrange("p a b c -> p (a b c)").bitcast(BF16)
    V["comb"] = _view(xs_ap, 0, [128, NP, SQ], BF16)
    V["fx"] = _view(xs_ap, 0, [128, NCH, FSL], F32)
    V["x2sl"] = _view(xs_ap, 8192, [128, NCH, FSL], F32)
    V["outsl"] = _view(xs_ap, 16384, [128, 4, FSL], F32)
    V["fat"] = _view(xs_ap, 20480, [128, NCH, FSL], BF16)
    V["xsq2"] = _view(xs_ap, 24576, [128, NCH, FSL], BF16)
    V["h3sl"] = _view(xs_ap, 28672, [128, NCH, FSL], BF16)
    assert 28672 + 4096 <= 32768
    V["wsl"] = _view(ar, 131072, [128, 2, 4 * D], BF16)
    V["h2sl"] = _view(ar, 131072, [128, 32, FSL], BF16)
    assert 131072 + 16384 <= ARENA_B

    pA = es.enter_context(nc.psum_tensor("pA", [128, 2048], F32))
    pB = es.enter_context(nc.psum_tensor("pB", [128, 2048], F32))

    p = Prog(nc, es)
    gen(p, dd, sb, V, pA, pB, dbg, out_cm, attn_sp, debug)

    with nc.Block() as block:
        @block.tensor
        def _(e):
            for f in p.ops["tensor"]:
                f(e)
        @block.scalar
        def _(e):
            for f in p.ops["scalar"]:
                f(e)
        @block.vector
        def _(e):
            for f in p.ops["vector"]:
                f(e)
        @block.sync
        def _(e):
            for f in p.ops["sync"]:
                f(e)
        @block.gpsimd
        def _(e):
            for f in p.ops["gpsimd"]:
                f(e)
    es.close()
    return nc


def gen(p, dd, sb, V, pA, pB, dbg, out_cm, attn_sp, debug):
    for name, dst in (("ident", "ident"), ("ones_r", "ones_r"), ("ones_rf", "ones_rf"),
                      ("onesc", "onesc"), ("onescf", "onescf"),
                      ("blkd", "blkd"), ("o642", "o642"), ("i2", "i2"),
                      ("sel_r", "sel_r"), ("sel_c", "sel_c"),
                      ("c_col", "c_col"), ("bq_c", "bq_h"), ("bk_c", "bk_c"),
                      ("bv_c", "bv_c"), ("bo_c", "bo_c"), ("b1_c", "b1_c"),
                      ("b2_c", "b2_c")):
        p.do("sync", lambda e, d=dd[name], s=sb[dst]: e.dma_start(out=s[:], in_=d),
             inc="ds", inc_n=16)
    p.do("sync", lambda e: e.dma_start(out=sb["gate"][0:1, :], in_=dd["galpha"]),
         inc="ds", inc_n=16)
    ds_consts = p.cnt["ds"]

    dg_relg = p.do("gpsimd", lambda e: e.dma_start(out=sb["relg"][:], in_=dd["relg"]),
                   inc="dgm", inc_n=16)
    wmod_idx = [(h, c) for h in range(2) for c in range(NCH)]
    wmod_loads = {}
    def emit_wmod_load(i, wait_pe=None):
        half, c = wmod_idx[i]
        if wait_pe is not None:
            p.wait("gpsimd", "pe", wait_pe)
        wmod_loads[(half, c)] = ("dgx0" if c % 2 == 0 else "dgx1",
            p.do("gpsimd", lambda e, half=half, c=c: e.dma_start(
                out=V["wmod_ch"][:, c % 2, :],
                in_=dd["wmod"][c * 128:(c + 1) * 128, half * 3 * D:(half + 1) * 3 * D]),
                inc="dgx0" if c % 2 == 0 else "dgx1", inc_n=16))
    emit_wmod_load(0)
    emit_wmod_load(1)
    x_loads = {}
    def emit_x_load(sl, wait_dv=None):
        ap = dd["xT"][:, sl * 512:(sl + 1) * 512].rearrange("(c p) t -> p c t", p=128)
        if wait_dv is not None:
            p.wait("gpsimd", "dv", wait_dv)
        x_loads[sl] = ("dgy0" if sl % 2 == 0 else "dgy1",
            p.do("gpsimd", lambda e, ap=ap, sl=sl: e.dma_start(
                out=sb["xslab"][:, sl % 2, :, :], in_=ap),
                inc="dgy0" if sl % 2 == 0 else "dgy1", inc_n=16))
    emit_x_load(0)
    emit_x_load(1)

    p.wait("scalar", "ds", ds_consts)
    p.do("scalar", lambda e: e.activation(out=sb["tmpa"][:, 0:8], in_=sb["c_col"][:],
                                          func=AF.Sigmoid))
    p.do("scalar", lambda e: e.activation(out=sb["tmpa"][0:1, 8:9], in_=sb["gate"][0:1, :],
                                          func=AF.Sigmoid), inc="ac")
    p.wait("vector", "ac", p.cnt["ac"])
    dv_silu = p.do("vector", lambda e: e.tensor_tensor(
        out=sb["silu_c"][:], in0=sb["c_col"][:], in1=sb["tmpa"][:, 0:8], op=OP.mult), inc="dv")
    p.wait("tensor", "dv", dv_silu)
    for half in range(2):
        if half == 1:
            p.wait("tensor", "dv", p.cnt["dv"])
        for c in range(NCH):
            wsem, wval = wmod_loads[(half, c)]
            p.wait("tensor", wsem, wval)
            for npos in range(6):
                ps = pA[0:1, npos * 512:(npos + 1) * 512] if npos < 4 else \
                     pB[0:1, (npos - 4) * 512:(npos - 3) * 512]
                p.do("tensor", lambda e, ps=ps, c=c, npos=npos: e.matmul(
                    ps, sb["silu_c"][:, c:c + 1],
                    V["wmod_ch"][:, c % 2, npos * 512:(npos + 1) * 512],
                    start=(c == 0), stop=(c == NCH - 1)),
                    inc="pe" if npos == 5 else None)
            gi = half * NCH + c
            if gi + 2 < 16:
                emit_wmod_load(gi + 2, wait_pe=p.cnt["pe"])
        p.wait("vector", "pe", p.cnt["pe"])
        for npos in range(6):
            ps = pA[0:1, npos * 512:(npos + 1) * 512] if npos < 4 else \
                 pB[0:1, (npos - 4) * 512:(npos - 3) * 512]
            p.do("vector", lambda e, ps=ps, half=half, npos=npos: e.tensor_copy(
                out=V["mrow"][0:1, half * 3 * D + npos * 512: half * 3 * D + (npos + 1) * 512],
                in_=ps), inc="dv")
    dv_mod = p.cnt["dv"]

    p.wait("scalar", "dv", dv_mod)
    for idx in (2, 5):
        sl = slice(idx * D, (idx + 1) * D)
        for ph in range(2):
            p.wait("scalar", "dv", p.cnt["dv"])
            p.do("scalar", lambda e, sl=sl, ph=ph: e.activation(
                out=sb["rowr"][0:1, :], in_=V["mrow"][0:1, sl][:, ph * 512:(ph + 1) * 512],
                func=AF.Abs), inc="ac")
            p.wait("vector", "ac", p.cnt["ac"])
            p.do("vector", lambda e: e.tensor_scalar(
                out=sb["rowr"][0:1, :], in0=sb["rowr"][0:1, :],
                scalar1=0.25, scalar2=None, op0=OP.max))
            p.do("vector", lambda e, sl=sl, ph=ph: e.tensor_scalar(
                out=sb["rowm"][0:1, :], in0=V["mrow"][0:1, sl][:, ph * 512:(ph + 1) * 512],
                scalar1=0.0, scalar2=2.0, op0=OP.is_ge, op1=OP.mult))
            p.hop("vector")
            p.do("vector", lambda e, sl=sl, ph=ph: e.scalar_tensor_tensor(
                out=V["mrow"][0:1, sl][:, ph * 512:(ph + 1) * 512],
                in0=sb["rowm"][0:1, :], scalar=-1.0,
                in1=sb["rowr"][0:1, :], op0=OP.add, op1=OP.mult), inc="dv")
    dv_clamp = p.cnt["dv"]

    p.wait("tensor", "dv", dv_clamp)
    names = ("sa", "ca1", "ga", "sbv", "cb1", "gb")
    for vi, vn in enumerate(names):
        if vi > 0:
            p.wait("tensor", "dv", p.cnt["dv"])
        for c in range(NCH):
            p.do("tensor", lambda e, vi=vi, c=c: e.matmul(
                pB[:, 1024 + c:1025 + c],
                V["mrow"][0:1, vi * D + c * 128: vi * D + (c + 1) * 128],
                sb["ones_rf"][0:1, 0:1], start=True, stop=True),
                inc="pe" if c == NCH - 1 else None)
        p.wait("vector", "pe", p.cnt["pe"])
        if vn in ("ca1", "cb1"):
            p.do("vector", lambda e, vn=vn: e.tensor_scalar(
                out=sb[vn][:], in0=pB[:, 1024:1032], scalar1=1.0, scalar2=None,
                op0=OP.add), inc="dv")
        else:
            p.do("vector", lambda e, vn=vn: e.tensor_copy(
                out=sb[vn][:], in_=pB[:, 1024:1032]), inc="dv")
    p.do("vector", lambda e: e.tensor_copy(out=sb["tinyb"][0:1, 0:1], in_=sb["tmpa"][0:1, 8:9]),
         inc="dv")
    p.wait("tensor", "dv", p.cnt["dv"])
    p.do("tensor", lambda e: e.matmul(pB[:, 1024:1025], sb["ones_r"][:],
                                      sb["tinyb"][0:1, 0:1], start=True, stop=True), inc="pe")
    p.wait("vector", "pe", p.cnt["pe"])
    p.do("vector", lambda e: e.tensor_copy(out=sb["gate"][:], in_=pB[:, 1024:1025]))
    p.do("vector", lambda e: e.tensor_scalar(
        out=sb["gate1m"][:], in0=pB[:, 1024:1025], scalar1=-1.0, scalar2=1.0,
        op0=OP.mult, op1=OP.add))
    p.do("vector", lambda e: e.tensor_scalar(
        out=sb["bq_h"][:], in0=sb["bq_h"][:], scalar1=0.125, scalar2=None, op0=OP.mult),
        inc="dv")
    dv_modcols = p.cnt["dv"]
    if debug == 1:
        p.wait("gpsimd", "dv", dv_clamp)
        p.do("gpsimd", lambda e: e.dma_start(out=dbg["d_mod"], in_=V["mrow"]),
             inc="dgd", inc_n=16)
        p.dbg_barrier()

    for sl in range(8):
        xs = sb["xslab"][:, sl % 2, :, :]
        if sl % 2 == 0:
            rsum, rsq = pB[0:1, 1024:1536], pB[0:1, 1536:2048]
            rrep, rmrep = pB[:, 0:512], pB[:, 512:1024]
        else:
            rsum, rsq = pA[0:1, 0:512], pA[0:1, 512:1024]
            rrep, rmrep = pA[:, 1024:1536], pA[:, 1536:2048]
        xsem, xval = x_loads[sl]
        p.wait("vector", xsem, xval)
        if sl == 0:
            p.wait("vector", "dv", dv_modcols)
        p.do("vector", lambda e, xs=xs: e.tensor_copy(out=V["xbf1"], in_=xs))
        p.hop("vector")
        p.do("vector", lambda e: e.tensor_tensor(
            out=V["xsq1"], in0=V["xbf1"], in1=V["xbf1"], op=OP.mult), inc="dv")
        p.wait("tensor", "dv", p.cnt["dv"])
        for c in range(NCH):
            p.do("tensor", lambda e, c=c, rsum=rsum: e.matmul(
                rsum, sb["onesc"][:], V["xbf1"][:, c, :],
                start=(c == 0), stop=(c == NCH - 1)))
        for c in range(NCH):
            p.do("tensor", lambda e, c=c, rsq=rsq: e.matmul(
                rsq, sb["onesc"][:], V["xsq1"][:, c, :],
                start=(c == 0), stop=(c == NCH - 1)),
                inc="pe" if c == NCH - 1 else None)
        p.wait("vector", "pe", p.cnt["pe"])
        emit_ln_rows(p, sb, rsum, rsq, 512)
        p.wait("tensor", "dv", p.cnt["dv"])
        p.do("tensor", lambda e, rrep=rrep: e.matmul(
            rrep, sb["ones_r"][:], sb["rowb"][:], start=True, stop=True), inc="pe")
        p.wait("vector", "pe", p.cnt["pe"])
        p.do("vector", lambda e: e.tensor_copy(out=sb["rowb"][:], in_=sb["rowmr"][:]), inc="dv")
        p.wait("tensor", "dv", p.cnt["dv"])
        p.do("tensor", lambda e, rmrep=rmrep: e.matmul(
            rmrep, sb["ones_r"][:], sb["rowb"][:], start=True, stop=True), inc="pe")
        p.wait("vector", "pe", p.cnt["pe"])
        for c in range(NCH):
            p.do("vector", lambda e, xs=xs, c=c, rrep=rrep: e.tensor_tensor(
                out=sb["tmpa"][:], in0=xs[:, c, :], in1=rrep, op=OP.mult))
            p.hop("vector")
            p.do("vector", lambda e, c=c, rmrep=rmrep: e.tensor_tensor(
                out=sb["tmpa"][:], in0=sb["tmpa"][:], in1=rmrep, op=OP.subtract))
            p.hop("vector")
            p.do("vector", lambda e, sl=sl, c=c: e.tensor_scalar(
                out=V["h1"][:, c, sl * 512:(sl + 1) * 512], in0=sb["tmpa"][:],
                scalar1=sb["ca1"][:, c:c + 1], scalar2=sb["sa"][:, c:c + 1],
                op0=OP.mult, op1=OP.add), inc="dv" if c == NCH - 1 else None)
            if c < NCH - 1:
                p.hop("vector")
        if sl + 2 < 8:
            emit_x_load(sl + 2, wait_dv=p.cnt["dv"])
    dv_h1 = p.cnt["dv"]
    if debug == 1:
        p.wait("gpsimd", "dv", dv_h1)
        p.do("gpsimd", lambda e: e.dma_start(out=dbg["d_h1"], in_=V["h1"]),
             inc="dgd", inc_n=16)
        p.dbg_barrier()

    p.wait("scalar", "dgm", dg_relg)
    p.wait("tensor", "dv", dv_h1)
    for ch in range(5):
        cs = slice(ch * 512, (ch + 1) * 512) if ch < 4 else slice(2048, 2049)
        n = 512 if ch < 4 else 1
        p.wait("scalar", "pe", p.cnt["pe"])
        p.do("scalar", lambda e, cs=cs, n=n: e.activation(
            out=sb["sqs"][0:64, 0:n], in_=sb["relg"][0:64, cs], func=AF.Square), inc="ac")
        p.wait("tensor", "ac", p.cnt["ac"])
        if ch > 0:
            p.wait("tensor", "dv", p.cnt["dv"])
        p.do("tensor", lambda e, n=n: e.matmul(
            pB[0:2, 1024:1024 + n], sb["o642"][:], sb["sqs"][0:64, 0:n],
            start=True, stop=True), inc="pe")
        p.wait("vector", "pe", p.cnt["pe"])
        if ch == 0:
            p.do("vector", lambda e: e.reduce_max(
                out=sb["lnrel"][:], in_=pB[0:2, 1024:1536], axis=AX.X), inc="dv")
        else:
            p.do("vector", lambda e, n=n: e.reduce_max(
                out=sb["tiny"][:, 9:10], in_=pB[0:2, 1024:1024 + n], axis=AX.X))
            p.hop("vector")
            p.do("vector", lambda e: e.tensor_tensor(
                out=sb["lnrel"][:], in0=sb["lnrel"][:], in1=sb["tiny"][:, 9:10],
                op=OP.max), inc="dv")
    p.wait("scalar", "dv", p.cnt["dv"])
    p.do("scalar", lambda e: e.activation(out=sb["lnrel"][:], in_=sb["lnrel"][:],
                                          func=AF.Ln), inc="ac")
    ac_lnrel = p.cnt["ac"]
    p.do("vector", lambda e: e.memset(sb["eW"][:], 0.0))
    p.hop("vector")
    p.do("vector", lambda e: e.tensor_copy(out=sb["eW"][0:64, 0:1], in_=sb["relg"][0:64, 0:1]))
    p.do("vector", lambda e: e.tensor_copy(out=sb["eW"][0:64, 1:2],
                                           in_=sb["relg"][0:64, 2048:2049]))
    p.do("vector", lambda e: e.tensor_copy(out=sb["eW"][64:128, 2:3],
                                           in_=sb["relg"][64:128, 0:1]))
    p.do("vector", lambda e: e.tensor_copy(out=sb["eW"][64:128, 3:4],
                                           in_=sb["relg"][64:128, 2048:2049]), inc="dv")
    dv_eW = p.cnt["dv"]

    p.pending = None
    p.pguse = 0
    p.guses = {}
    p.dp_hist = []
    p.pmm_rot = 0
    p.sdone_last = 0
    p.g65pe = 0
    p.o65ac = 0
    p.pe_vlast = p.cnt["pe"]
    p.t8_dv = 0
    kb_ranges = [band_range(qb) for qb in range(16)]
    for hp in range(NP):
        gen_pair(p, dd, sb, V, pA, pB, hp, kb_ranges, dbg, debug, ac_lnrel, dv_eW)

    p.flush_pending()
    p.wait("gpsimd", "pe", p.cnt["pe"])
    p.wait("gpsimd", "dv", p.cnt["dv"])
    p.wait("gpsimd", "ac", p.cnt["ac"])
    p.wait("gpsimd", "dsc", p.cnt["dsc"])
    dg_wo = p.do("gpsimd", lambda e: e.dma_start(
        out=V["wo"], in_=dd["woT"].rearrange("(c p) m -> p c m", p=128)), inc="dgo", inc_n=16)
    p.wait("tensor", "dgo", dg_wo)
    p.wait("tensor", "dv", p.cnt["dv"])
    p.wait("tensor", "ac", p.cnt["ac"])
    p.wait("tensor", "dp", p.cnt["dp"])
    p.wait("tensor", "dsc", p.cnt["dsc"])
    stag_dma = {}
    for blk in range(NCH):
        if blk > 0:
            p.wait("tensor", "dv", p.cnt["dv"])
        if blk >= 2:
            ssem, sval = stag_dma[blk - 2]
            p.wait("vector", ssem, sval)
        for c in range(NCH):
            for nch in range(4):
                ps = (pA if nch < 2 else pB)[:, (nch % 2) * 1024:(nch % 2) * 1024 + 512]
                p.do("tensor", lambda e, blk=blk, c=c, nch=nch, ps=ps: e.matmul(
                    ps, V["wo"][:, c, blk * 128:(blk + 1) * 128],
                    V["comb"][:, c, nch * 512:(nch + 1) * 512],
                    start=(c == 0), stop=(c == NCH - 1)),
                    inc="pe" if (c == NCH - 1 and nch == 3) else None)
        p.wait("vector", "pe", p.cnt["pe"])
        for nch in range(4):
            ps = (pA if nch < 2 else pB)[:, (nch % 2) * 1024:(nch % 2) * 1024 + 512]
            p.do("vector", lambda e, blk=blk, nch=nch, ps=ps: e.tensor_scalar(
                out=V["stag"][:, blk % 2, nch * 512:(nch + 1) * 512], in0=ps, scalar1=1.0,
                scalar2=sb["bo_c"][:, blk:blk + 1], op0=OP.mult, op1=OP.add),
                inc="dv" if nch == 3 else None)
        p.wait("sync", "dv", p.cnt["dv"])
        stag_dma[blk] = ("dss0" if blk % 2 == 0 else "dss1",
            p.do("sync", lambda e, blk=blk: e.dma_start(
                out=attn_sp[blk], in_=V["stag"][:, blk % 2, :]),
                inc="dss0" if blk % 2 == 0 else "dss1", inc_n=16))
    ds_attn = (p.cnt["dss0"], p.cnt["dss1"])
    if debug:
        p.wait("gpsimd", "dss0", ds_attn[0])
        p.wait("gpsimd", "dss1", ds_attn[1])
        p.do("gpsimd", lambda e: e.dma_start(out=dbg["d_attn"], in_=attn_sp), inc="dgd", inc_n=16)
        p.do("gpsimd", lambda e: e.dma_start(out=dbg["d_comb"], in_=V["comb"]), inc="dgd", inc_n=16)
        p.dbg_barrier()

    gen_ffn(p, dd, sb, V, pA, pB, out_cm, attn_sp, ds_attn, dbg, debug)

    p.wait("sync", "dso", p.cnt["dso"])
    p.wait("sync", "dsc", p.cnt["dsc"])
    p.wait("sync", "dsg", p.cnt["dsg"])
    p.wait("sync", "dss0", p.cnt["dss0"])
    p.wait("sync", "dss1", p.cnt["dss1"])
    p.wait("sync", "ds", p.cnt["ds"])
    for s in ("dgm", "dgz", "dgw", "dgo", "dgx0", "dgx1", "dgy0", "dgy1", "dgf", "dgd",
                  "hv", "ha", "ht", "hg", "dp", "dsh"):
        p.wait("gpsimd", s, p.cnt[s])


def emit_ln_rows(p, sb, rsum, rsq, n):
    p.do("vector", lambda e, rsum=rsum: e.tensor_scalar(
        out=sb["rowm"][0:1, 0:n], in0=rsum, scalar1=1.0 / D, scalar2=None, op0=OP.mult))
    p.hop("vector")
    p.do("vector", lambda e: e.tensor_tensor(
        out=sb["rowr"][0:1, 0:n], in0=sb["rowm"][0:1, 0:n], in1=sb["rowm"][0:1, 0:n],
        op=OP.mult))
    p.hop("vector")
    p.do("vector", lambda e: e.tensor_scalar(
        out=sb["rowr"][0:1, 0:n], in0=sb["rowr"][0:1, 0:n], scalar1=-1e-5, scalar2=None,
        op0=OP.add))
    p.hop("vector")
    p.do("vector", lambda e, rsq=rsq: e.scalar_tensor_tensor(
        out=sb["rowr"][0:1, 0:n], in0=rsq, scalar=1.0 / D, in1=sb["rowr"][0:1, 0:n],
        op0=OP.mult, op1=OP.subtract), inc="dv")
    p.wait("scalar", "dv", p.cnt["dv"])
    p.do("scalar", lambda e: e.activation(out=sb["rowr"][0:1, 0:n],
                                          in_=sb["rowr"][0:1, 0:n], func=AF.Sqrt), inc="ac")
    p.wait("vector", "ac", p.cnt["ac"])
    p.do("vector", lambda e: e.reciprocal(out=sb["rowr"][0:1, 0:n], in_=sb["rowr"][0:1, 0:n]))
    p.hop("vector")
    p.do("vector", lambda e: e.tensor_tensor(
        out=sb["rowmr"][0:1, 0:n], in0=sb["rowm"][0:1, 0:n], in1=sb["rowr"][0:1, 0:n],
        op=OP.mult))
    p.do("vector", lambda e: e.tensor_copy(out=sb["rowb"][0:1, 0:n],
                                           in_=sb["rowr"][0:1, 0:n]), inc="dv")


def gen_pair(p, dd, sb, V, pA, pB, hp, kb_ranges, dbg, debug, ac_lnrel, dv_eW):
    p.wait("gpsimd", "pe", p.pe_vlast)
    p.wait("gpsimd", "dp", p.cnt["dp"])
    p.wait("gpsimd", "dsg", p.cnt["dsg"])
    p.wait("gpsimd", "dsh", p.cnt["dsh"])
    lw = []
    for wn, dn in (("wq", "wq_p"), ("wk", "wk_p"), ("wv", "wv_p")):
        lw.append(p.do("gpsimd", lambda e, wn=wn, dn=dn: e.dma_start(
            out=V["wqkv_" + wn],
            in_=dd[dn][hp].rearrange("(c p) m -> p c m", p=128)), inc="dgw", inc_n=16))
    p.wait("gpsimd", "pe", p.g65pe)
    dg_mz = p.do("gpsimd", lambda e: e.dma_start(out=sb["mz"][:], in_=dd["mzp"][hp]),
                 inc="dgz", inc_n=16)
    p.flush_pending()
    p.slot2_fresh = True
    p.pair_pmm_first = True

    p.wait("tensor", "dgw", lw[2])
    p.wait("tensor", "ac", p.cnt["ac"])
    p.wait("tensor", "dv", p.cnt["dv"])
    p.wait("tensor", "dp", p.cnt["dp"])
    for c in range(NCH):
        for nch in range(4):
            p.do("tensor", lambda e, c=c, nch=nch: e.matmul(
                pA[:, nch * 512:(nch + 1) * 512], V["wqkv_wq"][:, c, :],
                V["h1"][:, c, nch * 512:(nch + 1) * 512],
                start=(c == 0), stop=(c == NCH - 1)),
                inc="pe" if (c == NCH - 1 and nch == 3) else None)
    p.wait("vector", "pe", p.cnt["pe"])
    p.do("vector", lambda e: e.tensor_scalar(
        out=V["qh"][:, :], in0=pA[:, 0:2048], scalar1=0.125,
        scalar2=sb["bq_h"][:, hp:hp + 1], op0=OP.mult, op1=OP.add), inc="dv")
    dv_q = p.cnt["dv"]
    for half in range(2):
        p.wait("tensor", "dv", p.cnt["dv"])
        for c in range(NCH):
            for nch in range(4):
                t0 = half * 2048 + nch * 512
                p.do("tensor", lambda e, c=c, nch=nch, t0=t0: e.matmul(
                    pA[:, nch * 512:(nch + 1) * 512], V["wqkv_wk"][:, c, :],
                    V["h1"][:, c, t0:t0 + 512],
                    start=(c == 0), stop=(c == NCH - 1)),
                    inc="pe" if (c == NCH - 1 and nch == 3) else None)
        p.wait("vector", "pe", p.cnt["pe"])
        p.do("vector", lambda e, half=half: e.tensor_scalar(
            out=V["kk"][:, half * 2048:(half + 1) * 2048], in0=pA[:, 0:2048], scalar1=1.0,
            scalar2=sb["bk_c"][:, hp:hp + 1], op0=OP.mult, op1=OP.add), inc="dv")
    for vch in range(4):
        p.wait("tensor", "dv", p.cnt["dv"])
        for c in range(NCH):
            for nn in range(2):
                t0 = vch * 1024 + nn * 512
                p.do("tensor", lambda e, c=c, nn=nn, t0=t0: e.matmul(
                    pB[:, nn * 512:(nn + 1) * 512], V["wqkv_wv"][:, c, :],
                    V["h1"][:, c, t0:t0 + 512],
                    start=(c == 0), stop=(c == NCH - 1)),
                    inc="pe" if (c == NCH - 1 and nn == 1) else None)
        p.wait("vector", "pe", p.cnt["pe"])
        p.do("vector", lambda e: e.tensor_scalar(
            out=V["vf"][:], in0=pB[:, 0:1024], scalar1=1.0,
            scalar2=sb["bv_c"][:, hp:hp + 1], op0=OP.mult, op1=OP.add), inc="dv")
        p.wait("tensor", "dv", p.cnt["dv"])
        pBb = pB[:, 1024:1536].bitcast(BF16)
        for t in range(8):
            p.do("tensor", lambda e, t=t, pBb=pBb: e.transpose(
                pBb[:, t * 128:(t + 1) * 128], V["vf"][:, t * 128:(t + 1) * 128],
                sb["ident"][:]), inc="pe" if t == 7 else None)
        p.wait("vector", "pe", p.cnt["pe"])
        for t in range(8):
            kb = vch * 8 + t
            p.do("vector", lambda e, t=t, kb=kb, pBb=pBb: e.tensor_copy(
                out=V["vp"][:, kb, 0, 0:64],
                in_=pBb[:, t * 128:t * 128 + 64]))
            p.do("vector", lambda e, t=t, kb=kb, pBb=pBb: e.tensor_copy(
                out=V["vp"][:, kb, 1, 0:64],
                in_=pBb[:, t * 128 + 64:t * 128 + 128]),
                inc="dv" if t == 7 else None)
        p.wait("tensor", "dv", p.cnt["dv"])
    p.do("vector", lambda e: e.memset(V["vp"][:, :, :, 64:65], 1.0), inc="dv")
    p.pe_vlast = p.cnt["pe"]


    for si, (srcv, n) in enumerate(((V["qh"], SQ), (V["kk"], S))):
        for ch in range(n // 512):
            p.do("vector", lambda e, srcv=srcv, ch=ch: e.tensor_tensor(
                out=sb["sqs"][:], in0=srcv[:, ch * 512:(ch + 1) * 512],
                in1=srcv[:, ch * 512:(ch + 1) * 512], op=OP.mult), inc="dv")
            p.wait("tensor", "dv", p.cnt["dv"])
            p.do("tensor", lambda e: e.matmul(
                pB[0:2, 1024:1536], sb["blkd"][:], sb["sqs"][:],
                start=True, stop=True), inc="pe")
            p.wait("vector", "pe", p.cnt["pe"])
            dst = sb["tiny"][:, si:si + 1]
            if ch == 0:
                p.do("vector", lambda e, dst=dst: e.reduce_max(
                    out=dst, in_=pB[0:2, 1024:1536], axis=AX.X), inc="dv")
            else:
                p.do("vector", lambda e: e.reduce_max(
                    out=sb["tiny"][:, 9:10], in_=pB[0:2, 1024:1536], axis=AX.X))
                p.hop("vector")
                p.do("vector", lambda e, dst=dst: e.tensor_tensor(
                    out=dst, in0=dst, in1=sb["tiny"][:, 9:10], op=OP.max), inc="dv")
    p.wait("scalar", "dv", p.cnt["dv"])
    p.wait("scalar", "ac", ac_lnrel)
    p.do("scalar", lambda e: e.activation(out=sb["tiny"][:, 2:3], in_=sb["tiny"][:, 0:1],
                                          func=AF.Ln))
    p.do("scalar", lambda e: e.activation(out=sb["tiny"][:, 3:4], in_=sb["tiny"][:, 1:2],
                                          func=AF.Ln), inc="ac")
    p.wait("vector", "ac", p.cnt["ac"])
    p.do("vector", lambda e: e.tensor_tensor(out=sb["tiny"][:, 4:5], in0=sb["tiny"][:, 2:3],
                                             in1=sb["tiny"][:, 3:4], op=OP.add))
    p.hop("vector")
    p.do("vector", lambda e: e.tensor_scalar(out=sb["tiny"][:, 4:5], in0=sb["tiny"][:, 4:5],
                                             scalar1=0.5, scalar2=None, op0=OP.mult))
    p.do("vector", lambda e: e.tensor_tensor(out=sb["tiny"][:, 5:6], in0=sb["tiny"][:, 2:3],
                                             in1=sb["lnrel"][:], op=OP.add))
    p.hop("vector")
    p.do("vector", lambda e: e.tensor_scalar(out=sb["tiny"][:, 5:6], in0=sb["tiny"][:, 5:6],
                                             scalar1=0.5, scalar2=None, op0=OP.mult), inc="dv")
    p.wait("scalar", "dv", p.cnt["dv"])
    p.do("scalar", lambda e: e.activation(out=sb["tiny"][:, 6:7], in_=sb["tiny"][:, 4:5],
                                          func=AF.Exp))
    p.do("scalar", lambda e: e.activation(out=sb["tiny"][:, 7:8], in_=sb["tiny"][:, 5:6],
                                          func=AF.Exp), inc="ac")
    p.wait("vector", "ac", p.cnt["ac"])
    p.do("vector", lambda e: e.tensor_tensor(out=sb["tiny"][:, 8:9], in0=sb["tiny"][:, 6:7],
                                             in1=sb["tiny"][:, 7:8], op=OP.add))
    p.hop("vector")
    p.do("vector", lambda e: e.tensor_copy(out=sb["tinyb"][:, 0:1], in_=sb["tiny"][:, 8:9]),
         inc="dv")
    p.wait("tensor", "dv", p.cnt["dv"])
    p.do("tensor", lambda e: e.matmul(pB[0:1, 1024:1026], sb["tinyb"][:, 0:1], sb["i2"][:],
                                      start=True, stop=True), inc="pe")
    p.wait("vector", "pe", p.cnt["pe"])
    p.do("vector", lambda e: e.reduce_max(out=sb["tiny"][0:1, 9:10], in_=pB[0:1, 1024:1026],
                                          axis=AX.X))
    p.hop("vector")
    p.do("vector", lambda e: e.tensor_copy(out=sb["tinyb"][0:1, 1:2],
                                           in_=sb["tiny"][0:1, 9:10]), inc="dv")
    p.wait("tensor", "dv", p.cnt["dv"])
    p.do("tensor", lambda e: e.matmul(pB[:, 1026:1027], sb["ones_r"][:], sb["tinyb"][0:1, 1:2],
                                      start=True, stop=True), inc="pe")
    p.wait("scalar", "pe", p.cnt["pe"])
    p.do("scalar", lambda e: e.activation(out=sb["negb"][:], in_=pB[:, 1026:1027],
                                          func=AF.Copy, scale=-1.0), inc="ac")
    ac_negb = p.cnt["ac"]

    p.wait("tensor", "dv", dv_eW)
    p.wait("tensor", "ac", ac_negb)
    for nch in range(4):
        reg = pB[0:4, 1024 + (nch % 2) * 512:1536 + (nch % 2) * 512]
        if nch >= 2:
            p.wait("tensor", "dv", p.cnt["dv"])
        p.do("tensor", lambda e, nch=nch, reg=reg: e.matmul(
            reg, sb["eW"][:], V["qh"][:, nch * 512:(nch + 1) * 512],
            start=True, stop=True), inc="pe")
        p.wait("vector", "pe", p.cnt["pe"])
        p.do("vector", lambda e, nch=nch, reg=reg: e.tensor_copy(
            out=sb["edges4"][:, nch * 512:(nch + 1) * 512],
            in_=reg), inc="dv")
    dv_edges = p.cnt["dv"]
    if debug == 1 and hp == 0:
        p.wait("gpsimd", "ac", ac_negb)
        p.wait("gpsimd", "dv", dv_edges)
        for nm, src in (("d_negb", sb["negb"][:]), ("d_qh", V["qh"]), ("d_k", V["kk"]),
                        ("d_vp", V["vp"]), ("d_edge", sb["edges4"][:])):
            p.do("gpsimd", lambda e, nm=nm, src=src: e.dma_start(out=dbg[nm], in_=src),
                 inc="dgd", inc_n=16)
        p.dbg_barrier()

    ec_ev = {}
    for qb in range(16):
        pos = 1024 + (qb % 2) * 512
        if qb < 2:
            p.wait("tensor", "dv", dv_edges)
        else:
            p.wait("tensor", "dv", ec_ev[qb - 2])
        for j in range(4):
            p.do("tensor", lambda e, qb=qb, j=j, pos=pos: e.matmul(
                pB[:, pos + j:pos + j + 1], sb["edges4"][:, qb * 128:(qb + 1) * 128],
                sb["sel_c"][:, j:j + 1], start=True, stop=True),
                inc="pe" if j == 3 else None)
        p.wait("vector", "pe", p.cnt["pe"])
        ec_ev[qb] = p.do("vector", lambda e, qb=qb, pos=pos: e.tensor_copy(
            out=sb["ecols"][:, qb * 4:(qb + 1) * 4], in_=pB[:, pos:pos + 4]), inc="dv")

    p.pair_dv0 = p.cnt["dv"]
    p.t8_dv = p.pair_dv0
    for qg in range(NG):
        gen_group(p, sb, V, pA, pB, hp, qg, kb_ranges, dbg, debug)


def rg_layout(qg):
    offs = {}
    pos = 0
    for qi in range(GQB):
        qb = qg * GQB + qi
        kb_lo, kb_hi = band_range(qb)
        w = (kb_hi - kb_lo + 1) * 128
        for h in range(2):
            offs[(qi, h)] = (pos, w)
            pos += w
    assert pos <= 18432, pos
    return offs


def gen_group(p, sb, V, pA, pB, hp, qg, kb_ranges, dbg, debug):
    qbs = [qg * GQB + i for i in range(GQB)]
    q0g = qg * GQB * 128
    offs = rg_layout(qg)
    denbL = sb["rowm"][:].bitcast(BF16)
    denbG = sb["rowr"][:].bitcast(BF16)
    row_ext = sb["arena"][:].ap[0][0]

    gmap = {}
    for qi, qb in enumerate(qbs):
        kb_lo, kb_hi = kb_ranges[qb]
        wb = (kb_hi - kb_lo + 1) * 128
        colstart = kb_lo * 128 - qb * 128 + 1024 + PAD
        assert colstart - 127 >= 0 and colstart + wb - 1 <= CP - 1
        for h in range(2):
            u = p.pguse
            sv = V["pgs"][u % 3]
            if u - 3 in p.guses:
                gs, gv = p.guses.pop(u - 3)
                p.wait("vector", gs, gv)
            if u % 3 == 2 and p.slot2_fresh:
                p.slot2_fresh = False
                p.wait("vector", "pe", p.pe_vlast)
            ecl = qb * 4 + 2 * h
            p.do("vector", lambda e, sv=sv, ecl=ecl: e.tensor_copy(
                out=sv[:, 0:PAD],
                in_=sb["ecols"][:, ecl:ecl + 1].broadcast_to([128, PAD])), inc="dp")
            p.do("vector", lambda e, sv=sv, ecl=ecl: e.tensor_copy(
                out=sv[:, PAD + 2048:CP],
                in_=sb["ecols"][:, ecl + 1:ecl + 2].broadcast_to(
                    [128, CP - PAD - 2048])), inc="dp")
            for j in range(4):
                r = p.pmm_rot % 2
                p.pmm_rot += 1
                reg = pB[:, 1024 + r * 512:1536 + r * 512]
                if len(p.dp_hist) >= 2:
                    p.wait("tensor", "dp", p.dp_hist[-2])
                if p.pair_pmm_first:
                    p.pair_pmm_first = False
                    p.wait("tensor", "dv", p.pair_dv0)
                p.do("tensor", lambda e, reg=reg, h=h, qb=qb, j=j: e.matmul(
                    reg, V["qh"][h * 64:(h + 1) * 64, qb * 128:(qb + 1) * 128],
                    sb["relg"][h * 64:(h + 1) * 64, j * 512:(j + 1) * 512],
                    start=True, stop=True, tile_position=(h * 64, 0)), inc="pe")
                p.wait("vector", "pe", p.cnt["pe"])
                ev = p.do("vector", lambda e, sv=sv, j=j, reg=reg: e.tensor_copy(
                    out=sv[:, PAD + j * 512:PAD + (j + 1) * 512], in_=reg), inc="dp")
                p.dp_hist.append(ev)
            qeng, qsem = ("sync", "dsg") if h == 0 else ("gpsimd", "dsh")
            p.wait(qeng, qsem, p.cnt[qsem])
            p.wait(qeng, "dp", p.cnt["dp"])
            p.wait(qeng, "pe", p.sdone_last)
            off0, w0 = offs[(qi, h)]
            assert w0 == wb
            src = AP(sv.tensor, sv.offset + colstart, [[row_ext - 1, 128], [1, wb]])
            g = p.do(qeng, lambda e, src=src, off0=off0, wb=wb: e.dma_start(
                out=V["rg"][:, off0:off0 + wb], in_=src), inc=qsem, inc_n=16)
            gmap[(qi, h)] = (qsem, g)
            p.guses[u] = (qsem, g)
            p.pguse += 1
        if debug == 1 and hp == 0 and qg == 0 and qb == 0:
            p.wait("gpsimd", "dp", p.cnt["dp"])
            p.do("gpsimd", lambda e: e.dma_start(
                out=dbg["d_pg"], in_=V["pgs"][0][:, :]), inc="dgd", inc_n=16)
            p.wait("gpsimd", "dsg", gmap[(0, 0)][1])
            p.do("gpsimd", lambda e, wb=wb: e.dma_start(
                out=dbg["d_rg"][:, 0:wb], in_=V["rg"][:, 0:wb]), inc="dgd", inc_n=16)
            p.dbg_barrier()

    p.flush_pending()

    p.wait("vector", "pe", p.g65pe)
    p.do("vector", lambda e: e.tensor_scalar(
        out=V["sigm"][:, :], in0=V["qh"][:, q0g:q0g + 512], scalar1=0.0, scalar2=None,
        op0=OP.min), inc="dv")
    p.wait("scalar", "dv", p.cnt["dv"])
    p.do("scalar", lambda e: e.activation(out=V["sig"][:, :], in_=V["sigm"][:, :],
                                          func=AF.Exp, scale=8.0), inc="ac")
    exp_base = p.cnt["ac"]
    p.wait("vector", "ac", exp_base)
    p.do("vector", lambda e: e.tensor_scalar(
        out=V["sigm"][:, :], in0=V["qh"][:, q0g:q0g + 512], scalar1=0.0, scalar2=None,
        op0=OP.max))
    p.hop("vector")
    p.do("vector", lambda e: e.scalar_tensor_tensor(
        out=V["sig"][:, :], in0=V["sigm"][:, :], scalar=8.0, in1=V["sig"][:, :],
        op0=OP.mult, op1=OP.add), inc="dv")
    p.wait("tensor", "dv", p.cnt["dv"])
    p.wait("tensor", "dgz", p.cnt["dgz"])
    p.wait("tensor", "ac", p.o65ac)
    p.do("tensor", lambda e: e.matmul(pA[0:65, 1024:1536], sb["mz"][0:64, :],
                                      V["sig"][0:64, :], start=True, stop=True,
                                      tile_position=(0, 0)))
    p.do("tensor", lambda e: e.matmul(pA[0:65, 1536:2048], sb["mz"][64:128, :],
                                      V["sig"][64:128, :], start=True, stop=True,
                                      tile_position=(64, 0)), inc="pe")
    p.g65pe = p.cnt["pe"]
    p.wait("vector", "pe", p.g65pe)
    for h in range(2):
        p.do("vector", lambda e, h=h: e.tensor_copy(
            out=sb["g65s"][:, h, :], in_=pA[0:65, 1024 + h * 512:1024 + (h + 1) * 512]))
        p.do("vector", lambda e, h=h, denbG=denbG: e.tensor_scalar(
            out=denbG[0:1, h * 512:(h + 1) * 512],
            in0=pA[64:65, 1024 + h * 512:1024 + (h + 1) * 512],
            scalar1=1e-6, scalar2=None, op0=OP.add), inc="dv" if h == 1 else None)
    kb_dv0 = p.cnt["dv"]

    sdone = []
    avdone = []
    waited = {"dsg": 0, "dsh": 0}

    def emit_S(kb):
        pp = kb % 2
        reg = pA[:, pp * 1024:(pp + 1) * 1024]
        if kb == 0:
            p.wait("tensor", "ac", p.o65ac)
            p.wait("tensor", "dv", p.t8_dv)
        elif kb == 1:
            p.wait("tensor", "dv", kb_dv0)
        else:
            p.wait("tensor", "ac", exp_base + (kb - 1))
        for qi, qb in enumerate(qbs):
            if kb_ranges[qb][0] <= kb <= kb_ranges[qb][1]:
                for h in range(2):
                    qsem, gv = gmap[(qi, h)]
                    if gv > waited[qsem]:
                        waited[qsem] = gv
                        p.wait("tensor", qsem, gv)
        for h in range(2):
            hreg = reg[:, h * 512:(h + 1) * 512]
            tp = (0, 0) if h == 0 else (64, 0)
            band = [(qi, qb, kb - kb_ranges[qb][0]) for qi, qb in enumerate(qbs)
                    if kb_ranges[qb][0] <= kb <= kb_ranges[qb][1]]
            farA = [qi for qi, qb in enumerate(qbs) if kb < kb_ranges[qb][0]]
            farB = [qi for qi, qb in enumerate(qbs) if kb > kb_ranges[qb][1]]
            nacc = len(band) + (1 if farA else 0) + (1 if farB else 0)
            p.do("tensor", lambda e, hreg=hreg, h=h, kb=kb, tp=tp, nacc=nacc: e.matmul(
                hreg, V["kk"][h * 64:(h + 1) * 64, kb * 128:(kb + 1) * 128],
                V["qh"][h * 64:(h + 1) * 64, q0g:q0g + 512],
                start=True, stop=(nacc == 0), tile_position=tp))
            i = 0
            for qi, qb, off in band:
                i += 1
                off0, w0 = offs[(qi, h)]
                last = (i == nacc)
                p.do("tensor", lambda e, hreg=hreg, qi=qi, off0=off0, off=off, last=last: e.matmul(
                    hreg[:, qi * 128:(qi + 1) * 128],
                    V["rg"][:, off0 + off * 128:off0 + (off + 1) * 128],
                    sb["ident"][:], start=False, stop=last))
            for falist, eidx in ((farA, 0), (farB, 1)):
                if not falist:
                    continue
                i += 1
                lo, hi = falist[0], falist[-1]
                assert falist == list(range(lo, hi + 1))
                last = (i == nacc)
                j = 2 * h + eidx
                p.do("tensor", lambda e, hreg=hreg, j=j, lo=lo, hi=hi, last=last: e.matmul(
                    hreg[:, lo * 128:(hi + 1) * 128], sb["sel_r"][:, j * 128:(j + 1) * 128],
                    sb["edges4"][:, q0g + lo * 128:q0g + (hi + 1) * 128],
                    start=False, stop=last))
        v = p.inc_last("tensor", "pe")
        sdone.append(v)

    def emit_attnv(kb):
        p.wait("tensor", "ac", exp_base + kb + 1)
        for h in range(2):
            p.do("tensor", lambda e, kb=kb, h=h: e.matmul(
                pB[0:65, h * 512:(h + 1) * 512], V["vp"][:, kb, h, :],
                V["ee"][:, kb % 2, h * 512:(h + 1) * 512],
                start=(kb == 0), stop=(kb == NKB - 1)))
        v = p.inc_last("tensor", "pe")
        avdone.append(v)

    emit_S(0)
    emit_S(1)
    for kb in range(NKB):
        if kb + 2 < NKB:
            emit_S(kb + 2)
        emit_attnv(kb)
    for kb in range(NKB):
        p.wait("scalar", "pe", sdone[kb])
        if kb >= 2:
            p.wait("scalar", "pe", avdone[kb - 2])
        p.do("scalar", lambda e, kb=kb: e.activation(
            out=V["ee"][:, kb % 2, :], in_=pA[:, (kb % 2) * 1024:(kb % 2 + 1) * 1024],
            func=AF.Exp, bias=sb["negb"][:], scale=1.0), inc="ac")
    p.sdone_last = sdone[-1]

    avlast = avdone[-1]

    def tail(avlast=avlast, q0g=q0g, hp=hp, qg=qg, denbL=denbL, denbG=denbG):
        p.wait("scalar", "pe", avlast)
        p.do("scalar", lambda e: e.activation(out=sb["o65"][:, 0, :], in_=pB[0:65, 0:512],
                                              func=AF.Copy))
        p.do("scalar", lambda e: e.activation(out=sb["o65"][:, 1, :], in_=pB[0:65, 512:1024],
                                              func=AF.Copy), inc="ac")
        t1 = p.cnt["ac"]
        p.o65ac = t1
        if debug == 1 and hp == 0 and qg == 0:
            p.wait("gpsimd", "ac", t1)
            p.do("gpsimd", lambda e: e.dma_start(out=dbg["d_o65"], in_=sb["o65"][:]),
                 inc="dgd", inc_n=16)
            p.dbg_barrier()
        p.wait("vector", "ac", t1)
        for h in range(2):
            p.do("vector", lambda e, h=h, denbL=denbL: e.tensor_copy(
                out=denbL[0:1, h * 512:(h + 1) * 512],
                in_=pB[64:65, h * 512:(h + 1) * 512]), inc="dv" if h == 1 else None)
        t2 = p.cnt["dv"]
        p.wait("tensor", "ac", t1)
        p.wait("tensor", "dv", t2)
        for c in range(16):
            srcr = denbL if c < 8 else denbG
            cc = c % 8
            p.do("tensor", lambda e, srcr=srcr, cc=cc, c=c: e.matmul(
                pB[:, c:c + 1], srcr[0:1, cc * 128:(cc + 1) * 128],
                sb["ones_r"][0:1, 0:1], start=True, stop=True),
                inc="pe" if c == 15 else None)
        p.wait("vector", "pe", p.cnt["pe"])
        p.do("vector", lambda e: e.reciprocal(out=sb["rcolf"][:], in_=pB[:, 0:16]))
        p.hop("vector")
        p.do("vector", lambda e: e.tensor_copy(out=sb["rcol"][:], in_=sb["rcolf"][:]),
             inc="dv")
        p.wait("tensor", "dv", p.cnt["dv"])
        for rr in range(4):
            for c in range(4):
                j = rr * 4 + c
                p.do("tensor", lambda e, rr=rr, c=c, j=j: e.matmul(
                    pA[0:64, rr * 512 + c * 128:rr * 512 + (c + 1) * 128],
                    sb["rcol"][:, j:j + 1].broadcast_to([128, 64]),
                    sb["ident"][:], start=True, stop=True),
                    inc="pe" if j == 15 else None)
        p.wait("vector", "pe", p.cnt["pe"])
        p.wait("vector", "dsc", p.cnt["dsc"])
        for h in range(2):
            if h == 1:
                p.wait("vector", "dv", p.cnt["dv"])
            p.do("vector", lambda e, h=h: e.scalar_tensor_tensor(
                out=sb["ct"][:, 0, :], in0=sb["o65"][0:64, h, :],
                scalar=sb["gate"][0:64, :],
                in1=pA[0:64, h * 512:(h + 1) * 512], op0=OP.mult, op1=OP.mult))
            p.do("vector", lambda e, h=h: e.scalar_tensor_tensor(
                out=sb["ct"][:, 1, :], in0=sb["g65s"][0:64, h, :],
                scalar=sb["gate1m"][0:64, :],
                in1=pA[0:64, 1024 + h * 512:1024 + (h + 1) * 512],
                op0=OP.mult, op1=OP.mult))
            p.hop("vector")
            if h == 0:
                p.do("vector", lambda e: e.tensor_tensor(
                    out=V["comb"][0:64, hp, q0g:q0g + 512], in0=sb["ct"][:, 0, :],
                    in1=sb["ct"][:, 1, :], op=OP.add), inc="dv")
            else:
                p.do("vector", lambda e: e.tensor_tensor(
                    out=sb["ct"][:, 0, :], in0=sb["ct"][:, 0, :],
                    in1=sb["ct"][:, 1, :], op=OP.add), inc="dv")
                p.wait("sync", "dv", p.cnt["dv"])
                p.do("sync", lambda e: e.dma_start(
                    out=V["comb"][64:128, hp, q0g:q0g + 512], in_=sb["ct"][:, 0, :]),
                    inc="dsc", inc_n=16)
        p.t8_dv = p.cnt["dv"]

    p.pending = tail


def gen_ffn(p, dd, sb, V, pA, pB, out_cm, attn_sp, ds_attn, dbg, debug):
    def wload(dname, c, width, wait_ac=None):
        if wait_ac is not None:
            p.wait("gpsimd", "ac", wait_ac)
        s = "dgx0" if c % 2 == 0 else "dgx1"
        return (s, p.do("gpsimd", lambda e, c=c, dname=dname, width=width: e.dma_start(
            out=V["wsl"][:, c % 2, 0:width], in_=dd[dname][c * 128:(c + 1) * 128, :]),
            inc=s, inc_n=16))

    p.wait("tensor", "ac", p.cnt["ac"])
    p.wait("tensor", "dv", p.cnt["dv"])
    p.wait("gpsimd", "dss0", ds_attn[0])
    p.wait("gpsimd", "dss1", ds_attn[1])
    loads = {0: wload("w1T", 0, 4 * D), 1: wload("w1T", 1, 4 * D)}
    done = {}
    p.wait("scalar", "pe", p.cnt["pe"])
    for c in range(NCH):
        lsem, lval = loads[c]
        p.wait("scalar", lsem, lval)
        p.do("scalar", lambda e, c=c: e.activation(
            out=V["s1"][:, c, :], in_=V["wsl"][:, c % 2, :], func=AF.Sign), inc="ac")
        for ch in range(8):
            p.wait("scalar", "pe", p.cnt["pe"])
            p.do("scalar", lambda e, c=c, ch=ch: e.activation(
                out=sb["sqs"][:], in_=V["wsl"][:, c % 2, ch * 512:(ch + 1) * 512],
                func=AF.Abs), inc="ac")
            p.wait("tensor", "ac", p.cnt["ac"])
            ps = (pA if ch < 4 else pB)[0:1, (ch % 4) * 512:(ch % 4 + 1) * 512]
            p.do("tensor", lambda e, ps=ps, c=c: e.matmul(
                ps, sb["onesc"][:], sb["sqs"][:], start=(c == 0), stop=(c == NCH - 1)),
                inc="pe")
        done[c] = p.cnt["ac"]
        if c + 2 < NCH:
            loads[c + 2] = wload("w1T", c + 2, 4 * D, wait_ac=done[c])
    p.wait("vector", "pe", p.cnt["pe"])
    for ch in range(8):
        ps = (pA if ch < 4 else pB)[0:1, (ch % 4) * 512:(ch % 4 + 1) * 512]
        p.do("vector", lambda e, ps=ps: e.tensor_scalar(
            out=sb["rowb"][:], in0=ps, scalar1=1.0 / D, scalar2=None, op0=OP.mult),
            inc="dv")
        p.wait("tensor", "dv", p.cnt["dv"])
        for q in range(4):
            p.do("tensor", lambda e, ch=ch, q=q: e.matmul(
                pB[:, 1536 + 4 * (ch % 2) + q:1537 + 4 * (ch % 2) + q],
                sb["rowb"][0:1, q * 128:(q + 1) * 128],
                sb["ones_r"][0:1, 0:1], start=True, stop=True),
                inc="pe" if q == 3 else None)
        p.wait("vector", "pe", p.cnt["pe"])
        p.do("vector", lambda e, ch=ch: e.tensor_copy(
            out=sb["a1c"][:, ch * 4:(ch + 1) * 4],
            in_=pB[:, 1536 + 4 * (ch % 2):1540 + 4 * (ch % 2)]), inc="dv")
    loads = {0: wload("w2T", 0, D, wait_ac=done[NCH - 2]),
             1: wload("w2T", 1, D, wait_ac=done[NCH - 1])}
    done2 = {}
    p.wait("scalar", "pe", p.cnt["pe"])
    p.wait("scalar", "dss0", p.cnt["dss0"])
    p.wait("scalar", "dss1", p.cnt["dss1"])
    for c in range(32):
        lsem, lval = loads[c]
        p.wait("scalar", lsem, lval)
        p.do("scalar", lambda e, c=c: e.activation(
            out=V["s2"][:, c, :], in_=V["wsl"][:, c % 2, 0:1024], func=AF.Sign), inc="ac")
        for ch in range(2):
            p.wait("scalar", "pe", p.cnt["pe"])
            p.do("scalar", lambda e, c=c, ch=ch: e.activation(
                out=sb["sqs"][:], in_=V["wsl"][:, c % 2, ch * 512:(ch + 1) * 512],
                func=AF.Abs), inc="ac")
            p.wait("tensor", "ac", p.cnt["ac"])
            ps = pA[0:1, ch * 512:(ch + 1) * 512]
            p.do("tensor", lambda e, ps=ps, c=c: e.matmul(
                ps, sb["onesc"][:], sb["sqs"][:], start=(c == 0), stop=(c == 31)),
                inc="pe")
        done2[c] = p.cnt["ac"]
        if c + 2 < 32:
            loads[c + 2] = wload("w2T", c + 2, D, wait_ac=done2[c])
    p.wait("vector", "pe", p.cnt["pe"])
    for ch in range(2):
        ps = pA[0:1, ch * 512:(ch + 1) * 512]
        p.do("vector", lambda e, ps=ps: e.tensor_scalar(
            out=sb["rowb"][:], in0=ps, scalar1=1.0 / (4 * D), scalar2=None, op0=OP.mult),
            inc="dv")
        p.wait("tensor", "dv", p.cnt["dv"])
        for q in range(4):
            p.do("tensor", lambda e, ch=ch, q=q: e.matmul(
                pB[:, 1536 + 4 * ch + q:1537 + 4 * ch + q],
                sb["rowb"][0:1, q * 128:(q + 1) * 128],
                sb["ones_r"][0:1, 0:1], start=True, stop=True),
                inc="pe" if q == 3 else None)
        p.wait("vector", "pe", p.cnt["pe"])
        p.do("vector", lambda e, ch=ch: e.tensor_copy(
            out=sb["a2c"][:, ch * 4:(ch + 1) * 4],
            in_=pB[:, 1536 + 4 * ch:1540 + 4 * ch]), inc="dv")
    dv_bw = p.cnt["dv"]
    if debug == 1:
        p.wait("gpsimd", "dv", dv_bw)
        p.do("gpsimd", lambda e: e.dma_start(out=dbg["d_s1"], in_=V["s1"]),
             inc="dgd", inc_n=16)
        p.dbg_barrier()

    xloads = {}
    atloads = {}
    ac_signs_done = p.cnt["ac"]
    def emit_fx(sl, wait_dv=None):
        ap = dd["xT"][:, sl * FSL:(sl + 1) * FSL].rearrange("(c p) t -> p c t", p=128)
        if wait_dv is not None:
            p.wait("gpsimd", "dv", wait_dv)
        if sl == 0:
            p.wait("gpsimd", "ac", ac_signs_done)
        xloads[sl] = p.do("gpsimd", lambda e, ap=ap: e.dma_start(
            out=V["fx"], in_=ap), inc="dgf", inc_n=16)
        ap2 = attn_sp[:, :, sl * FSL:(sl + 1) * FSL].rearrange("c p t -> p c t")
        atloads[sl] = p.do("gpsimd", lambda e, ap2=ap2: e.dma_start(
            out=V["fat"], in_=ap2), inc="dgf", inc_n=16)
    emit_fx(0)

    R8 = [pA[:, i * 512:(i + 1) * 512] for i in range(4)] + \
         [pB[:, i * 512:(i + 1) * 512] for i in range(4)]
    out_dma = {}
    import os
    SLAB_BARRIER = os.environ.get("DK_SLAB_BARRIER", "0") == "1"
    for sl in range(NFS):
        if SLAB_BARRIER and sl > 0:
            for eng in ("tensor", "scalar", "vector", "gpsimd", "sync"):
                p.wait(eng, "pe", p.cnt["pe"])
                p.wait(eng, "ac", p.cnt["ac"])
                p.wait(eng, "dv", p.cnt["dv"])
                p.wait(eng, "dso", p.cnt["dso"])
        t0 = sl * FSL
        p.wait("vector", "dgf", atloads[sl])
        if sl == 0:
            p.wait("vector", "dv", dv_bw)
            p.wait("vector", "ac", ac_signs_done)
        xs = V["fx"]
        at = V["fat"]
        p.wait("vector", "dv", p.cnt["dv"])
        for c in range(NCH):
            p.do("vector", lambda e, xs=xs, at=at, c=c: e.scalar_tensor_tensor(
                out=V["x2sl"][:, c, :], in0=at[:, c, :],
                scalar=sb["ga"][:, c:c + 1], in1=xs[:, c, :], op0=OP.mult, op1=OP.add),
                inc="dv" if c == NCH - 1 else None)
        if sl + 1 < NFS:
            emit_fx(sl + 1, wait_dv=p.cnt["dv"])
        p.wait("vector", "dv", p.cnt["dv"])
        p.do("vector", lambda e: e.tensor_tensor(
            out=V["xsq2"], in0=V["x2sl"], in1=V["x2sl"], op=OP.mult), inc="dv")
        p.wait("tensor", "dv", p.cnt["dv"])
        p.wait("tensor", "ac", p.cnt["ac"])
        rsum, rsq = pB[0:1, 1024:1024 + FSL], pB[0:1, 1536:1536 + FSL]
        rrep, rmrep = pA[:, 1024:1024 + FSL], pA[:, 1536:1536 + FSL]
        for c in range(NCH):
            p.do("tensor", lambda e, c=c, rsum=rsum: e.matmul(
                rsum, sb["onescf"][:], V["x2sl"][:, c, :],
                start=(c == 0), stop=(c == NCH - 1)))
        for c in range(NCH):
            p.do("tensor", lambda e, c=c, rsq=rsq: e.matmul(
                rsq, sb["onesc"][:], V["xsq2"][:, c, :],
                start=(c == 0), stop=(c == NCH - 1)),
                inc="pe" if c == NCH - 1 else None)
        p.wait("vector", "pe", p.cnt["pe"])
        emit_ln_rows(p, sb, rsum, rsq, FSL)
        p.wait("tensor", "dv", p.cnt["dv"])
        p.do("tensor", lambda e, rrep=rrep: e.matmul(
            rrep, sb["ones_r"][:], sb["rowb"][0:1, 0:FSL], start=True, stop=True), inc="pe")
        p.wait("vector", "pe", p.cnt["pe"])
        p.do("vector", lambda e: e.tensor_copy(out=sb["rowb"][0:1, 0:FSL],
                                               in_=sb["rowmr"][0:1, 0:FSL]), inc="dv")
        p.wait("tensor", "dv", p.cnt["dv"])
        p.do("tensor", lambda e, rmrep=rmrep: e.matmul(
            rmrep, sb["ones_r"][:], sb["rowb"][0:1, 0:FSL], start=True, stop=True), inc="pe")
        p.wait("vector", "pe", p.cnt["pe"])
        for c in range(NCH):
            p.do("vector", lambda e, c=c, rrep=rrep: e.tensor_tensor(
                out=sb["tmpa"][:, 0:FSL], in0=V["x2sl"][:, c, :], in1=rrep, op=OP.mult))
            p.hop("vector")
            p.do("vector", lambda e, c=c, rmrep=rmrep: e.tensor_tensor(
                out=sb["tmpa"][:, 0:FSL], in0=sb["tmpa"][:, 0:FSL], in1=rmrep,
                op=OP.subtract))
            p.hop("vector")
            p.do("vector", lambda e, c=c: e.tensor_scalar(
                out=V["h3sl"][:, c, :], in0=sb["tmpa"][:, 0:FSL],
                scalar1=sb["cb1"][:, c:c + 1], scalar2=sb["sbv"][:, c:c + 1],
                op0=OP.mult, op1=OP.add), inc="dv" if c == NCH - 1 else None)
            if c < NCH - 1:
                p.hop("vector")
        dv_h3 = p.cnt["dv"]
        if debug == 1 and sl == 0:
            p.wait("gpsimd", "dv", dv_h3)
            p.do("gpsimd", lambda e: e.dma_start(out=dbg["d_h3"], in_=V["h3sl"]),
                 inc="dgd", inc_n=16)
            p.dbg_barrier()
        p.wait("tensor", "dv", dv_h3)
        evac = {}
        for ho in range(32):
            reg = R8[ho % 8]
            if ho >= 8:
                p.wait("tensor", "ac", evac[ho - 8])
            for c in range(NCH):
                p.do("tensor", lambda e, ho=ho, c=c, reg=reg: e.matmul(
                    reg[:, 0:FSL], V["s1"][:, c, ho * 128:(ho + 1) * 128],
                    V["h3sl"][:, c, :], start=(c == 0), stop=(c == NCH - 1)),
                    inc="pe" if c == NCH - 1 else None)
            p.wait("scalar", "pe", p.cnt["pe"])
            evac[ho] = p.do("scalar", lambda e, ho=ho, reg=reg: e.activation(
                out=V["h2sl"][:, ho, :], in_=reg[:, 0:FSL], func=AF.Gelu,
                bias=sb["b1_c"][:, ho:ho + 1], scale=sb["a1c"][:, ho:ho + 1]), inc="ac")
        if sl >= 1:
            p.wait("vector", "dso", out_dma[(sl - 1, 1)])
        p.wait("tensor", "ac", evac[31])
        for blk in range(NCH):
            reg = R8[blk]
            for hc in range(32):
                p.do("tensor", lambda e, blk=blk, hc=hc, reg=reg: e.matmul(
                    reg[:, 0:FSL], V["s2"][:, hc, blk * 128:(blk + 1) * 128],
                    V["h2sl"][:, hc, :], start=(hc == 0), stop=(hc == 31)),
                    inc="pe" if hc == 31 else None)
            p.wait("vector", "pe", p.cnt["pe"])
            p.wait("vector", "dv", p.cnt["dv"])
            if blk == 4:
                p.wait("vector", "dso", out_dma[(sl, 0)])
            p.do("vector", lambda e, blk=blk, reg=reg: e.tensor_scalar(
                out=sb["sqs"][:, 0:FSL], in0=reg[:, 0:FSL], scalar1=sb["a2c"][:, blk:blk + 1],
                scalar2=sb["b2_c"][:, blk:blk + 1], op0=OP.mult, op1=OP.add))
            p.hop("vector")
            p.do("vector", lambda e, sl=sl, blk=blk: e.scalar_tensor_tensor(
                out=V["outsl"][:, blk % 4, :], in0=sb["sqs"][:, 0:FSL],
                scalar=sb["gb"][:, blk:blk + 1], in1=V["x2sl"][:, blk, :],
                op0=OP.mult, op1=OP.add), inc="dv")
            if blk % 4 == 3:
                half4 = blk // 4
                p.wait("sync", "dv", p.cnt["dv"])
                dst = out_cm[half4 * 4 * 128:(half4 + 1) * 4 * 128,
                             t0:t0 + FSL].rearrange("(c p) t -> p c t", p=128)
                out_dma[(sl, half4)] = p.do("sync", lambda e, dst=dst, sl=sl: e.dma_start(
                    out=dst, in_=V["outsl"][:, :, :]), inc="dso", inc_n=16)
        dv_out = p.cnt["dv"]



def prep_inputs(inputs):
    f32 = np.float32
    bf = ml_dtypes.bfloat16
    x = np.asarray(inputs["L0_attn_hidden_states"], f32)
    relT = np.ascontiguousarray(np.asarray(inputs["L0_attn_rel_emb"], f32).T)
    rel_rev = np.ascontiguousarray(relT[:, ::-1])
    Mz = np.concatenate([np.asarray(inputs["L0_attn_M"], f32),
                         np.asarray(inputs["L0_attn_z"], f32)[:, :, None]], axis=2)
    mzp = np.ascontiguousarray(Mz.reshape(8, 128, 65))
    WqT = np.ascontiguousarray(np.asarray(inputs["L0_attn_Wq"], f32).T)
    WkT = np.ascontiguousarray(np.asarray(inputs["L0_attn_Wk"], f32).T)
    WvT = np.ascontiguousarray(np.asarray(inputs["L0_attn_Wv"], f32).T)
    WoT = np.ascontiguousarray(np.asarray(inputs["L0_attn_Wo"], f32).T)
    w1T = np.ascontiguousarray(np.asarray(inputs["L0_ffn_w1"], f32).T)
    w2T = np.ascontiguousarray(np.asarray(inputs["L0_ffn_w2"], f32).T)
    wq_p = np.ascontiguousarray(WqT.reshape(D, NP, 128).transpose(1, 0, 2))
    wk_p = np.ascontiguousarray(WkT.reshape(D, NP, 128).transpose(1, 0, 2))
    wv_p = np.ascontiguousarray(WvT.reshape(D, NP, 128).transpose(1, 0, 2))

    def col8(v):
        return np.ascontiguousarray(np.asarray(v, f32).reshape(8, 128).T)
    def col32(v):
        return np.ascontiguousarray(np.asarray(v, f32).reshape(32, 128).T)

    sel_r = np.zeros((4, 512), bf)
    sel_c = np.zeros((4, 4), bf)
    for j in range(4):
        sel_r[j, j * 128:(j + 1) * 128] = 1
        sel_c[j, j] = 1
    blkd = np.zeros((128, 2), bf)
    blkd[0:64, 0] = 1
    blkd[64:128, 1] = 1

    common = dict(
        wmod=np.asarray(inputs["L0_wmod"], f32),
        wq_p=wq_p, wk_p=wk_p, wv_p=wv_p, woT=WoT, w1T=w1T, w2T=w2T,
        mzp=mzp,
        bq_c=col8(inputs["L0_attn_bq"]), bk_c=col8(inputs["L0_attn_bk"]),
        bv_c=col8(inputs["L0_attn_bv"]), bo_c=col8(inputs["L0_attn_bo"]),
        b1_c=col32(inputs["L0_ffn_b1"]), b2_c=col8(inputs["L0_ffn_b2"]),
        galpha=np.asarray(inputs["L0_attn_gate_alpha"], f32).reshape(1, 1),
        ident=np.eye(128, dtype=bf),
        ones_r=np.ones((1, 128), bf),
        ones_rf=np.ones((1, 128), np.float32),
        onesc=np.ones((128, 1), bf),
        onescf=np.ones((128, 1), np.float32),
        blkd=blkd,
        o642=np.ones((64, 2), bf),
        i2=np.eye(2, dtype=bf),
        sel_r=sel_r, sel_c=sel_c,
    )
    relg0 = np.ascontiguousarray(np.concatenate([rel_rev, rel_rev], 0))
    relg1 = np.ascontiguousarray(np.concatenate([relT, relT], 0))
    in_maps = []
    for core in range(8):
        b, half = core // 2, core % 2
        xT = np.ascontiguousarray(x[b].T)
        if half == 1:
            xT = np.ascontiguousarray(xT[:, ::-1])
        m = dict(common)
        m["xT"] = xT
        m["relg"] = relg0 if half == 0 else relg1
        m["c_col"] = np.ascontiguousarray(
            np.asarray(inputs["L0_c"], f32)[b].reshape(8, 128).T)
        in_maps.append(m)
    return in_maps


def assemble(results):
    out = np.zeros((4, S, D), np.float32)
    for core in range(8):
        b, half = core // 2, core % 2
        oc = results[core]["out_cm"]
        if half == 0:
            out[b, 0:SQ, :] = oc.T
        else:
            out[b, SQ:S, :] = oc[:, ::-1].T
    return out


def kernel(**inputs):
    from concourse.bass_utils import run_bass_kernel_spmd
    in_maps = prep_inputs(inputs)
    nc = build_program(debug=0)
    res = run_bass_kernel_spmd(nc, in_maps, core_ids=list(range(8)))
    return assemble(res.results)


# version 04
# speedup vs baseline: 1.2206x; 1.0076x over previous
import contextlib
import numpy as np
import ml_dtypes
import concourse.bass as bass
import concourse.mybir as mybir
from concourse.ap import AP

BF16 = mybir.dt.bfloat16
F32 = mybir.dt.float32
AF = mybir.ActivationFunctionType
OP = mybir.AluOpType
AX = mybir.AxisListType

D = 1024
S = 4096
SQ = 2048
NP = 8
NCH = 8
MR = 1024
PAD = 254
CP = 2049 + 2 * PAD
GQB = 4
NG = SQ // (GQB * 128)
BAND = MR + 127
NKB = S // 128
FSL = 256
NFS = SQ // FSL

ENGINES = ("tensor", "scalar", "vector", "sync", "gpsimd")


def band_range(qb):
    q0 = qb * 128
    kb_lo = max(0, -(-(q0 - BAND) // 128))
    kb_hi = min(NKB - 1, (q0 + BAND) // 128)
    return kb_lo, kb_hi


class Prog:
    def __init__(self, nc, es):
        self.nc = nc
        self.ops = {e: [] for e in ENGINES}
        self.last_has_inc = {}
        self.cnt = {}
        self.sems = {}
        for s in ("dg", "ds", "pe", "ac", "dv", "dsg", "dsh", "dsc", "dss0", "dss1",
                  "dso", "dgm", "dgz", "dgw", "dgo", "dgx0", "dgx1", "dgy0", "dgy1", "dgf", "dgd",
                  "hv", "ha", "ht", "hg", "dp"):
            self.sems[s] = es.enter_context(nc.semaphore(s))
            self.cnt[s] = 0

    def wait(self, eng, sem, val):
        if val <= 0:
            return
        h = self.sems[sem]
        self.ops[eng].append(lambda e, h=h, v=val: e.wait_ge(h, v))
        self.last_has_inc[eng] = False

    def inc_last(self, eng, sem, n=1):
        assert not self.last_has_inc.get(eng), f"double inc on {eng}"
        h = self.sems[sem]
        self.cnt[sem] += n
        f = self.ops[eng].pop()
        self.ops[eng].append(lambda e, f=f, h=h, n=n: f(e).then_inc(h, n))
        self.last_has_inc[eng] = True
        return self.cnt[sem]

    def flush_pending(self):
        fn = getattr(self, "pending", None)
        self.pending = None
        if fn is not None:
            fn()

    def dbg_barrier(self):
        for eng in ("tensor", "scalar", "vector", "gpsimd", "sync"):
            self.wait(eng, "dgd", self.cnt["dgd"])

    def hop(self, eng):
        sem = {"vector": "hv", "scalar": "ha", "tensor": "ht", "gpsimd": "hg"}[eng]
        h = self.sems[sem]
        self.cnt[sem] += 1
        v = self.cnt[sem]
        assert not self.last_has_inc.get(eng), f"double inc (hop) on {eng}"
        f = self.ops[eng].pop()
        self.ops[eng].append(lambda e, f=f, h=h: f(e).then_inc(h, 1))
        self.ops[eng].append(lambda e, h=h, v=v: e.wait_ge(h, v))
        self.last_has_inc[eng] = False

    def do(self, eng, fn, inc=None, inc_n=1):
        if inc is None:
            self.ops[eng].append(fn)
            self.last_has_inc[eng] = False
            return None
        h = self.sems[inc]
        self.cnt[inc] += inc_n
        self.ops[eng].append(lambda e, fn=fn, h=h, n=inc_n: fn(e).then_inc(h, n))
        self.last_has_inc[eng] = True
        return self.cnt[inc]


def _view(tensor_ap, off_bytes, shape, dt):
    es_ = mybir.dt.size(dt)
    nel = int(np.prod(shape[1:]))
    assert off_bytes % 2 == 0
    v = tensor_ap[0:shape[0], off_bytes // 2: off_bytes // 2 + (nel * es_) // 2]
    if dt != BF16:
        v = v.bitcast(dt)
    if len(shape) == 2:
        return v
    if len(shape) == 3:
        return v.rearrange("p (a b) -> p a b", a=shape[1])
    if len(shape) == 4:
        return v.rearrange("p (a b c) -> p a b c", a=shape[1], b=shape[2])
    raise ValueError(shape)


def build_program(debug=0):
    nc = bass.Bass("TRN2", target_bir_lowering=False)
    es = contextlib.ExitStack()

    dd = {}
    def din(name, shape, dt=F32):
        dd[name] = nc.dram_tensor(name, shape, dt, kind="ExternalInput").ap()
    din("xT", [D, S])
    din("c_col", [128, 8])
    din("wmod", [D, 6 * D])
    din("wq_p", [NP, D, 128]); din("wk_p", [NP, D, 128]); din("wv_p", [NP, D, 128])
    din("woT", [D, D])
    din("w1T", [D, 4 * D])
    din("w2T", [4 * D, D])
    din("relg", [128, 2049])
    din("mzp", [NP, 128, 65])
    din("bq_c", [128, 8]); din("bk_c", [128, 8]); din("bv_c", [128, 8]); din("bo_c", [128, 8])
    din("b1_c", [128, 32]); din("b2_c", [128, 8])
    din("galpha", [1, 1])
    din("ident", [128, 128], BF16)
    din("ones_r", [1, 128], BF16)
    din("ones_rf", [1, 128], F32)
    din("onesc", [128, 1], BF16)
    din("onescf", [128, 1], F32)
    din("blkd", [128, 2], BF16)
    din("o642", [64, 2], BF16)
    din("i2", [2, 2], BF16)
    din("sel_r", [4, 512], BF16)
    din("sel_c", [4, 4], BF16)
    out_cm = nc.dram_tensor("out_cm", [D, SQ], F32, kind="ExternalOutput").ap()
    attn_sp = nc.dram_tensor("attn_sp", [NCH, 128, SQ], BF16, kind="Internal").ap()

    dbg = {}
    if debug:
        def dout(name, shape, dt):
            dbg[name] = nc.dram_tensor(name, shape, dt, kind="ExternalOutput").ap()
        dout("d_mod", [1, 6 * D], BF16)
        dout("d_h1", [128, NCH, S], BF16)
        dout("d_qh", [128, SQ], BF16)
        dout("d_k", [128, S], BF16)
        dout("d_vp", [128, NKB, 2, 65], BF16)
        dout("d_pg", [128, CP], BF16)
        dout("d_rg", [128, 2304], BF16)
        dout("d_negb", [128, 1], F32)
        dout("d_edge", [4, SQ], BF16)
        dout("d_o65", [65, 2, 512], F32)
        dout("d_comb", [128, NP, SQ], BF16)
        dout("d_attn", [NCH, 128, SQ], BF16)
        dout("d_s1", [128, NCH, 4 * D], BF16)
        dout("d_h3", [128, NCH, FSL], BF16)

    sb = {}
    def sbuf(name, shape, dt):
        sb[name] = es.enter_context(nc.sbuf_tensor("s_" + name, shape, dt))
    sbuf("ident", [128, 128], BF16)
    sbuf("ones_r", [1, 128], BF16)
    sbuf("onesc", [128, 1], BF16)
    sbuf("onescf", [128, 1], F32)
    sbuf("blkd", [128, 2], BF16)
    sbuf("o642", [64, 2], BF16)
    sbuf("i2", [2, 2], BF16)
    sbuf("sel_r", [4, 512], BF16)
    sbuf("sel_c", [4, 4], BF16)
    sbuf("c_col", [128, 8], F32)
    sbuf("silu_c", [128, 8], F32)
    sbuf("ones_rf", [1, 128], F32)
    for n in ("sa", "ca1", "ga", "sbv", "cb1", "gb", "bq_h", "bk_c", "bv_c", "bo_c", "b2_c"):
        sbuf(n, [128, 8], F32)
    sbuf("b1_c", [128, 32], F32)
    sbuf("a1c", [128, 32], F32)
    sbuf("a2c", [128, 8], F32)
    sbuf("gate", [128, 1], F32); sbuf("gate1m", [128, 1], F32)
    sbuf("relg", [128, 2049], BF16)
    sbuf("lnrel", [2, 1], F32)
    sbuf("tiny", [2, 10], F32)
    sbuf("tinyb", [2, 2], BF16)
    sbuf("rowb", [1, 512], BF16)
    sbuf("rowm", [1, 512], F32); sbuf("rowr", [1, 512], F32); sbuf("rowmr", [1, 512], F32)
    sbuf("tmpa", [128, 512], F32)
    sbuf("sqs", [128, 512], BF16)
    sbuf("xslab", [128, 2, NCH, 512], F32)
    sbuf("o65", [65, 2, 512], F32)
    sbuf("ct", [64, 2, 512], BF16)
    sbuf("negb", [128, 1], F32)
    sbuf("edges4", [4, SQ], BF16)
    sbuf("ecols", [128, 64], BF16)
    sbuf("g65s", [65, 2, 512], BF16)
    sbuf("rcol", [128, 16], BF16)
    sbuf("rcolf", [128, 16], F32)
    sbuf("eW", [128, 4], BF16)
    sbuf("mz", [128, 65], BF16)

    ARENA_B = 147584
    sbuf("arena", [128, ARENA_B // 2], BF16)
    ar = sb["arena"][:]

    V = {}
    V["h1"] = _view(ar, 0, [128, NCH, S], BF16)
    V["s1"] = _view(ar, 0, [128, NCH, 4 * D], BF16)
    B0 = 65536
    V["mrow"] = _view(ar, B0, [1, 6 * D], F32)
    V["wmod_ch"] = _view(ar, B0 + 24576, [128, 2, 3 * D], F32)
    V["xbf1"] = _view(ar, B0 + 24576, [128, NCH, 512], BF16)
    V["xsq1"] = _view(ar, B0 + 32768, [128, NCH, 512], BF16)
    V["kk"] = _view(ar, B0, [128, S], BF16)
    V["qh"] = _view(ar, B0 + 8192, [128, SQ], BF16)
    V["vp"] = _view(ar, B0 + 12288, [128, NKB, 2, 65], BF16)
    V["vf"] = _view(ar, B0 + 20608, [128, 1024], BF16)
    V["sigB"] = _view(ar, B0 + 20608, [128, 512], BF16)
    V["pgs"] = [_view(ar, B0 + 22656, [128, CP], BF16),
                _view(ar, B0 + 22656 + 2 * CP, [128, CP], BF16),
                _view(ar, B0 + 39040, [128, CP], BF16)]
    V["ee"] = _view(ar, B0 + 32896, [128, 2, 1024], BF16)
    V["sig"] = _view(ar, B0 + 36992, [128, 512], BF16)
    V["sigm"] = _view(ar, B0 + 38016, [128, 512], BF16)
    V["wqkv_wq"] = _view(ar, B0 + 39040, [128, NCH, 128], BF16)
    V["wqkv_wk"] = _view(ar, B0 + 41088, [128, NCH, 128], BF16)
    V["wqkv_wv"] = _view(ar, B0 + 43136, [128, NCH, 128], BF16)
    V["rg"] = _view(ar, B0 + 45184, [128, 18432], BF16)
    V["wo"] = _view(ar, B0, [128, NCH, D], BF16)
    V["stag"] = _view(ar, B0 + 16384, [128, 2, SQ], BF16)
    V["s2"] = _view(ar, B0, [128, 32, D], BF16)
    xs_ap = sb["xslab"][:].rearrange("p a b c -> p (a b c)").bitcast(BF16)
    V["comb"] = _view(xs_ap, 0, [128, NP, SQ], BF16)
    V["fx"] = _view(xs_ap, 0, [128, NCH, FSL], F32)
    V["x2sl"] = _view(xs_ap, 8192, [128, NCH, FSL], F32)
    V["outsl"] = _view(xs_ap, 16384, [128, 4, FSL], F32)
    V["fat"] = _view(xs_ap, 20480, [128, NCH, FSL], BF16)
    V["xsq2"] = _view(xs_ap, 24576, [128, NCH, FSL], BF16)
    V["h3sl"] = _view(xs_ap, 28672, [128, NCH, FSL], BF16)
    assert 28672 + 4096 <= 32768
    V["wsl"] = _view(ar, 131072, [128, 2, 4 * D], BF16)
    V["h2sl"] = _view(ar, 131072, [128, 32, FSL], BF16)
    assert 131072 + 16384 <= ARENA_B

    pA = es.enter_context(nc.psum_tensor("pA", [128, 2048], F32))
    pB = es.enter_context(nc.psum_tensor("pB", [128, 2048], F32))

    p = Prog(nc, es)
    gen(p, dd, sb, V, pA, pB, dbg, out_cm, attn_sp, debug)

    with nc.Block() as block:
        @block.tensor
        def _(e):
            for f in p.ops["tensor"]:
                f(e)
        @block.scalar
        def _(e):
            for f in p.ops["scalar"]:
                f(e)
        @block.vector
        def _(e):
            for f in p.ops["vector"]:
                f(e)
        @block.sync
        def _(e):
            for f in p.ops["sync"]:
                f(e)
        @block.gpsimd
        def _(e):
            for f in p.ops["gpsimd"]:
                f(e)
    es.close()
    return nc


def gen(p, dd, sb, V, pA, pB, dbg, out_cm, attn_sp, debug):
    for name, dst in (("ident", "ident"), ("ones_r", "ones_r"), ("ones_rf", "ones_rf"),
                      ("onesc", "onesc"), ("onescf", "onescf"),
                      ("blkd", "blkd"), ("o642", "o642"), ("i2", "i2"),
                      ("sel_r", "sel_r"), ("sel_c", "sel_c"),
                      ("c_col", "c_col"), ("bq_c", "bq_h"), ("bk_c", "bk_c"),
                      ("bv_c", "bv_c"), ("bo_c", "bo_c"), ("b1_c", "b1_c"),
                      ("b2_c", "b2_c")):
        p.do("sync", lambda e, d=dd[name], s=sb[dst]: e.dma_start(out=s[:], in_=d),
             inc="ds", inc_n=16)
    p.do("sync", lambda e: e.dma_start(out=sb["gate"][0:1, :], in_=dd["galpha"]),
         inc="ds", inc_n=16)
    ds_consts = p.cnt["ds"]

    dg_relg = p.do("gpsimd", lambda e: e.dma_start(out=sb["relg"][:], in_=dd["relg"]),
                   inc="dgm", inc_n=16)
    wmod_idx = [(h, c) for h in range(2) for c in range(NCH)]
    wmod_loads = {}
    def emit_wmod_load(i, wait_pe=None):
        half, c = wmod_idx[i]
        if wait_pe is not None:
            p.wait("gpsimd", "pe", wait_pe)
        wmod_loads[(half, c)] = ("dgx0" if c % 2 == 0 else "dgx1",
            p.do("gpsimd", lambda e, half=half, c=c: e.dma_start(
                out=V["wmod_ch"][:, c % 2, :],
                in_=dd["wmod"][c * 128:(c + 1) * 128, half * 3 * D:(half + 1) * 3 * D]),
                inc="dgx0" if c % 2 == 0 else "dgx1", inc_n=16))
    emit_wmod_load(0)
    emit_wmod_load(1)
    x_loads = {}
    def emit_x_load(sl, wait_dv=None):
        ap = dd["xT"][:, sl * 512:(sl + 1) * 512].rearrange("(c p) t -> p c t", p=128)
        if wait_dv is not None:
            p.wait("gpsimd", "dv", wait_dv)
        x_loads[sl] = ("dgy0" if sl % 2 == 0 else "dgy1",
            p.do("gpsimd", lambda e, ap=ap, sl=sl: e.dma_start(
                out=sb["xslab"][:, sl % 2, :, :], in_=ap),
                inc="dgy0" if sl % 2 == 0 else "dgy1", inc_n=16))
    emit_x_load(0)
    emit_x_load(1)

    p.wait("scalar", "ds", ds_consts)
    p.do("scalar", lambda e: e.activation(out=sb["tmpa"][:, 0:8], in_=sb["c_col"][:],
                                          func=AF.Sigmoid))
    p.do("scalar", lambda e: e.activation(out=sb["tmpa"][0:1, 8:9], in_=sb["gate"][0:1, :],
                                          func=AF.Sigmoid), inc="ac")
    p.wait("vector", "ac", p.cnt["ac"])
    dv_silu = p.do("vector", lambda e: e.tensor_tensor(
        out=sb["silu_c"][:], in0=sb["c_col"][:], in1=sb["tmpa"][:, 0:8], op=OP.mult), inc="dv")
    p.wait("tensor", "dv", dv_silu)
    for half in range(2):
        if half == 1:
            p.wait("tensor", "dv", p.cnt["dv"])
        for c in range(NCH):
            wsem, wval = wmod_loads[(half, c)]
            p.wait("tensor", wsem, wval)
            for npos in range(6):
                ps = pA[0:1, npos * 512:(npos + 1) * 512] if npos < 4 else \
                     pB[0:1, (npos - 4) * 512:(npos - 3) * 512]
                p.do("tensor", lambda e, ps=ps, c=c, npos=npos: e.matmul(
                    ps, sb["silu_c"][:, c:c + 1],
                    V["wmod_ch"][:, c % 2, npos * 512:(npos + 1) * 512],
                    start=(c == 0), stop=(c == NCH - 1)),
                    inc="pe" if npos == 5 else None)
            gi = half * NCH + c
            if gi + 2 < 16:
                emit_wmod_load(gi + 2, wait_pe=p.cnt["pe"])
        p.wait("vector", "pe", p.cnt["pe"])
        for npos in range(6):
            ps = pA[0:1, npos * 512:(npos + 1) * 512] if npos < 4 else \
                 pB[0:1, (npos - 4) * 512:(npos - 3) * 512]
            p.do("vector", lambda e, ps=ps, half=half, npos=npos: e.tensor_copy(
                out=V["mrow"][0:1, half * 3 * D + npos * 512: half * 3 * D + (npos + 1) * 512],
                in_=ps), inc="dv")
    dv_mod = p.cnt["dv"]

    p.wait("scalar", "dv", dv_mod)
    for idx in (2, 5):
        sl = slice(idx * D, (idx + 1) * D)
        for ph in range(2):
            p.wait("scalar", "dv", p.cnt["dv"])
            p.do("scalar", lambda e, sl=sl, ph=ph: e.activation(
                out=sb["rowr"][0:1, :], in_=V["mrow"][0:1, sl][:, ph * 512:(ph + 1) * 512],
                func=AF.Abs), inc="ac")
            p.wait("vector", "ac", p.cnt["ac"])
            p.do("vector", lambda e: e.tensor_scalar(
                out=sb["rowr"][0:1, :], in0=sb["rowr"][0:1, :],
                scalar1=0.25, scalar2=None, op0=OP.max))
            p.do("vector", lambda e, sl=sl, ph=ph: e.tensor_scalar(
                out=sb["rowm"][0:1, :], in0=V["mrow"][0:1, sl][:, ph * 512:(ph + 1) * 512],
                scalar1=0.0, scalar2=2.0, op0=OP.is_ge, op1=OP.mult))
            p.hop("vector")
            p.do("vector", lambda e, sl=sl, ph=ph: e.scalar_tensor_tensor(
                out=V["mrow"][0:1, sl][:, ph * 512:(ph + 1) * 512],
                in0=sb["rowm"][0:1, :], scalar=-1.0,
                in1=sb["rowr"][0:1, :], op0=OP.add, op1=OP.mult), inc="dv")
    dv_clamp = p.cnt["dv"]

    p.wait("tensor", "dv", dv_clamp)
    names = ("sa", "ca1", "ga", "sbv", "cb1", "gb")
    for vi, vn in enumerate(names):
        if vi > 0:
            p.wait("tensor", "dv", p.cnt["dv"])
        for c in range(NCH):
            p.do("tensor", lambda e, vi=vi, c=c: e.matmul(
                pB[:, 1024 + c:1025 + c],
                V["mrow"][0:1, vi * D + c * 128: vi * D + (c + 1) * 128],
                sb["ones_rf"][0:1, 0:1], start=True, stop=True),
                inc="pe" if c == NCH - 1 else None)
        p.wait("vector", "pe", p.cnt["pe"])
        if vn in ("ca1", "cb1"):
            p.do("vector", lambda e, vn=vn: e.tensor_scalar(
                out=sb[vn][:], in0=pB[:, 1024:1032], scalar1=1.0, scalar2=None,
                op0=OP.add), inc="dv")
        else:
            p.do("vector", lambda e, vn=vn: e.tensor_copy(
                out=sb[vn][:], in_=pB[:, 1024:1032]), inc="dv")
    p.do("vector", lambda e: e.tensor_copy(out=sb["tinyb"][0:1, 0:1], in_=sb["tmpa"][0:1, 8:9]),
         inc="dv")
    p.wait("tensor", "dv", p.cnt["dv"])
    p.do("tensor", lambda e: e.matmul(pB[:, 1024:1025], sb["ones_r"][:],
                                      sb["tinyb"][0:1, 0:1], start=True, stop=True), inc="pe")
    p.wait("vector", "pe", p.cnt["pe"])
    p.do("vector", lambda e: e.tensor_copy(out=sb["gate"][:], in_=pB[:, 1024:1025]))
    p.do("vector", lambda e: e.tensor_scalar(
        out=sb["gate1m"][:], in0=pB[:, 1024:1025], scalar1=-1.0, scalar2=1.0,
        op0=OP.mult, op1=OP.add))
    p.do("vector", lambda e: e.tensor_scalar(
        out=sb["bq_h"][:], in0=sb["bq_h"][:], scalar1=0.125, scalar2=None, op0=OP.mult),
        inc="dv")
    dv_modcols = p.cnt["dv"]
    if debug == 1:
        p.wait("gpsimd", "dv", dv_clamp)
        p.do("gpsimd", lambda e: e.dma_start(out=dbg["d_mod"], in_=V["mrow"]),
             inc="dgd", inc_n=16)
        p.dbg_barrier()

    for sl in range(8):
        xs = sb["xslab"][:, sl % 2, :, :]
        if sl % 2 == 0:
            rsum, rsq = pB[0:1, 1024:1536], pB[0:1, 1536:2048]
            rrep, rmrep = pB[:, 0:512], pB[:, 512:1024]
        else:
            rsum, rsq = pA[0:1, 0:512], pA[0:1, 512:1024]
            rrep, rmrep = pA[:, 1024:1536], pA[:, 1536:2048]
        xsem, xval = x_loads[sl]
        p.wait("vector", xsem, xval)
        if sl == 0:
            p.wait("vector", "dv", dv_modcols)
        p.do("vector", lambda e, xs=xs: e.tensor_copy(out=V["xbf1"], in_=xs))
        p.hop("vector")
        p.do("vector", lambda e: e.tensor_tensor(
            out=V["xsq1"], in0=V["xbf1"], in1=V["xbf1"], op=OP.mult), inc="dv")
        p.wait("tensor", "dv", p.cnt["dv"])
        for c in range(NCH):
            p.do("tensor", lambda e, c=c, rsum=rsum: e.matmul(
                rsum, sb["onesc"][:], V["xbf1"][:, c, :],
                start=(c == 0), stop=(c == NCH - 1)))
        for c in range(NCH):
            p.do("tensor", lambda e, c=c, rsq=rsq: e.matmul(
                rsq, sb["onesc"][:], V["xsq1"][:, c, :],
                start=(c == 0), stop=(c == NCH - 1)),
                inc="pe" if c == NCH - 1 else None)
        p.wait("vector", "pe", p.cnt["pe"])
        emit_ln_rows(p, sb, rsum, rsq, 512)
        p.wait("tensor", "dv", p.cnt["dv"])
        p.do("tensor", lambda e, rrep=rrep: e.matmul(
            rrep, sb["ones_r"][:], sb["rowb"][:], start=True, stop=True), inc="pe")
        p.wait("vector", "pe", p.cnt["pe"])
        p.do("vector", lambda e: e.tensor_copy(out=sb["rowb"][:], in_=sb["rowmr"][:]), inc="dv")
        p.wait("tensor", "dv", p.cnt["dv"])
        p.do("tensor", lambda e, rmrep=rmrep: e.matmul(
            rmrep, sb["ones_r"][:], sb["rowb"][:], start=True, stop=True), inc="pe")
        p.wait("vector", "pe", p.cnt["pe"])
        for c in range(NCH):
            p.do("vector", lambda e, xs=xs, c=c, rrep=rrep: e.tensor_tensor(
                out=sb["tmpa"][:], in0=xs[:, c, :], in1=rrep, op=OP.mult))
            p.hop("vector")
            p.do("vector", lambda e, c=c, rmrep=rmrep: e.tensor_tensor(
                out=sb["tmpa"][:], in0=sb["tmpa"][:], in1=rmrep, op=OP.subtract))
            p.hop("vector")
            p.do("vector", lambda e, sl=sl, c=c: e.tensor_scalar(
                out=V["h1"][:, c, sl * 512:(sl + 1) * 512], in0=sb["tmpa"][:],
                scalar1=sb["ca1"][:, c:c + 1], scalar2=sb["sa"][:, c:c + 1],
                op0=OP.mult, op1=OP.add), inc="dv" if c == NCH - 1 else None)
            if c < NCH - 1:
                p.hop("vector")
        if sl + 2 < 8:
            emit_x_load(sl + 2, wait_dv=p.cnt["dv"])
    dv_h1 = p.cnt["dv"]
    if debug == 1:
        p.wait("gpsimd", "dv", dv_h1)
        p.do("gpsimd", lambda e: e.dma_start(out=dbg["d_h1"], in_=V["h1"]),
             inc="dgd", inc_n=16)
        p.dbg_barrier()

    p.wait("scalar", "dgm", dg_relg)
    p.wait("tensor", "dv", dv_h1)
    for ch in range(5):
        cs = slice(ch * 512, (ch + 1) * 512) if ch < 4 else slice(2048, 2049)
        n = 512 if ch < 4 else 1
        p.wait("scalar", "pe", p.cnt["pe"])
        p.do("scalar", lambda e, cs=cs, n=n: e.activation(
            out=sb["sqs"][0:64, 0:n], in_=sb["relg"][0:64, cs], func=AF.Square), inc="ac")
        p.wait("tensor", "ac", p.cnt["ac"])
        if ch > 0:
            p.wait("tensor", "dv", p.cnt["dv"])
        p.do("tensor", lambda e, n=n: e.matmul(
            pB[0:2, 1024:1024 + n], sb["o642"][:], sb["sqs"][0:64, 0:n],
            start=True, stop=True), inc="pe")
        p.wait("vector", "pe", p.cnt["pe"])
        if ch == 0:
            p.do("vector", lambda e: e.reduce_max(
                out=sb["lnrel"][:], in_=pB[0:2, 1024:1536], axis=AX.X), inc="dv")
        else:
            p.do("vector", lambda e, n=n: e.reduce_max(
                out=sb["tiny"][:, 9:10], in_=pB[0:2, 1024:1024 + n], axis=AX.X))
            p.hop("vector")
            p.do("vector", lambda e: e.tensor_tensor(
                out=sb["lnrel"][:], in0=sb["lnrel"][:], in1=sb["tiny"][:, 9:10],
                op=OP.max), inc="dv")
    p.wait("scalar", "dv", p.cnt["dv"])
    p.do("scalar", lambda e: e.activation(out=sb["lnrel"][:], in_=sb["lnrel"][:],
                                          func=AF.Ln), inc="ac")
    ac_lnrel = p.cnt["ac"]
    p.do("vector", lambda e: e.memset(sb["eW"][:], 0.0))
    p.hop("vector")
    p.do("vector", lambda e: e.tensor_copy(out=sb["eW"][0:64, 0:1], in_=sb["relg"][0:64, 0:1]))
    p.do("vector", lambda e: e.tensor_copy(out=sb["eW"][0:64, 1:2],
                                           in_=sb["relg"][0:64, 2048:2049]))
    p.do("vector", lambda e: e.tensor_copy(out=sb["eW"][64:128, 2:3],
                                           in_=sb["relg"][64:128, 0:1]))
    p.do("vector", lambda e: e.tensor_copy(out=sb["eW"][64:128, 3:4],
                                           in_=sb["relg"][64:128, 2048:2049]), inc="dv")
    dv_eW = p.cnt["dv"]

    p.pending = None
    p.pguse = 0
    p.guses = {}
    p.dp_hist = []
    p.pmm_rot = 0
    p.sdone_last = 0
    p.g65pe = 0
    p.o65ac = 0
    p.pe_vlast = p.cnt["pe"]
    p.t8_dv = 0
    kb_ranges = [band_range(qb) for qb in range(16)]
    for hp in range(NP):
        gen_pair(p, dd, sb, V, pA, pB, hp, kb_ranges, dbg, debug, ac_lnrel, dv_eW)

    p.flush_pending()
    p.wait("gpsimd", "pe", p.cnt["pe"])
    p.wait("gpsimd", "dv", p.cnt["dv"])
    p.wait("gpsimd", "ac", p.cnt["ac"])
    p.wait("gpsimd", "dsc", p.cnt["dsc"])
    dg_wo = p.do("gpsimd", lambda e: e.dma_start(
        out=V["wo"], in_=dd["woT"].rearrange("(c p) m -> p c m", p=128)), inc="dgo", inc_n=16)
    p.wait("tensor", "dgo", dg_wo)
    p.wait("tensor", "dv", p.cnt["dv"])
    p.wait("tensor", "ac", p.cnt["ac"])
    p.wait("tensor", "dp", p.cnt["dp"])
    p.wait("tensor", "dsc", p.cnt["dsc"])
    stag_dma = {}
    for blk in range(NCH):
        if blk > 0:
            p.wait("tensor", "dv", p.cnt["dv"])
        if blk >= 2:
            ssem, sval = stag_dma[blk - 2]
            p.wait("vector", ssem, sval)
        for c in range(NCH):
            for nch in range(4):
                ps = (pA if nch < 2 else pB)[:, (nch % 2) * 1024:(nch % 2) * 1024 + 512]
                p.do("tensor", lambda e, blk=blk, c=c, nch=nch, ps=ps: e.matmul(
                    ps, V["wo"][:, c, blk * 128:(blk + 1) * 128],
                    V["comb"][:, c, nch * 512:(nch + 1) * 512],
                    start=(c == 0), stop=(c == NCH - 1)),
                    inc="pe" if (c == NCH - 1 and nch == 3) else None)
        p.wait("vector", "pe", p.cnt["pe"])
        for nch in range(4):
            ps = (pA if nch < 2 else pB)[:, (nch % 2) * 1024:(nch % 2) * 1024 + 512]
            p.do("vector", lambda e, blk=blk, nch=nch, ps=ps: e.tensor_scalar(
                out=V["stag"][:, blk % 2, nch * 512:(nch + 1) * 512], in0=ps, scalar1=1.0,
                scalar2=sb["bo_c"][:, blk:blk + 1], op0=OP.mult, op1=OP.add),
                inc="dv" if nch == 3 else None)
        p.wait("sync", "dv", p.cnt["dv"])
        stag_dma[blk] = ("dss0" if blk % 2 == 0 else "dss1",
            p.do("sync", lambda e, blk=blk: e.dma_start(
                out=attn_sp[blk], in_=V["stag"][:, blk % 2, :]),
                inc="dss0" if blk % 2 == 0 else "dss1", inc_n=16))
    ds_attn = (p.cnt["dss0"], p.cnt["dss1"])
    if debug:
        p.wait("gpsimd", "dss0", ds_attn[0])
        p.wait("gpsimd", "dss1", ds_attn[1])
        p.do("gpsimd", lambda e: e.dma_start(out=dbg["d_attn"], in_=attn_sp), inc="dgd", inc_n=16)
        p.do("gpsimd", lambda e: e.dma_start(out=dbg["d_comb"], in_=V["comb"]), inc="dgd", inc_n=16)
        p.dbg_barrier()

    gen_ffn(p, dd, sb, V, pA, pB, out_cm, attn_sp, ds_attn, dbg, debug)

    p.wait("sync", "dso", p.cnt["dso"])
    p.wait("sync", "dsc", p.cnt["dsc"])
    p.wait("sync", "dsg", p.cnt["dsg"])
    p.wait("sync", "dss0", p.cnt["dss0"])
    p.wait("sync", "dss1", p.cnt["dss1"])
    p.wait("sync", "ds", p.cnt["ds"])
    for s in ("dgm", "dgz", "dgw", "dgo", "dgx0", "dgx1", "dgy0", "dgy1", "dgf", "dgd",
                  "hv", "ha", "ht", "hg", "dp", "dsh"):
        p.wait("gpsimd", s, p.cnt[s])


def emit_ln_rows(p, sb, rsum, rsq, n):
    p.do("vector", lambda e, rsum=rsum: e.tensor_scalar(
        out=sb["rowm"][0:1, 0:n], in0=rsum, scalar1=1.0 / D, scalar2=None, op0=OP.mult))
    p.hop("vector")
    p.do("vector", lambda e: e.tensor_tensor(
        out=sb["rowr"][0:1, 0:n], in0=sb["rowm"][0:1, 0:n], in1=sb["rowm"][0:1, 0:n],
        op=OP.mult))
    p.hop("vector")
    p.do("vector", lambda e: e.tensor_scalar(
        out=sb["rowr"][0:1, 0:n], in0=sb["rowr"][0:1, 0:n], scalar1=-1e-5, scalar2=None,
        op0=OP.add))
    p.hop("vector")
    p.do("vector", lambda e, rsq=rsq: e.scalar_tensor_tensor(
        out=sb["rowr"][0:1, 0:n], in0=rsq, scalar=1.0 / D, in1=sb["rowr"][0:1, 0:n],
        op0=OP.mult, op1=OP.subtract), inc="dv")
    p.wait("scalar", "dv", p.cnt["dv"])
    p.do("scalar", lambda e: e.activation(out=sb["rowr"][0:1, 0:n],
                                          in_=sb["rowr"][0:1, 0:n], func=AF.Sqrt), inc="ac")
    p.wait("vector", "ac", p.cnt["ac"])
    p.do("vector", lambda e: e.reciprocal(out=sb["rowr"][0:1, 0:n], in_=sb["rowr"][0:1, 0:n]))
    p.hop("vector")
    p.do("vector", lambda e: e.tensor_tensor(
        out=sb["rowmr"][0:1, 0:n], in0=sb["rowm"][0:1, 0:n], in1=sb["rowr"][0:1, 0:n],
        op=OP.mult))
    p.do("vector", lambda e: e.tensor_copy(out=sb["rowb"][0:1, 0:n],
                                           in_=sb["rowr"][0:1, 0:n]), inc="dv")


def gen_pair(p, dd, sb, V, pA, pB, hp, kb_ranges, dbg, debug, ac_lnrel, dv_eW):
    p.wait("gpsimd", "pe", p.pe_vlast)
    p.wait("gpsimd", "dp", p.cnt["dp"])
    p.wait("gpsimd", "dsg", p.cnt["dsg"])
    p.wait("gpsimd", "dsh", p.cnt["dsh"])
    lw = []
    for wn, dn in (("wq", "wq_p"), ("wk", "wk_p"), ("wv", "wv_p")):
        lw.append(p.do("gpsimd", lambda e, wn=wn, dn=dn: e.dma_start(
            out=V["wqkv_" + wn],
            in_=dd[dn][hp].rearrange("(c p) m -> p c m", p=128)), inc="dgw", inc_n=16))
    p.wait("gpsimd", "pe", p.g65pe)
    dg_mz = p.do("gpsimd", lambda e: e.dma_start(out=sb["mz"][:], in_=dd["mzp"][hp]),
                 inc="dgz", inc_n=16)
    p.flush_pending()
    p.slot2_fresh = True
    p.pair_pmm_first = True

    p.wait("tensor", "dgw", lw[2])
    p.wait("tensor", "ac", p.cnt["ac"])
    p.wait("tensor", "dv", p.cnt["dv"])
    p.wait("tensor", "dp", p.cnt["dp"])
    for c in range(NCH):
        for nch in range(4):
            p.do("tensor", lambda e, c=c, nch=nch: e.matmul(
                pA[:, nch * 512:(nch + 1) * 512], V["wqkv_wq"][:, c, :],
                V["h1"][:, c, nch * 512:(nch + 1) * 512],
                start=(c == 0), stop=(c == NCH - 1)),
                inc="pe" if (c == NCH - 1 and nch == 3) else None)
    p.wait("vector", "pe", p.cnt["pe"])
    p.do("vector", lambda e: e.tensor_scalar(
        out=V["qh"][:, :], in0=pA[:, 0:2048], scalar1=0.125,
        scalar2=sb["bq_h"][:, hp:hp + 1], op0=OP.mult, op1=OP.add), inc="dv")
    dv_q = p.cnt["dv"]
    for half in range(2):
        p.wait("tensor", "dv", p.cnt["dv"])
        for c in range(NCH):
            for nch in range(4):
                t0 = half * 2048 + nch * 512
                p.do("tensor", lambda e, c=c, nch=nch, t0=t0: e.matmul(
                    pA[:, nch * 512:(nch + 1) * 512], V["wqkv_wk"][:, c, :],
                    V["h1"][:, c, t0:t0 + 512],
                    start=(c == 0), stop=(c == NCH - 1)),
                    inc="pe" if (c == NCH - 1 and nch == 3) else None)
        p.wait("vector", "pe", p.cnt["pe"])
        p.do("vector", lambda e, half=half: e.tensor_scalar(
            out=V["kk"][:, half * 2048:(half + 1) * 2048], in0=pA[:, 0:2048], scalar1=1.0,
            scalar2=sb["bk_c"][:, hp:hp + 1], op0=OP.mult, op1=OP.add), inc="dv")
    for vch in range(4):
        p.wait("tensor", "dv", p.cnt["dv"])
        for c in range(NCH):
            for nn in range(2):
                t0 = vch * 1024 + nn * 512
                p.do("tensor", lambda e, c=c, nn=nn, t0=t0: e.matmul(
                    pB[:, nn * 512:(nn + 1) * 512], V["wqkv_wv"][:, c, :],
                    V["h1"][:, c, t0:t0 + 512],
                    start=(c == 0), stop=(c == NCH - 1)),
                    inc="pe" if (c == NCH - 1 and nn == 1) else None)
        p.wait("vector", "pe", p.cnt["pe"])
        p.do("vector", lambda e: e.tensor_scalar(
            out=V["vf"][:], in0=pB[:, 0:1024], scalar1=1.0,
            scalar2=sb["bv_c"][:, hp:hp + 1], op0=OP.mult, op1=OP.add), inc="dv")
        p.wait("tensor", "dv", p.cnt["dv"])
        pBb = pB[:, 1024:1536].bitcast(BF16)
        for t in range(8):
            p.do("tensor", lambda e, t=t, pBb=pBb: e.transpose(
                pBb[:, t * 128:(t + 1) * 128], V["vf"][:, t * 128:(t + 1) * 128],
                sb["ident"][:]), inc="pe" if t == 7 else None)
        p.wait("vector", "pe", p.cnt["pe"])
        for t in range(8):
            kb = vch * 8 + t
            p.do("vector", lambda e, t=t, kb=kb, pBb=pBb: e.tensor_copy(
                out=V["vp"][:, kb, 0, 0:64],
                in_=pBb[:, t * 128:t * 128 + 64]))
            p.do("vector", lambda e, t=t, kb=kb, pBb=pBb: e.tensor_copy(
                out=V["vp"][:, kb, 1, 0:64],
                in_=pBb[:, t * 128 + 64:t * 128 + 128]),
                inc="dv" if t == 7 else None)
        p.wait("tensor", "dv", p.cnt["dv"])
    p.do("vector", lambda e: e.memset(V["vp"][:, :, :, 64:65], 1.0), inc="dv")
    p.pe_vlast = p.cnt["pe"]


    for si, (srcv, n) in enumerate(((V["qh"], SQ), (V["kk"], S))):
        for ch in range(n // 512):
            p.do("vector", lambda e, srcv=srcv, ch=ch: e.tensor_tensor(
                out=sb["sqs"][:], in0=srcv[:, ch * 512:(ch + 1) * 512],
                in1=srcv[:, ch * 512:(ch + 1) * 512], op=OP.mult), inc="dv")
            p.wait("tensor", "dv", p.cnt["dv"])
            p.do("tensor", lambda e: e.matmul(
                pB[0:2, 1024:1536], sb["blkd"][:], sb["sqs"][:],
                start=True, stop=True), inc="pe")
            p.wait("vector", "pe", p.cnt["pe"])
            dst = sb["tiny"][:, si:si + 1]
            if ch == 0:
                p.do("vector", lambda e, dst=dst: e.reduce_max(
                    out=dst, in_=pB[0:2, 1024:1536], axis=AX.X), inc="dv")
            else:
                p.do("vector", lambda e: e.reduce_max(
                    out=sb["tiny"][:, 9:10], in_=pB[0:2, 1024:1536], axis=AX.X))
                p.hop("vector")
                p.do("vector", lambda e, dst=dst: e.tensor_tensor(
                    out=dst, in0=dst, in1=sb["tiny"][:, 9:10], op=OP.max), inc="dv")
    p.wait("scalar", "dv", p.cnt["dv"])
    p.wait("scalar", "ac", ac_lnrel)
    p.do("scalar", lambda e: e.activation(out=sb["tiny"][:, 2:3], in_=sb["tiny"][:, 0:1],
                                          func=AF.Ln))
    p.do("scalar", lambda e: e.activation(out=sb["tiny"][:, 3:4], in_=sb["tiny"][:, 1:2],
                                          func=AF.Ln), inc="ac")
    p.wait("vector", "ac", p.cnt["ac"])
    p.do("vector", lambda e: e.tensor_tensor(out=sb["tiny"][:, 4:5], in0=sb["tiny"][:, 2:3],
                                             in1=sb["tiny"][:, 3:4], op=OP.add))
    p.hop("vector")
    p.do("vector", lambda e: e.tensor_scalar(out=sb["tiny"][:, 4:5], in0=sb["tiny"][:, 4:5],
                                             scalar1=0.5, scalar2=None, op0=OP.mult))
    p.do("vector", lambda e: e.tensor_tensor(out=sb["tiny"][:, 5:6], in0=sb["tiny"][:, 2:3],
                                             in1=sb["lnrel"][:], op=OP.add))
    p.hop("vector")
    p.do("vector", lambda e: e.tensor_scalar(out=sb["tiny"][:, 5:6], in0=sb["tiny"][:, 5:6],
                                             scalar1=0.5, scalar2=None, op0=OP.mult), inc="dv")
    p.wait("scalar", "dv", p.cnt["dv"])
    p.do("scalar", lambda e: e.activation(out=sb["tiny"][:, 6:7], in_=sb["tiny"][:, 4:5],
                                          func=AF.Exp))
    p.do("scalar", lambda e: e.activation(out=sb["tiny"][:, 7:8], in_=sb["tiny"][:, 5:6],
                                          func=AF.Exp), inc="ac")
    p.wait("vector", "ac", p.cnt["ac"])
    p.do("vector", lambda e: e.tensor_tensor(out=sb["tiny"][:, 8:9], in0=sb["tiny"][:, 6:7],
                                             in1=sb["tiny"][:, 7:8], op=OP.add))
    p.hop("vector")
    p.do("vector", lambda e: e.tensor_copy(out=sb["tinyb"][:, 0:1], in_=sb["tiny"][:, 8:9]),
         inc="dv")
    p.wait("tensor", "dv", p.cnt["dv"])
    p.do("tensor", lambda e: e.matmul(pB[0:1, 1024:1026], sb["tinyb"][:, 0:1], sb["i2"][:],
                                      start=True, stop=True), inc="pe")
    p.wait("vector", "pe", p.cnt["pe"])
    p.do("vector", lambda e: e.reduce_max(out=sb["tiny"][0:1, 9:10], in_=pB[0:1, 1024:1026],
                                          axis=AX.X))
    p.hop("vector")
    p.do("vector", lambda e: e.tensor_copy(out=sb["tinyb"][0:1, 1:2],
                                           in_=sb["tiny"][0:1, 9:10]), inc="dv")
    p.wait("tensor", "dv", p.cnt["dv"])
    p.do("tensor", lambda e: e.matmul(pB[:, 1026:1027], sb["ones_r"][:], sb["tinyb"][0:1, 1:2],
                                      start=True, stop=True), inc="pe")
    p.wait("scalar", "pe", p.cnt["pe"])
    p.do("scalar", lambda e: e.activation(out=sb["negb"][:], in_=pB[:, 1026:1027],
                                          func=AF.Copy, scale=-1.0), inc="ac")
    ac_negb = p.cnt["ac"]

    p.wait("tensor", "dv", dv_eW)
    p.wait("tensor", "ac", ac_negb)
    for nch in range(4):
        reg = pB[0:4, 1024 + (nch % 2) * 512:1536 + (nch % 2) * 512]
        if nch >= 2:
            p.wait("tensor", "dv", p.cnt["dv"])
        p.do("tensor", lambda e, nch=nch, reg=reg: e.matmul(
            reg, sb["eW"][:], V["qh"][:, nch * 512:(nch + 1) * 512],
            start=True, stop=True), inc="pe")
        p.wait("vector", "pe", p.cnt["pe"])
        p.do("vector", lambda e, nch=nch, reg=reg: e.tensor_copy(
            out=sb["edges4"][:, nch * 512:(nch + 1) * 512],
            in_=reg), inc="dv")
    dv_edges = p.cnt["dv"]
    if debug == 1 and hp == 0:
        p.wait("gpsimd", "ac", ac_negb)
        p.wait("gpsimd", "dv", dv_edges)
        for nm, src in (("d_negb", sb["negb"][:]), ("d_qh", V["qh"]), ("d_k", V["kk"]),
                        ("d_vp", V["vp"]), ("d_edge", sb["edges4"][:])):
            p.do("gpsimd", lambda e, nm=nm, src=src: e.dma_start(out=dbg[nm], in_=src),
                 inc="dgd", inc_n=16)
        p.dbg_barrier()

    ec_ev = {}
    for qb in range(16):
        pos = 1024 + (qb % 2) * 512
        if qb < 2:
            p.wait("tensor", "dv", dv_edges)
        else:
            p.wait("tensor", "dv", ec_ev[qb - 2])
        for j in range(4):
            p.do("tensor", lambda e, qb=qb, j=j, pos=pos: e.matmul(
                pB[:, pos + j:pos + j + 1], sb["edges4"][:, qb * 128:(qb + 1) * 128],
                sb["sel_c"][:, j:j + 1], start=True, stop=True),
                inc="pe" if j == 3 else None)
        p.wait("vector", "pe", p.cnt["pe"])
        ec_ev[qb] = p.do("vector", lambda e, qb=qb, pos=pos: e.tensor_copy(
            out=sb["ecols"][:, qb * 4:(qb + 1) * 4], in_=pB[:, pos:pos + 4]), inc="dv")

    p.pair_dv0 = p.cnt["dv"]
    p.t8_dv = p.pair_dv0
    for qg in range(NG):
        gen_group(p, sb, V, pA, pB, hp, qg, kb_ranges, dbg, debug)


def rg_layout(qg):
    offs = {}
    pos = 0
    for qi in range(GQB):
        qb = qg * GQB + qi
        kb_lo, kb_hi = band_range(qb)
        w = (kb_hi - kb_lo + 1) * 128
        for h in range(2):
            offs[(qi, h)] = (pos, w)
            pos += w
    assert pos <= 18432, pos
    return offs


def gen_group(p, sb, V, pA, pB, hp, qg, kb_ranges, dbg, debug):
    qbs = [qg * GQB + i for i in range(GQB)]
    q0g = qg * GQB * 128
    offs = rg_layout(qg)
    denbL = sb["rowm"][:].bitcast(BF16)
    denbG = sb["rowr"][:].bitcast(BF16)
    row_ext = sb["arena"][:].ap[0][0]

    sgv = V["sig"] if qg % 2 == 0 else V["sigB"]
    p.wait("vector", "pe", p.g65pe)
    p.do("vector", lambda e: e.tensor_scalar(
        out=V["sigm"][:, :], in0=V["qh"][:, q0g:q0g + 512], scalar1=0.0, scalar2=None,
        op0=OP.min), inc="dv")
    p.wait("scalar", "dv", p.cnt["dv"])
    p.do("scalar", lambda e, sgv=sgv: e.activation(out=sgv[:, :], in_=V["sigm"][:, :],
                                                   func=AF.Exp, scale=8.0), inc="ac")
    p.wait("vector", "ac", p.cnt["ac"])
    p.do("vector", lambda e: e.tensor_scalar(
        out=V["sigm"][:, :], in0=V["qh"][:, q0g:q0g + 512], scalar1=0.0, scalar2=None,
        op0=OP.max))
    p.hop("vector")
    p.do("vector", lambda e, sgv=sgv: e.scalar_tensor_tensor(
        out=sgv[:, :], in0=V["sigm"][:, :], scalar=8.0, in1=sgv[:, :],
        op0=OP.mult, op1=OP.add), inc="dv")
    p.sig_dv = p.cnt["dv"]

    gmap = {}
    for qi, qb in enumerate(qbs):
        kb_lo, kb_hi = kb_ranges[qb]
        wb = (kb_hi - kb_lo + 1) * 128
        colstart = kb_lo * 128 - qb * 128 + 1024 + PAD
        assert colstart - 127 >= 0 and colstart + wb - 1 <= CP - 1
        for h in range(2):
            u = p.pguse
            sv = V["pgs"][u % 3]
            if u - 3 in p.guses:
                gs, gv = p.guses.pop(u - 3)
                p.wait("vector", gs, gv)
            if u % 3 == 2 and p.slot2_fresh:
                p.slot2_fresh = False
                p.wait("vector", "pe", p.pe_vlast)
            ecl = qb * 4 + 2 * h
            p.do("vector", lambda e, sv=sv, ecl=ecl: e.tensor_copy(
                out=sv[:, 0:PAD],
                in_=sb["ecols"][:, ecl:ecl + 1].broadcast_to([128, PAD])), inc="dp")
            p.do("vector", lambda e, sv=sv, ecl=ecl: e.tensor_copy(
                out=sv[:, PAD + 2048:CP],
                in_=sb["ecols"][:, ecl + 1:ecl + 2].broadcast_to(
                    [128, CP - PAD - 2048])), inc="dp")
            for j in range(4):
                r = p.pmm_rot % 2
                p.pmm_rot += 1
                reg = pB[:, 1024 + r * 512:1536 + r * 512]
                if len(p.dp_hist) >= 2:
                    p.wait("tensor", "dp", p.dp_hist[-2])
                if p.pair_pmm_first:
                    p.pair_pmm_first = False
                    p.wait("tensor", "dv", p.pair_dv0)
                p.do("tensor", lambda e, reg=reg, h=h, qb=qb, j=j: e.matmul(
                    reg, V["qh"][h * 64:(h + 1) * 64, qb * 128:(qb + 1) * 128],
                    sb["relg"][h * 64:(h + 1) * 64, j * 512:(j + 1) * 512],
                    start=True, stop=True, tile_position=(h * 64, 0)), inc="pe")
                p.wait("vector", "pe", p.cnt["pe"])
                ev = p.do("vector", lambda e, sv=sv, j=j, reg=reg: e.tensor_copy(
                    out=sv[:, PAD + j * 512:PAD + (j + 1) * 512], in_=reg), inc="dp")
                p.dp_hist.append(ev)
            qeng, qsem = ("sync", "dsg") if h == 0 else ("gpsimd", "dsh")
            p.wait(qeng, qsem, p.cnt[qsem])
            p.wait(qeng, "dp", p.cnt["dp"])
            p.wait(qeng, "pe", p.sdone_last)
            off0, w0 = offs[(qi, h)]
            assert w0 == wb
            src = AP(sv.tensor, sv.offset + colstart, [[row_ext - 1, 128], [1, wb]])
            g = p.do(qeng, lambda e, src=src, off0=off0, wb=wb: e.dma_start(
                out=V["rg"][:, off0:off0 + wb], in_=src), inc=qsem, inc_n=16)
            gmap[(qi, h)] = (qsem, g)
            p.guses[u] = (qsem, g)
            p.pguse += 1
        if debug == 1 and hp == 0 and qg == 0 and qb == 0:
            p.wait("gpsimd", "dp", p.cnt["dp"])
            p.do("gpsimd", lambda e: e.dma_start(
                out=dbg["d_pg"], in_=V["pgs"][0][:, :]), inc="dgd", inc_n=16)
            p.wait("gpsimd", "dsg", gmap[(0, 0)][1])
            p.do("gpsimd", lambda e, wb=wb: e.dma_start(
                out=dbg["d_rg"][:, 0:wb], in_=V["rg"][:, 0:wb]), inc="dgd", inc_n=16)
            p.dbg_barrier()

    p.flush_pending()

    p.wait("tensor", "dv", p.cnt["dv"])
    p.wait("tensor", "dgz", p.cnt["dgz"])
    p.wait("tensor", "ac", p.o65ac)
    p.do("tensor", lambda e, sgv=sgv: e.matmul(pA[0:65, 1024:1536], sb["mz"][0:64, :],
                                               sgv[0:64, :], start=True, stop=True,
                                               tile_position=(0, 0)))
    p.do("tensor", lambda e, sgv=sgv: e.matmul(pA[0:65, 1536:2048], sb["mz"][64:128, :],
                                               sgv[64:128, :], start=True, stop=True,
                                               tile_position=(64, 0)), inc="pe")
    p.g65pe = p.cnt["pe"]
    p.wait("vector", "pe", p.g65pe)
    for h in range(2):
        p.do("vector", lambda e, h=h: e.tensor_copy(
            out=sb["g65s"][:, h, :], in_=pA[0:65, 1024 + h * 512:1024 + (h + 1) * 512]))
        p.do("vector", lambda e, h=h, denbG=denbG: e.tensor_scalar(
            out=denbG[0:1, h * 512:(h + 1) * 512],
            in0=pA[64:65, 1024 + h * 512:1024 + (h + 1) * 512],
            scalar1=1e-6, scalar2=None, op0=OP.add), inc="dv" if h == 1 else None)
    kb_dv0 = p.cnt["dv"]

    exp_base = p.cnt["ac"]
    sdone = []
    avdone = []
    waited = {"dsg": 0, "dsh": 0}

    def emit_S(kb):
        pp = kb % 2
        reg = pA[:, pp * 1024:(pp + 1) * 1024]
        if kb == 0:
            p.wait("tensor", "ac", p.o65ac)
            p.wait("tensor", "dv", p.t8_dv)
        elif kb == 1:
            p.wait("tensor", "dv", kb_dv0)
        else:
            p.wait("tensor", "ac", exp_base + (kb - 1))
        for qi, qb in enumerate(qbs):
            if kb_ranges[qb][0] <= kb <= kb_ranges[qb][1]:
                for h in range(2):
                    qsem, gv = gmap[(qi, h)]
                    if gv > waited[qsem]:
                        waited[qsem] = gv
                        p.wait("tensor", qsem, gv)
        for h in range(2):
            hreg = reg[:, h * 512:(h + 1) * 512]
            tp = (0, 0) if h == 0 else (64, 0)
            band = [(qi, qb, kb - kb_ranges[qb][0]) for qi, qb in enumerate(qbs)
                    if kb_ranges[qb][0] <= kb <= kb_ranges[qb][1]]
            farA = [qi for qi, qb in enumerate(qbs) if kb < kb_ranges[qb][0]]
            farB = [qi for qi, qb in enumerate(qbs) if kb > kb_ranges[qb][1]]
            nacc = len(band) + (1 if farA else 0) + (1 if farB else 0)
            p.do("tensor", lambda e, hreg=hreg, h=h, kb=kb, tp=tp, nacc=nacc: e.matmul(
                hreg, V["kk"][h * 64:(h + 1) * 64, kb * 128:(kb + 1) * 128],
                V["qh"][h * 64:(h + 1) * 64, q0g:q0g + 512],
                start=True, stop=(nacc == 0), tile_position=tp))
            i = 0
            for qi, qb, off in band:
                i += 1
                off0, w0 = offs[(qi, h)]
                last = (i == nacc)
                p.do("tensor", lambda e, hreg=hreg, qi=qi, off0=off0, off=off, last=last: e.matmul(
                    hreg[:, qi * 128:(qi + 1) * 128],
                    V["rg"][:, off0 + off * 128:off0 + (off + 1) * 128],
                    sb["ident"][:], start=False, stop=last))
            for falist, eidx in ((farA, 0), (farB, 1)):
                if not falist:
                    continue
                i += 1
                lo, hi = falist[0], falist[-1]
                assert falist == list(range(lo, hi + 1))
                last = (i == nacc)
                j = 2 * h + eidx
                p.do("tensor", lambda e, hreg=hreg, j=j, lo=lo, hi=hi, last=last: e.matmul(
                    hreg[:, lo * 128:(hi + 1) * 128], sb["sel_r"][:, j * 128:(j + 1) * 128],
                    sb["edges4"][:, q0g + lo * 128:q0g + (hi + 1) * 128],
                    start=False, stop=last))
        v = p.inc_last("tensor", "pe")
        sdone.append(v)

    def emit_attnv(kb):
        p.wait("tensor", "ac", exp_base + kb + 1)
        for h in range(2):
            p.do("tensor", lambda e, kb=kb, h=h: e.matmul(
                pB[0:65, h * 512:(h + 1) * 512], V["vp"][:, kb, h, :],
                V["ee"][:, kb % 2, h * 512:(h + 1) * 512],
                start=(kb == 0), stop=(kb == NKB - 1)))
        v = p.inc_last("tensor", "pe")
        avdone.append(v)

    emit_S(0)
    emit_S(1)
    for kb in range(NKB):
        if kb + 2 < NKB:
            emit_S(kb + 2)
        emit_attnv(kb)
    for kb in range(NKB):
        p.wait("scalar", "pe", sdone[kb])
        if kb >= 2:
            p.wait("scalar", "pe", avdone[kb - 2])
        p.do("scalar", lambda e, kb=kb: e.activation(
            out=V["ee"][:, kb % 2, :], in_=pA[:, (kb % 2) * 1024:(kb % 2 + 1) * 1024],
            func=AF.Exp, bias=sb["negb"][:], scale=1.0), inc="ac")
    p.sdone_last = sdone[-1]

    avlast = avdone[-1]

    def tail(avlast=avlast, q0g=q0g, hp=hp, qg=qg, denbL=denbL, denbG=denbG):
        p.wait("scalar", "pe", avlast)
        p.do("scalar", lambda e: e.activation(out=sb["o65"][:, 0, :], in_=pB[0:65, 0:512],
                                              func=AF.Copy))
        p.do("scalar", lambda e: e.activation(out=sb["o65"][:, 1, :], in_=pB[0:65, 512:1024],
                                              func=AF.Copy), inc="ac")
        t1 = p.cnt["ac"]
        p.o65ac = t1
        if debug == 1 and hp == 0 and qg == 0:
            p.wait("gpsimd", "ac", t1)
            p.do("gpsimd", lambda e: e.dma_start(out=dbg["d_o65"], in_=sb["o65"][:]),
                 inc="dgd", inc_n=16)
            p.dbg_barrier()
        p.wait("vector", "ac", t1)
        for h in range(2):
            p.do("vector", lambda e, h=h, denbL=denbL: e.tensor_copy(
                out=denbL[0:1, h * 512:(h + 1) * 512],
                in_=pB[64:65, h * 512:(h + 1) * 512]), inc="dv" if h == 1 else None)
        t2 = p.cnt["dv"]
        p.wait("tensor", "ac", t1)
        p.wait("tensor", "dv", t2)
        for c in range(16):
            srcr = denbL if c < 8 else denbG
            cc = c % 8
            p.do("tensor", lambda e, srcr=srcr, cc=cc, c=c: e.matmul(
                pB[:, c:c + 1], srcr[0:1, cc * 128:(cc + 1) * 128],
                sb["ones_r"][0:1, 0:1], start=True, stop=True),
                inc="pe" if c == 15 else None)
        p.wait("vector", "pe", p.cnt["pe"])
        p.do("vector", lambda e: e.reciprocal(out=sb["rcolf"][:], in_=pB[:, 0:16]))
        p.hop("vector")
        p.do("vector", lambda e: e.tensor_copy(out=sb["rcol"][:], in_=sb["rcolf"][:]),
             inc="dv")
        p.wait("tensor", "dv", p.cnt["dv"])
        for rr in range(4):
            for c in range(4):
                j = rr * 4 + c
                p.do("tensor", lambda e, rr=rr, c=c, j=j: e.matmul(
                    pA[0:64, rr * 512 + c * 128:rr * 512 + (c + 1) * 128],
                    sb["rcol"][:, j:j + 1].broadcast_to([128, 64]),
                    sb["ident"][:], start=True, stop=True),
                    inc="pe" if j == 15 else None)
        p.wait("vector", "pe", p.cnt["pe"])
        p.wait("vector", "dsc", p.cnt["dsc"])
        for h in range(2):
            if h == 1:
                p.wait("vector", "dv", p.cnt["dv"])
            p.do("vector", lambda e, h=h: e.scalar_tensor_tensor(
                out=sb["ct"][:, 0, :], in0=sb["o65"][0:64, h, :],
                scalar=sb["gate"][0:64, :],
                in1=pA[0:64, h * 512:(h + 1) * 512], op0=OP.mult, op1=OP.mult))
            p.do("vector", lambda e, h=h: e.scalar_tensor_tensor(
                out=sb["ct"][:, 1, :], in0=sb["g65s"][0:64, h, :],
                scalar=sb["gate1m"][0:64, :],
                in1=pA[0:64, 1024 + h * 512:1024 + (h + 1) * 512],
                op0=OP.mult, op1=OP.mult))
            p.hop("vector")
            if h == 0:
                p.do("vector", lambda e: e.tensor_tensor(
                    out=V["comb"][0:64, hp, q0g:q0g + 512], in0=sb["ct"][:, 0, :],
                    in1=sb["ct"][:, 1, :], op=OP.add), inc="dv")
            else:
                p.do("vector", lambda e: e.tensor_tensor(
                    out=sb["ct"][:, 0, :], in0=sb["ct"][:, 0, :],
                    in1=sb["ct"][:, 1, :], op=OP.add), inc="dv")
                p.wait("sync", "dv", p.cnt["dv"])
                p.do("sync", lambda e: e.dma_start(
                    out=V["comb"][64:128, hp, q0g:q0g + 512], in_=sb["ct"][:, 0, :]),
                    inc="dsc", inc_n=16)
        p.t8_dv = p.cnt["dv"]

    p.pending = tail


def gen_ffn(p, dd, sb, V, pA, pB, out_cm, attn_sp, ds_attn, dbg, debug):
    def wload(dname, c, width, wait_ac=None):
        if wait_ac is not None:
            p.wait("gpsimd", "ac", wait_ac)
        s = "dgx0" if c % 2 == 0 else "dgx1"
        return (s, p.do("gpsimd", lambda e, c=c, dname=dname, width=width: e.dma_start(
            out=V["wsl"][:, c % 2, 0:width], in_=dd[dname][c * 128:(c + 1) * 128, :]),
            inc=s, inc_n=16))

    p.wait("tensor", "ac", p.cnt["ac"])
    p.wait("tensor", "dv", p.cnt["dv"])
    p.wait("gpsimd", "dss0", ds_attn[0])
    p.wait("gpsimd", "dss1", ds_attn[1])
    loads = {0: wload("w1T", 0, 4 * D), 1: wload("w1T", 1, 4 * D)}
    done = {}
    p.wait("scalar", "pe", p.cnt["pe"])
    for c in range(NCH):
        lsem, lval = loads[c]
        p.wait("scalar", lsem, lval)
        p.do("scalar", lambda e, c=c: e.activation(
            out=V["s1"][:, c, :], in_=V["wsl"][:, c % 2, :], func=AF.Sign), inc="ac")
        for ch in range(8):
            p.wait("scalar", "pe", p.cnt["pe"])
            p.do("scalar", lambda e, c=c, ch=ch: e.activation(
                out=sb["sqs"][:], in_=V["wsl"][:, c % 2, ch * 512:(ch + 1) * 512],
                func=AF.Abs), inc="ac")
            p.wait("tensor", "ac", p.cnt["ac"])
            ps = (pA if ch < 4 else pB)[0:1, (ch % 4) * 512:(ch % 4 + 1) * 512]
            p.do("tensor", lambda e, ps=ps, c=c: e.matmul(
                ps, sb["onesc"][:], sb["sqs"][:], start=(c == 0), stop=(c == NCH - 1)),
                inc="pe")
        done[c] = p.cnt["ac"]
        if c + 2 < NCH:
            loads[c + 2] = wload("w1T", c + 2, 4 * D, wait_ac=done[c])
    p.wait("vector", "pe", p.cnt["pe"])
    for ch in range(8):
        ps = (pA if ch < 4 else pB)[0:1, (ch % 4) * 512:(ch % 4 + 1) * 512]
        p.do("vector", lambda e, ps=ps: e.tensor_scalar(
            out=sb["rowb"][:], in0=ps, scalar1=1.0 / D, scalar2=None, op0=OP.mult),
            inc="dv")
        p.wait("tensor", "dv", p.cnt["dv"])
        for q in range(4):
            p.do("tensor", lambda e, ch=ch, q=q: e.matmul(
                pB[:, 1536 + 4 * (ch % 2) + q:1537 + 4 * (ch % 2) + q],
                sb["rowb"][0:1, q * 128:(q + 1) * 128],
                sb["ones_r"][0:1, 0:1], start=True, stop=True),
                inc="pe" if q == 3 else None)
        p.wait("vector", "pe", p.cnt["pe"])
        p.do("vector", lambda e, ch=ch: e.tensor_copy(
            out=sb["a1c"][:, ch * 4:(ch + 1) * 4],
            in_=pB[:, 1536 + 4 * (ch % 2):1540 + 4 * (ch % 2)]), inc="dv")
    loads = {0: wload("w2T", 0, D, wait_ac=done[NCH - 2]),
             1: wload("w2T", 1, D, wait_ac=done[NCH - 1])}
    done2 = {}
    p.wait("scalar", "pe", p.cnt["pe"])
    p.wait("scalar", "dss0", p.cnt["dss0"])
    p.wait("scalar", "dss1", p.cnt["dss1"])
    for c in range(32):
        lsem, lval = loads[c]
        p.wait("scalar", lsem, lval)
        p.do("scalar", lambda e, c=c: e.activation(
            out=V["s2"][:, c, :], in_=V["wsl"][:, c % 2, 0:1024], func=AF.Sign), inc="ac")
        for ch in range(2):
            p.wait("scalar", "pe", p.cnt["pe"])
            p.do("scalar", lambda e, c=c, ch=ch: e.activation(
                out=sb["sqs"][:], in_=V["wsl"][:, c % 2, ch * 512:(ch + 1) * 512],
                func=AF.Abs), inc="ac")
            p.wait("tensor", "ac", p.cnt["ac"])
            ps = pA[0:1, ch * 512:(ch + 1) * 512]
            p.do("tensor", lambda e, ps=ps, c=c: e.matmul(
                ps, sb["onesc"][:], sb["sqs"][:], start=(c == 0), stop=(c == 31)),
                inc="pe")
        done2[c] = p.cnt["ac"]
        if c + 2 < 32:
            loads[c + 2] = wload("w2T", c + 2, D, wait_ac=done2[c])
    p.wait("vector", "pe", p.cnt["pe"])
    for ch in range(2):
        ps = pA[0:1, ch * 512:(ch + 1) * 512]
        p.do("vector", lambda e, ps=ps: e.tensor_scalar(
            out=sb["rowb"][:], in0=ps, scalar1=1.0 / (4 * D), scalar2=None, op0=OP.mult),
            inc="dv")
        p.wait("tensor", "dv", p.cnt["dv"])
        for q in range(4):
            p.do("tensor", lambda e, ch=ch, q=q: e.matmul(
                pB[:, 1536 + 4 * ch + q:1537 + 4 * ch + q],
                sb["rowb"][0:1, q * 128:(q + 1) * 128],
                sb["ones_r"][0:1, 0:1], start=True, stop=True),
                inc="pe" if q == 3 else None)
        p.wait("vector", "pe", p.cnt["pe"])
        p.do("vector", lambda e, ch=ch: e.tensor_copy(
            out=sb["a2c"][:, ch * 4:(ch + 1) * 4],
            in_=pB[:, 1536 + 4 * ch:1540 + 4 * ch]), inc="dv")
    dv_bw = p.cnt["dv"]
    if debug == 1:
        p.wait("gpsimd", "dv", dv_bw)
        p.do("gpsimd", lambda e: e.dma_start(out=dbg["d_s1"], in_=V["s1"]),
             inc="dgd", inc_n=16)
        p.dbg_barrier()

    xloads = {}
    atloads = {}
    ac_signs_done = p.cnt["ac"]
    def emit_fx(sl, wait_dv=None):
        ap = dd["xT"][:, sl * FSL:(sl + 1) * FSL].rearrange("(c p) t -> p c t", p=128)
        if wait_dv is not None:
            p.wait("gpsimd", "dv", wait_dv)
        if sl == 0:
            p.wait("gpsimd", "ac", ac_signs_done)
        xloads[sl] = p.do("gpsimd", lambda e, ap=ap: e.dma_start(
            out=V["fx"], in_=ap), inc="dgf", inc_n=16)
        ap2 = attn_sp[:, :, sl * FSL:(sl + 1) * FSL].rearrange("c p t -> p c t")
        atloads[sl] = p.do("gpsimd", lambda e, ap2=ap2: e.dma_start(
            out=V["fat"], in_=ap2), inc="dgf", inc_n=16)
    emit_fx(0)

    R8 = [pA[:, i * 512:(i + 1) * 512] for i in range(4)] + \
         [pB[:, i * 512:(i + 1) * 512] for i in range(4)]
    out_dma = {}
    import os
    SLAB_BARRIER = os.environ.get("DK_SLAB_BARRIER", "0") == "1"
    for sl in range(NFS):
        if SLAB_BARRIER and sl > 0:
            for eng in ("tensor", "scalar", "vector", "gpsimd", "sync"):
                p.wait(eng, "pe", p.cnt["pe"])
                p.wait(eng, "ac", p.cnt["ac"])
                p.wait(eng, "dv", p.cnt["dv"])
                p.wait(eng, "dso", p.cnt["dso"])
        t0 = sl * FSL
        p.wait("vector", "dgf", atloads[sl])
        if sl == 0:
            p.wait("vector", "dv", dv_bw)
            p.wait("vector", "ac", ac_signs_done)
        xs = V["fx"]
        at = V["fat"]
        p.wait("vector", "dv", p.cnt["dv"])
        for c in range(NCH):
            p.do("vector", lambda e, xs=xs, at=at, c=c: e.scalar_tensor_tensor(
                out=V["x2sl"][:, c, :], in0=at[:, c, :],
                scalar=sb["ga"][:, c:c + 1], in1=xs[:, c, :], op0=OP.mult, op1=OP.add),
                inc="dv" if c == NCH - 1 else None)
        if sl + 1 < NFS:
            emit_fx(sl + 1, wait_dv=p.cnt["dv"])
        p.wait("vector", "dv", p.cnt["dv"])
        p.do("vector", lambda e: e.tensor_tensor(
            out=V["xsq2"], in0=V["x2sl"], in1=V["x2sl"], op=OP.mult), inc="dv")
        p.wait("tensor", "dv", p.cnt["dv"])
        p.wait("tensor", "ac", p.cnt["ac"])
        rsum, rsq = pB[0:1, 1024:1024 + FSL], pB[0:1, 1536:1536 + FSL]
        rrep, rmrep = pA[:, 1024:1024 + FSL], pA[:, 1536:1536 + FSL]
        for c in range(NCH):
            p.do("tensor", lambda e, c=c, rsum=rsum: e.matmul(
                rsum, sb["onescf"][:], V["x2sl"][:, c, :],
                start=(c == 0), stop=(c == NCH - 1)))
        for c in range(NCH):
            p.do("tensor", lambda e, c=c, rsq=rsq: e.matmul(
                rsq, sb["onesc"][:], V["xsq2"][:, c, :],
                start=(c == 0), stop=(c == NCH - 1)),
                inc="pe" if c == NCH - 1 else None)
        p.wait("vector", "pe", p.cnt["pe"])
        emit_ln_rows(p, sb, rsum, rsq, FSL)
        p.wait("tensor", "dv", p.cnt["dv"])
        p.do("tensor", lambda e, rrep=rrep: e.matmul(
            rrep, sb["ones_r"][:], sb["rowb"][0:1, 0:FSL], start=True, stop=True), inc="pe")
        p.wait("vector", "pe", p.cnt["pe"])
        p.do("vector", lambda e: e.tensor_copy(out=sb["rowb"][0:1, 0:FSL],
                                               in_=sb["rowmr"][0:1, 0:FSL]), inc="dv")
        p.wait("tensor", "dv", p.cnt["dv"])
        p.do("tensor", lambda e, rmrep=rmrep: e.matmul(
            rmrep, sb["ones_r"][:], sb["rowb"][0:1, 0:FSL], start=True, stop=True), inc="pe")
        p.wait("vector", "pe", p.cnt["pe"])
        for c in range(NCH):
            p.do("vector", lambda e, c=c, rrep=rrep: e.tensor_tensor(
                out=sb["tmpa"][:, 0:FSL], in0=V["x2sl"][:, c, :], in1=rrep, op=OP.mult))
            p.hop("vector")
            p.do("vector", lambda e, c=c, rmrep=rmrep: e.tensor_tensor(
                out=sb["tmpa"][:, 0:FSL], in0=sb["tmpa"][:, 0:FSL], in1=rmrep,
                op=OP.subtract))
            p.hop("vector")
            p.do("vector", lambda e, c=c: e.tensor_scalar(
                out=V["h3sl"][:, c, :], in0=sb["tmpa"][:, 0:FSL],
                scalar1=sb["cb1"][:, c:c + 1], scalar2=sb["sbv"][:, c:c + 1],
                op0=OP.mult, op1=OP.add), inc="dv" if c == NCH - 1 else None)
            if c < NCH - 1:
                p.hop("vector")
        dv_h3 = p.cnt["dv"]
        if debug == 1 and sl == 0:
            p.wait("gpsimd", "dv", dv_h3)
            p.do("gpsimd", lambda e: e.dma_start(out=dbg["d_h3"], in_=V["h3sl"]),
                 inc="dgd", inc_n=16)
            p.dbg_barrier()
        p.wait("tensor", "dv", dv_h3)
        evac = {}
        for ho in range(32):
            reg = R8[ho % 8]
            if ho >= 8:
                p.wait("tensor", "ac", evac[ho - 8])
            for c in range(NCH):
                p.do("tensor", lambda e, ho=ho, c=c, reg=reg: e.matmul(
                    reg[:, 0:FSL], V["s1"][:, c, ho * 128:(ho + 1) * 128],
                    V["h3sl"][:, c, :], start=(c == 0), stop=(c == NCH - 1)),
                    inc="pe" if c == NCH - 1 else None)
            p.wait("scalar", "pe", p.cnt["pe"])
            evac[ho] = p.do("scalar", lambda e, ho=ho, reg=reg: e.activation(
                out=V["h2sl"][:, ho, :], in_=reg[:, 0:FSL], func=AF.Gelu,
                bias=sb["b1_c"][:, ho:ho + 1], scale=sb["a1c"][:, ho:ho + 1]), inc="ac")
        if sl >= 1:
            p.wait("vector", "dso", out_dma[(sl - 1, 1)])
        p.wait("tensor", "ac", evac[31])
        for blk in range(NCH):
            reg = R8[blk]
            for hc in range(32):
                p.do("tensor", lambda e, blk=blk, hc=hc, reg=reg: e.matmul(
                    reg[:, 0:FSL], V["s2"][:, hc, blk * 128:(blk + 1) * 128],
                    V["h2sl"][:, hc, :], start=(hc == 0), stop=(hc == 31)),
                    inc="pe" if hc == 31 else None)
            p.wait("vector", "pe", p.cnt["pe"])
            p.wait("vector", "dv", p.cnt["dv"])
            if blk == 4:
                p.wait("vector", "dso", out_dma[(sl, 0)])
            p.do("vector", lambda e, blk=blk, reg=reg: e.tensor_scalar(
                out=sb["sqs"][:, 0:FSL], in0=reg[:, 0:FSL], scalar1=sb["a2c"][:, blk:blk + 1],
                scalar2=sb["b2_c"][:, blk:blk + 1], op0=OP.mult, op1=OP.add))
            p.hop("vector")
            p.do("vector", lambda e, sl=sl, blk=blk: e.scalar_tensor_tensor(
                out=V["outsl"][:, blk % 4, :], in0=sb["sqs"][:, 0:FSL],
                scalar=sb["gb"][:, blk:blk + 1], in1=V["x2sl"][:, blk, :],
                op0=OP.mult, op1=OP.add), inc="dv")
            if blk % 4 == 3:
                half4 = blk // 4
                p.wait("sync", "dv", p.cnt["dv"])
                dst = out_cm[half4 * 4 * 128:(half4 + 1) * 4 * 128,
                             t0:t0 + FSL].rearrange("(c p) t -> p c t", p=128)
                out_dma[(sl, half4)] = p.do("sync", lambda e, dst=dst, sl=sl: e.dma_start(
                    out=dst, in_=V["outsl"][:, :, :]), inc="dso", inc_n=16)
        dv_out = p.cnt["dv"]



def prep_inputs(inputs):
    f32 = np.float32
    bf = ml_dtypes.bfloat16
    x = np.asarray(inputs["L0_attn_hidden_states"], f32)
    relT = np.ascontiguousarray(np.asarray(inputs["L0_attn_rel_emb"], f32).T)
    rel_rev = np.ascontiguousarray(relT[:, ::-1])
    Mz = np.concatenate([np.asarray(inputs["L0_attn_M"], f32),
                         np.asarray(inputs["L0_attn_z"], f32)[:, :, None]], axis=2)
    mzp = np.ascontiguousarray(Mz.reshape(8, 128, 65))
    WqT = np.ascontiguousarray(np.asarray(inputs["L0_attn_Wq"], f32).T)
    WkT = np.ascontiguousarray(np.asarray(inputs["L0_attn_Wk"], f32).T)
    WvT = np.ascontiguousarray(np.asarray(inputs["L0_attn_Wv"], f32).T)
    WoT = np.ascontiguousarray(np.asarray(inputs["L0_attn_Wo"], f32).T)
    w1T = np.ascontiguousarray(np.asarray(inputs["L0_ffn_w1"], f32).T)
    w2T = np.ascontiguousarray(np.asarray(inputs["L0_ffn_w2"], f32).T)
    wq_p = np.ascontiguousarray(WqT.reshape(D, NP, 128).transpose(1, 0, 2))
    wk_p = np.ascontiguousarray(WkT.reshape(D, NP, 128).transpose(1, 0, 2))
    wv_p = np.ascontiguousarray(WvT.reshape(D, NP, 128).transpose(1, 0, 2))

    def col8(v):
        return np.ascontiguousarray(np.asarray(v, f32).reshape(8, 128).T)
    def col32(v):
        return np.ascontiguousarray(np.asarray(v, f32).reshape(32, 128).T)

    sel_r = np.zeros((4, 512), bf)
    sel_c = np.zeros((4, 4), bf)
    for j in range(4):
        sel_r[j, j * 128:(j + 1) * 128] = 1
        sel_c[j, j] = 1
    blkd = np.zeros((128, 2), bf)
    blkd[0:64, 0] = 1
    blkd[64:128, 1] = 1

    common = dict(
        wmod=np.asarray(inputs["L0_wmod"], f32),
        wq_p=wq_p, wk_p=wk_p, wv_p=wv_p, woT=WoT, w1T=w1T, w2T=w2T,
        mzp=mzp,
        bq_c=col8(inputs["L0_attn_bq"]), bk_c=col8(inputs["L0_attn_bk"]),
        bv_c=col8(inputs["L0_attn_bv"]), bo_c=col8(inputs["L0_attn_bo"]),
        b1_c=col32(inputs["L0_ffn_b1"]), b2_c=col8(inputs["L0_ffn_b2"]),
        galpha=np.asarray(inputs["L0_attn_gate_alpha"], f32).reshape(1, 1),
        ident=np.eye(128, dtype=bf),
        ones_r=np.ones((1, 128), bf),
        ones_rf=np.ones((1, 128), np.float32),
        onesc=np.ones((128, 1), bf),
        onescf=np.ones((128, 1), np.float32),
        blkd=blkd,
        o642=np.ones((64, 2), bf),
        i2=np.eye(2, dtype=bf),
        sel_r=sel_r, sel_c=sel_c,
    )
    relg0 = np.ascontiguousarray(np.concatenate([rel_rev, rel_rev], 0))
    relg1 = np.ascontiguousarray(np.concatenate([relT, relT], 0))
    in_maps = []
    for core in range(8):
        b, half = core // 2, core % 2
        xT = np.ascontiguousarray(x[b].T)
        if half == 1:
            xT = np.ascontiguousarray(xT[:, ::-1])
        m = dict(common)
        m["xT"] = xT
        m["relg"] = relg0 if half == 0 else relg1
        m["c_col"] = np.ascontiguousarray(
            np.asarray(inputs["L0_c"], f32)[b].reshape(8, 128).T)
        in_maps.append(m)
    return in_maps


def assemble(results):
    out = np.zeros((4, S, D), np.float32)
    for core in range(8):
        b, half = core // 2, core % 2
        oc = results[core]["out_cm"]
        if half == 0:
            out[b, 0:SQ, :] = oc.T
        else:
            out[b, SQ:S, :] = oc[:, ::-1].T
    return out


def kernel(**inputs):
    from concourse.bass_utils import run_bass_kernel_spmd
    in_maps = prep_inputs(inputs)
    nc = build_program(debug=0)
    res = run_bass_kernel_spmd(nc, in_maps, core_ids=list(range(8)))
    return assemble(res.results)


# version 05
# speedup vs baseline: 1.2229x; 1.0018x over previous
import contextlib
import numpy as np
import ml_dtypes
import concourse.bass as bass
import concourse.mybir as mybir
from concourse.ap import AP

BF16 = mybir.dt.bfloat16
F32 = mybir.dt.float32
AF = mybir.ActivationFunctionType
OP = mybir.AluOpType
AX = mybir.AxisListType

D = 1024
S = 4096
SQ = 2048
NP = 8
NCH = 8
MR = 1024
PAD = 254
CP = 2049 + 2 * PAD
GQB = 4
NG = SQ // (GQB * 128)
BAND = MR + 127
NKB = S // 128
FSL = 256
NFS = SQ // FSL

ENGINES = ("tensor", "scalar", "vector", "sync", "gpsimd")


def band_range(qb):
    q0 = qb * 128
    kb_lo = max(0, -(-(q0 - BAND) // 128))
    kb_hi = min(NKB - 1, (q0 + BAND) // 128)
    return kb_lo, kb_hi


class Prog:
    def __init__(self, nc, es):
        self.nc = nc
        self.ops = {e: [] for e in ENGINES}
        self.last_has_inc = {}
        self.cnt = {}
        self.sems = {}
        for s in ("dg", "ds", "pe", "ac", "dv", "dsg", "dsh", "dsc", "dss0", "dss1",
                  "dso", "dgm", "dgz", "dgw", "dgo", "dgx0", "dgx1", "dgy0", "dgy1", "dgf", "dgd",
                  "hv", "ha", "ht", "hg", "dp"):
            self.sems[s] = es.enter_context(nc.semaphore(s))
            self.cnt[s] = 0

    def wait(self, eng, sem, val):
        if val <= 0:
            return
        h = self.sems[sem]
        self.ops[eng].append(lambda e, h=h, v=val: e.wait_ge(h, v))
        self.last_has_inc[eng] = False

    def inc_last(self, eng, sem, n=1):
        assert not self.last_has_inc.get(eng), f"double inc on {eng}"
        h = self.sems[sem]
        self.cnt[sem] += n
        f = self.ops[eng].pop()
        self.ops[eng].append(lambda e, f=f, h=h, n=n: f(e).then_inc(h, n))
        self.last_has_inc[eng] = True
        return self.cnt[sem]

    def flush_pending(self):
        fn = getattr(self, "pending", None)
        self.pending = None
        if fn is not None:
            fn()

    def dbg_barrier(self):
        for eng in ("tensor", "scalar", "vector", "gpsimd", "sync"):
            self.wait(eng, "dgd", self.cnt["dgd"])

    def hop(self, eng):
        sem = {"vector": "hv", "scalar": "ha", "tensor": "ht", "gpsimd": "hg"}[eng]
        h = self.sems[sem]
        self.cnt[sem] += 1
        v = self.cnt[sem]
        assert not self.last_has_inc.get(eng), f"double inc (hop) on {eng}"
        f = self.ops[eng].pop()
        self.ops[eng].append(lambda e, f=f, h=h: f(e).then_inc(h, 1))
        self.ops[eng].append(lambda e, h=h, v=v: e.wait_ge(h, v))
        self.last_has_inc[eng] = False

    def do(self, eng, fn, inc=None, inc_n=1):
        if inc is None:
            self.ops[eng].append(fn)
            self.last_has_inc[eng] = False
            return None
        h = self.sems[inc]
        self.cnt[inc] += inc_n
        self.ops[eng].append(lambda e, fn=fn, h=h, n=inc_n: fn(e).then_inc(h, n))
        self.last_has_inc[eng] = True
        return self.cnt[inc]


def _view(tensor_ap, off_bytes, shape, dt):
    es_ = mybir.dt.size(dt)
    nel = int(np.prod(shape[1:]))
    assert off_bytes % 2 == 0
    v = tensor_ap[0:shape[0], off_bytes // 2: off_bytes // 2 + (nel * es_) // 2]
    if dt != BF16:
        v = v.bitcast(dt)
    if len(shape) == 2:
        return v
    if len(shape) == 3:
        return v.rearrange("p (a b) -> p a b", a=shape[1])
    if len(shape) == 4:
        return v.rearrange("p (a b c) -> p a b c", a=shape[1], b=shape[2])
    raise ValueError(shape)


def build_program(debug=0):
    nc = bass.Bass("TRN2", target_bir_lowering=False)
    es = contextlib.ExitStack()

    dd = {}
    def din(name, shape, dt=F32):
        dd[name] = nc.dram_tensor(name, shape, dt, kind="ExternalInput").ap()
    din("xT", [D, S])
    din("c_col", [128, 8])
    din("wmod", [D, 6 * D])
    din("wq_p", [NP, D, 128]); din("wk_p", [NP, D, 128]); din("wv_p", [NP, D, 128])
    din("woT", [D, D])
    din("w1T", [D, 4 * D])
    din("w2T", [4 * D, D])
    din("relg", [128, 2049])
    din("mzp", [NP, 128, 65])
    din("bq_c", [128, 8]); din("bk_c", [128, 8]); din("bv_c", [128, 8]); din("bo_c", [128, 8])
    din("b1_c", [128, 32]); din("b2_c", [128, 8])
    din("galpha", [1, 1])
    din("ident", [128, 128], BF16)
    din("ones_r", [1, 128], BF16)
    din("ones_rf", [1, 128], F32)
    din("onesc", [128, 1], BF16)
    din("onescf", [128, 1], F32)
    din("blkd", [128, 2], BF16)
    din("o642", [64, 2], BF16)
    din("i2", [2, 2], BF16)
    din("sel_r", [4, 512], BF16)
    din("sel_c", [4, 4], BF16)
    out_cm = nc.dram_tensor("out_cm", [D, SQ], F32, kind="ExternalOutput").ap()
    attn_sp = nc.dram_tensor("attn_sp", [NCH, 128, SQ], BF16, kind="Internal").ap()

    dbg = {}
    if debug:
        def dout(name, shape, dt):
            dbg[name] = nc.dram_tensor(name, shape, dt, kind="ExternalOutput").ap()
        dout("d_mod", [1, 6 * D], BF16)
        dout("d_h1", [128, NCH, S], BF16)
        dout("d_qh", [128, SQ], BF16)
        dout("d_k", [128, S], BF16)
        dout("d_vp", [128, NKB, 2, 65], BF16)
        dout("d_pg", [128, CP], BF16)
        dout("d_rg", [128, 2304], BF16)
        dout("d_negb", [128, 1], F32)
        dout("d_edge", [4, SQ], BF16)
        dout("d_o65", [65, 2, 512], F32)
        dout("d_comb", [128, NP, SQ], BF16)
        dout("d_attn", [NCH, 128, SQ], BF16)
        dout("d_s1", [128, NCH, 4 * D], BF16)
        dout("d_h3", [128, NCH, FSL], BF16)

    sb = {}
    def sbuf(name, shape, dt):
        sb[name] = es.enter_context(nc.sbuf_tensor("s_" + name, shape, dt))
    sbuf("ident", [128, 128], BF16)
    sbuf("ones_r", [1, 128], BF16)
    sbuf("onesc", [128, 1], BF16)
    sbuf("onescf", [128, 1], F32)
    sbuf("blkd", [128, 2], BF16)
    sbuf("o642", [64, 2], BF16)
    sbuf("i2", [2, 2], BF16)
    sbuf("sel_r", [4, 512], BF16)
    sbuf("sel_c", [4, 4], BF16)
    sbuf("c_col", [128, 8], F32)
    sbuf("silu_c", [128, 8], F32)
    sbuf("ones_rf", [1, 128], F32)
    for n in ("sa", "ca1", "ga", "sbv", "cb1", "gb", "bq_h", "bk_c", "bv_c", "bo_c", "b2_c"):
        sbuf(n, [128, 8], F32)
    sbuf("b1_c", [128, 32], F32)
    sbuf("a1c", [128, 32], F32)
    sbuf("a2c", [128, 8], F32)
    sbuf("gate", [128, 1], F32); sbuf("gate1m", [128, 1], F32)
    sbuf("relg", [128, 2049], BF16)
    sbuf("lnrel", [2, 1], F32)
    sbuf("tiny", [2, 10], F32)
    sbuf("tinyb", [2, 2], BF16)
    sbuf("rowb", [1, 512], BF16)
    sbuf("rowm", [1, 512], F32); sbuf("rowr", [1, 512], F32); sbuf("rowmr", [1, 512], F32)
    sbuf("tmpa", [128, 512], F32)
    sbuf("sqs", [128, 512], BF16)
    sbuf("xslab", [128, 2, NCH, 512], F32)
    sbuf("o65", [65, 2, 512], F32)
    sbuf("ct", [64, 2, 512], BF16)
    sbuf("negb", [128, 1], F32)
    sbuf("edges4", [4, SQ], BF16)
    sbuf("ecols", [128, 64], BF16)
    sbuf("g65s", [65, 2, 512], BF16)
    sbuf("rcol", [128, 16], BF16)
    sbuf("rcolf", [128, 16], F32)
    sbuf("eW", [128, 4], BF16)
    sbuf("mz", [128, 65], BF16)

    ARENA_B = 147584
    sbuf("arena", [128, ARENA_B // 2], BF16)
    ar = sb["arena"][:]

    V = {}
    V["h1"] = _view(ar, 0, [128, NCH, S], BF16)
    V["s1"] = _view(ar, 0, [128, NCH, 4 * D], BF16)
    B0 = 65536
    V["mrow"] = _view(ar, B0, [1, 6 * D], F32)
    V["wmod_ch"] = _view(ar, B0 + 24576, [128, 2, 3 * D], F32)
    V["xbf1"] = _view(ar, B0 + 24576, [128, NCH, 512], BF16)
    V["xsq1"] = _view(ar, B0 + 32768, [128, NCH, 512], BF16)
    V["kk"] = _view(ar, B0, [128, S], BF16)
    V["qh"] = _view(ar, B0 + 8192, [128, SQ], BF16)
    V["vp"] = _view(ar, B0 + 12288, [128, NKB, 2, 65], BF16)
    V["vf"] = _view(ar, B0 + 20608, [128, 1024], BF16)
    V["sigB"] = _view(ar, B0 + 20608, [128, 512], BF16)
    V["pgs"] = [_view(ar, B0 + 22656, [128, CP], BF16),
                _view(ar, B0 + 22656 + 2 * CP, [128, CP], BF16),
                _view(ar, B0 + 39040, [128, CP], BF16)]
    V["ee"] = _view(ar, B0 + 32896, [128, 2, 1024], BF16)
    V["sig"] = _view(ar, B0 + 36992, [128, 512], BF16)
    V["sigm"] = _view(ar, B0 + 38016, [128, 512], BF16)
    V["wqkv_wq"] = _view(ar, B0 + 39040, [128, NCH, 128], BF16)
    V["wqkv_wk"] = _view(ar, B0 + 41088, [128, NCH, 128], BF16)
    V["wqkv_wv"] = _view(ar, B0 + 43136, [128, NCH, 128], BF16)
    V["rg"] = _view(ar, B0 + 45184, [128, 18432], BF16)
    V["wo"] = _view(ar, B0, [128, NCH, D], BF16)
    V["stag"] = _view(ar, B0 + 16384, [128, 2, SQ], BF16)
    V["s2"] = _view(ar, B0, [128, 32, D], BF16)
    xs_ap = sb["xslab"][:].rearrange("p a b c -> p (a b c)").bitcast(BF16)
    V["comb"] = _view(xs_ap, 0, [128, NP, SQ], BF16)
    V["fx"] = _view(xs_ap, 0, [128, NCH, FSL], F32)
    V["x2sl"] = _view(xs_ap, 8192, [128, NCH, FSL], F32)
    V["outsl"] = _view(xs_ap, 16384, [128, 4, FSL], F32)
    V["fat"] = _view(xs_ap, 20480, [128, NCH, FSL], BF16)
    V["xsq2"] = _view(xs_ap, 24576, [128, NCH, FSL], BF16)
    V["h3sl"] = _view(xs_ap, 28672, [128, NCH, FSL], BF16)
    assert 28672 + 4096 <= 32768
    V["wsl"] = _view(ar, 131072, [128, 2, 4 * D], BF16)
    V["h2sl"] = _view(ar, 131072, [128, 32, FSL], BF16)
    assert 131072 + 16384 <= ARENA_B

    pA = es.enter_context(nc.psum_tensor("pA", [128, 2048], F32))
    pB = es.enter_context(nc.psum_tensor("pB", [128, 2048], F32))

    p = Prog(nc, es)
    gen(p, dd, sb, V, pA, pB, dbg, out_cm, attn_sp, debug)

    with nc.Block() as block:
        @block.tensor
        def _(e):
            for f in p.ops["tensor"]:
                f(e)
        @block.scalar
        def _(e):
            for f in p.ops["scalar"]:
                f(e)
        @block.vector
        def _(e):
            for f in p.ops["vector"]:
                f(e)
        @block.sync
        def _(e):
            for f in p.ops["sync"]:
                f(e)
        @block.gpsimd
        def _(e):
            for f in p.ops["gpsimd"]:
                f(e)
    es.close()
    return nc


def gen(p, dd, sb, V, pA, pB, dbg, out_cm, attn_sp, debug):
    for name, dst in (("ident", "ident"), ("ones_r", "ones_r"), ("ones_rf", "ones_rf"),
                      ("onesc", "onesc"), ("onescf", "onescf"),
                      ("blkd", "blkd"), ("o642", "o642"), ("i2", "i2"),
                      ("sel_r", "sel_r"), ("sel_c", "sel_c"),
                      ("c_col", "c_col"), ("bq_c", "bq_h"), ("bk_c", "bk_c"),
                      ("bv_c", "bv_c"), ("bo_c", "bo_c"), ("b1_c", "b1_c"),
                      ("b2_c", "b2_c")):
        p.do("sync", lambda e, d=dd[name], s=sb[dst]: e.dma_start(out=s[:], in_=d),
             inc="ds", inc_n=16)
    p.do("sync", lambda e: e.dma_start(out=sb["gate"][0:1, :], in_=dd["galpha"]),
         inc="ds", inc_n=16)
    ds_consts = p.cnt["ds"]

    dg_relg = p.do("gpsimd", lambda e: e.dma_start(out=sb["relg"][:], in_=dd["relg"]),
                   inc="dgm", inc_n=16)
    wmod_idx = [(h, c) for h in range(2) for c in range(NCH)]
    wmod_loads = {}
    def emit_wmod_load(i, wait_pe=None):
        half, c = wmod_idx[i]
        if wait_pe is not None:
            p.wait("gpsimd", "pe", wait_pe)
        wmod_loads[(half, c)] = ("dgx0" if c % 2 == 0 else "dgx1",
            p.do("gpsimd", lambda e, half=half, c=c: e.dma_start(
                out=V["wmod_ch"][:, c % 2, :],
                in_=dd["wmod"][c * 128:(c + 1) * 128, half * 3 * D:(half + 1) * 3 * D]),
                inc="dgx0" if c % 2 == 0 else "dgx1", inc_n=16))
    emit_wmod_load(0)
    emit_wmod_load(1)
    x_loads = {}
    def emit_x_load(sl, wait_dv=None):
        ap = dd["xT"][:, sl * 512:(sl + 1) * 512].rearrange("(c p) t -> p c t", p=128)
        if wait_dv is not None:
            p.wait("gpsimd", "dv", wait_dv)
        x_loads[sl] = ("dgy0" if sl % 2 == 0 else "dgy1",
            p.do("gpsimd", lambda e, ap=ap, sl=sl: e.dma_start(
                out=sb["xslab"][:, sl % 2, :, :], in_=ap),
                inc="dgy0" if sl % 2 == 0 else "dgy1", inc_n=16))
    emit_x_load(0)
    emit_x_load(1)

    p.wait("scalar", "ds", ds_consts)
    p.do("scalar", lambda e: e.activation(out=sb["tmpa"][:, 0:8], in_=sb["c_col"][:],
                                          func=AF.Sigmoid))
    p.do("scalar", lambda e: e.activation(out=sb["tmpa"][0:1, 8:9], in_=sb["gate"][0:1, :],
                                          func=AF.Sigmoid), inc="ac")
    p.wait("vector", "ac", p.cnt["ac"])
    dv_silu = p.do("vector", lambda e: e.tensor_tensor(
        out=sb["silu_c"][:], in0=sb["c_col"][:], in1=sb["tmpa"][:, 0:8], op=OP.mult), inc="dv")
    p.wait("tensor", "dv", dv_silu)
    for half in range(2):
        if half == 1:
            p.wait("tensor", "dv", p.cnt["dv"])
        for c in range(NCH):
            wsem, wval = wmod_loads[(half, c)]
            p.wait("tensor", wsem, wval)
            for npos in range(6):
                ps = pA[0:1, npos * 512:(npos + 1) * 512] if npos < 4 else \
                     pB[0:1, (npos - 4) * 512:(npos - 3) * 512]
                p.do("tensor", lambda e, ps=ps, c=c, npos=npos: e.matmul(
                    ps, sb["silu_c"][:, c:c + 1],
                    V["wmod_ch"][:, c % 2, npos * 512:(npos + 1) * 512],
                    start=(c == 0), stop=(c == NCH - 1)),
                    inc="pe" if npos == 5 else None)
            gi = half * NCH + c
            if gi + 2 < 16:
                emit_wmod_load(gi + 2, wait_pe=p.cnt["pe"])
        p.wait("vector", "pe", p.cnt["pe"])
        for npos in range(6):
            ps = pA[0:1, npos * 512:(npos + 1) * 512] if npos < 4 else \
                 pB[0:1, (npos - 4) * 512:(npos - 3) * 512]
            p.do("vector", lambda e, ps=ps, half=half, npos=npos: e.tensor_copy(
                out=V["mrow"][0:1, half * 3 * D + npos * 512: half * 3 * D + (npos + 1) * 512],
                in_=ps), inc="dv")
    dv_mod = p.cnt["dv"]

    p.wait("scalar", "dv", dv_mod)
    for idx in (2, 5):
        sl = slice(idx * D, (idx + 1) * D)
        for ph in range(2):
            p.wait("scalar", "dv", p.cnt["dv"])
            p.do("scalar", lambda e, sl=sl, ph=ph: e.activation(
                out=sb["rowr"][0:1, :], in_=V["mrow"][0:1, sl][:, ph * 512:(ph + 1) * 512],
                func=AF.Abs), inc="ac")
            p.wait("vector", "ac", p.cnt["ac"])
            p.do("vector", lambda e: e.tensor_scalar(
                out=sb["rowr"][0:1, :], in0=sb["rowr"][0:1, :],
                scalar1=0.25, scalar2=None, op0=OP.max))
            p.do("vector", lambda e, sl=sl, ph=ph: e.tensor_scalar(
                out=sb["rowm"][0:1, :], in0=V["mrow"][0:1, sl][:, ph * 512:(ph + 1) * 512],
                scalar1=0.0, scalar2=2.0, op0=OP.is_ge, op1=OP.mult))
            p.hop("vector")
            p.do("vector", lambda e, sl=sl, ph=ph: e.scalar_tensor_tensor(
                out=V["mrow"][0:1, sl][:, ph * 512:(ph + 1) * 512],
                in0=sb["rowm"][0:1, :], scalar=-1.0,
                in1=sb["rowr"][0:1, :], op0=OP.add, op1=OP.mult), inc="dv")
    dv_clamp = p.cnt["dv"]

    p.wait("tensor", "dv", dv_clamp)
    names = ("sa", "ca1", "ga", "sbv", "cb1", "gb")
    for vi, vn in enumerate(names):
        if vi > 0:
            p.wait("tensor", "dv", p.cnt["dv"])
        for c in range(NCH):
            p.do("tensor", lambda e, vi=vi, c=c: e.matmul(
                pB[:, 1024 + c:1025 + c],
                V["mrow"][0:1, vi * D + c * 128: vi * D + (c + 1) * 128],
                sb["ones_rf"][0:1, 0:1], start=True, stop=True),
                inc="pe" if c == NCH - 1 else None)
        p.wait("vector", "pe", p.cnt["pe"])
        if vn in ("ca1", "cb1"):
            p.do("vector", lambda e, vn=vn: e.tensor_scalar(
                out=sb[vn][:], in0=pB[:, 1024:1032], scalar1=1.0, scalar2=None,
                op0=OP.add), inc="dv")
        else:
            p.do("vector", lambda e, vn=vn: e.tensor_copy(
                out=sb[vn][:], in_=pB[:, 1024:1032]), inc="dv")
    p.do("vector", lambda e: e.tensor_copy(out=sb["tinyb"][0:1, 0:1], in_=sb["tmpa"][0:1, 8:9]),
         inc="dv")
    p.wait("tensor", "dv", p.cnt["dv"])
    p.do("tensor", lambda e: e.matmul(pB[:, 1024:1025], sb["ones_r"][:],
                                      sb["tinyb"][0:1, 0:1], start=True, stop=True), inc="pe")
    p.wait("vector", "pe", p.cnt["pe"])
    p.do("vector", lambda e: e.tensor_copy(out=sb["gate"][:], in_=pB[:, 1024:1025]))
    p.do("vector", lambda e: e.tensor_scalar(
        out=sb["gate1m"][:], in0=pB[:, 1024:1025], scalar1=-1.0, scalar2=1.0,
        op0=OP.mult, op1=OP.add))
    p.do("vector", lambda e: e.tensor_scalar(
        out=sb["bq_h"][:], in0=sb["bq_h"][:], scalar1=0.125, scalar2=None, op0=OP.mult),
        inc="dv")
    dv_modcols = p.cnt["dv"]
    if debug == 1:
        p.wait("gpsimd", "dv", dv_clamp)
        p.do("gpsimd", lambda e: e.dma_start(out=dbg["d_mod"], in_=V["mrow"]),
             inc="dgd", inc_n=16)
        p.dbg_barrier()

    for sl in range(8):
        xs = sb["xslab"][:, sl % 2, :, :]
        if sl % 2 == 0:
            rsum, rsq = pB[0:1, 1024:1536], pB[0:1, 1536:2048]
            rrep, rmrep = pB[:, 0:512], pB[:, 512:1024]
        else:
            rsum, rsq = pA[0:1, 0:512], pA[0:1, 512:1024]
            rrep, rmrep = pA[:, 1024:1536], pA[:, 1536:2048]
        xsem, xval = x_loads[sl]
        p.wait("vector", xsem, xval)
        if sl == 0:
            p.wait("vector", "dv", dv_modcols)
        p.do("vector", lambda e, xs=xs: e.tensor_copy(out=V["xbf1"], in_=xs))
        p.hop("vector")
        p.do("vector", lambda e: e.tensor_tensor(
            out=V["xsq1"], in0=V["xbf1"], in1=V["xbf1"], op=OP.mult), inc="dv")
        p.wait("tensor", "dv", p.cnt["dv"])
        for c in range(NCH):
            p.do("tensor", lambda e, c=c, rsum=rsum: e.matmul(
                rsum, sb["onesc"][:], V["xbf1"][:, c, :],
                start=(c == 0), stop=(c == NCH - 1)))
        for c in range(NCH):
            p.do("tensor", lambda e, c=c, rsq=rsq: e.matmul(
                rsq, sb["onesc"][:], V["xsq1"][:, c, :],
                start=(c == 0), stop=(c == NCH - 1)),
                inc="pe" if c == NCH - 1 else None)
        p.wait("vector", "pe", p.cnt["pe"])
        emit_ln_rows(p, sb, rsum, rsq, 512)
        p.wait("tensor", "dv", p.cnt["dv"])
        p.do("tensor", lambda e, rrep=rrep: e.matmul(
            rrep, sb["ones_r"][:], sb["rowb"][:], start=True, stop=True), inc="pe")
        p.wait("vector", "pe", p.cnt["pe"])
        p.do("vector", lambda e: e.tensor_copy(out=sb["rowb"][:], in_=sb["rowmr"][:]), inc="dv")
        p.wait("tensor", "dv", p.cnt["dv"])
        p.do("tensor", lambda e, rmrep=rmrep: e.matmul(
            rmrep, sb["ones_r"][:], sb["rowb"][:], start=True, stop=True), inc="pe")
        p.wait("vector", "pe", p.cnt["pe"])
        for c in range(NCH):
            p.do("vector", lambda e, xs=xs, c=c, rrep=rrep: e.tensor_tensor(
                out=sb["tmpa"][:], in0=xs[:, c, :], in1=rrep, op=OP.mult))
            p.hop("vector")
            p.do("vector", lambda e, c=c, rmrep=rmrep: e.tensor_tensor(
                out=sb["tmpa"][:], in0=sb["tmpa"][:], in1=rmrep, op=OP.subtract))
            p.hop("vector")
            p.do("vector", lambda e, sl=sl, c=c: e.tensor_scalar(
                out=V["h1"][:, c, sl * 512:(sl + 1) * 512], in0=sb["tmpa"][:],
                scalar1=sb["ca1"][:, c:c + 1], scalar2=sb["sa"][:, c:c + 1],
                op0=OP.mult, op1=OP.add), inc="dv" if c == NCH - 1 else None)
            if c < NCH - 1:
                p.hop("vector")
        if sl + 2 < 8:
            emit_x_load(sl + 2, wait_dv=p.cnt["dv"])
    dv_h1 = p.cnt["dv"]
    if debug == 1:
        p.wait("gpsimd", "dv", dv_h1)
        p.do("gpsimd", lambda e: e.dma_start(out=dbg["d_h1"], in_=V["h1"]),
             inc="dgd", inc_n=16)
        p.dbg_barrier()

    p.wait("scalar", "dgm", dg_relg)
    p.wait("tensor", "dv", dv_h1)
    for ch in range(5):
        cs = slice(ch * 512, (ch + 1) * 512) if ch < 4 else slice(2048, 2049)
        n = 512 if ch < 4 else 1
        p.wait("scalar", "pe", p.cnt["pe"])
        p.do("scalar", lambda e, cs=cs, n=n: e.activation(
            out=sb["sqs"][0:64, 0:n], in_=sb["relg"][0:64, cs], func=AF.Square), inc="ac")
        p.wait("tensor", "ac", p.cnt["ac"])
        if ch > 0:
            p.wait("tensor", "dv", p.cnt["dv"])
        p.do("tensor", lambda e, n=n: e.matmul(
            pB[0:2, 1024:1024 + n], sb["o642"][:], sb["sqs"][0:64, 0:n],
            start=True, stop=True), inc="pe")
        p.wait("vector", "pe", p.cnt["pe"])
        if ch == 0:
            p.do("vector", lambda e: e.reduce_max(
                out=sb["lnrel"][:], in_=pB[0:2, 1024:1536], axis=AX.X), inc="dv")
        else:
            p.do("vector", lambda e, n=n: e.reduce_max(
                out=sb["tiny"][:, 9:10], in_=pB[0:2, 1024:1024 + n], axis=AX.X))
            p.hop("vector")
            p.do("vector", lambda e: e.tensor_tensor(
                out=sb["lnrel"][:], in0=sb["lnrel"][:], in1=sb["tiny"][:, 9:10],
                op=OP.max), inc="dv")
    p.wait("scalar", "dv", p.cnt["dv"])
    p.do("scalar", lambda e: e.activation(out=sb["lnrel"][:], in_=sb["lnrel"][:],
                                          func=AF.Ln), inc="ac")
    ac_lnrel = p.cnt["ac"]
    p.do("vector", lambda e: e.memset(sb["eW"][:], 0.0))
    p.hop("vector")
    p.do("vector", lambda e: e.tensor_copy(out=sb["eW"][0:64, 0:1], in_=sb["relg"][0:64, 0:1]))
    p.do("vector", lambda e: e.tensor_copy(out=sb["eW"][0:64, 1:2],
                                           in_=sb["relg"][0:64, 2048:2049]))
    p.do("vector", lambda e: e.tensor_copy(out=sb["eW"][64:128, 2:3],
                                           in_=sb["relg"][64:128, 0:1]))
    p.do("vector", lambda e: e.tensor_copy(out=sb["eW"][64:128, 3:4],
                                           in_=sb["relg"][64:128, 2048:2049]), inc="dv")
    dv_eW = p.cnt["dv"]

    p.pending = None
    p.pguse = 0
    p.guses = {}
    p.dp_hist = []
    p.pmm_rot = 0
    p.sdone_last = 0
    p.g65pe = 0
    p.o65ac = 0
    p.pe_vlast = p.cnt["pe"]
    p.t8_dv = 0
    p.rcol_dv = 0
    p.denb_dv = 0
    kb_ranges = [band_range(qb) for qb in range(16)]
    for hp in range(NP):
        gen_pair(p, dd, sb, V, pA, pB, hp, kb_ranges, dbg, debug, ac_lnrel, dv_eW)

    p.flush_pending()
    p.wait("gpsimd", "pe", p.cnt["pe"])
    p.wait("gpsimd", "dv", p.cnt["dv"])
    p.wait("gpsimd", "ac", p.cnt["ac"])
    p.wait("gpsimd", "dsc", p.cnt["dsc"])
    dg_wo = p.do("gpsimd", lambda e: e.dma_start(
        out=V["wo"], in_=dd["woT"].rearrange("(c p) m -> p c m", p=128)), inc="dgo", inc_n=16)
    p.wait("tensor", "dgo", dg_wo)
    p.wait("tensor", "dv", p.cnt["dv"])
    p.wait("tensor", "ac", p.cnt["ac"])
    p.wait("tensor", "dp", p.cnt["dp"])
    p.wait("tensor", "dsc", p.cnt["dsc"])
    stag_dma = {}
    for blk in range(NCH):
        if blk > 0:
            p.wait("tensor", "dv", p.cnt["dv"])
        if blk >= 2:
            ssem, sval = stag_dma[blk - 2]
            p.wait("vector", ssem, sval)
        for c in range(NCH):
            for nch in range(4):
                ps = (pA if nch < 2 else pB)[:, (nch % 2) * 1024:(nch % 2) * 1024 + 512]
                p.do("tensor", lambda e, blk=blk, c=c, nch=nch, ps=ps: e.matmul(
                    ps, V["wo"][:, c, blk * 128:(blk + 1) * 128],
                    V["comb"][:, c, nch * 512:(nch + 1) * 512],
                    start=(c == 0), stop=(c == NCH - 1)),
                    inc="pe" if (c == NCH - 1 and nch == 3) else None)
        p.wait("vector", "pe", p.cnt["pe"])
        for nch in range(4):
            ps = (pA if nch < 2 else pB)[:, (nch % 2) * 1024:(nch % 2) * 1024 + 512]
            p.do("vector", lambda e, blk=blk, nch=nch, ps=ps: e.tensor_scalar(
                out=V["stag"][:, blk % 2, nch * 512:(nch + 1) * 512], in0=ps, scalar1=1.0,
                scalar2=sb["bo_c"][:, blk:blk + 1], op0=OP.mult, op1=OP.add),
                inc="dv" if nch == 3 else None)
        p.wait("sync", "dv", p.cnt["dv"])
        stag_dma[blk] = ("dss0" if blk % 2 == 0 else "dss1",
            p.do("sync", lambda e, blk=blk: e.dma_start(
                out=attn_sp[blk], in_=V["stag"][:, blk % 2, :]),
                inc="dss0" if blk % 2 == 0 else "dss1", inc_n=16))
    ds_attn = (p.cnt["dss0"], p.cnt["dss1"])
    if debug:
        p.wait("gpsimd", "dss0", ds_attn[0])
        p.wait("gpsimd", "dss1", ds_attn[1])
        p.do("gpsimd", lambda e: e.dma_start(out=dbg["d_attn"], in_=attn_sp), inc="dgd", inc_n=16)
        p.do("gpsimd", lambda e: e.dma_start(out=dbg["d_comb"], in_=V["comb"]), inc="dgd", inc_n=16)
        p.dbg_barrier()

    gen_ffn(p, dd, sb, V, pA, pB, out_cm, attn_sp, ds_attn, dbg, debug)

    p.wait("sync", "dso", p.cnt["dso"])
    p.wait("sync", "dsc", p.cnt["dsc"])
    p.wait("sync", "dsg", p.cnt["dsg"])
    p.wait("sync", "dss0", p.cnt["dss0"])
    p.wait("sync", "dss1", p.cnt["dss1"])
    p.wait("sync", "ds", p.cnt["ds"])
    for s in ("dgm", "dgz", "dgw", "dgo", "dgx0", "dgx1", "dgy0", "dgy1", "dgf", "dgd",
                  "hv", "ha", "ht", "hg", "dp", "dsh"):
        p.wait("gpsimd", s, p.cnt[s])


def emit_ln_rows(p, sb, rsum, rsq, n):
    p.do("vector", lambda e, rsum=rsum: e.tensor_scalar(
        out=sb["rowm"][0:1, 0:n], in0=rsum, scalar1=1.0 / D, scalar2=None, op0=OP.mult))
    p.hop("vector")
    p.do("vector", lambda e: e.tensor_tensor(
        out=sb["rowr"][0:1, 0:n], in0=sb["rowm"][0:1, 0:n], in1=sb["rowm"][0:1, 0:n],
        op=OP.mult))
    p.hop("vector")
    p.do("vector", lambda e: e.tensor_scalar(
        out=sb["rowr"][0:1, 0:n], in0=sb["rowr"][0:1, 0:n], scalar1=-1e-5, scalar2=None,
        op0=OP.add))
    p.hop("vector")
    p.do("vector", lambda e, rsq=rsq: e.scalar_tensor_tensor(
        out=sb["rowr"][0:1, 0:n], in0=rsq, scalar=1.0 / D, in1=sb["rowr"][0:1, 0:n],
        op0=OP.mult, op1=OP.subtract), inc="dv")
    p.wait("scalar", "dv", p.cnt["dv"])
    p.do("scalar", lambda e: e.activation(out=sb["rowr"][0:1, 0:n],
                                          in_=sb["rowr"][0:1, 0:n], func=AF.Sqrt), inc="ac")
    p.wait("vector", "ac", p.cnt["ac"])
    p.do("vector", lambda e: e.reciprocal(out=sb["rowr"][0:1, 0:n], in_=sb["rowr"][0:1, 0:n]))
    p.hop("vector")
    p.do("vector", lambda e: e.tensor_tensor(
        out=sb["rowmr"][0:1, 0:n], in0=sb["rowm"][0:1, 0:n], in1=sb["rowr"][0:1, 0:n],
        op=OP.mult))
    p.do("vector", lambda e: e.tensor_copy(out=sb["rowb"][0:1, 0:n],
                                           in_=sb["rowr"][0:1, 0:n]), inc="dv")


def gen_pair(p, dd, sb, V, pA, pB, hp, kb_ranges, dbg, debug, ac_lnrel, dv_eW):
    p.wait("gpsimd", "pe", p.pe_vlast)
    p.wait("gpsimd", "dp", p.cnt["dp"])
    p.wait("gpsimd", "dsg", p.cnt["dsg"])
    p.wait("gpsimd", "dsh", p.cnt["dsh"])
    lw = []
    for wn, dn in (("wq", "wq_p"), ("wk", "wk_p"), ("wv", "wv_p")):
        lw.append(p.do("gpsimd", lambda e, wn=wn, dn=dn: e.dma_start(
            out=V["wqkv_" + wn],
            in_=dd[dn][hp].rearrange("(c p) m -> p c m", p=128)), inc="dgw", inc_n=16))
    p.wait("gpsimd", "pe", p.g65pe)
    dg_mz = p.do("gpsimd", lambda e: e.dma_start(out=sb["mz"][:], in_=dd["mzp"][hp]),
                 inc="dgz", inc_n=16)
    p.flush_pending()
    p.slot2_fresh = True
    p.pair_pmm_first = True

    p.wait("tensor", "dgw", lw[2])
    p.wait("tensor", "ac", p.cnt["ac"])
    p.wait("tensor", "dv", p.cnt["dv"])
    p.wait("tensor", "dp", p.cnt["dp"])
    for c in range(NCH):
        for nch in range(4):
            p.do("tensor", lambda e, c=c, nch=nch: e.matmul(
                pA[:, nch * 512:(nch + 1) * 512], V["wqkv_wq"][:, c, :],
                V["h1"][:, c, nch * 512:(nch + 1) * 512],
                start=(c == 0), stop=(c == NCH - 1)),
                inc="pe" if (c == NCH - 1 and nch == 3) else None)
    p.wait("vector", "pe", p.cnt["pe"])
    p.do("vector", lambda e: e.tensor_scalar(
        out=V["qh"][:, :], in0=pA[:, 0:2048], scalar1=0.125,
        scalar2=sb["bq_h"][:, hp:hp + 1], op0=OP.mult, op1=OP.add), inc="dv")
    dv_q = p.cnt["dv"]
    for half in range(2):
        p.wait("tensor", "dv", p.cnt["dv"])
        for c in range(NCH):
            for nch in range(4):
                t0 = half * 2048 + nch * 512
                p.do("tensor", lambda e, c=c, nch=nch, t0=t0: e.matmul(
                    pA[:, nch * 512:(nch + 1) * 512], V["wqkv_wk"][:, c, :],
                    V["h1"][:, c, t0:t0 + 512],
                    start=(c == 0), stop=(c == NCH - 1)),
                    inc="pe" if (c == NCH - 1 and nch == 3) else None)
        p.wait("vector", "pe", p.cnt["pe"])
        p.do("vector", lambda e, half=half: e.tensor_scalar(
            out=V["kk"][:, half * 2048:(half + 1) * 2048], in0=pA[:, 0:2048], scalar1=1.0,
            scalar2=sb["bk_c"][:, hp:hp + 1], op0=OP.mult, op1=OP.add), inc="dv")
    for vch in range(4):
        p.wait("tensor", "dv", p.cnt["dv"])
        for c in range(NCH):
            for nn in range(2):
                t0 = vch * 1024 + nn * 512
                p.do("tensor", lambda e, c=c, nn=nn, t0=t0: e.matmul(
                    pB[:, nn * 512:(nn + 1) * 512], V["wqkv_wv"][:, c, :],
                    V["h1"][:, c, t0:t0 + 512],
                    start=(c == 0), stop=(c == NCH - 1)),
                    inc="pe" if (c == NCH - 1 and nn == 1) else None)
        p.wait("vector", "pe", p.cnt["pe"])
        p.do("vector", lambda e: e.tensor_scalar(
            out=V["vf"][:], in0=pB[:, 0:1024], scalar1=1.0,
            scalar2=sb["bv_c"][:, hp:hp + 1], op0=OP.mult, op1=OP.add), inc="dv")
        p.wait("tensor", "dv", p.cnt["dv"])
        pBb = pB[:, 1024:1536].bitcast(BF16)
        for t in range(8):
            p.do("tensor", lambda e, t=t, pBb=pBb: e.transpose(
                pBb[:, t * 128:(t + 1) * 128], V["vf"][:, t * 128:(t + 1) * 128],
                sb["ident"][:]), inc="pe" if t == 7 else None)
        p.wait("vector", "pe", p.cnt["pe"])
        for t in range(8):
            kb = vch * 8 + t
            p.do("vector", lambda e, t=t, kb=kb, pBb=pBb: e.tensor_copy(
                out=V["vp"][:, kb, 0, 0:64],
                in_=pBb[:, t * 128:t * 128 + 64]))
            p.do("vector", lambda e, t=t, kb=kb, pBb=pBb: e.tensor_copy(
                out=V["vp"][:, kb, 1, 0:64],
                in_=pBb[:, t * 128 + 64:t * 128 + 128]),
                inc="dv" if t == 7 else None)
        p.wait("tensor", "dv", p.cnt["dv"])
    p.do("vector", lambda e: e.memset(V["vp"][:, :, :, 64:65], 1.0), inc="dv")
    p.pe_vlast = p.cnt["pe"]


    for si, (srcv, n) in enumerate(((V["qh"], SQ), (V["kk"], S))):
        for ch in range(n // 512):
            p.do("vector", lambda e, srcv=srcv, ch=ch: e.tensor_tensor(
                out=sb["sqs"][:], in0=srcv[:, ch * 512:(ch + 1) * 512],
                in1=srcv[:, ch * 512:(ch + 1) * 512], op=OP.mult), inc="dv")
            p.wait("tensor", "dv", p.cnt["dv"])
            p.do("tensor", lambda e: e.matmul(
                pB[0:2, 1024:1536], sb["blkd"][:], sb["sqs"][:],
                start=True, stop=True), inc="pe")
            p.wait("vector", "pe", p.cnt["pe"])
            dst = sb["tiny"][:, si:si + 1]
            if ch == 0:
                p.do("vector", lambda e, dst=dst: e.reduce_max(
                    out=dst, in_=pB[0:2, 1024:1536], axis=AX.X), inc="dv")
            else:
                p.do("vector", lambda e: e.reduce_max(
                    out=sb["tiny"][:, 9:10], in_=pB[0:2, 1024:1536], axis=AX.X))
                p.hop("vector")
                p.do("vector", lambda e, dst=dst: e.tensor_tensor(
                    out=dst, in0=dst, in1=sb["tiny"][:, 9:10], op=OP.max), inc="dv")
    p.wait("scalar", "dv", p.cnt["dv"])
    p.wait("scalar", "ac", ac_lnrel)
    p.do("scalar", lambda e: e.activation(out=sb["tiny"][:, 2:3], in_=sb["tiny"][:, 0:1],
                                          func=AF.Ln))
    p.do("scalar", lambda e: e.activation(out=sb["tiny"][:, 3:4], in_=sb["tiny"][:, 1:2],
                                          func=AF.Ln), inc="ac")
    p.wait("vector", "ac", p.cnt["ac"])
    p.do("vector", lambda e: e.tensor_tensor(out=sb["tiny"][:, 4:5], in0=sb["tiny"][:, 2:3],
                                             in1=sb["tiny"][:, 3:4], op=OP.add))
    p.hop("vector")
    p.do("vector", lambda e: e.tensor_scalar(out=sb["tiny"][:, 4:5], in0=sb["tiny"][:, 4:5],
                                             scalar1=0.5, scalar2=None, op0=OP.mult))
    p.do("vector", lambda e: e.tensor_tensor(out=sb["tiny"][:, 5:6], in0=sb["tiny"][:, 2:3],
                                             in1=sb["lnrel"][:], op=OP.add))
    p.hop("vector")
    p.do("vector", lambda e: e.tensor_scalar(out=sb["tiny"][:, 5:6], in0=sb["tiny"][:, 5:6],
                                             scalar1=0.5, scalar2=None, op0=OP.mult), inc="dv")
    p.wait("scalar", "dv", p.cnt["dv"])
    p.do("scalar", lambda e: e.activation(out=sb["tiny"][:, 6:7], in_=sb["tiny"][:, 4:5],
                                          func=AF.Exp))
    p.do("scalar", lambda e: e.activation(out=sb["tiny"][:, 7:8], in_=sb["tiny"][:, 5:6],
                                          func=AF.Exp), inc="ac")
    p.wait("vector", "ac", p.cnt["ac"])
    p.do("vector", lambda e: e.tensor_tensor(out=sb["tiny"][:, 8:9], in0=sb["tiny"][:, 6:7],
                                             in1=sb["tiny"][:, 7:8], op=OP.add))
    p.hop("vector")
    p.do("vector", lambda e: e.tensor_copy(out=sb["tinyb"][:, 0:1], in_=sb["tiny"][:, 8:9]),
         inc="dv")
    p.wait("tensor", "dv", p.cnt["dv"])
    p.do("tensor", lambda e: e.matmul(pB[0:1, 1024:1026], sb["tinyb"][:, 0:1], sb["i2"][:],
                                      start=True, stop=True), inc="pe")
    p.wait("vector", "pe", p.cnt["pe"])
    p.do("vector", lambda e: e.reduce_max(out=sb["tiny"][0:1, 9:10], in_=pB[0:1, 1024:1026],
                                          axis=AX.X))
    p.hop("vector")
    p.do("vector", lambda e: e.tensor_copy(out=sb["tinyb"][0:1, 1:2],
                                           in_=sb["tiny"][0:1, 9:10]), inc="dv")
    p.wait("tensor", "dv", p.cnt["dv"])
    p.do("tensor", lambda e: e.matmul(pB[:, 1026:1027], sb["ones_r"][:], sb["tinyb"][0:1, 1:2],
                                      start=True, stop=True), inc="pe")
    p.wait("scalar", "pe", p.cnt["pe"])
    p.do("scalar", lambda e: e.activation(out=sb["negb"][:], in_=pB[:, 1026:1027],
                                          func=AF.Copy, scale=-1.0), inc="ac")
    ac_negb = p.cnt["ac"]

    p.wait("tensor", "dv", dv_eW)
    p.wait("tensor", "ac", ac_negb)
    for nch in range(4):
        reg = pB[0:4, 1024 + (nch % 2) * 512:1536 + (nch % 2) * 512]
        if nch >= 2:
            p.wait("tensor", "dv", p.cnt["dv"])
        p.do("tensor", lambda e, nch=nch, reg=reg: e.matmul(
            reg, sb["eW"][:], V["qh"][:, nch * 512:(nch + 1) * 512],
            start=True, stop=True), inc="pe")
        p.wait("vector", "pe", p.cnt["pe"])
        p.do("vector", lambda e, nch=nch, reg=reg: e.tensor_copy(
            out=sb["edges4"][:, nch * 512:(nch + 1) * 512],
            in_=reg), inc="dv")
    dv_edges = p.cnt["dv"]
    if debug == 1 and hp == 0:
        p.wait("gpsimd", "ac", ac_negb)
        p.wait("gpsimd", "dv", dv_edges)
        for nm, src in (("d_negb", sb["negb"][:]), ("d_qh", V["qh"]), ("d_k", V["kk"]),
                        ("d_vp", V["vp"]), ("d_edge", sb["edges4"][:])):
            p.do("gpsimd", lambda e, nm=nm, src=src: e.dma_start(out=dbg[nm], in_=src),
                 inc="dgd", inc_n=16)
        p.dbg_barrier()

    ec_ev = {}
    for qb in range(16):
        pos = 1024 + (qb % 2) * 512
        if qb < 2:
            p.wait("tensor", "dv", dv_edges)
        else:
            p.wait("tensor", "dv", ec_ev[qb - 2])
        for j in range(4):
            p.do("tensor", lambda e, qb=qb, j=j, pos=pos: e.matmul(
                pB[:, pos + j:pos + j + 1], sb["edges4"][:, qb * 128:(qb + 1) * 128],
                sb["sel_c"][:, j:j + 1], start=True, stop=True),
                inc="pe" if j == 3 else None)
        p.wait("vector", "pe", p.cnt["pe"])
        ec_ev[qb] = p.do("vector", lambda e, qb=qb, pos=pos: e.tensor_copy(
            out=sb["ecols"][:, qb * 4:(qb + 1) * 4], in_=pB[:, pos:pos + 4]), inc="dv")

    p.pair_dv0 = p.cnt["dv"]
    p.t8_dv = p.pair_dv0
    for qg in range(NG):
        gen_group(p, sb, V, pA, pB, hp, qg, kb_ranges, dbg, debug)


def rg_layout(qg):
    offs = {}
    pos = 0
    for qi in range(GQB):
        qb = qg * GQB + qi
        kb_lo, kb_hi = band_range(qb)
        w = (kb_hi - kb_lo + 1) * 128
        for h in range(2):
            offs[(qi, h)] = (pos, w)
            pos += w
    assert pos <= 18432, pos
    return offs


def gen_group(p, sb, V, pA, pB, hp, qg, kb_ranges, dbg, debug):
    qbs = [qg * GQB + i for i in range(GQB)]
    q0g = qg * GQB * 128
    offs = rg_layout(qg)
    denbL = sb["rowm"][:].bitcast(BF16)
    denbG = sb["rowr"][:].bitcast(BF16)
    row_ext = sb["arena"][:].ap[0][0]

    p.grp_low_first = True
    sgv = V["sig"] if qg % 2 == 0 else V["sigB"]
    p.wait("vector", "pe", p.g65pe)
    p.do("vector", lambda e: e.tensor_scalar(
        out=V["sigm"][:, :], in0=V["qh"][:, q0g:q0g + 512], scalar1=0.0, scalar2=None,
        op0=OP.min), inc="dv")
    p.wait("scalar", "dv", p.cnt["dv"])
    p.do("scalar", lambda e, sgv=sgv: e.activation(out=sgv[:, :], in_=V["sigm"][:, :],
                                                   func=AF.Exp, scale=8.0), inc="ac")
    p.wait("vector", "ac", p.cnt["ac"])
    p.do("vector", lambda e: e.tensor_scalar(
        out=V["sigm"][:, :], in0=V["qh"][:, q0g:q0g + 512], scalar1=0.0, scalar2=None,
        op0=OP.max))
    p.hop("vector")
    p.do("vector", lambda e, sgv=sgv: e.scalar_tensor_tensor(
        out=sgv[:, :], in0=V["sigm"][:, :], scalar=8.0, in1=sgv[:, :],
        op0=OP.mult, op1=OP.add), inc="dv")
    p.sig_dv = p.cnt["dv"]

    gmap = {}
    for qi, qb in enumerate(qbs):
        kb_lo, kb_hi = kb_ranges[qb]
        wb = (kb_hi - kb_lo + 1) * 128
        colstart = kb_lo * 128 - qb * 128 + 1024 + PAD
        assert colstart - 127 >= 0 and colstart + wb - 1 <= CP - 1
        for h in range(2):
            u = p.pguse
            sv = V["pgs"][u % 3]
            if u - 3 in p.guses:
                gs, gv = p.guses.pop(u - 3)
                p.wait("vector", gs, gv)
            if u % 3 == 2 and p.slot2_fresh:
                p.slot2_fresh = False
                p.wait("vector", "pe", p.pe_vlast)
            ecl = qb * 4 + 2 * h
            p.do("vector", lambda e, sv=sv, ecl=ecl: e.tensor_copy(
                out=sv[:, 0:PAD],
                in_=sb["ecols"][:, ecl:ecl + 1].broadcast_to([128, PAD])), inc="dp")
            p.do("vector", lambda e, sv=sv, ecl=ecl: e.tensor_copy(
                out=sv[:, PAD + 2048:CP],
                in_=sb["ecols"][:, ecl + 1:ecl + 2].broadcast_to(
                    [128, CP - PAD - 2048])), inc="dp")
            for j in range(4):
                r = p.pmm_rot % 4
                p.pmm_rot += 1
                c0 = (1024 + r * 512) if r < 2 else ((r - 2) * 512)
                reg = pB[:, c0:c0 + 512]
                if len(p.dp_hist) >= 4:
                    p.wait("tensor", "dp", p.dp_hist[-4])
                if r == 2 and p.grp_low_first:
                    p.grp_low_first = False
                    p.wait("tensor", "ac", p.o65ac)
                    p.wait("tensor", "dv", max(p.denb_dv, p.rcol_dv))
                if p.pair_pmm_first:
                    p.pair_pmm_first = False
                    p.wait("tensor", "dv", p.pair_dv0)
                p.do("tensor", lambda e, reg=reg, h=h, qb=qb, j=j: e.matmul(
                    reg, V["qh"][h * 64:(h + 1) * 64, qb * 128:(qb + 1) * 128],
                    sb["relg"][h * 64:(h + 1) * 64, j * 512:(j + 1) * 512],
                    start=True, stop=True, tile_position=(h * 64, 0)), inc="pe")
                p.wait("vector", "pe", p.cnt["pe"])
                ev = p.do("vector", lambda e, sv=sv, j=j, reg=reg: e.tensor_copy(
                    out=sv[:, PAD + j * 512:PAD + (j + 1) * 512], in_=reg), inc="dp")
                p.dp_hist.append(ev)
            qeng, qsem = ("sync", "dsg") if h == 0 else ("gpsimd", "dsh")
            p.wait(qeng, qsem, p.cnt[qsem])
            p.wait(qeng, "dp", p.cnt["dp"])
            p.wait(qeng, "pe", p.sdone_last)
            off0, w0 = offs[(qi, h)]
            assert w0 == wb
            src = AP(sv.tensor, sv.offset + colstart, [[row_ext - 1, 128], [1, wb]])
            g = p.do(qeng, lambda e, src=src, off0=off0, wb=wb: e.dma_start(
                out=V["rg"][:, off0:off0 + wb], in_=src), inc=qsem, inc_n=16)
            gmap[(qi, h)] = (qsem, g)
            p.guses[u] = (qsem, g)
            p.pguse += 1
        if debug == 1 and hp == 0 and qg == 0 and qb == 0:
            p.wait("gpsimd", "dp", p.cnt["dp"])
            p.do("gpsimd", lambda e: e.dma_start(
                out=dbg["d_pg"], in_=V["pgs"][0][:, :]), inc="dgd", inc_n=16)
            p.wait("gpsimd", "dsg", gmap[(0, 0)][1])
            p.do("gpsimd", lambda e, wb=wb: e.dma_start(
                out=dbg["d_rg"][:, 0:wb], in_=V["rg"][:, 0:wb]), inc="dgd", inc_n=16)
            p.dbg_barrier()

    p.flush_pending()

    p.wait("tensor", "dv", p.cnt["dv"])
    p.wait("tensor", "dgz", p.cnt["dgz"])
    p.wait("tensor", "ac", p.o65ac)
    p.do("tensor", lambda e, sgv=sgv: e.matmul(pA[0:65, 1024:1536], sb["mz"][0:64, :],
                                               sgv[0:64, :], start=True, stop=True,
                                               tile_position=(0, 0)))
    p.do("tensor", lambda e, sgv=sgv: e.matmul(pA[0:65, 1536:2048], sb["mz"][64:128, :],
                                               sgv[64:128, :], start=True, stop=True,
                                               tile_position=(64, 0)), inc="pe")
    p.g65pe = p.cnt["pe"]
    p.wait("vector", "pe", p.g65pe)
    for h in range(2):
        p.do("vector", lambda e, h=h: e.tensor_copy(
            out=sb["g65s"][:, h, :], in_=pA[0:65, 1024 + h * 512:1024 + (h + 1) * 512]))
        p.do("vector", lambda e, h=h, denbG=denbG: e.tensor_scalar(
            out=denbG[0:1, h * 512:(h + 1) * 512],
            in0=pA[64:65, 1024 + h * 512:1024 + (h + 1) * 512],
            scalar1=1e-6, scalar2=None, op0=OP.add), inc="dv" if h == 1 else None)
    kb_dv0 = p.cnt["dv"]

    exp_base = p.cnt["ac"]
    sdone = []
    avdone = []
    waited = {"dsg": 0, "dsh": 0}

    def emit_S(kb):
        pp = kb % 2
        reg = pA[:, pp * 1024:(pp + 1) * 1024]
        if kb == 0:
            p.wait("tensor", "ac", p.o65ac)
            p.wait("tensor", "dv", p.t8_dv)
        elif kb == 1:
            p.wait("tensor", "dv", kb_dv0)
        else:
            p.wait("tensor", "ac", exp_base + (kb - 1))
        for qi, qb in enumerate(qbs):
            if kb_ranges[qb][0] <= kb <= kb_ranges[qb][1]:
                for h in range(2):
                    qsem, gv = gmap[(qi, h)]
                    if gv > waited[qsem]:
                        waited[qsem] = gv
                        p.wait("tensor", qsem, gv)
        for h in range(2):
            hreg = reg[:, h * 512:(h + 1) * 512]
            tp = (0, 0) if h == 0 else (64, 0)
            band = [(qi, qb, kb - kb_ranges[qb][0]) for qi, qb in enumerate(qbs)
                    if kb_ranges[qb][0] <= kb <= kb_ranges[qb][1]]
            farA = [qi for qi, qb in enumerate(qbs) if kb < kb_ranges[qb][0]]
            farB = [qi for qi, qb in enumerate(qbs) if kb > kb_ranges[qb][1]]
            nacc = len(band) + (1 if farA else 0) + (1 if farB else 0)
            p.do("tensor", lambda e, hreg=hreg, h=h, kb=kb, tp=tp, nacc=nacc: e.matmul(
                hreg, V["kk"][h * 64:(h + 1) * 64, kb * 128:(kb + 1) * 128],
                V["qh"][h * 64:(h + 1) * 64, q0g:q0g + 512],
                start=True, stop=(nacc == 0), tile_position=tp))
            i = 0
            for qi, qb, off in band:
                i += 1
                off0, w0 = offs[(qi, h)]
                last = (i == nacc)
                p.do("tensor", lambda e, hreg=hreg, qi=qi, off0=off0, off=off, last=last: e.matmul(
                    hreg[:, qi * 128:(qi + 1) * 128],
                    V["rg"][:, off0 + off * 128:off0 + (off + 1) * 128],
                    sb["ident"][:], start=False, stop=last))
            for falist, eidx in ((farA, 0), (farB, 1)):
                if not falist:
                    continue
                i += 1
                lo, hi = falist[0], falist[-1]
                assert falist == list(range(lo, hi + 1))
                last = (i == nacc)
                j = 2 * h + eidx
                p.do("tensor", lambda e, hreg=hreg, j=j, lo=lo, hi=hi, last=last: e.matmul(
                    hreg[:, lo * 128:(hi + 1) * 128], sb["sel_r"][:, j * 128:(j + 1) * 128],
                    sb["edges4"][:, q0g + lo * 128:q0g + (hi + 1) * 128],
                    start=False, stop=last))
        v = p.inc_last("tensor", "pe")
        sdone.append(v)

    dp_all = p.cnt["dp"]

    def emit_attnv(kb):
        p.wait("tensor", "ac", exp_base + kb + 1)
        if kb == 0:
            p.wait("tensor", "dp", dp_all)
        for h in range(2):
            p.do("tensor", lambda e, kb=kb, h=h: e.matmul(
                pB[0:65, h * 512:(h + 1) * 512], V["vp"][:, kb, h, :],
                V["ee"][:, kb % 2, h * 512:(h + 1) * 512],
                start=(kb == 0), stop=(kb == NKB - 1)))
        v = p.inc_last("tensor", "pe")
        avdone.append(v)

    emit_S(0)
    emit_S(1)
    for kb in range(NKB):
        if kb + 2 < NKB:
            emit_S(kb + 2)
        emit_attnv(kb)
    for kb in range(NKB):
        p.wait("scalar", "pe", sdone[kb])
        if kb >= 2:
            p.wait("scalar", "pe", avdone[kb - 2])
        p.do("scalar", lambda e, kb=kb: e.activation(
            out=V["ee"][:, kb % 2, :], in_=pA[:, (kb % 2) * 1024:(kb % 2 + 1) * 1024],
            func=AF.Exp, bias=sb["negb"][:], scale=1.0), inc="ac")
    p.sdone_last = sdone[-1]

    avlast = avdone[-1]
    p.wait("scalar", "pe", avlast)
    p.do("scalar", lambda e: e.activation(out=sb["o65"][:, 0, :], in_=pB[0:65, 0:512],
                                          func=AF.Copy))
    p.do("scalar", lambda e: e.activation(out=sb["o65"][:, 1, :], in_=pB[0:65, 512:1024],
                                          func=AF.Copy), inc="ac")
    p.o65ac = p.cnt["ac"]
    if debug == 1 and hp == 0 and qg == 0:
        p.wait("gpsimd", "ac", p.o65ac)
        p.do("gpsimd", lambda e: e.dma_start(out=dbg["d_o65"], in_=sb["o65"][:]),
             inc="dgd", inc_n=16)
        p.dbg_barrier()
    p.wait("vector", "ac", p.o65ac)
    for h in range(2):
        p.do("vector", lambda e, h=h, denbL=denbL: e.tensor_copy(
            out=denbL[0:1, h * 512:(h + 1) * 512],
            in_=pB[64:65, h * 512:(h + 1) * 512]), inc="dv" if h == 1 else None)
    p.denb_dv = p.cnt["dv"]

    t1 = p.o65ac
    t2 = p.denb_dv

    def tail(t1=t1, t2=t2, q0g=q0g, hp=hp, qg=qg, denbL=denbL, denbG=denbG):
        p.wait("tensor", "ac", t1)
        p.wait("tensor", "dv", t2)
        p.wait("tensor", "dp", p.cnt["dp"])
        for c in range(16):
            srcr = denbL if c < 8 else denbG
            cc = c % 8
            p.do("tensor", lambda e, srcr=srcr, cc=cc, c=c: e.matmul(
                pB[:, c:c + 1], srcr[0:1, cc * 128:(cc + 1) * 128],
                sb["ones_r"][0:1, 0:1], start=True, stop=True),
                inc="pe" if c == 15 else None)
        p.wait("vector", "pe", p.cnt["pe"])
        p.do("vector", lambda e: e.reciprocal(out=sb["rcolf"][:], in_=pB[:, 0:16]))
        p.hop("vector")
        p.do("vector", lambda e: e.tensor_copy(out=sb["rcol"][:], in_=sb["rcolf"][:]),
             inc="dv")
        p.rcol_dv = p.cnt["dv"]
        p.wait("tensor", "dv", p.cnt["dv"])
        for rr in range(4):
            for c in range(4):
                j = rr * 4 + c
                p.do("tensor", lambda e, rr=rr, c=c, j=j: e.matmul(
                    pA[0:64, rr * 512 + c * 128:rr * 512 + (c + 1) * 128],
                    sb["rcol"][:, j:j + 1].broadcast_to([128, 64]),
                    sb["ident"][:], start=True, stop=True),
                    inc="pe" if j == 15 else None)
        p.wait("vector", "pe", p.cnt["pe"])
        p.wait("vector", "dsc", p.cnt["dsc"])
        for h in range(2):
            if h == 1:
                p.wait("vector", "dv", p.cnt["dv"])
            p.do("vector", lambda e, h=h: e.scalar_tensor_tensor(
                out=sb["ct"][:, 0, :], in0=sb["o65"][0:64, h, :],
                scalar=sb["gate"][0:64, :],
                in1=pA[0:64, h * 512:(h + 1) * 512], op0=OP.mult, op1=OP.mult))
            p.do("vector", lambda e, h=h: e.scalar_tensor_tensor(
                out=sb["ct"][:, 1, :], in0=sb["g65s"][0:64, h, :],
                scalar=sb["gate1m"][0:64, :],
                in1=pA[0:64, 1024 + h * 512:1024 + (h + 1) * 512],
                op0=OP.mult, op1=OP.mult))
            p.hop("vector")
            if h == 0:
                p.do("vector", lambda e: e.tensor_tensor(
                    out=V["comb"][0:64, hp, q0g:q0g + 512], in0=sb["ct"][:, 0, :],
                    in1=sb["ct"][:, 1, :], op=OP.add), inc="dv")
            else:
                p.do("vector", lambda e: e.tensor_tensor(
                    out=sb["ct"][:, 0, :], in0=sb["ct"][:, 0, :],
                    in1=sb["ct"][:, 1, :], op=OP.add), inc="dv")
                p.wait("sync", "dv", p.cnt["dv"])
                p.do("sync", lambda e: e.dma_start(
                    out=V["comb"][64:128, hp, q0g:q0g + 512], in_=sb["ct"][:, 0, :]),
                    inc="dsc", inc_n=16)
        p.t8_dv = p.cnt["dv"]

    p.pending = tail


def gen_ffn(p, dd, sb, V, pA, pB, out_cm, attn_sp, ds_attn, dbg, debug):
    def wload(dname, c, width, wait_ac=None):
        if wait_ac is not None:
            p.wait("gpsimd", "ac", wait_ac)
        s = "dgx0" if c % 2 == 0 else "dgx1"
        return (s, p.do("gpsimd", lambda e, c=c, dname=dname, width=width: e.dma_start(
            out=V["wsl"][:, c % 2, 0:width], in_=dd[dname][c * 128:(c + 1) * 128, :]),
            inc=s, inc_n=16))

    p.wait("tensor", "ac", p.cnt["ac"])
    p.wait("tensor", "dv", p.cnt["dv"])
    p.wait("gpsimd", "dss0", ds_attn[0])
    p.wait("gpsimd", "dss1", ds_attn[1])
    loads = {0: wload("w1T", 0, 4 * D), 1: wload("w1T", 1, 4 * D)}
    done = {}
    p.wait("scalar", "pe", p.cnt["pe"])
    for c in range(NCH):
        lsem, lval = loads[c]
        p.wait("scalar", lsem, lval)
        p.do("scalar", lambda e, c=c: e.activation(
            out=V["s1"][:, c, :], in_=V["wsl"][:, c % 2, :], func=AF.Sign), inc="ac")
        for ch in range(8):
            p.wait("scalar", "pe", p.cnt["pe"])
            p.do("scalar", lambda e, c=c, ch=ch: e.activation(
                out=sb["sqs"][:], in_=V["wsl"][:, c % 2, ch * 512:(ch + 1) * 512],
                func=AF.Abs), inc="ac")
            p.wait("tensor", "ac", p.cnt["ac"])
            ps = (pA if ch < 4 else pB)[0:1, (ch % 4) * 512:(ch % 4 + 1) * 512]
            p.do("tensor", lambda e, ps=ps, c=c: e.matmul(
                ps, sb["onesc"][:], sb["sqs"][:], start=(c == 0), stop=(c == NCH - 1)),
                inc="pe")
        done[c] = p.cnt["ac"]
        if c + 2 < NCH:
            loads[c + 2] = wload("w1T", c + 2, 4 * D, wait_ac=done[c])
    p.wait("vector", "pe", p.cnt["pe"])
    for ch in range(8):
        ps = (pA if ch < 4 else pB)[0:1, (ch % 4) * 512:(ch % 4 + 1) * 512]
        p.do("vector", lambda e, ps=ps: e.tensor_scalar(
            out=sb["rowb"][:], in0=ps, scalar1=1.0 / D, scalar2=None, op0=OP.mult),
            inc="dv")
        p.wait("tensor", "dv", p.cnt["dv"])
        for q in range(4):
            p.do("tensor", lambda e, ch=ch, q=q: e.matmul(
                pB[:, 1536 + 4 * (ch % 2) + q:1537 + 4 * (ch % 2) + q],
                sb["rowb"][0:1, q * 128:(q + 1) * 128],
                sb["ones_r"][0:1, 0:1], start=True, stop=True),
                inc="pe" if q == 3 else None)
        p.wait("vector", "pe", p.cnt["pe"])
        p.do("vector", lambda e, ch=ch: e.tensor_copy(
            out=sb["a1c"][:, ch * 4:(ch + 1) * 4],
            in_=pB[:, 1536 + 4 * (ch % 2):1540 + 4 * (ch % 2)]), inc="dv")
    loads = {0: wload("w2T", 0, D, wait_ac=done[NCH - 2]),
             1: wload("w2T", 1, D, wait_ac=done[NCH - 1])}
    done2 = {}
    p.wait("scalar", "pe", p.cnt["pe"])
    p.wait("scalar", "dss0", p.cnt["dss0"])
    p.wait("scalar", "dss1", p.cnt["dss1"])
    for c in range(32):
        lsem, lval = loads[c]
        p.wait("scalar", lsem, lval)
        p.do("scalar", lambda e, c=c: e.activation(
            out=V["s2"][:, c, :], in_=V["wsl"][:, c % 2, 0:1024], func=AF.Sign), inc="ac")
        for ch in range(2):
            p.wait("scalar", "pe", p.cnt["pe"])
            p.do("scalar", lambda e, c=c, ch=ch: e.activation(
                out=sb["sqs"][:], in_=V["wsl"][:, c % 2, ch * 512:(ch + 1) * 512],
                func=AF.Abs), inc="ac")
            p.wait("tensor", "ac", p.cnt["ac"])
            ps = pA[0:1, ch * 512:(ch + 1) * 512]
            p.do("tensor", lambda e, ps=ps, c=c: e.matmul(
                ps, sb["onesc"][:], sb["sqs"][:], start=(c == 0), stop=(c == 31)),
                inc="pe")
        done2[c] = p.cnt["ac"]
        if c + 2 < 32:
            loads[c + 2] = wload("w2T", c + 2, D, wait_ac=done2[c])
    p.wait("vector", "pe", p.cnt["pe"])
    for ch in range(2):
        ps = pA[0:1, ch * 512:(ch + 1) * 512]
        p.do("vector", lambda e, ps=ps: e.tensor_scalar(
            out=sb["rowb"][:], in0=ps, scalar1=1.0 / (4 * D), scalar2=None, op0=OP.mult),
            inc="dv")
        p.wait("tensor", "dv", p.cnt["dv"])
        for q in range(4):
            p.do("tensor", lambda e, ch=ch, q=q: e.matmul(
                pB[:, 1536 + 4 * ch + q:1537 + 4 * ch + q],
                sb["rowb"][0:1, q * 128:(q + 1) * 128],
                sb["ones_r"][0:1, 0:1], start=True, stop=True),
                inc="pe" if q == 3 else None)
        p.wait("vector", "pe", p.cnt["pe"])
        p.do("vector", lambda e, ch=ch: e.tensor_copy(
            out=sb["a2c"][:, ch * 4:(ch + 1) * 4],
            in_=pB[:, 1536 + 4 * ch:1540 + 4 * ch]), inc="dv")
    dv_bw = p.cnt["dv"]
    if debug == 1:
        p.wait("gpsimd", "dv", dv_bw)
        p.do("gpsimd", lambda e: e.dma_start(out=dbg["d_s1"], in_=V["s1"]),
             inc="dgd", inc_n=16)
        p.dbg_barrier()

    xloads = {}
    atloads = {}
    ac_signs_done = p.cnt["ac"]
    def emit_fx(sl, wait_dv=None):
        ap = dd["xT"][:, sl * FSL:(sl + 1) * FSL].rearrange("(c p) t -> p c t", p=128)
        if wait_dv is not None:
            p.wait("gpsimd", "dv", wait_dv)
        if sl == 0:
            p.wait("gpsimd", "ac", ac_signs_done)
        xloads[sl] = p.do("gpsimd", lambda e, ap=ap: e.dma_start(
            out=V["fx"], in_=ap), inc="dgf", inc_n=16)
        ap2 = attn_sp[:, :, sl * FSL:(sl + 1) * FSL].rearrange("c p t -> p c t")
        atloads[sl] = p.do("gpsimd", lambda e, ap2=ap2: e.dma_start(
            out=V["fat"], in_=ap2), inc="dgf", inc_n=16)
    emit_fx(0)

    R8 = [pA[:, i * 512:(i + 1) * 512] for i in range(4)] + \
         [pB[:, i * 512:(i + 1) * 512] for i in range(4)]
    out_dma = {}
    import os
    SLAB_BARRIER = os.environ.get("DK_SLAB_BARRIER", "0") == "1"
    for sl in range(NFS):
        if SLAB_BARRIER and sl > 0:
            for eng in ("tensor", "scalar", "vector", "gpsimd", "sync"):
                p.wait(eng, "pe", p.cnt["pe"])
                p.wait(eng, "ac", p.cnt["ac"])
                p.wait(eng, "dv", p.cnt["dv"])
                p.wait(eng, "dso", p.cnt["dso"])
        t0 = sl * FSL
        p.wait("vector", "dgf", atloads[sl])
        if sl == 0:
            p.wait("vector", "dv", dv_bw)
            p.wait("vector", "ac", ac_signs_done)
        xs = V["fx"]
        at = V["fat"]
        p.wait("vector", "dv", p.cnt["dv"])
        for c in range(NCH):
            p.do("vector", lambda e, xs=xs, at=at, c=c: e.scalar_tensor_tensor(
                out=V["x2sl"][:, c, :], in0=at[:, c, :],
                scalar=sb["ga"][:, c:c + 1], in1=xs[:, c, :], op0=OP.mult, op1=OP.add),
                inc="dv" if c == NCH - 1 else None)
        if sl + 1 < NFS:
            emit_fx(sl + 1, wait_dv=p.cnt["dv"])
        p.wait("vector", "dv", p.cnt["dv"])
        p.do("vector", lambda e: e.tensor_tensor(
            out=V["xsq2"], in0=V["x2sl"], in1=V["x2sl"], op=OP.mult), inc="dv")
        p.wait("tensor", "dv", p.cnt["dv"])
        p.wait("tensor", "ac", p.cnt["ac"])
        rsum, rsq = pB[0:1, 1024:1024 + FSL], pB[0:1, 1536:1536 + FSL]
        rrep, rmrep = pA[:, 1024:1024 + FSL], pA[:, 1536:1536 + FSL]
        for c in range(NCH):
            p.do("tensor", lambda e, c=c, rsum=rsum: e.matmul(
                rsum, sb["onescf"][:], V["x2sl"][:, c, :],
                start=(c == 0), stop=(c == NCH - 1)))
        for c in range(NCH):
            p.do("tensor", lambda e, c=c, rsq=rsq: e.matmul(
                rsq, sb["onesc"][:], V["xsq2"][:, c, :],
                start=(c == 0), stop=(c == NCH - 1)),
                inc="pe" if c == NCH - 1 else None)
        p.wait("vector", "pe", p.cnt["pe"])
        emit_ln_rows(p, sb, rsum, rsq, FSL)
        p.wait("tensor", "dv", p.cnt["dv"])
        p.do("tensor", lambda e, rrep=rrep: e.matmul(
            rrep, sb["ones_r"][:], sb["rowb"][0:1, 0:FSL], start=True, stop=True), inc="pe")
        p.wait("vector", "pe", p.cnt["pe"])
        p.do("vector", lambda e: e.tensor_copy(out=sb["rowb"][0:1, 0:FSL],
                                               in_=sb["rowmr"][0:1, 0:FSL]), inc="dv")
        p.wait("tensor", "dv", p.cnt["dv"])
        p.do("tensor", lambda e, rmrep=rmrep: e.matmul(
            rmrep, sb["ones_r"][:], sb["rowb"][0:1, 0:FSL], start=True, stop=True), inc="pe")
        p.wait("vector", "pe", p.cnt["pe"])
        for c in range(NCH):
            p.do("vector", lambda e, c=c, rrep=rrep: e.tensor_tensor(
                out=sb["tmpa"][:, 0:FSL], in0=V["x2sl"][:, c, :], in1=rrep, op=OP.mult))
            p.hop("vector")
            p.do("vector", lambda e, c=c, rmrep=rmrep: e.tensor_tensor(
                out=sb["tmpa"][:, 0:FSL], in0=sb["tmpa"][:, 0:FSL], in1=rmrep,
                op=OP.subtract))
            p.hop("vector")
            p.do("vector", lambda e, c=c: e.tensor_scalar(
                out=V["h3sl"][:, c, :], in0=sb["tmpa"][:, 0:FSL],
                scalar1=sb["cb1"][:, c:c + 1], scalar2=sb["sbv"][:, c:c + 1],
                op0=OP.mult, op1=OP.add), inc="dv" if c == NCH - 1 else None)
            if c < NCH - 1:
                p.hop("vector")
        dv_h3 = p.cnt["dv"]
        if debug == 1 and sl == 0:
            p.wait("gpsimd", "dv", dv_h3)
            p.do("gpsimd", lambda e: e.dma_start(out=dbg["d_h3"], in_=V["h3sl"]),
                 inc="dgd", inc_n=16)
            p.dbg_barrier()
        p.wait("tensor", "dv", dv_h3)
        evac = {}
        for ho in range(32):
            reg = R8[ho % 8]
            if ho >= 8:
                p.wait("tensor", "ac", evac[ho - 8])
            for c in range(NCH):
                p.do("tensor", lambda e, ho=ho, c=c, reg=reg: e.matmul(
                    reg[:, 0:FSL], V["s1"][:, c, ho * 128:(ho + 1) * 128],
                    V["h3sl"][:, c, :], start=(c == 0), stop=(c == NCH - 1)),
                    inc="pe" if c == NCH - 1 else None)
            p.wait("scalar", "pe", p.cnt["pe"])
            evac[ho] = p.do("scalar", lambda e, ho=ho, reg=reg: e.activation(
                out=V["h2sl"][:, ho, :], in_=reg[:, 0:FSL], func=AF.Gelu,
                bias=sb["b1_c"][:, ho:ho + 1], scale=sb["a1c"][:, ho:ho + 1]), inc="ac")
        if sl >= 1:
            p.wait("vector", "dso", out_dma[(sl - 1, 1)])
        p.wait("tensor", "ac", evac[31])
        for blk in range(NCH):
            reg = R8[blk]
            for hc in range(32):
                p.do("tensor", lambda e, blk=blk, hc=hc, reg=reg: e.matmul(
                    reg[:, 0:FSL], V["s2"][:, hc, blk * 128:(blk + 1) * 128],
                    V["h2sl"][:, hc, :], start=(hc == 0), stop=(hc == 31)),
                    inc="pe" if hc == 31 else None)
            p.wait("vector", "pe", p.cnt["pe"])
            p.wait("vector", "dv", p.cnt["dv"])
            if blk == 4:
                p.wait("vector", "dso", out_dma[(sl, 0)])
            p.do("vector", lambda e, blk=blk, reg=reg: e.tensor_scalar(
                out=sb["sqs"][:, 0:FSL], in0=reg[:, 0:FSL], scalar1=sb["a2c"][:, blk:blk + 1],
                scalar2=sb["b2_c"][:, blk:blk + 1], op0=OP.mult, op1=OP.add))
            p.hop("vector")
            p.do("vector", lambda e, sl=sl, blk=blk: e.scalar_tensor_tensor(
                out=V["outsl"][:, blk % 4, :], in0=sb["sqs"][:, 0:FSL],
                scalar=sb["gb"][:, blk:blk + 1], in1=V["x2sl"][:, blk, :],
                op0=OP.mult, op1=OP.add), inc="dv")
            if blk % 4 == 3:
                half4 = blk // 4
                p.wait("sync", "dv", p.cnt["dv"])
                dst = out_cm[half4 * 4 * 128:(half4 + 1) * 4 * 128,
                             t0:t0 + FSL].rearrange("(c p) t -> p c t", p=128)
                out_dma[(sl, half4)] = p.do("sync", lambda e, dst=dst, sl=sl: e.dma_start(
                    out=dst, in_=V["outsl"][:, :, :]), inc="dso", inc_n=16)
        dv_out = p.cnt["dv"]



def prep_inputs(inputs):
    f32 = np.float32
    bf = ml_dtypes.bfloat16
    x = np.asarray(inputs["L0_attn_hidden_states"], f32)
    relT = np.ascontiguousarray(np.asarray(inputs["L0_attn_rel_emb"], f32).T)
    rel_rev = np.ascontiguousarray(relT[:, ::-1])
    Mz = np.concatenate([np.asarray(inputs["L0_attn_M"], f32),
                         np.asarray(inputs["L0_attn_z"], f32)[:, :, None]], axis=2)
    mzp = np.ascontiguousarray(Mz.reshape(8, 128, 65))
    WqT = np.ascontiguousarray(np.asarray(inputs["L0_attn_Wq"], f32).T)
    WkT = np.ascontiguousarray(np.asarray(inputs["L0_attn_Wk"], f32).T)
    WvT = np.ascontiguousarray(np.asarray(inputs["L0_attn_Wv"], f32).T)
    WoT = np.ascontiguousarray(np.asarray(inputs["L0_attn_Wo"], f32).T)
    w1T = np.ascontiguousarray(np.asarray(inputs["L0_ffn_w1"], f32).T)
    w2T = np.ascontiguousarray(np.asarray(inputs["L0_ffn_w2"], f32).T)
    wq_p = np.ascontiguousarray(WqT.reshape(D, NP, 128).transpose(1, 0, 2))
    wk_p = np.ascontiguousarray(WkT.reshape(D, NP, 128).transpose(1, 0, 2))
    wv_p = np.ascontiguousarray(WvT.reshape(D, NP, 128).transpose(1, 0, 2))

    def col8(v):
        return np.ascontiguousarray(np.asarray(v, f32).reshape(8, 128).T)
    def col32(v):
        return np.ascontiguousarray(np.asarray(v, f32).reshape(32, 128).T)

    sel_r = np.zeros((4, 512), bf)
    sel_c = np.zeros((4, 4), bf)
    for j in range(4):
        sel_r[j, j * 128:(j + 1) * 128] = 1
        sel_c[j, j] = 1
    blkd = np.zeros((128, 2), bf)
    blkd[0:64, 0] = 1
    blkd[64:128, 1] = 1

    common = dict(
        wmod=np.asarray(inputs["L0_wmod"], f32),
        wq_p=wq_p, wk_p=wk_p, wv_p=wv_p, woT=WoT, w1T=w1T, w2T=w2T,
        mzp=mzp,
        bq_c=col8(inputs["L0_attn_bq"]), bk_c=col8(inputs["L0_attn_bk"]),
        bv_c=col8(inputs["L0_attn_bv"]), bo_c=col8(inputs["L0_attn_bo"]),
        b1_c=col32(inputs["L0_ffn_b1"]), b2_c=col8(inputs["L0_ffn_b2"]),
        galpha=np.asarray(inputs["L0_attn_gate_alpha"], f32).reshape(1, 1),
        ident=np.eye(128, dtype=bf),
        ones_r=np.ones((1, 128), bf),
        ones_rf=np.ones((1, 128), np.float32),
        onesc=np.ones((128, 1), bf),
        onescf=np.ones((128, 1), np.float32),
        blkd=blkd,
        o642=np.ones((64, 2), bf),
        i2=np.eye(2, dtype=bf),
        sel_r=sel_r, sel_c=sel_c,
    )
    relg0 = np.ascontiguousarray(np.concatenate([rel_rev, rel_rev], 0))
    relg1 = np.ascontiguousarray(np.concatenate([relT, relT], 0))
    in_maps = []
    for core in range(8):
        b, half = core // 2, core % 2
        xT = np.ascontiguousarray(x[b].T)
        if half == 1:
            xT = np.ascontiguousarray(xT[:, ::-1])
        m = dict(common)
        m["xT"] = xT
        m["relg"] = relg0 if half == 0 else relg1
        m["c_col"] = np.ascontiguousarray(
            np.asarray(inputs["L0_c"], f32)[b].reshape(8, 128).T)
        in_maps.append(m)
    return in_maps


def assemble(results):
    out = np.zeros((4, S, D), np.float32)
    for core in range(8):
        b, half = core // 2, core % 2
        oc = results[core]["out_cm"]
        if half == 0:
            out[b, 0:SQ, :] = oc.T
        else:
            out[b, SQ:S, :] = oc[:, ::-1].T
    return out


def kernel(**inputs):
    from concourse.bass_utils import run_bass_kernel_spmd
    in_maps = prep_inputs(inputs)
    nc = build_program(debug=0)
    res = run_bass_kernel_spmd(nc, in_maps, core_ids=list(range(8)))
    return assemble(res.results)


# version 06
# speedup vs baseline: 1.2535x; 1.0251x over previous
import contextlib
import numpy as np
import ml_dtypes
import concourse.bass as bass
import concourse.mybir as mybir
from concourse.ap import AP

BF16 = mybir.dt.bfloat16
F32 = mybir.dt.float32
AF = mybir.ActivationFunctionType
OP = mybir.AluOpType
AX = mybir.AxisListType

D = 1024
S = 4096
SQ = 2048
NP = 8
NCH = 8
MR = 1024
PAD = 254
CP = 2049 + 2 * PAD
GQB = 4
NG = SQ // (GQB * 128)
BAND = MR + 127
NKB = S // 128
FSL = 256
NFS = SQ // FSL

ENGINES = ("tensor", "scalar", "vector", "sync", "gpsimd")


def band_range(qb):
    q0 = qb * 128
    kb_lo = max(0, -(-(q0 - BAND) // 128))
    kb_hi = min(NKB - 1, (q0 + BAND) // 128)
    return kb_lo, kb_hi


class Prog:
    def __init__(self, nc, es):
        self.nc = nc
        self.ops = {e: [] for e in ENGINES}
        self.last_has_inc = {}
        self.cnt = {}
        self.sems = {}
        for s in ("dg", "ds", "pe", "ac", "dv", "dsg", "dsh", "dsc", "dss0", "dss1",
                  "dso", "dgm", "dgz", "dgw", "dgo", "dgx0", "dgx1", "dgy0", "dgy1", "dgf", "dgd",
                  "hv", "ha", "ht", "hg", "dp"):
            self.sems[s] = es.enter_context(nc.semaphore(s))
            self.cnt[s] = 0

    def wait(self, eng, sem, val):
        if val <= 0:
            return
        h = self.sems[sem]
        self.ops[eng].append(lambda e, h=h, v=val: e.wait_ge(h, v))
        self.last_has_inc[eng] = False

    def inc_last(self, eng, sem, n=1):
        assert not self.last_has_inc.get(eng), f"double inc on {eng}"
        h = self.sems[sem]
        self.cnt[sem] += n
        f = self.ops[eng].pop()
        self.ops[eng].append(lambda e, f=f, h=h, n=n: f(e).then_inc(h, n))
        self.last_has_inc[eng] = True
        return self.cnt[sem]

    def flush_pending(self):
        fn = getattr(self, "pending", None)
        self.pending = None
        if fn is not None:
            fn()

    def dbg_barrier(self):
        for eng in ("tensor", "scalar", "vector", "gpsimd", "sync"):
            self.wait(eng, "dgd", self.cnt["dgd"])

    def hop(self, eng):
        sem = {"vector": "hv", "scalar": "ha", "tensor": "ht", "gpsimd": "hg"}[eng]
        h = self.sems[sem]
        self.cnt[sem] += 1
        v = self.cnt[sem]
        assert not self.last_has_inc.get(eng), f"double inc (hop) on {eng}"
        f = self.ops[eng].pop()
        self.ops[eng].append(lambda e, f=f, h=h: f(e).then_inc(h, 1))
        self.ops[eng].append(lambda e, h=h, v=v: e.wait_ge(h, v))
        self.last_has_inc[eng] = False

    def do(self, eng, fn, inc=None, inc_n=1):
        if inc is None:
            self.ops[eng].append(fn)
            self.last_has_inc[eng] = False
            return None
        h = self.sems[inc]
        self.cnt[inc] += inc_n
        self.ops[eng].append(lambda e, fn=fn, h=h, n=inc_n: fn(e).then_inc(h, n))
        self.last_has_inc[eng] = True
        return self.cnt[inc]


def _view(tensor_ap, off_bytes, shape, dt):
    es_ = mybir.dt.size(dt)
    nel = int(np.prod(shape[1:]))
    assert off_bytes % 2 == 0
    v = tensor_ap[0:shape[0], off_bytes // 2: off_bytes // 2 + (nel * es_) // 2]
    if dt != BF16:
        v = v.bitcast(dt)
    if len(shape) == 2:
        return v
    if len(shape) == 3:
        return v.rearrange("p (a b) -> p a b", a=shape[1])
    if len(shape) == 4:
        return v.rearrange("p (a b c) -> p a b c", a=shape[1], b=shape[2])
    raise ValueError(shape)


def build_program(debug=0):
    nc = bass.Bass("TRN2", target_bir_lowering=False)
    es = contextlib.ExitStack()

    dd = {}
    def din(name, shape, dt=F32):
        dd[name] = nc.dram_tensor(name, shape, dt, kind="ExternalInput").ap()
    din("xT", [D, S])
    din("c_col", [128, 8])
    din("wmod", [D, 6 * D])
    din("wq_p", [NP, D, 128]); din("wk_p", [NP, D, 128]); din("wv_p", [NP, D, 128])
    din("woT", [D, D])
    din("w1T", [D, 4 * D])
    din("w2T", [4 * D, D])
    din("relg", [128, 2049])
    din("mzp", [NP, 128, 65])
    din("bq_c", [128, 8]); din("bk_c", [128, 8]); din("bv_c", [128, 8]); din("bo_c", [128, 8])
    din("b1_c", [128, 32]); din("b2_c", [128, 8])
    din("galpha", [1, 1])
    din("ident", [128, 128], BF16)
    din("ones_r", [1, 128], BF16)
    din("ones_rf", [1, 128], F32)
    din("onesc", [128, 1], BF16)
    din("onescf", [128, 1], F32)
    din("blkd", [128, 2], BF16)
    din("o642", [64, 2], BF16)
    din("i2", [2, 2], BF16)
    din("sel_r", [4, 512], BF16)
    din("sel_c", [4, 4], BF16)
    out_cm = nc.dram_tensor("out_cm", [D, SQ], F32, kind="ExternalOutput").ap()
    attn_sp = nc.dram_tensor("attn_sp", [NCH, 128, SQ], BF16, kind="Internal").ap()

    dbg = {}
    if debug:
        def dout(name, shape, dt):
            dbg[name] = nc.dram_tensor(name, shape, dt, kind="ExternalOutput").ap()
        dout("d_mod", [1, 6 * D], BF16)
        dout("d_h1", [128, NCH, S], BF16)
        dout("d_qh", [128, SQ], BF16)
        dout("d_k", [128, S], BF16)
        dout("d_vp", [128, NKB, 2, 65], BF16)
        dout("d_pg", [128, CP], BF16)
        dout("d_rg", [128, 2304], BF16)
        dout("d_negb", [128, 1], F32)
        dout("d_edge", [4, SQ], BF16)
        dout("d_o65", [65, 2, 512], F32)
        dout("d_comb", [128, NP, SQ], BF16)
        dout("d_attn", [NCH, 128, SQ], BF16)
        dout("d_s1", [128, NCH, 4 * D], BF16)
        dout("d_h3", [128, NCH, FSL], BF16)

    sb = {}
    def sbuf(name, shape, dt):
        sb[name] = es.enter_context(nc.sbuf_tensor("s_" + name, shape, dt))
    sbuf("ident", [128, 128], BF16)
    sbuf("ones_r", [1, 128], BF16)
    sbuf("onesc", [128, 1], BF16)
    sbuf("onescf", [128, 1], F32)
    sbuf("blkd", [128, 2], BF16)
    sbuf("o642", [64, 2], BF16)
    sbuf("i2", [2, 2], BF16)
    sbuf("sel_r", [4, 512], BF16)
    sbuf("sel_c", [4, 4], BF16)
    sbuf("c_col", [128, 8], F32)
    sbuf("silu_c", [128, 8], F32)
    sbuf("ones_rf", [1, 128], F32)
    for n in ("sa", "ca1", "ga", "sbv", "cb1", "gb", "bq_h", "bk_c", "bv_c", "bo_c", "b2_c"):
        sbuf(n, [128, 8], F32)
    sbuf("b1_c", [128, 32], F32)
    sbuf("a1c", [128, 32], F32)
    sbuf("a2c", [128, 8], F32)
    sbuf("gate", [128, 1], F32); sbuf("gate1m", [128, 1], F32)
    sbuf("relg", [128, 2049], BF16)
    sbuf("lnrel", [2, 1], F32)
    sbuf("tiny", [2, 10], F32)
    sbuf("tinyb", [2, 2], BF16)
    sbuf("rowb", [1, 512], BF16)
    sbuf("rowm", [1, 512], F32); sbuf("rowr", [1, 512], F32); sbuf("rowmr", [1, 512], F32)
    sbuf("tmpa", [128, 512], F32)
    sbuf("sqs", [128, 512], BF16)
    sbuf("xslab", [128, 2, NCH, 512], F32)
    sbuf("o65", [65, 2, 512], F32)
    sbuf("ct", [64, 2, 512], BF16)
    sbuf("negb", [128, 1], F32)
    sbuf("edges4", [4, SQ], BF16)
    sbuf("ecols", [128, 64], BF16)
    sbuf("g65s", [65, 2, 512], BF16)
    sbuf("rcol", [128, 16], BF16)
    sbuf("rcolf", [128, 16], F32)
    sbuf("eW", [128, 4], BF16)
    sbuf("mz", [128, 65], BF16)

    ARENA_B = 147584
    sbuf("arena", [128, ARENA_B // 2], BF16)
    ar = sb["arena"][:]

    V = {}
    V["h1"] = _view(ar, 0, [128, NCH, S], BF16)
    V["s1"] = _view(ar, 0, [128, NCH, 4 * D], BF16)
    B0 = 65536
    V["mrow"] = _view(ar, B0, [1, 6 * D], F32)
    V["wmod_ch"] = _view(ar, B0 + 24576, [128, 2, 3 * D], F32)
    V["xbf1"] = _view(ar, B0 + 24576, [128, NCH, 512], BF16)
    V["xsq1"] = _view(ar, B0 + 32768, [128, NCH, 512], BF16)
    V["kk"] = _view(ar, B0, [128, S], BF16)
    V["qh"] = _view(ar, B0 + 8192, [128, SQ], BF16)
    V["vp"] = _view(ar, B0 + 12288, [128, NKB, 2, 65], BF16)
    V["vf"] = _view(ar, B0 + 20608, [128, 1024], BF16)
    V["sigB"] = _view(ar, B0 + 20608, [128, 512], BF16)
    V["pgs"] = [_view(ar, B0 + 22656, [128, CP], BF16),
                _view(ar, B0 + 22656 + 2 * CP, [128, CP], BF16),
                _view(ar, B0 + 39040, [128, CP], BF16)]
    V["ee"] = _view(ar, B0 + 32896, [128, 2, 1024], BF16)
    V["sig"] = _view(ar, B0 + 36992, [128, 512], BF16)
    V["sigm"] = _view(ar, B0 + 38016, [128, 512], BF16)
    V["xsl3"] = _view(ar, B0, [128, NCH, 512], F32)
    V["wqkv_wq"] = _view(ar, B0 + 39040, [128, NCH, 128], BF16)
    V["wqkv_wk"] = _view(ar, B0 + 41088, [128, NCH, 128], BF16)
    V["wqkv_wv"] = _view(ar, B0 + 43136, [128, NCH, 128], BF16)
    V["rg"] = _view(ar, B0 + 45184, [128, 18432], BF16)
    V["wo"] = _view(ar, B0, [128, NCH, D], BF16)
    V["stag"] = _view(ar, B0 + 16384, [128, 2, SQ], BF16)
    V["s2"] = _view(ar, B0, [128, 32, D], BF16)
    xs_ap = sb["xslab"][:].rearrange("p a b c -> p (a b c)").bitcast(BF16)
    V["comb"] = _view(xs_ap, 0, [128, NP, SQ], BF16)
    V["fx"] = _view(xs_ap, 0, [128, NCH, FSL], F32)
    V["x2sl"] = _view(xs_ap, 8192, [128, NCH, FSL], F32)
    V["outsl"] = _view(xs_ap, 16384, [128, 4, FSL], F32)
    V["fat"] = _view(xs_ap, 20480, [128, NCH, FSL], BF16)
    V["xsq2"] = _view(xs_ap, 24576, [128, NCH, FSL], BF16)
    V["h3sl"] = _view(xs_ap, 28672, [128, NCH, FSL], BF16)
    assert 28672 + 4096 <= 32768
    V["wsl"] = _view(ar, 131072, [128, 2, 4 * D], BF16)
    V["h2sl"] = _view(ar, 131072, [128, 32, FSL], BF16)
    assert 131072 + 16384 <= ARENA_B

    pA = es.enter_context(nc.psum_tensor("pA", [128, 2048], F32))
    pB = es.enter_context(nc.psum_tensor("pB", [128, 2048], F32))

    p = Prog(nc, es)
    gen(p, dd, sb, V, pA, pB, dbg, out_cm, attn_sp, debug)

    with nc.Block() as block:
        @block.tensor
        def _(e):
            for f in p.ops["tensor"]:
                f(e)
        @block.scalar
        def _(e):
            for f in p.ops["scalar"]:
                f(e)
        @block.vector
        def _(e):
            for f in p.ops["vector"]:
                f(e)
        @block.sync
        def _(e):
            for f in p.ops["sync"]:
                f(e)
        @block.gpsimd
        def _(e):
            for f in p.ops["gpsimd"]:
                f(e)
    es.close()
    return nc


def gen(p, dd, sb, V, pA, pB, dbg, out_cm, attn_sp, debug):
    for name, dst in (("ident", "ident"), ("ones_r", "ones_r"), ("ones_rf", "ones_rf"),
                      ("onesc", "onesc"), ("onescf", "onescf"),
                      ("blkd", "blkd"), ("o642", "o642"), ("i2", "i2"),
                      ("sel_r", "sel_r"), ("sel_c", "sel_c"),
                      ("c_col", "c_col"), ("bq_c", "bq_h"), ("bk_c", "bk_c"),
                      ("bv_c", "bv_c"), ("bo_c", "bo_c"), ("b1_c", "b1_c"),
                      ("b2_c", "b2_c")):
        p.do("sync", lambda e, d=dd[name], s=sb[dst]: e.dma_start(out=s[:], in_=d),
             inc="ds", inc_n=16)
    p.do("sync", lambda e: e.dma_start(out=sb["gate"][0:1, :], in_=dd["galpha"]),
         inc="ds", inc_n=16)
    ds_consts = p.cnt["ds"]

    dg_relg = p.do("gpsimd", lambda e: e.dma_start(out=sb["relg"][:], in_=dd["relg"]),
                   inc="dgm", inc_n=16)
    wmod_idx = [(h, c) for h in range(2) for c in range(NCH)]
    wmod_loads = {}
    def emit_wmod_load(i, wait_pe=None):
        half, c = wmod_idx[i]
        if wait_pe is not None:
            p.wait("gpsimd", "pe", wait_pe)
        wmod_loads[(half, c)] = ("dgx0" if c % 2 == 0 else "dgx1",
            p.do("gpsimd", lambda e, half=half, c=c: e.dma_start(
                out=V["wmod_ch"][:, c % 2, :],
                in_=dd["wmod"][c * 128:(c + 1) * 128, half * 3 * D:(half + 1) * 3 * D]),
                inc="dgx0" if c % 2 == 0 else "dgx1", inc_n=16))
    emit_wmod_load(0)
    emit_wmod_load(1)
    x_loads = {}
    xslot = lambda sl: (sb["xslab"][:, sl % 3, :, :] if sl % 3 < 2 else V["xsl3"])
    def emit_x_load(sl, wait_dv=None):
        ap = dd["xT"][:, sl * 512:(sl + 1) * 512].rearrange("(c p) t -> p c t", p=128)
        if wait_dv is not None:
            p.wait("gpsimd", "dv", wait_dv)
        sem = "dgy0" if sl % 2 == 0 else "dgy1"
        p.wait("gpsimd", sem, p.cnt[sem])
        x_loads[sl] = ("dgy0" if sl % 2 == 0 else "dgy1",
            p.do("gpsimd", lambda e, ap=ap, sl=sl: e.dma_start(
                out=xslot(sl), in_=ap),
                inc="dgy0" if sl % 2 == 0 else "dgy1", inc_n=16))
    emit_x_load(0)
    emit_x_load(1)

    p.wait("scalar", "ds", ds_consts)
    p.do("scalar", lambda e: e.activation(out=sb["tmpa"][:, 0:8], in_=sb["c_col"][:],
                                          func=AF.Sigmoid))
    p.do("scalar", lambda e: e.activation(out=sb["tmpa"][0:1, 8:9], in_=sb["gate"][0:1, :],
                                          func=AF.Sigmoid), inc="ac")
    p.wait("vector", "ac", p.cnt["ac"])
    dv_silu = p.do("vector", lambda e: e.tensor_tensor(
        out=sb["silu_c"][:], in0=sb["c_col"][:], in1=sb["tmpa"][:, 0:8], op=OP.mult), inc="dv")
    p.wait("tensor", "dv", dv_silu)
    for half in range(2):
        if half == 1:
            p.wait("tensor", "dv", p.cnt["dv"])
        for c in range(NCH):
            wsem, wval = wmod_loads[(half, c)]
            p.wait("tensor", wsem, wval)
            for npos in range(6):
                ps = pA[0:1, npos * 512:(npos + 1) * 512] if npos < 4 else \
                     pB[0:1, (npos - 4) * 512:(npos - 3) * 512]
                p.do("tensor", lambda e, ps=ps, c=c, npos=npos: e.matmul(
                    ps, sb["silu_c"][:, c:c + 1],
                    V["wmod_ch"][:, c % 2, npos * 512:(npos + 1) * 512],
                    start=(c == 0), stop=(c == NCH - 1)),
                    inc="pe" if npos == 5 else None)
            gi = half * NCH + c
            if gi + 2 < 16:
                emit_wmod_load(gi + 2, wait_pe=p.cnt["pe"])
        p.wait("vector", "pe", p.cnt["pe"])
        for npos in range(6):
            ps = pA[0:1, npos * 512:(npos + 1) * 512] if npos < 4 else \
                 pB[0:1, (npos - 4) * 512:(npos - 3) * 512]
            p.do("vector", lambda e, ps=ps, half=half, npos=npos: e.tensor_copy(
                out=V["mrow"][0:1, half * 3 * D + npos * 512: half * 3 * D + (npos + 1) * 512],
                in_=ps), inc="dv")
    dv_mod = p.cnt["dv"]

    p.wait("scalar", "dv", dv_mod)
    for idx in (2, 5):
        sl = slice(idx * D, (idx + 1) * D)
        for ph in range(2):
            p.wait("scalar", "dv", p.cnt["dv"])
            p.do("scalar", lambda e, sl=sl, ph=ph: e.activation(
                out=sb["rowr"][0:1, :], in_=V["mrow"][0:1, sl][:, ph * 512:(ph + 1) * 512],
                func=AF.Abs), inc="ac")
            p.wait("vector", "ac", p.cnt["ac"])
            p.do("vector", lambda e: e.tensor_scalar(
                out=sb["rowr"][0:1, :], in0=sb["rowr"][0:1, :],
                scalar1=0.25, scalar2=None, op0=OP.max))
            p.do("vector", lambda e, sl=sl, ph=ph: e.tensor_scalar(
                out=sb["rowm"][0:1, :], in0=V["mrow"][0:1, sl][:, ph * 512:(ph + 1) * 512],
                scalar1=0.0, scalar2=2.0, op0=OP.is_ge, op1=OP.mult))
            p.hop("vector")
            p.do("vector", lambda e, sl=sl, ph=ph: e.scalar_tensor_tensor(
                out=V["mrow"][0:1, sl][:, ph * 512:(ph + 1) * 512],
                in0=sb["rowm"][0:1, :], scalar=-1.0,
                in1=sb["rowr"][0:1, :], op0=OP.add, op1=OP.mult), inc="dv")
    dv_clamp = p.cnt["dv"]

    p.wait("tensor", "dv", dv_clamp)
    names = ("sa", "ca1", "ga", "sbv", "cb1", "gb")
    for vi, vn in enumerate(names):
        if vi > 0:
            p.wait("tensor", "dv", p.cnt["dv"])
        for c in range(NCH):
            p.do("tensor", lambda e, vi=vi, c=c: e.matmul(
                pB[:, 1024 + c:1025 + c],
                V["mrow"][0:1, vi * D + c * 128: vi * D + (c + 1) * 128],
                sb["ones_rf"][0:1, 0:1], start=True, stop=True),
                inc="pe" if c == NCH - 1 else None)
        p.wait("vector", "pe", p.cnt["pe"])
        if vn in ("ca1", "cb1"):
            p.do("vector", lambda e, vn=vn: e.tensor_scalar(
                out=sb[vn][:], in0=pB[:, 1024:1032], scalar1=1.0, scalar2=None,
                op0=OP.add), inc="dv")
        else:
            p.do("vector", lambda e, vn=vn: e.tensor_copy(
                out=sb[vn][:], in_=pB[:, 1024:1032]), inc="dv")
    p.do("vector", lambda e: e.tensor_copy(out=sb["tinyb"][0:1, 0:1], in_=sb["tmpa"][0:1, 8:9]),
         inc="dv")
    p.wait("tensor", "dv", p.cnt["dv"])
    p.do("tensor", lambda e: e.matmul(pB[:, 1024:1025], sb["ones_r"][:],
                                      sb["tinyb"][0:1, 0:1], start=True, stop=True), inc="pe")
    p.wait("vector", "pe", p.cnt["pe"])
    p.do("vector", lambda e: e.tensor_copy(out=sb["gate"][:], in_=pB[:, 1024:1025]))
    p.do("vector", lambda e: e.tensor_scalar(
        out=sb["gate1m"][:], in0=pB[:, 1024:1025], scalar1=-1.0, scalar2=1.0,
        op0=OP.mult, op1=OP.add))
    p.do("vector", lambda e: e.tensor_scalar(
        out=sb["bq_h"][:], in0=sb["bq_h"][:], scalar1=0.125, scalar2=None, op0=OP.mult),
        inc="dv")
    dv_modcols = p.cnt["dv"]
    emit_x_load(2, wait_dv=dv_modcols)
    if debug == 1:
        p.wait("gpsimd", "dv", dv_clamp)
        p.do("gpsimd", lambda e: e.dma_start(out=dbg["d_mod"], in_=V["mrow"]),
             inc="dgd", inc_n=16)
        p.dbg_barrier()

    for sl in range(8):
        xs = xslot(sl)
        if sl % 2 == 0:
            rsum, rsq = pB[0:1, 1024:1536], pB[0:1, 1536:2048]
            rrep, rmrep = pB[:, 0:512], pB[:, 512:1024]
        else:
            rsum, rsq = pA[0:1, 0:512], pA[0:1, 512:1024]
            rrep, rmrep = pA[:, 1024:1536], pA[:, 1536:2048]
        xsem, xval = x_loads[sl]
        p.wait("vector", xsem, xval)
        if sl == 0:
            p.wait("vector", "dv", dv_modcols)
        p.do("vector", lambda e, xs=xs: e.tensor_copy(out=V["xbf1"], in_=xs))
        p.hop("vector")
        p.do("vector", lambda e: e.tensor_tensor(
            out=V["xsq1"], in0=V["xbf1"], in1=V["xbf1"], op=OP.mult), inc="dv")
        p.wait("tensor", "dv", p.cnt["dv"])
        for c in range(NCH):
            p.do("tensor", lambda e, c=c, rsum=rsum: e.matmul(
                rsum, sb["onesc"][:], V["xbf1"][:, c, :],
                start=(c == 0), stop=(c == NCH - 1)))
        for c in range(NCH):
            p.do("tensor", lambda e, c=c, rsq=rsq: e.matmul(
                rsq, sb["onesc"][:], V["xsq1"][:, c, :],
                start=(c == 0), stop=(c == NCH - 1)),
                inc="pe" if c == NCH - 1 else None)
        p.wait("vector", "pe", p.cnt["pe"])
        emit_ln_rows(p, sb, rsum, rsq, 512)
        p.wait("tensor", "dv", p.cnt["dv"])
        p.do("tensor", lambda e, rrep=rrep: e.matmul(
            rrep, sb["ones_r"][:], sb["rowb"][:], start=True, stop=True), inc="pe")
        p.wait("vector", "pe", p.cnt["pe"])
        p.do("vector", lambda e: e.tensor_copy(out=sb["rowb"][:], in_=sb["rowmr"][:]), inc="dv")
        p.wait("tensor", "dv", p.cnt["dv"])
        p.do("tensor", lambda e, rmrep=rmrep: e.matmul(
            rmrep, sb["ones_r"][:], sb["rowb"][:], start=True, stop=True), inc="pe")
        p.wait("vector", "pe", p.cnt["pe"])
        for c in range(NCH):
            p.do("vector", lambda e, xs=xs, c=c, rrep=rrep: e.tensor_tensor(
                out=sb["tmpa"][:], in0=xs[:, c, :], in1=rrep, op=OP.mult))
            p.hop("vector")
            p.do("vector", lambda e, c=c, rmrep=rmrep: e.tensor_tensor(
                out=sb["tmpa"][:], in0=sb["tmpa"][:], in1=rmrep, op=OP.subtract))
            p.hop("vector")
            p.do("vector", lambda e, sl=sl, c=c: e.tensor_scalar(
                out=V["h1"][:, c, sl * 512:(sl + 1) * 512], in0=sb["tmpa"][:],
                scalar1=sb["ca1"][:, c:c + 1], scalar2=sb["sa"][:, c:c + 1],
                op0=OP.mult, op1=OP.add), inc="dv" if c == NCH - 1 else None)
            if c < NCH - 1:
                p.hop("vector")
        if sl + 3 < 8:
            emit_x_load(sl + 3, wait_dv=p.cnt["dv"])
    dv_h1 = p.cnt["dv"]
    if debug == 1:
        p.wait("gpsimd", "dv", dv_h1)
        p.do("gpsimd", lambda e: e.dma_start(out=dbg["d_h1"], in_=V["h1"]),
             inc="dgd", inc_n=16)
        p.dbg_barrier()

    p.wait("scalar", "dgm", dg_relg)
    p.wait("tensor", "dv", dv_h1)
    for ch in range(5):
        cs = slice(ch * 512, (ch + 1) * 512) if ch < 4 else slice(2048, 2049)
        n = 512 if ch < 4 else 1
        p.wait("scalar", "pe", p.cnt["pe"])
        p.do("scalar", lambda e, cs=cs, n=n: e.activation(
            out=sb["sqs"][0:64, 0:n], in_=sb["relg"][0:64, cs], func=AF.Square), inc="ac")
        p.wait("tensor", "ac", p.cnt["ac"])
        if ch > 0:
            p.wait("tensor", "dv", p.cnt["dv"])
        p.do("tensor", lambda e, n=n: e.matmul(
            pB[0:2, 1024:1024 + n], sb["o642"][:], sb["sqs"][0:64, 0:n],
            start=True, stop=True), inc="pe")
        p.wait("vector", "pe", p.cnt["pe"])
        if ch == 0:
            p.do("vector", lambda e: e.reduce_max(
                out=sb["lnrel"][:], in_=pB[0:2, 1024:1536], axis=AX.X), inc="dv")
        else:
            p.do("vector", lambda e, n=n: e.reduce_max(
                out=sb["tiny"][:, 9:10], in_=pB[0:2, 1024:1024 + n], axis=AX.X))
            p.hop("vector")
            p.do("vector", lambda e: e.tensor_tensor(
                out=sb["lnrel"][:], in0=sb["lnrel"][:], in1=sb["tiny"][:, 9:10],
                op=OP.max), inc="dv")
    p.wait("scalar", "dv", p.cnt["dv"])
    p.do("scalar", lambda e: e.activation(out=sb["lnrel"][:], in_=sb["lnrel"][:],
                                          func=AF.Ln), inc="ac")
    ac_lnrel = p.cnt["ac"]
    p.do("vector", lambda e: e.memset(sb["eW"][:], 0.0))
    p.hop("vector")
    p.do("vector", lambda e: e.tensor_copy(out=sb["eW"][0:64, 0:1], in_=sb["relg"][0:64, 0:1]))
    p.do("vector", lambda e: e.tensor_copy(out=sb["eW"][0:64, 1:2],
                                           in_=sb["relg"][0:64, 2048:2049]))
    p.do("vector", lambda e: e.tensor_copy(out=sb["eW"][64:128, 2:3],
                                           in_=sb["relg"][64:128, 0:1]))
    p.do("vector", lambda e: e.tensor_copy(out=sb["eW"][64:128, 3:4],
                                           in_=sb["relg"][64:128, 2048:2049]), inc="dv")
    dv_eW = p.cnt["dv"]

    p.pending = None
    p.pguse = 0
    p.guses = {}
    p.dp_hist = []
    p.pmm_rot = 0
    p.sdone_last = 0
    p.g65pe = 0
    p.o65ac = 0
    p.pe_vlast = p.cnt["pe"]
    p.t8_dv = 0
    p.rcol_dv = 0
    p.denb_dv = 0
    kb_ranges = [band_range(qb) for qb in range(16)]
    for hp in range(NP):
        gen_pair(p, dd, sb, V, pA, pB, hp, kb_ranges, dbg, debug, ac_lnrel, dv_eW)

    p.flush_pending()
    p.wait("gpsimd", "pe", p.cnt["pe"])
    p.wait("gpsimd", "dv", p.cnt["dv"])
    p.wait("gpsimd", "ac", p.cnt["ac"])
    p.wait("gpsimd", "dsc", p.cnt["dsc"])
    dg_wo = p.do("gpsimd", lambda e: e.dma_start(
        out=V["wo"], in_=dd["woT"].rearrange("(c p) m -> p c m", p=128)), inc="dgo", inc_n=16)
    p.wait("tensor", "dgo", dg_wo)
    p.wait("tensor", "dv", p.cnt["dv"])
    p.wait("tensor", "ac", p.cnt["ac"])
    p.wait("tensor", "dp", p.cnt["dp"])
    p.wait("tensor", "dsc", p.cnt["dsc"])
    stag_dma = {}
    for blk in range(NCH):
        if blk > 0:
            p.wait("tensor", "dv", p.cnt["dv"])
        if blk >= 2:
            ssem, sval = stag_dma[blk - 2]
            p.wait("vector", ssem, sval)
        for c in range(NCH):
            for nch in range(4):
                ps = (pA if nch < 2 else pB)[:, (nch % 2) * 1024:(nch % 2) * 1024 + 512]
                p.do("tensor", lambda e, blk=blk, c=c, nch=nch, ps=ps: e.matmul(
                    ps, V["wo"][:, c, blk * 128:(blk + 1) * 128],
                    V["comb"][:, c, nch * 512:(nch + 1) * 512],
                    start=(c == 0), stop=(c == NCH - 1)),
                    inc="pe" if (c == NCH - 1 and nch == 3) else None)
        p.wait("vector", "pe", p.cnt["pe"])
        for nch in range(4):
            ps = (pA if nch < 2 else pB)[:, (nch % 2) * 1024:(nch % 2) * 1024 + 512]
            p.do("vector", lambda e, blk=blk, nch=nch, ps=ps: e.tensor_scalar(
                out=V["stag"][:, blk % 2, nch * 512:(nch + 1) * 512], in0=ps, scalar1=1.0,
                scalar2=sb["bo_c"][:, blk:blk + 1], op0=OP.mult, op1=OP.add),
                inc="dv" if nch == 3 else None)
        p.wait("sync", "dv", p.cnt["dv"])
        stag_dma[blk] = ("dss0" if blk % 2 == 0 else "dss1",
            p.do("sync", lambda e, blk=blk: e.dma_start(
                out=attn_sp[blk], in_=V["stag"][:, blk % 2, :]),
                inc="dss0" if blk % 2 == 0 else "dss1", inc_n=16))
    ds_attn = (p.cnt["dss0"], p.cnt["dss1"])
    if debug:
        p.wait("gpsimd", "dss0", ds_attn[0])
        p.wait("gpsimd", "dss1", ds_attn[1])
        p.do("gpsimd", lambda e: e.dma_start(out=dbg["d_attn"], in_=attn_sp), inc="dgd", inc_n=16)
        p.do("gpsimd", lambda e: e.dma_start(out=dbg["d_comb"], in_=V["comb"]), inc="dgd", inc_n=16)
        p.dbg_barrier()

    gen_ffn(p, dd, sb, V, pA, pB, out_cm, attn_sp, ds_attn, dbg, debug)

    p.wait("sync", "dso", p.cnt["dso"])
    p.wait("sync", "dsc", p.cnt["dsc"])
    p.wait("sync", "dsg", p.cnt["dsg"])
    p.wait("sync", "dss0", p.cnt["dss0"])
    p.wait("sync", "dss1", p.cnt["dss1"])
    p.wait("sync", "ds", p.cnt["ds"])
    for s in ("dgm", "dgz", "dgw", "dgo", "dgx0", "dgx1", "dgy0", "dgy1", "dgf", "dgd",
                  "hv", "ha", "ht", "hg", "dp", "dsh"):
        p.wait("gpsimd", s, p.cnt[s])


def emit_ln_rows(p, sb, rsum, rsq, n):
    p.do("vector", lambda e, rsum=rsum: e.tensor_scalar(
        out=sb["rowm"][0:1, 0:n], in0=rsum, scalar1=1.0 / D, scalar2=None, op0=OP.mult))
    p.hop("vector")
    p.do("vector", lambda e: e.tensor_tensor(
        out=sb["rowr"][0:1, 0:n], in0=sb["rowm"][0:1, 0:n], in1=sb["rowm"][0:1, 0:n],
        op=OP.mult))
    p.hop("vector")
    p.do("vector", lambda e: e.tensor_scalar(
        out=sb["rowr"][0:1, 0:n], in0=sb["rowr"][0:1, 0:n], scalar1=-1e-5, scalar2=None,
        op0=OP.add))
    p.hop("vector")
    p.do("vector", lambda e, rsq=rsq: e.scalar_tensor_tensor(
        out=sb["rowr"][0:1, 0:n], in0=rsq, scalar=1.0 / D, in1=sb["rowr"][0:1, 0:n],
        op0=OP.mult, op1=OP.subtract), inc="dv")
    p.wait("scalar", "dv", p.cnt["dv"])
    p.do("scalar", lambda e: e.activation(out=sb["rowr"][0:1, 0:n],
                                          in_=sb["rowr"][0:1, 0:n], func=AF.Sqrt), inc="ac")
    p.wait("vector", "ac", p.cnt["ac"])
    p.do("vector", lambda e: e.reciprocal(out=sb["rowr"][0:1, 0:n], in_=sb["rowr"][0:1, 0:n]))
    p.hop("vector")
    p.do("vector", lambda e: e.tensor_tensor(
        out=sb["rowmr"][0:1, 0:n], in0=sb["rowm"][0:1, 0:n], in1=sb["rowr"][0:1, 0:n],
        op=OP.mult))
    p.do("vector", lambda e: e.tensor_copy(out=sb["rowb"][0:1, 0:n],
                                           in_=sb["rowr"][0:1, 0:n]), inc="dv")


def gen_pair(p, dd, sb, V, pA, pB, hp, kb_ranges, dbg, debug, ac_lnrel, dv_eW):
    p.wait("gpsimd", "pe", p.pe_vlast)
    p.wait("gpsimd", "dp", p.cnt["dp"])
    p.wait("gpsimd", "dsg", p.cnt["dsg"])
    p.wait("gpsimd", "dsh", p.cnt["dsh"])
    lw = []
    for wn, dn in (("wq", "wq_p"), ("wk", "wk_p"), ("wv", "wv_p")):
        lw.append(p.do("gpsimd", lambda e, wn=wn, dn=dn: e.dma_start(
            out=V["wqkv_" + wn],
            in_=dd[dn][hp].rearrange("(c p) m -> p c m", p=128)), inc="dgw", inc_n=16))
    p.wait("gpsimd", "pe", p.g65pe)
    dg_mz = p.do("gpsimd", lambda e: e.dma_start(out=sb["mz"][:], in_=dd["mzp"][hp]),
                 inc="dgz", inc_n=16)
    p.flush_pending()
    p.slot2_fresh = True
    p.pair_pmm_first = True

    p.wait("tensor", "dgw", lw[2])
    p.wait("tensor", "ac", p.cnt["ac"])
    p.wait("tensor", "dv", p.cnt["dv"])
    p.wait("tensor", "dp", p.cnt["dp"])
    for c in range(NCH):
        for nch in range(4):
            p.do("tensor", lambda e, c=c, nch=nch: e.matmul(
                pA[:, nch * 512:(nch + 1) * 512], V["wqkv_wq"][:, c, :],
                V["h1"][:, c, nch * 512:(nch + 1) * 512],
                start=(c == 0), stop=(c == NCH - 1)),
                inc="pe" if (c == NCH - 1 and nch == 3) else None)
    p.wait("vector", "pe", p.cnt["pe"])
    p.do("vector", lambda e: e.tensor_scalar(
        out=V["qh"][:, :], in0=pA[:, 0:2048], scalar1=0.125,
        scalar2=sb["bq_h"][:, hp:hp + 1], op0=OP.mult, op1=OP.add), inc="dv")
    dv_q = p.cnt["dv"]
    for half in range(2):
        p.wait("tensor", "dv", p.cnt["dv"])
        for c in range(NCH):
            for nch in range(4):
                t0 = half * 2048 + nch * 512
                p.do("tensor", lambda e, c=c, nch=nch, t0=t0: e.matmul(
                    pA[:, nch * 512:(nch + 1) * 512], V["wqkv_wk"][:, c, :],
                    V["h1"][:, c, t0:t0 + 512],
                    start=(c == 0), stop=(c == NCH - 1)),
                    inc="pe" if (c == NCH - 1 and nch == 3) else None)
        p.wait("vector", "pe", p.cnt["pe"])
        p.do("vector", lambda e, half=half: e.tensor_scalar(
            out=V["kk"][:, half * 2048:(half + 1) * 2048], in0=pA[:, 0:2048], scalar1=1.0,
            scalar2=sb["bk_c"][:, hp:hp + 1], op0=OP.mult, op1=OP.add), inc="dv")
    for vch in range(4):
        p.wait("tensor", "dv", p.cnt["dv"])
        for c in range(NCH):
            for nn in range(2):
                t0 = vch * 1024 + nn * 512
                p.do("tensor", lambda e, c=c, nn=nn, t0=t0: e.matmul(
                    pB[:, nn * 512:(nn + 1) * 512], V["wqkv_wv"][:, c, :],
                    V["h1"][:, c, t0:t0 + 512],
                    start=(c == 0), stop=(c == NCH - 1)),
                    inc="pe" if (c == NCH - 1 and nn == 1) else None)
        p.wait("vector", "pe", p.cnt["pe"])
        p.do("vector", lambda e: e.tensor_scalar(
            out=V["vf"][:], in0=pB[:, 0:1024], scalar1=1.0,
            scalar2=sb["bv_c"][:, hp:hp + 1], op0=OP.mult, op1=OP.add), inc="dv")
        p.wait("tensor", "dv", p.cnt["dv"])
        pBb = pB[:, 1024:1536].bitcast(BF16)
        for t in range(8):
            p.do("tensor", lambda e, t=t, pBb=pBb: e.transpose(
                pBb[:, t * 128:(t + 1) * 128], V["vf"][:, t * 128:(t + 1) * 128],
                sb["ident"][:]), inc="pe" if t == 7 else None)
        p.wait("vector", "pe", p.cnt["pe"])
        for t in range(8):
            kb = vch * 8 + t
            p.do("vector", lambda e, t=t, kb=kb, pBb=pBb: e.tensor_copy(
                out=V["vp"][:, kb, 0, 0:64],
                in_=pBb[:, t * 128:t * 128 + 64]))
            p.do("vector", lambda e, t=t, kb=kb, pBb=pBb: e.tensor_copy(
                out=V["vp"][:, kb, 1, 0:64],
                in_=pBb[:, t * 128 + 64:t * 128 + 128]),
                inc="dv" if t == 7 else None)
        p.wait("tensor", "dv", p.cnt["dv"])
    p.do("vector", lambda e: e.memset(V["vp"][:, :, :, 64:65], 1.0), inc="dv")
    p.pe_vlast = p.cnt["pe"]

    tmpb = sb["tmpa"][:].bitcast(BF16)
    bbufs = [sb["sqs"][:], tmpb[:, 0:512]]
    chunks = [(0, V["qh"], ch) for ch in range(SQ // 512)] + \
             [(1, V["kk"], ch) for ch in range(S // 512)]
    sq_dv = {}
    sq_pe = {}
    red_dv = {}

    def b_square(k):
        si, srcv, ch = chunks[k]
        buf = bbufs[k % 2]
        if k >= 2:
            p.wait("vector", "pe", sq_pe[k - 2])
        p.do("vector", lambda e, srcv=srcv, ch=ch, buf=buf: e.tensor_tensor(
            out=buf, in0=srcv[:, ch * 512:(ch + 1) * 512],
            in1=srcv[:, ch * 512:(ch + 1) * 512], op=OP.mult), inc="dv")
        sq_dv[k] = p.cnt["dv"]

    b_square(0)
    b_square(1)
    for k in range(len(chunks)):
        si, srcv, ch = chunks[k]
        buf = bbufs[k % 2]
        reg = pB[0:2, 1024 + (k % 2) * 512:1536 + (k % 2) * 512]
        p.wait("tensor", "dv", sq_dv[k])
        if k >= 2:
            p.wait("tensor", "dv", red_dv[k - 2])
        p.do("tensor", lambda e, buf=buf, reg=reg: e.matmul(
            reg, sb["blkd"][:], buf, start=True, stop=True), inc="pe")
        sq_pe[k] = p.cnt["pe"]
        if k + 2 < len(chunks):
            b_square(k + 2)
        p.wait("vector", "pe", sq_pe[k])
        dst = sb["tiny"][:, si:si + 1]
        if ch == 0:
            p.do("vector", lambda e, dst=dst, reg=reg: e.reduce_max(
                out=dst, in_=reg, axis=AX.X), inc="dv")
        else:
            p.wait("vector", "dv", p.cnt["dv"])
            p.do("vector", lambda e, reg=reg: e.reduce_max(
                out=sb["tiny"][:, 9:10], in_=reg, axis=AX.X))
            p.hop("vector")
            p.do("vector", lambda e, dst=dst: e.tensor_tensor(
                out=dst, in0=dst, in1=sb["tiny"][:, 9:10], op=OP.max), inc="dv")
        red_dv[k] = p.cnt["dv"]
    p.wait("scalar", "dv", p.cnt["dv"])
    p.wait("scalar", "ac", ac_lnrel)
    p.do("scalar", lambda e: e.activation(out=sb["tiny"][:, 2:3], in_=sb["tiny"][:, 0:1],
                                          func=AF.Ln))
    p.do("scalar", lambda e: e.activation(out=sb["tiny"][:, 3:4], in_=sb["tiny"][:, 1:2],
                                          func=AF.Ln), inc="ac")
    p.wait("vector", "ac", p.cnt["ac"])
    p.do("vector", lambda e: e.tensor_tensor(out=sb["tiny"][:, 4:5], in0=sb["tiny"][:, 2:3],
                                             in1=sb["tiny"][:, 3:4], op=OP.add))
    p.hop("vector")
    p.do("vector", lambda e: e.tensor_scalar(out=sb["tiny"][:, 4:5], in0=sb["tiny"][:, 4:5],
                                             scalar1=0.5, scalar2=None, op0=OP.mult))
    p.do("vector", lambda e: e.tensor_tensor(out=sb["tiny"][:, 5:6], in0=sb["tiny"][:, 2:3],
                                             in1=sb["lnrel"][:], op=OP.add))
    p.hop("vector")
    p.do("vector", lambda e: e.tensor_scalar(out=sb["tiny"][:, 5:6], in0=sb["tiny"][:, 5:6],
                                             scalar1=0.5, scalar2=None, op0=OP.mult), inc="dv")
    p.wait("scalar", "dv", p.cnt["dv"])
    p.do("scalar", lambda e: e.activation(out=sb["tiny"][:, 6:7], in_=sb["tiny"][:, 4:5],
                                          func=AF.Exp))
    p.do("scalar", lambda e: e.activation(out=sb["tiny"][:, 7:8], in_=sb["tiny"][:, 5:6],
                                          func=AF.Exp), inc="ac")
    p.wait("vector", "ac", p.cnt["ac"])
    p.do("vector", lambda e: e.tensor_tensor(out=sb["tiny"][:, 8:9], in0=sb["tiny"][:, 6:7],
                                             in1=sb["tiny"][:, 7:8], op=OP.add))
    p.hop("vector")
    p.do("vector", lambda e: e.tensor_copy(out=sb["tinyb"][:, 0:1], in_=sb["tiny"][:, 8:9]),
         inc="dv")
    p.wait("tensor", "dv", p.cnt["dv"])
    p.do("tensor", lambda e: e.matmul(pB[0:1, 1024:1026], sb["tinyb"][:, 0:1], sb["i2"][:],
                                      start=True, stop=True), inc="pe")
    p.wait("vector", "pe", p.cnt["pe"])
    p.do("vector", lambda e: e.reduce_max(out=sb["tiny"][0:1, 9:10], in_=pB[0:1, 1024:1026],
                                          axis=AX.X))
    p.hop("vector")
    p.do("vector", lambda e: e.tensor_copy(out=sb["tinyb"][0:1, 1:2],
                                           in_=sb["tiny"][0:1, 9:10]), inc="dv")
    p.wait("tensor", "dv", p.cnt["dv"])
    p.do("tensor", lambda e: e.matmul(pB[:, 1026:1027], sb["ones_r"][:], sb["tinyb"][0:1, 1:2],
                                      start=True, stop=True), inc="pe")
    p.wait("scalar", "pe", p.cnt["pe"])
    p.do("scalar", lambda e: e.activation(out=sb["negb"][:], in_=pB[:, 1026:1027],
                                          func=AF.Copy, scale=-1.0), inc="ac")
    ac_negb = p.cnt["ac"]

    p.wait("tensor", "dv", dv_eW)
    p.wait("tensor", "ac", ac_negb)
    for nch in range(4):
        reg = pB[0:4, 1024 + (nch % 2) * 512:1536 + (nch % 2) * 512]
        if nch >= 2:
            p.wait("tensor", "dv", p.cnt["dv"])
        p.do("tensor", lambda e, nch=nch, reg=reg: e.matmul(
            reg, sb["eW"][:], V["qh"][:, nch * 512:(nch + 1) * 512],
            start=True, stop=True), inc="pe")
        p.wait("vector", "pe", p.cnt["pe"])
        p.do("vector", lambda e, nch=nch, reg=reg: e.tensor_copy(
            out=sb["edges4"][:, nch * 512:(nch + 1) * 512],
            in_=reg), inc="dv")
    dv_edges = p.cnt["dv"]
    if debug == 1 and hp == 0:
        p.wait("gpsimd", "ac", ac_negb)
        p.wait("gpsimd", "dv", dv_edges)
        for nm, src in (("d_negb", sb["negb"][:]), ("d_qh", V["qh"]), ("d_k", V["kk"]),
                        ("d_vp", V["vp"]), ("d_edge", sb["edges4"][:])):
            p.do("gpsimd", lambda e, nm=nm, src=src: e.dma_start(out=dbg[nm], in_=src),
                 inc="dgd", inc_n=16)
        p.dbg_barrier()

    ec_ev = {}
    for qb in range(16):
        pos = 1024 + (qb % 2) * 512
        if qb < 2:
            p.wait("tensor", "dv", dv_edges)
        else:
            p.wait("tensor", "dv", ec_ev[qb - 2])
        for j in range(4):
            p.do("tensor", lambda e, qb=qb, j=j, pos=pos: e.matmul(
                pB[:, pos + j:pos + j + 1], sb["edges4"][:, qb * 128:(qb + 1) * 128],
                sb["sel_c"][:, j:j + 1], start=True, stop=True),
                inc="pe" if j == 3 else None)
        p.wait("vector", "pe", p.cnt["pe"])
        ec_ev[qb] = p.do("vector", lambda e, qb=qb, pos=pos: e.tensor_copy(
            out=sb["ecols"][:, qb * 4:(qb + 1) * 4], in_=pB[:, pos:pos + 4]), inc="dv")

    p.pair_dv0 = p.cnt["dv"]
    p.t8_dv = p.pair_dv0
    for qg in range(NG):
        gen_group(p, sb, V, pA, pB, hp, qg, kb_ranges, dbg, debug)


def rg_layout(qg):
    offs = {}
    pos = 0
    for qi in range(GQB):
        qb = qg * GQB + qi
        kb_lo, kb_hi = band_range(qb)
        w = (kb_hi - kb_lo + 1) * 128
        for h in range(2):
            offs[(qi, h)] = (pos, w)
            pos += w
    assert pos <= 18432, pos
    return offs


def gen_group(p, sb, V, pA, pB, hp, qg, kb_ranges, dbg, debug):
    qbs = [qg * GQB + i for i in range(GQB)]
    q0g = qg * GQB * 128
    offs = rg_layout(qg)
    denbL = sb["rowm"][:].bitcast(BF16)
    denbG = sb["rowr"][:].bitcast(BF16)
    row_ext = sb["arena"][:].ap[0][0]

    p.grp_low_first = True
    sgv = V["sig"] if qg % 2 == 0 else V["sigB"]
    p.wait("vector", "pe", p.g65pe)
    p.do("vector", lambda e: e.tensor_scalar(
        out=V["sigm"][:, :], in0=V["qh"][:, q0g:q0g + 512], scalar1=0.0, scalar2=None,
        op0=OP.min), inc="dv")
    p.wait("scalar", "dv", p.cnt["dv"])
    p.do("scalar", lambda e, sgv=sgv: e.activation(out=sgv[:, :], in_=V["sigm"][:, :],
                                                   func=AF.Exp, scale=8.0), inc="ac")
    p.wait("vector", "ac", p.cnt["ac"])
    p.do("vector", lambda e: e.tensor_scalar(
        out=V["sigm"][:, :], in0=V["qh"][:, q0g:q0g + 512], scalar1=0.0, scalar2=None,
        op0=OP.max))
    p.hop("vector")
    p.do("vector", lambda e, sgv=sgv: e.scalar_tensor_tensor(
        out=sgv[:, :], in0=V["sigm"][:, :], scalar=8.0, in1=sgv[:, :],
        op0=OP.mult, op1=OP.add), inc="dv")
    p.sig_dv = p.cnt["dv"]

    gmap = {}
    for qi, qb in enumerate(qbs):
        kb_lo, kb_hi = kb_ranges[qb]
        wb = (kb_hi - kb_lo + 1) * 128
        colstart = kb_lo * 128 - qb * 128 + 1024 + PAD
        assert colstart - 127 >= 0 and colstart + wb - 1 <= CP - 1
        for h in range(2):
            u = p.pguse
            sv = V["pgs"][u % 3]
            if u - 3 in p.guses:
                gs, gv = p.guses.pop(u - 3)
                p.wait("vector", gs, gv)
            if u % 3 == 2 and p.slot2_fresh:
                p.slot2_fresh = False
                p.wait("vector", "pe", p.pe_vlast)
            ecl = qb * 4 + 2 * h
            p.do("vector", lambda e, sv=sv, ecl=ecl: e.tensor_copy(
                out=sv[:, 0:PAD],
                in_=sb["ecols"][:, ecl:ecl + 1].broadcast_to([128, PAD])), inc="dp")
            p.do("vector", lambda e, sv=sv, ecl=ecl: e.tensor_copy(
                out=sv[:, PAD + 2048:CP],
                in_=sb["ecols"][:, ecl + 1:ecl + 2].broadcast_to(
                    [128, CP - PAD - 2048])), inc="dp")
            for j in range(4):
                r = p.pmm_rot % 4
                p.pmm_rot += 1
                c0 = (1024 + r * 512) if r < 2 else ((r - 2) * 512)
                reg = pB[:, c0:c0 + 512]
                if len(p.dp_hist) >= 4:
                    p.wait("tensor", "dp", p.dp_hist[-4])
                if r == 2 and p.grp_low_first:
                    p.grp_low_first = False
                    p.wait("tensor", "ac", p.o65ac)
                    p.wait("tensor", "dv", max(p.denb_dv, p.rcol_dv))
                if p.pair_pmm_first:
                    p.pair_pmm_first = False
                    p.wait("tensor", "dv", p.pair_dv0)
                p.do("tensor", lambda e, reg=reg, h=h, qb=qb, j=j: e.matmul(
                    reg, V["qh"][h * 64:(h + 1) * 64, qb * 128:(qb + 1) * 128],
                    sb["relg"][h * 64:(h + 1) * 64, j * 512:(j + 1) * 512],
                    start=True, stop=True, tile_position=(h * 64, 0)), inc="pe")
                p.wait("vector", "pe", p.cnt["pe"])
                ev = p.do("vector", lambda e, sv=sv, j=j, reg=reg: e.tensor_copy(
                    out=sv[:, PAD + j * 512:PAD + (j + 1) * 512], in_=reg), inc="dp")
                p.dp_hist.append(ev)
            qeng, qsem = ("sync", "dsg") if h == 0 else ("gpsimd", "dsh")
            p.wait(qeng, qsem, p.cnt[qsem])
            p.wait(qeng, "dp", p.cnt["dp"])
            p.wait(qeng, "pe", p.sdone_last)
            off0, w0 = offs[(qi, h)]
            assert w0 == wb
            src = AP(sv.tensor, sv.offset + colstart, [[row_ext - 1, 128], [1, wb]])
            g = p.do(qeng, lambda e, src=src, off0=off0, wb=wb: e.dma_start(
                out=V["rg"][:, off0:off0 + wb], in_=src), inc=qsem, inc_n=16)
            gmap[(qi, h)] = (qsem, g)
            p.guses[u] = (qsem, g)
            p.pguse += 1
        if debug == 1 and hp == 0 and qg == 0 and qb == 0:
            p.wait("gpsimd", "dp", p.cnt["dp"])
            p.do("gpsimd", lambda e: e.dma_start(
                out=dbg["d_pg"], in_=V["pgs"][0][:, :]), inc="dgd", inc_n=16)
            p.wait("gpsimd", "dsg", gmap[(0, 0)][1])
            p.do("gpsimd", lambda e, wb=wb: e.dma_start(
                out=dbg["d_rg"][:, 0:wb], in_=V["rg"][:, 0:wb]), inc="dgd", inc_n=16)
            p.dbg_barrier()

    p.flush_pending()

    p.wait("tensor", "dv", p.cnt["dv"])
    p.wait("tensor", "dgz", p.cnt["dgz"])
    p.wait("tensor", "ac", p.o65ac)
    p.do("tensor", lambda e, sgv=sgv: e.matmul(pA[0:65, 1024:1536], sb["mz"][0:64, :],
                                               sgv[0:64, :], start=True, stop=True,
                                               tile_position=(0, 0)))
    p.do("tensor", lambda e, sgv=sgv: e.matmul(pA[0:65, 1536:2048], sb["mz"][64:128, :],
                                               sgv[64:128, :], start=True, stop=True,
                                               tile_position=(64, 0)), inc="pe")
    p.g65pe = p.cnt["pe"]
    p.wait("vector", "pe", p.g65pe)
    for h in range(2):
        p.do("vector", lambda e, h=h: e.tensor_copy(
            out=sb["g65s"][:, h, :], in_=pA[0:65, 1024 + h * 512:1024 + (h + 1) * 512]))
        p.do("vector", lambda e, h=h, denbG=denbG: e.tensor_scalar(
            out=denbG[0:1, h * 512:(h + 1) * 512],
            in0=pA[64:65, 1024 + h * 512:1024 + (h + 1) * 512],
            scalar1=1e-6, scalar2=None, op0=OP.add), inc="dv" if h == 1 else None)
    kb_dv0 = p.cnt["dv"]

    exp_base = p.cnt["ac"]
    sdone = []
    avdone = []
    waited = {"dsg": 0, "dsh": 0}

    def emit_S(kb):
        pp = kb % 2
        reg = pA[:, pp * 1024:(pp + 1) * 1024]
        if kb == 0:
            p.wait("tensor", "ac", p.o65ac)
            p.wait("tensor", "dv", p.t8_dv)
        elif kb == 1:
            p.wait("tensor", "dv", kb_dv0)
        else:
            p.wait("tensor", "ac", exp_base + (kb - 1))
        for qi, qb in enumerate(qbs):
            if kb_ranges[qb][0] <= kb <= kb_ranges[qb][1]:
                for h in range(2):
                    qsem, gv = gmap[(qi, h)]
                    if gv > waited[qsem]:
                        waited[qsem] = gv
                        p.wait("tensor", qsem, gv)
        for h in range(2):
            hreg = reg[:, h * 512:(h + 1) * 512]
            tp = (0, 0) if h == 0 else (64, 0)
            band = [(qi, qb, kb - kb_ranges[qb][0]) for qi, qb in enumerate(qbs)
                    if kb_ranges[qb][0] <= kb <= kb_ranges[qb][1]]
            farA = [qi for qi, qb in enumerate(qbs) if kb < kb_ranges[qb][0]]
            farB = [qi for qi, qb in enumerate(qbs) if kb > kb_ranges[qb][1]]
            nacc = len(band) + (1 if farA else 0) + (1 if farB else 0)
            p.do("tensor", lambda e, hreg=hreg, h=h, kb=kb, tp=tp, nacc=nacc: e.matmul(
                hreg, V["kk"][h * 64:(h + 1) * 64, kb * 128:(kb + 1) * 128],
                V["qh"][h * 64:(h + 1) * 64, q0g:q0g + 512],
                start=True, stop=(nacc == 0), tile_position=tp))
            i = 0
            for qi, qb, off in band:
                i += 1
                off0, w0 = offs[(qi, h)]
                last = (i == nacc)
                p.do("tensor", lambda e, hreg=hreg, qi=qi, off0=off0, off=off, last=last: e.matmul(
                    hreg[:, qi * 128:(qi + 1) * 128],
                    V["rg"][:, off0 + off * 128:off0 + (off + 1) * 128],
                    sb["ident"][:], start=False, stop=last))
            for falist, eidx in ((farA, 0), (farB, 1)):
                if not falist:
                    continue
                i += 1
                lo, hi = falist[0], falist[-1]
                assert falist == list(range(lo, hi + 1))
                last = (i == nacc)
                j = 2 * h + eidx
                p.do("tensor", lambda e, hreg=hreg, j=j, lo=lo, hi=hi, last=last: e.matmul(
                    hreg[:, lo * 128:(hi + 1) * 128], sb["sel_r"][:, j * 128:(j + 1) * 128],
                    sb["edges4"][:, q0g + lo * 128:q0g + (hi + 1) * 128],
                    start=False, stop=last))
        v = p.inc_last("tensor", "pe")
        sdone.append(v)

    dp_all = p.cnt["dp"]

    def emit_attnv(kb):
        p.wait("tensor", "ac", exp_base + kb + 1)
        if kb == 0:
            p.wait("tensor", "dp", dp_all)
        for h in range(2):
            p.do("tensor", lambda e, kb=kb, h=h: e.matmul(
                pB[0:65, h * 512:(h + 1) * 512], V["vp"][:, kb, h, :],
                V["ee"][:, kb % 2, h * 512:(h + 1) * 512],
                start=(kb == 0), stop=(kb == NKB - 1)))
        v = p.inc_last("tensor", "pe")
        avdone.append(v)

    emit_S(0)
    emit_S(1)
    for kb in range(NKB):
        if kb + 2 < NKB:
            emit_S(kb + 2)
        emit_attnv(kb)
    for kb in range(NKB):
        p.wait("scalar", "pe", sdone[kb])
        if kb >= 2:
            p.wait("scalar", "pe", avdone[kb - 2])
        p.do("scalar", lambda e, kb=kb: e.activation(
            out=V["ee"][:, kb % 2, :], in_=pA[:, (kb % 2) * 1024:(kb % 2 + 1) * 1024],
            func=AF.Exp, bias=sb["negb"][:], scale=1.0), inc="ac")
    p.sdone_last = sdone[-1]

    avlast = avdone[-1]
    p.wait("scalar", "pe", avlast)
    p.do("scalar", lambda e: e.activation(out=sb["o65"][:, 0, :], in_=pB[0:65, 0:512],
                                          func=AF.Copy))
    p.do("scalar", lambda e: e.activation(out=sb["o65"][:, 1, :], in_=pB[0:65, 512:1024],
                                          func=AF.Copy), inc="ac")
    p.o65ac = p.cnt["ac"]
    if debug == 1 and hp == 0 and qg == 0:
        p.wait("gpsimd", "ac", p.o65ac)
        p.do("gpsimd", lambda e: e.dma_start(out=dbg["d_o65"], in_=sb["o65"][:]),
             inc="dgd", inc_n=16)
        p.dbg_barrier()
    p.wait("vector", "ac", p.o65ac)
    for h in range(2):
        p.do("vector", lambda e, h=h, denbL=denbL: e.tensor_copy(
            out=denbL[0:1, h * 512:(h + 1) * 512],
            in_=pB[64:65, h * 512:(h + 1) * 512]), inc="dv" if h == 1 else None)
    p.denb_dv = p.cnt["dv"]

    t1 = p.o65ac
    t2 = p.denb_dv

    def tail(t1=t1, t2=t2, q0g=q0g, hp=hp, qg=qg, denbL=denbL, denbG=denbG):
        p.wait("tensor", "ac", t1)
        p.wait("tensor", "dv", t2)
        p.wait("tensor", "dp", p.cnt["dp"])
        for c in range(16):
            srcr = denbL if c < 8 else denbG
            cc = c % 8
            p.do("tensor", lambda e, srcr=srcr, cc=cc, c=c: e.matmul(
                pB[:, c:c + 1], srcr[0:1, cc * 128:(cc + 1) * 128],
                sb["ones_r"][0:1, 0:1], start=True, stop=True),
                inc="pe" if c == 15 else None)
        p.wait("vector", "pe", p.cnt["pe"])
        p.do("vector", lambda e: e.reciprocal(out=sb["rcolf"][:], in_=pB[:, 0:16]))
        p.hop("vector")
        p.do("vector", lambda e: e.tensor_copy(out=sb["rcol"][:], in_=sb["rcolf"][:]),
             inc="dv")
        p.rcol_dv = p.cnt["dv"]
        p.wait("tensor", "dv", p.cnt["dv"])
        for rr in range(4):
            for c in range(4):
                j = rr * 4 + c
                p.do("tensor", lambda e, rr=rr, c=c, j=j: e.matmul(
                    pA[0:64, rr * 512 + c * 128:rr * 512 + (c + 1) * 128],
                    sb["rcol"][:, j:j + 1].broadcast_to([128, 64]),
                    sb["ident"][:], start=True, stop=True),
                    inc="pe" if j == 15 else None)
        p.wait("vector", "pe", p.cnt["pe"])
        p.wait("vector", "dsc", p.cnt["dsc"])
        for h in range(2):
            if h == 1:
                p.wait("vector", "dv", p.cnt["dv"])
            p.do("vector", lambda e, h=h: e.scalar_tensor_tensor(
                out=sb["ct"][:, 0, :], in0=sb["o65"][0:64, h, :],
                scalar=sb["gate"][0:64, :],
                in1=pA[0:64, h * 512:(h + 1) * 512], op0=OP.mult, op1=OP.mult))
            p.do("vector", lambda e, h=h: e.scalar_tensor_tensor(
                out=sb["ct"][:, 1, :], in0=sb["g65s"][0:64, h, :],
                scalar=sb["gate1m"][0:64, :],
                in1=pA[0:64, 1024 + h * 512:1024 + (h + 1) * 512],
                op0=OP.mult, op1=OP.mult))
            p.hop("vector")
            if h == 0:
                p.do("vector", lambda e: e.tensor_tensor(
                    out=V["comb"][0:64, hp, q0g:q0g + 512], in0=sb["ct"][:, 0, :],
                    in1=sb["ct"][:, 1, :], op=OP.add), inc="dv")
            else:
                p.do("vector", lambda e: e.tensor_tensor(
                    out=sb["ct"][:, 0, :], in0=sb["ct"][:, 0, :],
                    in1=sb["ct"][:, 1, :], op=OP.add), inc="dv")
                p.wait("sync", "dv", p.cnt["dv"])
                p.do("sync", lambda e: e.dma_start(
                    out=V["comb"][64:128, hp, q0g:q0g + 512], in_=sb["ct"][:, 0, :]),
                    inc="dsc", inc_n=16)
        p.t8_dv = p.cnt["dv"]

    p.pending = tail


def gen_ffn(p, dd, sb, V, pA, pB, out_cm, attn_sp, ds_attn, dbg, debug):
    def wload(dname, c, width, wait_ac=None, wait_dv=None):
        if wait_ac is not None:
            p.wait("gpsimd", "ac", wait_ac)
        if wait_dv is not None:
            p.wait("gpsimd", "dv", wait_dv)
        s = "dgx0" if c % 2 == 0 else "dgx1"
        return (s, p.do("gpsimd", lambda e, c=c, dname=dname, width=width: e.dma_start(
            out=V["wsl"][:, c % 2, 0:width], in_=dd[dname][c * 128:(c + 1) * 128, :]),
            inc=s, inc_n=16))

    p.wait("tensor", "ac", p.cnt["ac"])
    p.wait("tensor", "dv", p.cnt["dv"])
    p.wait("gpsimd", "dss0", ds_attn[0])
    p.wait("gpsimd", "dss1", ds_attn[1])
    loads = {0: wload("w1T", 0, 4 * D), 1: wload("w1T", 1, 4 * D)}
    done = {}
    tmpb = sb["tmpa"][:].bitcast(BF16)
    abufs = [sb["sqs"][:], tmpb[:, 0:512]]
    p.wait("scalar", "pe", p.cnt["pe"])
    p.wait("vector", "pe", p.cnt["pe"])
    a_dv = {}
    a_pe = {}

    def a_abs(c, ch):
        k = c * 8 + ch
        buf = abufs[k % 2]
        if k >= 2:
            p.wait("vector", "pe", a_pe[k - 2])
        lsem, lval = loads[c]
        if ch == 0:
            p.wait("vector", lsem, lval)
        p.do("vector", lambda e, c=c, ch=ch, buf=buf: e.scalar_tensor_tensor(
            out=buf, in0=V["wsl"][:, c % 2, ch * 512:(ch + 1) * 512],
            scalar=-1.0, in1=V["wsl"][:, c % 2, ch * 512:(ch + 1) * 512],
            op0=OP.mult, op1=OP.max), inc="dv")
        a_dv[k] = p.cnt["dv"]

    a_abs(0, 0)
    a_abs(0, 1)
    for c in range(NCH):
        lsem, lval = loads[c]
        p.wait("scalar", lsem, lval)
        p.do("scalar", lambda e, c=c: e.activation(
            out=V["s1"][:, c, :], in_=V["wsl"][:, c % 2, :], func=AF.Sign), inc="ac")
        for ch in range(8):
            k = c * 8 + ch
            p.wait("tensor", "dv", a_dv[k])
            ps = (pA if ch < 4 else pB)[0:1, (ch % 4) * 512:(ch % 4 + 1) * 512]
            p.do("tensor", lambda e, ps=ps, c=c, buf=abufs[k % 2]: e.matmul(
                ps, sb["onesc"][:], buf, start=(c == 0), stop=(c == NCH - 1)),
                inc="pe")
            a_pe[k] = p.cnt["pe"]
            if k + 2 < 64:
                a_abs((k + 2) // 8, (k + 2) % 8)
        done[c] = p.cnt["ac"]
        if c + 2 < NCH:
            loads[c + 2] = wload("w1T", c + 2, 4 * D, wait_ac=done[c],
                                 wait_dv=a_dv[c * 8 + 7])
    p.wait("vector", "pe", p.cnt["pe"])
    for ch in range(8):
        ps = (pA if ch < 4 else pB)[0:1, (ch % 4) * 512:(ch % 4 + 1) * 512]
        p.do("vector", lambda e, ps=ps: e.tensor_scalar(
            out=sb["rowb"][:], in0=ps, scalar1=1.0 / D, scalar2=None, op0=OP.mult),
            inc="dv")
        p.wait("tensor", "dv", p.cnt["dv"])
        for q in range(4):
            p.do("tensor", lambda e, ch=ch, q=q: e.matmul(
                pB[:, 1536 + 4 * (ch % 2) + q:1537 + 4 * (ch % 2) + q],
                sb["rowb"][0:1, q * 128:(q + 1) * 128],
                sb["ones_r"][0:1, 0:1], start=True, stop=True),
                inc="pe" if q == 3 else None)
        p.wait("vector", "pe", p.cnt["pe"])
        p.do("vector", lambda e, ch=ch: e.tensor_copy(
            out=sb["a1c"][:, ch * 4:(ch + 1) * 4],
            in_=pB[:, 1536 + 4 * (ch % 2):1540 + 4 * (ch % 2)]), inc="dv")
    loads = {0: wload("w2T", 0, D, wait_ac=done[NCH - 2],
                      wait_dv=a_dv[(NCH - 2) * 8 + 7]),
             1: wload("w2T", 1, D, wait_ac=done[NCH - 1],
                      wait_dv=a_dv[(NCH - 1) * 8 + 7])}
    done2 = {}
    p.wait("scalar", "pe", p.cnt["pe"])
    p.wait("scalar", "dss0", p.cnt["dss0"])
    p.wait("scalar", "dss1", p.cnt["dss1"])
    p.wait("vector", "pe", p.cnt["pe"])
    a2_dv = {}
    a2_pe = {}

    def a2_abs(c, ch):
        k = c * 2 + ch
        buf = abufs[k % 2]
        if k >= 2:
            p.wait("vector", "pe", a2_pe[k - 2])
        if ch == 0:
            lsem, lval = loads[c]
            p.wait("vector", lsem, lval)
        p.do("vector", lambda e, c=c, ch=ch, buf=buf: e.scalar_tensor_tensor(
            out=buf, in0=V["wsl"][:, c % 2, ch * 512:(ch + 1) * 512],
            scalar=-1.0, in1=V["wsl"][:, c % 2, ch * 512:(ch + 1) * 512],
            op0=OP.mult, op1=OP.max), inc="dv")
        a2_dv[k] = p.cnt["dv"]

    a2_abs(0, 0)
    a2_abs(0, 1)
    for c in range(32):
        lsem, lval = loads[c]
        p.wait("scalar", lsem, lval)
        p.do("scalar", lambda e, c=c: e.activation(
            out=V["s2"][:, c, :], in_=V["wsl"][:, c % 2, 0:1024], func=AF.Sign), inc="ac")
        for ch in range(2):
            k = c * 2 + ch
            p.wait("tensor", "dv", a2_dv[k])
            ps = pA[0:1, ch * 512:(ch + 1) * 512]
            p.do("tensor", lambda e, ps=ps, c=c, buf=abufs[k % 2]: e.matmul(
                ps, sb["onesc"][:], buf, start=(c == 0), stop=(c == 31)),
                inc="pe")
            a2_pe[k] = p.cnt["pe"]
            if k + 2 < 64:
                a2_abs((k + 2) // 2, (k + 2) % 2)
        done2[c] = p.cnt["ac"]
        if c + 2 < 32:
            loads[c + 2] = wload("w2T", c + 2, D, wait_ac=done2[c],
                                 wait_dv=a2_dv[c * 2 + 1])
    p.wait("vector", "pe", p.cnt["pe"])
    for ch in range(2):
        ps = pA[0:1, ch * 512:(ch + 1) * 512]
        p.do("vector", lambda e, ps=ps: e.tensor_scalar(
            out=sb["rowb"][:], in0=ps, scalar1=1.0 / (4 * D), scalar2=None, op0=OP.mult),
            inc="dv")
        p.wait("tensor", "dv", p.cnt["dv"])
        for q in range(4):
            p.do("tensor", lambda e, ch=ch, q=q: e.matmul(
                pB[:, 1536 + 4 * ch + q:1537 + 4 * ch + q],
                sb["rowb"][0:1, q * 128:(q + 1) * 128],
                sb["ones_r"][0:1, 0:1], start=True, stop=True),
                inc="pe" if q == 3 else None)
        p.wait("vector", "pe", p.cnt["pe"])
        p.do("vector", lambda e, ch=ch: e.tensor_copy(
            out=sb["a2c"][:, ch * 4:(ch + 1) * 4],
            in_=pB[:, 1536 + 4 * ch:1540 + 4 * ch]), inc="dv")
    dv_bw = p.cnt["dv"]
    if debug == 1:
        p.wait("gpsimd", "dv", dv_bw)
        p.do("gpsimd", lambda e: e.dma_start(out=dbg["d_s1"], in_=V["s1"]),
             inc="dgd", inc_n=16)
        p.dbg_barrier()

    xloads = {}
    atloads = {}
    ac_signs_done = p.cnt["ac"]
    def emit_fx(sl, wait_dv=None):
        ap = dd["xT"][:, sl * FSL:(sl + 1) * FSL].rearrange("(c p) t -> p c t", p=128)
        if wait_dv is not None:
            p.wait("gpsimd", "dv", wait_dv)
        if sl == 0:
            p.wait("gpsimd", "ac", ac_signs_done)
        xloads[sl] = p.do("gpsimd", lambda e, ap=ap: e.dma_start(
            out=V["fx"], in_=ap), inc="dgf", inc_n=16)
        ap2 = attn_sp[:, :, sl * FSL:(sl + 1) * FSL].rearrange("c p t -> p c t")
        atloads[sl] = p.do("gpsimd", lambda e, ap2=ap2: e.dma_start(
            out=V["fat"], in_=ap2), inc="dgf", inc_n=16)
    emit_fx(0)

    R8 = [pA[:, i * 512:(i + 1) * 512] for i in range(4)] + \
         [pB[:, i * 512:(i + 1) * 512] for i in range(4)]
    out_dma = {}
    import os
    SLAB_BARRIER = os.environ.get("DK_SLAB_BARRIER", "0") == "1"
    for sl in range(NFS):
        if SLAB_BARRIER and sl > 0:
            for eng in ("tensor", "scalar", "vector", "gpsimd", "sync"):
                p.wait(eng, "pe", p.cnt["pe"])
                p.wait(eng, "ac", p.cnt["ac"])
                p.wait(eng, "dv", p.cnt["dv"])
                p.wait(eng, "dso", p.cnt["dso"])
        t0 = sl * FSL
        p.wait("vector", "dgf", atloads[sl])
        if sl == 0:
            p.wait("vector", "dv", dv_bw)
            p.wait("vector", "ac", ac_signs_done)
        xs = V["fx"]
        at = V["fat"]
        p.wait("vector", "dv", p.cnt["dv"])
        for c in range(NCH):
            p.do("vector", lambda e, xs=xs, at=at, c=c: e.scalar_tensor_tensor(
                out=V["x2sl"][:, c, :], in0=at[:, c, :],
                scalar=sb["ga"][:, c:c + 1], in1=xs[:, c, :], op0=OP.mult, op1=OP.add),
                inc="dv" if c == NCH - 1 else None)
        if sl + 1 < NFS:
            emit_fx(sl + 1, wait_dv=p.cnt["dv"])
        p.wait("vector", "dv", p.cnt["dv"])
        p.do("vector", lambda e: e.tensor_tensor(
            out=V["xsq2"], in0=V["x2sl"], in1=V["x2sl"], op=OP.mult), inc="dv")
        p.wait("tensor", "dv", p.cnt["dv"])
        p.wait("tensor", "ac", p.cnt["ac"])
        rsum, rsq = pB[0:1, 1024:1024 + FSL], pB[0:1, 1536:1536 + FSL]
        rrep, rmrep = pA[:, 1024:1024 + FSL], pA[:, 1536:1536 + FSL]
        for c in range(NCH):
            p.do("tensor", lambda e, c=c, rsum=rsum: e.matmul(
                rsum, sb["onescf"][:], V["x2sl"][:, c, :],
                start=(c == 0), stop=(c == NCH - 1)))
        for c in range(NCH):
            p.do("tensor", lambda e, c=c, rsq=rsq: e.matmul(
                rsq, sb["onesc"][:], V["xsq2"][:, c, :],
                start=(c == 0), stop=(c == NCH - 1)),
                inc="pe" if c == NCH - 1 else None)
        p.wait("vector", "pe", p.cnt["pe"])
        emit_ln_rows(p, sb, rsum, rsq, FSL)
        p.wait("tensor", "dv", p.cnt["dv"])
        p.do("tensor", lambda e, rrep=rrep: e.matmul(
            rrep, sb["ones_r"][:], sb["rowb"][0:1, 0:FSL], start=True, stop=True), inc="pe")
        p.wait("vector", "pe", p.cnt["pe"])
        p.do("vector", lambda e: e.tensor_copy(out=sb["rowb"][0:1, 0:FSL],
                                               in_=sb["rowmr"][0:1, 0:FSL]), inc="dv")
        p.wait("tensor", "dv", p.cnt["dv"])
        p.do("tensor", lambda e, rmrep=rmrep: e.matmul(
            rmrep, sb["ones_r"][:], sb["rowb"][0:1, 0:FSL], start=True, stop=True), inc="pe")
        p.wait("vector", "pe", p.cnt["pe"])
        for c in range(NCH):
            p.do("vector", lambda e, c=c, rrep=rrep: e.tensor_tensor(
                out=sb["tmpa"][:, 0:FSL], in0=V["x2sl"][:, c, :], in1=rrep, op=OP.mult))
            p.hop("vector")
            p.do("vector", lambda e, c=c, rmrep=rmrep: e.tensor_tensor(
                out=sb["tmpa"][:, 0:FSL], in0=sb["tmpa"][:, 0:FSL], in1=rmrep,
                op=OP.subtract))
            p.hop("vector")
            p.do("vector", lambda e, c=c: e.tensor_scalar(
                out=V["h3sl"][:, c, :], in0=sb["tmpa"][:, 0:FSL],
                scalar1=sb["cb1"][:, c:c + 1], scalar2=sb["sbv"][:, c:c + 1],
                op0=OP.mult, op1=OP.add), inc="dv" if c == NCH - 1 else None)
            if c < NCH - 1:
                p.hop("vector")
        dv_h3 = p.cnt["dv"]
        if debug == 1 and sl == 0:
            p.wait("gpsimd", "dv", dv_h3)
            p.do("gpsimd", lambda e: e.dma_start(out=dbg["d_h3"], in_=V["h3sl"]),
                 inc="dgd", inc_n=16)
            p.dbg_barrier()
        p.wait("tensor", "dv", dv_h3)
        evac = {}
        for ho in range(32):
            reg = R8[ho % 8]
            if ho >= 8:
                p.wait("tensor", "ac", evac[ho - 8])
            for c in range(NCH):
                p.do("tensor", lambda e, ho=ho, c=c, reg=reg: e.matmul(
                    reg[:, 0:FSL], V["s1"][:, c, ho * 128:(ho + 1) * 128],
                    V["h3sl"][:, c, :], start=(c == 0), stop=(c == NCH - 1)),
                    inc="pe" if c == NCH - 1 else None)
            p.wait("scalar", "pe", p.cnt["pe"])
            evac[ho] = p.do("scalar", lambda e, ho=ho, reg=reg: e.activation(
                out=V["h2sl"][:, ho, :], in_=reg[:, 0:FSL], func=AF.Gelu,
                bias=sb["b1_c"][:, ho:ho + 1], scale=sb["a1c"][:, ho:ho + 1]), inc="ac")
        if sl >= 1:
            p.wait("vector", "dso", out_dma[(sl - 1, 1)])
        p.wait("tensor", "ac", evac[31])
        for blk in range(NCH):
            reg = R8[blk]
            for hc in range(32):
                p.do("tensor", lambda e, blk=blk, hc=hc, reg=reg: e.matmul(
                    reg[:, 0:FSL], V["s2"][:, hc, blk * 128:(blk + 1) * 128],
                    V["h2sl"][:, hc, :], start=(hc == 0), stop=(hc == 31)),
                    inc="pe" if hc == 31 else None)
            p.wait("vector", "pe", p.cnt["pe"])
            p.wait("vector", "dv", p.cnt["dv"])
            if blk == 4:
                p.wait("vector", "dso", out_dma[(sl, 0)])
            p.do("vector", lambda e, blk=blk, reg=reg: e.tensor_scalar(
                out=sb["sqs"][:, 0:FSL], in0=reg[:, 0:FSL], scalar1=sb["a2c"][:, blk:blk + 1],
                scalar2=sb["b2_c"][:, blk:blk + 1], op0=OP.mult, op1=OP.add))
            p.hop("vector")
            p.do("vector", lambda e, sl=sl, blk=blk: e.scalar_tensor_tensor(
                out=V["outsl"][:, blk % 4, :], in0=sb["sqs"][:, 0:FSL],
                scalar=sb["gb"][:, blk:blk + 1], in1=V["x2sl"][:, blk, :],
                op0=OP.mult, op1=OP.add), inc="dv")
            if blk % 4 == 3:
                half4 = blk // 4
                p.wait("sync", "dv", p.cnt["dv"])
                dst = out_cm[half4 * 4 * 128:(half4 + 1) * 4 * 128,
                             t0:t0 + FSL].rearrange("(c p) t -> p c t", p=128)
                out_dma[(sl, half4)] = p.do("sync", lambda e, dst=dst, sl=sl: e.dma_start(
                    out=dst, in_=V["outsl"][:, :, :]), inc="dso", inc_n=16)
        dv_out = p.cnt["dv"]



def prep_inputs(inputs):
    f32 = np.float32
    bf = ml_dtypes.bfloat16
    x = np.asarray(inputs["L0_attn_hidden_states"], f32)
    relT = np.ascontiguousarray(np.asarray(inputs["L0_attn_rel_emb"], f32).T)
    rel_rev = np.ascontiguousarray(relT[:, ::-1])
    Mz = np.concatenate([np.asarray(inputs["L0_attn_M"], f32),
                         np.asarray(inputs["L0_attn_z"], f32)[:, :, None]], axis=2)
    mzp = np.ascontiguousarray(Mz.reshape(8, 128, 65))
    WqT = np.ascontiguousarray(np.asarray(inputs["L0_attn_Wq"], f32).T)
    WkT = np.ascontiguousarray(np.asarray(inputs["L0_attn_Wk"], f32).T)
    WvT = np.ascontiguousarray(np.asarray(inputs["L0_attn_Wv"], f32).T)
    WoT = np.ascontiguousarray(np.asarray(inputs["L0_attn_Wo"], f32).T)
    w1T = np.ascontiguousarray(np.asarray(inputs["L0_ffn_w1"], f32).T)
    w2T = np.ascontiguousarray(np.asarray(inputs["L0_ffn_w2"], f32).T)
    wq_p = np.ascontiguousarray(WqT.reshape(D, NP, 128).transpose(1, 0, 2))
    wk_p = np.ascontiguousarray(WkT.reshape(D, NP, 128).transpose(1, 0, 2))
    wv_p = np.ascontiguousarray(WvT.reshape(D, NP, 128).transpose(1, 0, 2))

    def col8(v):
        return np.ascontiguousarray(np.asarray(v, f32).reshape(8, 128).T)
    def col32(v):
        return np.ascontiguousarray(np.asarray(v, f32).reshape(32, 128).T)

    sel_r = np.zeros((4, 512), bf)
    sel_c = np.zeros((4, 4), bf)
    for j in range(4):
        sel_r[j, j * 128:(j + 1) * 128] = 1
        sel_c[j, j] = 1
    blkd = np.zeros((128, 2), bf)
    blkd[0:64, 0] = 1
    blkd[64:128, 1] = 1

    common = dict(
        wmod=np.asarray(inputs["L0_wmod"], f32),
        wq_p=wq_p, wk_p=wk_p, wv_p=wv_p, woT=WoT, w1T=w1T, w2T=w2T,
        mzp=mzp,
        bq_c=col8(inputs["L0_attn_bq"]), bk_c=col8(inputs["L0_attn_bk"]),
        bv_c=col8(inputs["L0_attn_bv"]), bo_c=col8(inputs["L0_attn_bo"]),
        b1_c=col32(inputs["L0_ffn_b1"]), b2_c=col8(inputs["L0_ffn_b2"]),
        galpha=np.asarray(inputs["L0_attn_gate_alpha"], f32).reshape(1, 1),
        ident=np.eye(128, dtype=bf),
        ones_r=np.ones((1, 128), bf),
        ones_rf=np.ones((1, 128), np.float32),
        onesc=np.ones((128, 1), bf),
        onescf=np.ones((128, 1), np.float32),
        blkd=blkd,
        o642=np.ones((64, 2), bf),
        i2=np.eye(2, dtype=bf),
        sel_r=sel_r, sel_c=sel_c,
    )
    relg0 = np.ascontiguousarray(np.concatenate([rel_rev, rel_rev], 0))
    relg1 = np.ascontiguousarray(np.concatenate([relT, relT], 0))
    in_maps = []
    for core in range(8):
        b, half = core // 2, core % 2
        xT = np.ascontiguousarray(x[b].T)
        if half == 1:
            xT = np.ascontiguousarray(xT[:, ::-1])
        m = dict(common)
        m["xT"] = xT
        m["relg"] = relg0 if half == 0 else relg1
        m["c_col"] = np.ascontiguousarray(
            np.asarray(inputs["L0_c"], f32)[b].reshape(8, 128).T)
        in_maps.append(m)
    return in_maps


def assemble(results):
    out = np.zeros((4, S, D), np.float32)
    for core in range(8):
        b, half = core // 2, core % 2
        oc = results[core]["out_cm"]
        if half == 0:
            out[b, 0:SQ, :] = oc.T
        else:
            out[b, SQ:S, :] = oc[:, ::-1].T
    return out


def kernel(**inputs):
    from concourse.bass_utils import run_bass_kernel_spmd
    in_maps = prep_inputs(inputs)
    nc = build_program(debug=0)
    res = run_bass_kernel_spmd(nc, in_maps, core_ids=list(range(8)))
    return assemble(res.results)
